# version 15
# baseline (speedup 1.0000x reference)
import numpy as np
import ml_dtypes
from contextlib import ExitStack

import concourse.bass as bass
import concourse.mybir as mybir
from concourse.bass_utils import run_bass_kernel_spmd
from concourse.alu_op_type import AluOpType as Alu

F32 = mybir.dt.float32
BF16 = mybir.dt.bfloat16
AFT = mybir.ActivationFunctionType
AX = mybir.AxisListType

D = 1024
S = 4096
B = 4
NH = 16
HD = 64
WIN = 128
LD = 256
LH = 4
NLAT = 64
INNER = 4096
T = 2048
NT = 16
NEG = -1.0e9
C_RND = 12582912.0
EPS = 1e-5

import os
PHLIM = int(os.environ.get("KPHLIM", "99"))
DBG = os.environ.get("KDBG", "0") == "1"

ARENA0 = 23552
ARENA_CAP = 229376


class Ctr:
    def __init__(self, es, nc, name):
        self.h = es.enter_context(nc.semaphore(name))
        self.n = 0
        self.name = name

    def tag(self, inst, k=1):
        if k:
            if not isinstance(inst, _StubInst) and inst is not None:
                inst.then_inc(self.h, k)
            self.n += k
        return self.n


class _StubInst:
    def then_inc(self, sem, k):
        return self


class _StubEng:
    def __getattr__(self, name):
        if name == "wait_ge":
            return lambda *a, **kw: None
        return lambda *a, **kw: _StubInst()


def build_nc():
    nc = bass.Bass("TRN2", target_bir_lowering=False)

    def din(name, shape, dt=F32):
        return nc.dram_tensor(name, list(shape), dt, kind="ExternalInput").ap()

    x_own = din("x_own", [T, D])
    x_halo = din("x_halo", [WIN, D])
    x_other = din("x_other", [T, D])
    qkv_w = din("qkv_w", [D, 3 * D])
    rq_w = din("rq_w", [D, LD])
    rkv_w = din("rkv_w", [D, 2 * LD])
    rout_w = din("rout_w", [LD, D])
    out_w = din("out_w", [D, D])
    wgT = din("wgT", [D, INNER])
    wuT = din("wuT", [D, INNER])
    wdT = din("wdT", [INNER, D])
    qkb_fm = din("qkb_fm", [128, 16])
    vb_bc = din("vb_bc", [128, D])
    rqb_fm = din("rqb_fm", [128, 2])
    rkvb_bc = din("rkvb_bc", [128, 2 * LD])
    routb_fm = din("routb_fm", [128, 8])
    outb_bc = din("outb_bc", [128, D])
    glgr_in = din("glgr_in", [128, 4])
    sel2_in = din("sel2_in", [128, 128])
    identb_in = din("identb_in", [128, 128], BF16)
    poolz_in = din("poolz_in", [128, 196], BF16)
    lmask = din("lmask", [2, 128, 512])
    latmask = din("latmask", [16, 128, 64])
    lbin = din("lbin", [128, 1])
    latbin = din("latbin", [128, 1])

    y = nc.dram_tensor("y", [T, D], F32, kind="ExternalOutput").ap()
    xq_scr = nc.dram_tensor("xq_scr", [NT, 128, D], BF16, kind="Internal").ap()
    attn_scr = nc.dram_tensor("attn_scr", [NT, 128, D], F32, kind="Internal").ap()
    h_scr = nc.dram_tensor("h_scr", [NT, 128, 8, 512], F32, kind="Internal").ap()

    dbg = {}
    if DBG:
        for nm, shape in [
            ("dbg_xn", [128, 8 * 2176]), ("dbg_sxr", [128, 16]),
            ("dbg_wsc", [128, 8]), ("dbg_rq", [128, 2 * 2048]),
            ("dbg_rk", [128, 2 * 64]), ("dbg_rkv1", [128, 512]),
            ("dbg_rkv2", [128, 512]), ("dbg_q", [128, 8 * 2048]),
            ("dbg_k", [128, 8 * 2176]), ("dbg_v", [128, 17 * 1024]),
            ("dbg_lat", [128, 2 * 2048]), ("dbg_loc", [128, 8 * 2048]),
            ("dbg_rem", [128, 8 * 2048]), ("dbg_mix", [128, 8 * 2048]),
            ("dbg_xq", [128, 8 * 2048]), ("dbg_pooled", [128, 8 * 192]),
        ]:
            dbg[nm] = nc.dram_tensor(nm, shape, F32, kind="ExternalOutput").ap()

    cursor = [0]

    def AT(name, shape, dt, off=None):
        nbytes = int(np.prod(shape[1:])) * (2 if dt == BF16 else 4)
        if off is None:
            off = ((cursor[0] + 63) // 64) * 64
        end = off + nbytes
        cursor[0] = max(cursor[0], end)
        assert end <= ARENA_CAP - ARENA0, (name, off, nbytes, end)
        t = nc.alloc_sbuf_tensor_at(name, list(shape), dt, offset=ARENA0 + off)
        return t.ap()

    identb = AT("identb", [128, 128], BF16)
    poolz = AT("poolz", [128, 196], BF16)
    lmask_t = AT("lmask_t", [128, 2, 512], F32)
    latmask_t = AT("latmask_t", [128, 16, 64], F32)
    lbin_t = AT("lbin_t", [128, 1], F32)
    latbin_t = AT("latbin_t", [128, 1], F32)
    qkb_t = AT("qkb_t", [128, 16], F32)
    vb_t = AT("vb_t", [128, D], F32)
    rqb_t = AT("rqb_t", [128, 2], F32)
    rkvb_t = AT("rkvb_t", [128, 2 * LD], F32)
    routb_t = AT("routb_t", [128, 8], F32)
    outb_t = AT("outb_t", [128, D], F32)
    glgr_f = AT("glgr_f", [128, 4], F32)
    glgr_b = AT("glgr_b", [128, 4], BF16)
    sel2_f = AT("sel2_f", [128, 128], F32)
    sel2_b = AT("sel2_b", [128, 128], BF16)
    onesf = AT("onesf", [128, 130], F32)
    sxr = AT("sxr", [128, NT], F32)
    gsc = AT("gsc", [128, NT], F32)
    usc = AT("usc", [128, NT], F32)
    wsc = AT("wsc", [128, 8], F32)
    absS = AT("absS", [128, 96], F32)
    stt = AT("stt", [128, 16], F32)
    mv = AT("mv", [128, 1], F32)
    t1 = AT("t1", [128, 1], F32)
    t2 = AT("t2", [128, 1], F32)
    t3 = AT("t3", [128, 1], F32)
    t4 = AT("t4", [128, 1], F32)
    rr = AT("rr", [128, 1], F32)
    r0 = AT("r0", [128, 1], F32)
    s_t = AT("s_t", [128, 2], F32)
    am = AT("am", [128, 1], F32)
    smax = AT("smax", [128, 8], F32)
    ssum = AT("ssum", [128, 8], F32)
    srcp = AT("srcp", [128, 8], F32)
    sh_t = AT("sh_t", [128, 2], F32)
    dsc_t = AT("dsc_t", [128, 2], F32)
    PERS_END = cursor[0]
    BIG0 = ((PERS_END + 63) // 64) * 64
    assert BIG0 <= 27648, BIG0

    XN = AT("XN", [128, 8, 2176], BF16, BIG0)
    Q0 = BIG0 + 34816
    q_fm = AT("q_fm", [128, 8, T], BF16, Q0)
    k_fm = AT("k_fm", [128, 8, 2176], BF16, BIG0 + 67584)
    V0 = BIG0 + 102400
    v_tm = AT("v_tm", [128, 17, D], BF16, V0)
    R0 = BIG0 + 137216
    rq_fm = AT("rq_fm", [128, 2, T], BF16, R0)
    xbuf = AT("xbuf", [128, 2, D], F32, Q0)
    xnf = AT("xnf", [128, D], F32, Q0 + 8192)
    xnb = AT("xnb", [128, 2, D], BF16, Q0 + 12288)
    xqb = AT("xqb", [128, 2, D], BF16, Q0 + 16384)
    qtmp = AT("qtmp", [128, D], F32, Q0 + 20480)
    sqd = AT("sqd", [128, 2, D], F32, Q0 + 24576)
    outw_b = AT("outw_b", [128, 8, D], BF16, V0)
    attn_buf = AT("attn_buf", [128, 2, D], F32, V0 + 16384)
    gate_sb = AT("gate_sb", [128, 512], BF16, R0)
    gdiff = AT("gdiff", [128, 512], BF16, R0 + 1024)
    gtmp = AT("gtmp", [128, 512], BF16, R0 + 2048)
    TR0 = BIG0 + 145408
    wqk_b = AT("wqk_b", [128, 2, 8, 128], BF16, TR0)
    wv_b = AT("wv_b", [128, 2, 8, 256], BF16, TR0 + 4096)
    rkvw_b = AT("rkvw_b", [128, 8, 512], BF16, TR0 + 4096)
    routw_b = AT("routw_b", [128, 2, D], BF16, TR0 + 4096)
    rkv1 = AT("rkv1", [128, 2 * LD], BF16, TR0 + 12288)
    rkv2 = AT("rkv2", [128, 2 * LD], BF16, TR0 + 13312)
    rk_fm = AT("rk_fm", [128, 2, NLAT], BF16, TR0 + 14336)
    ltmp = AT("ltmp", [128, 512], F32, TR0 + 19776)
    qz = AT("qz", [128, 2, T], BF16, TR0 + 16704)
    pooled_pad = AT("pooled_pad", [128, 8, 192], BF16, TR0 + 16704)
    pe_sb = AT("pe_sb", [128, 4, 512], BF16, TR0 + 24896)
    pt_sb = AT("pt_sb", [128, 2, 512], BF16, TR0 + 14656)
    latout = AT("latout", [128, 2, T], BF16, TR0 + 28992)
    absb = AT("absb", [128, 2, 1024], F32, TR0 + 16704)
    absd = AT("absd", [128, 2, 1024], BF16, TR0 + 28992)
    local_fm = AT("local_fm", [128, 8, T], BF16, BIG0)
    remote = AT("remote", [128, 8, T], BF16, Q0)
    mixed = AT("mixed", [128, 8, T], BF16, BIG0 + 67584)

    XQ = AT("XQ", [128, 8, T], BF16, BIG0)
    xql = AT("xql", [128, 2, D], BF16, BIG0 + 32768)
    wgq = AT("wgq", [128, 2, 8, 512], BF16, BIG0 + 36864)
    wuq = AT("wuq", [128, 2, 8, 512], BF16, BIG0 + 53248)
    sg_sb = AT("sg_sb", [128, 512], F32, BIG0 + 69632)
    hwb = AT("hwb", [128, 2, 4, 512], F32, BIG0 + 71680)
    wq32 = AT("wq32", [128, 2, 8, 512], F32, BIG0 + 88064)
    wtmp = AT("wtmp", [128, 8, 512], F32, BIG0 + 120832)
    wdq2 = AT("wdq2", [128, 32, D], BF16, BIG0 + 36864)
    hrb = AT("hrb", [128, 2, INNER], F32, BIG0)
    tq_sb = AT("tq_sb", [128, INNER], F32, BIG0 + 102400)
    xh_tm = AT("xh_tm", [128, INNER], BF16, BIG0 + 118784)
    xh_fm = AT("xh_fm", [128, 32, 128], BF16, BIG0 + 126976)
    atl = AT("atl", [128, 2, D], F32, BIG0 + 135168)
    xres = AT("xres", [128, 2, D], F32, BIG0 + 143360)
    obuf = AT("obuf", [128, 2, D], F32, BIG0 + 151552)
    wdf = AT("wdf", [128, 2, D], F32, BIG0 + 159744)
    wdt_r = AT("wdt_r", [128, 2, D], F32, BIG0 + 167936)
    assert BIG0 + 180544 <= 205824, BIG0

    pb = [nc.alloc_psum_tensor("pb%d" % i, [128, 512], F32).ap() for i in range(8)]

    es = ExitStack()
    dSp = [Ctr(es, nc, "dSa"), Ctr(es, nc, "dSb")]
    dGp = [Ctr(es, nc, "dGa"), Ctr(es, nc, "dGb")]
    dWp = [Ctr(es, nc, "dWa"), Ctr(es, nc, "dWb")]
    dAp = [Ctr(es, nc, "dAa"), Ctr(es, nc, "dAb")]
    dS, dSb = dSp
    dG, dGb = dGp
    dW, dWb = dWp
    cV = Ctr(es, nc, "cV")
    cS = Ctr(es, nc, "cS")
    cT = Ctr(es, nc, "cT")
    cG = Ctr(es, nc, "cG")
    ALLC = dSp + dGp + dWp + dAp + [cV, cS, cT, cG]

    blk = es.enter_context(nc.Block())

    def full_barrier():
        sn = [(c, c.n) for c in ALLC]

        def w_all(e):
            for c, v in sn:
                if v > 0:
                    e.wait_ge(c.h, v)
        blk.sync(w_all)
        blk.vector(w_all)
        blk.scalar(w_all)
        blk.tensor(w_all)
        blk.gpsimd(w_all)

    CK = {}

    def run_phase(name, sections):
        base = {c.name: c.n for c in ALLC}
        mode = ["count"]

        def ck(label):
            if mode[0] == "count":
                CK[(name, label)] = {c.name: c.n for c in ALLC}

        def w(e, label, ctr):
            if mode[0] == "count":
                return
            key = (name, label)
            if key in CK:
                v = CK[key][ctr.name]
                if v > 0:
                    e.wait_ge(ctr.h, v)

        stub = _StubEng()
        for _, gen in sections:
            gen(stub, ck, w)
        after = {c.name: c.n for c in ALLC}
        for c in ALLC:
            c.n = base[c.name]
        mode[0] = "real"
        for meth, gen in sections:
            meth(lambda e, g=gen: g(e, ck, w))
        for c in ALLC:
            assert c.n == after[c.name], (name, c.name, c.n, after[c.name])
        full_barrier()

    def dbg_dump(pairs):
        if not DBG:
            return

        def g(e):
            for nm, src in pairs:
                if nm in dbg:
                    sflat = src if len(src.shape) == 2 else src.rearrange(
                        "p a n -> p (a n)")
                    dG.tag(e.dma_start(out=dbg[nm], in_=sflat), 16)
            e.wait_ge(dG.h, dG.n)
        blk.gpsimd(g)
        full_barrier()

    def p0_sync(e, ck, w):
        p0i = [0]
        for dst, src in [
            (identb[:, :], identb_in[:, :]),
            (poolz[:, :], poolz_in[:, :]),
            (lmask_t[:, :, :], lmask[:, :, :].rearrange("a p n -> p a n")),
            (latmask_t[:, :, :], latmask[:, :, :].rearrange("a p n -> p a n")),
            (lbin_t[:, :], lbin[:, :]), (latbin_t[:, :], latbin[:, :]),
            (qkb_t[:, :], qkb_fm[:, :]), (vb_t[:, :], vb_bc[:, :]),
            (rqb_t[:, :], rqb_fm[:, :]), (rkvb_t[:, :], rkvb_bc[:, :]),
            (routb_t[:, :], routb_fm[:, :]), (outb_t[:, :], outb_bc[:, :]),
            (glgr_f[:, :], glgr_in[:, :]), (sel2_f[:, :], sel2_in[:, :]),
        ]:
            dSp[p0i[0] % 2].tag(e.dma_start(out=dst, in_=src), 16)
            p0i[0] += 1
        ck("loads")

    def p0_vec(e, ck, w):
        w(e, "loads", dS)
        w(e, "loads", dSb)
        e.tensor_copy(glgr_b[:, :], glgr_f[:, :])
        e.tensor_copy(sel2_b[:, :], sel2_f[:, :])
        e.memset(onesf[:, 0:1], 1.0)
        e.memset(onesf[0:1, 1:130], 1.0)
        cV.tag(e.memset(wsc[:, :], 0.0))

    run_phase("p0", [(blk.sync, p0_sync), (blk.vector, p0_vec)])

    if PHLIM < 1:
        es.close()
        return nc

    if PHLIM < 2:
        es.close()
        return nc

    tiles = [("halo", 0)] + [("own", i) for i in range(NT)] + \
            [("other", i) for i in range(NT)]
    n_pool = 2 * NT

    def a_src(kind, i):
        if kind == "halo":
            return x_halo[:, :]
        if kind == "own":
            return x_own[128 * i:128 * i + 128, :]
        return x_other[128 * i:128 * i + 128, :]

    def pa_pre(e, ck, w):
        ck("d0")

    NSLAB = 96

    def abs_src(sl):
        wg3 = wgT.rearrange("(c p) n -> p c n", p=128)
        wu3 = wuT.rearrange("(c p) n -> p c n", p=128)
        wd3 = wdT.rearrange("(c p) n -> p c n", p=128)
        mat, k = divmod(sl, 32)
        if mat == 0:
            return wg3[:, :, 128 * k:128 * k + 128]
        if mat == 1:
            return wu3[:, :, 128 * k:128 * k + 128]
        return wd3[:, k:k + 1, :]

    def slabs_for(j):
        return range((NSLAB * j + 32) // 33, (NSLAB * (j + 1) + 32) // 33)

    def pa_sync(e, ck, w):
        for j, (kind, i) in enumerate(tiles):
            if j >= 2:
                w(e, "v_xnb_%d" % (j - 2), cV)
                w(e, "s_sq_%d" % (j - 2), cS)
            dSp[j % 2].tag(e.dma_start(out=xbuf[:, j % 2, :],
                                       in_=a_src(kind, i)), 16)

    def pa_vec(e, ck, w):
        base = CK.get(("pa", "d0"))
        last_own = None
        for j, (kind, i) in enumerate(tiles):
            if base is not None and not isinstance(e, _StubEng):
                c = dSp[j % 2]
                e.wait_ge(c.h, base[c.name] + 16 * (j // 2 + 1))
            e.reduce_sum(stt[:, 0:1], xbuf[:, j % 2, :], axis=AX.X)
            w(e, "s_sq_%d" % j, cS)
            e.drain()
            e.tensor_scalar(out=mv[:, :], in0=stt[:, 0:1], scalar1=1.0 / D,
                            scalar2=None, op0=Alu.mult)
            e.tensor_scalar(out=t1[:, :], in0=stt[:, 1:2], scalar1=1.0 / D,
                            scalar2=EPS, op0=Alu.mult, op1=Alu.add)
            e.drain()
            e.tensor_scalar(out=t2[:, :], in0=mv[:, :], scalar1=mv[:, :],
                            scalar2=None, op0=Alu.mult)
            e.drain()
            cV.tag(e.tensor_tensor(out=t1[:, :], in0=t1[:, :], in1=t2[:, :],
                                   op=Alu.subtract))
            ck("v_var_%d" % j)
            w(e, "s_sqrt_%d" % j, cS)
            cV.tag(e.reciprocal(out=r0[:, :], in_=s_t[:, 0:1]))
            ck("v_r0_%d" % j)
            e.drain()
            e.tensor_scalar(out=t3[:, :], in0=r0[:, :], scalar1=r0[:, :],
                            scalar2=None, op0=Alu.mult)
            e.drain()
            e.tensor_scalar(out=t3[:, :], in0=t3[:, :], scalar1=t1[:, :],
                            scalar2=None, op0=Alu.mult)
            e.drain()
            e.tensor_scalar(out=t3[:, :], in0=t3[:, :], scalar1=-0.5,
                            scalar2=1.5, op0=Alu.mult, op1=Alu.add)
            e.drain()
            e.tensor_scalar(out=rr[:, :], in0=r0[:, :], scalar1=t3[:, :],
                            scalar2=None, op0=Alu.mult)
            e.drain()
            e.tensor_scalar(out=t4[:, :], in0=mv[:, :], scalar1=rr[:, :],
                            scalar2=-1.0, op0=Alu.mult, op1=Alu.mult)
            e.drain()
            if j >= 2:
                w(e, "t_done_%d" % (j - 2), cT)
            if kind == "own":
                if last_own is not None:
                    w(e, "s_rnd_%d" % last_own, cS)
                e.tensor_scalar(out=xnf[:, :], in0=xbuf[:, j % 2, :],
                                scalar1=rr[:, :], scalar2=t4[:, :],
                                op0=Alu.mult, op1=Alu.add)
                e.drain()
                cV.tag(e.tensor_copy(xnb[:, j % 2, :], xnf[:, :]))
                ck("v_xnb_%d" % j)
                e.reduce_max(am[:, :], xnf[:, :], axis=AX.X,
                             apply_absolute_value=True)
                e.drain()
                e.tensor_scalar(out=am[:, :], in0=am[:, :], scalar1=1e-5,
                                scalar2=None, op0=Alu.max)
                e.drain()
                e.reciprocal(out=t3[:, :], in_=am[:, :])
                e.drain()
                cV.tag(e.tensor_scalar(out=s_t[:, 1:2], in0=t3[:, :],
                                       scalar1=127.0, scalar2=None, op0=Alu.mult))
                ck("v_st_%d" % j)
                e.drain()
                e.reciprocal(out=sxr[:, i:i + 1], in_=s_t[:, 1:2])
                w(e, "s_rnd_%d" % j, cS)
                if i >= 2:
                    w(e, "g_xq_%d" % (i - 2), dGp[i % 2])
                cV.tag(e.tensor_scalar(out=xqb[:, i % 2, :], in0=qtmp[:, :],
                                       scalar1=C_RND, scalar2=None,
                                       op0=Alu.subtract))
                ck("v_xq_%d" % j)
                last_own = j
            else:
                cV.tag(e.tensor_scalar(out=xnb[:, j % 2, :], in0=xbuf[:, j % 2, :],
                                       scalar1=rr[:, :], scalar2=t4[:, :],
                                       op0=Alu.mult, op1=Alu.add))
                ck("v_xnb_%d" % j)
            if kind != "other":
                w(e, "t_tr_%d" % j, cT)
                c0 = 0 if kind == "halo" else 128 + 128 * i
                bA, bB = pb[2 * (j % 2)], pb[2 * (j % 2) + 1]
                e.tensor_copy(XN[:, 0:4, c0:c0 + 128],
                              bA[:, :].rearrange("p (a n) -> p a n", a=4))
                cV.tag(e.tensor_copy(XN[:, 4:8, c0:c0 + 128],
                                     bB[:, :].rearrange("p (a n) -> p a n", a=4)))
            ck("v_evac_%d" % j)

    def pa_scalar(e, ck, w):
        base = CK.get(("pa", "d0"))
        last_own = None
        for j, (kind, i) in enumerate(tiles):
            if base is not None and not isinstance(e, _StubEng):
                c = dSp[j % 2]
                e.wait_ge(c.h, base[c.name] + 16 * (j // 2 + 1))
            if j >= 1:
                w(e, "v_var_%d" % (j - 1), cV)
            cS.tag(e.activation(out=sqd[:, j % 2, :], in_=xbuf[:, j % 2, :],
                                func=AFT.Square, accum_out=stt[:, 1:2]))
            ck("s_sq_%d" % j)
            w(e, "v_var_%d" % j, cV)
            if j >= 1:
                w(e, "v_r0_%d" % (j - 1), cV)
            cS.tag(e.activation(out=s_t[:, 0:1], in_=t1[:, :], func=AFT.Sqrt))
            ck("s_sqrt_%d" % j)
            if kind == "own":
                w(e, "v_st_%d" % j, cV)
                if last_own is not None:
                    w(e, "v_xq_%d" % last_own, cV)
                cS.tag(e.activation(out=qtmp[:, :], in_=xnf[:, :], func=AFT.Copy,
                                    scale=s_t[:, 1:2], bias=C_RND))
                ck("s_rnd_%d" % j)
                last_own = j
            for sl in slabs_for(j):
                if base is not None and not isinstance(e, _StubEng):
                    c = dAp[sl % 2]
                    e.wait_ge(c.h, base[c.name] + 16 * (sl // 2 + 1))
                cS.tag(e.activation(out=absd[:, sl % 2, :],
                                    in_=absb[:, sl % 2, :],
                                    func=AFT.Abs,
                                    accum_out=absS[:, sl:sl + 1]))
                ck("abs_%d" % sl)

    def pa_tensor(e, ck, w):
        pool_seq = 0
        for j, (kind, i) in enumerate(tiles):
            w(e, "v_xnb_%d" % j, cV)
            if kind != "other":
                if j >= 2:
                    w(e, "v_evac_%d" % (j - 2), cV)
                bA, bB = pb[2 * (j % 2)], pb[2 * (j % 2) + 1]
                for c in range(8):
                    bnk = bA if c < 4 else bB
                    inst = e.matmul(bnk[:, 128 * (c % 4):128 * (c % 4) + 128],
                                    xnb[:, j % 2, 128 * c:128 * c + 128],
                                    identb[:, :], start=True, stop=True)
                    if c == 7:
                        cT.tag(inst)
                ck("t_tr_%d" % j)
            if kind != "halo":
                if kind == "own":
                    z = poolz[:, 64 - 2 * i:192 - 2 * i]
                else:
                    z = poolz[:, 32 - 2 * i:160 - 2 * i]
                first = pool_seq == 0
                last = pool_seq == n_pool - 1
                e.matmul(pb[6][:, :], z, xnb[:, j % 2, 0:512],
                         start=first, stop=last)
                cT.tag(e.matmul(pb[7][:, :], z, xnb[:, j % 2, 512:1024],
                                start=first, stop=last))
                pool_seq += 1
            ck("t_done_%d" % j)

    def pa_gp(e, ck, w):
        for j, (kind, i) in enumerate(tiles):
            for sl in slabs_for(j):
                if sl >= 2:
                    w(e, "abs_%d" % (sl - 2), cS)
                src = abs_src(sl)
                dAp[sl % 2].tag(e.dma_start(
                    out=absb[:, sl % 2, :].rearrange(
                        "p (a n) -> p a n", a=src.shape[1]),
                    in_=src), 16)
            if kind == "own":
                w(e, "v_xq_%d" % j, cV)
                dGp[i % 2].tag(e.dma_start(out=xq_scr[i, :, :],
                                           in_=xqb[:, i % 2, :]), 16)
                ck("g_xq_%d" % i)

    run_phase("pa", [
        (blk.sync, pa_pre), (blk.sync, pa_sync), (blk.vector, pa_vec),
        (blk.scalar, pa_scalar), (blk.tensor, pa_tensor), (blk.gpsimd, pa_gp),
    ])
    dbg_dump([("dbg_xn", XN), ("dbg_sxr", sxr[:, :])])

    if PHLIM < 3:
        es.close()
        return nc

    def pa2a_gp(e, ck, w):
        rkv3 = rkv_w.rearrange("(c p) n -> p c n", p=128)
        rq3 = rq_w.rearrange("(c p) n -> p c n", p=128)
        dW.tag(e.dma_start(out=rkvw_b[:, :, :], in_=rkv3), 16)
        dWb.tag(e.dma_start(out=wqk_b[:, 0, :, :], in_=rq3[:, :, 0:128]), 16)
        dW.tag(e.dma_start(out=wqk_b[:, 1, :, :], in_=rq3[:, :, 128:256]), 16)
        e.wait_ge(dW.h, dW.n)
        e.wait_ge(dWb.h, dWb.n)

    run_phase("pa2a", [(blk.gpsimd, pa2a_gp)])

    def pa2_vec(e, ck, w):
        e.reduce_sum(stt[:, 8:9], absS[:, 0:32], axis=AX.X)
        e.reduce_sum(stt[:, 9:10], absS[:, 32:64], axis=AX.X)
        cV.tag(e.reduce_sum(stt[:, 10:11], absS[:, 64:96], axis=AX.X))
        ck("asums")
        e.memset(pooled_pad[:, :, 0:64], 0.0)
        cV.tag(e.memset(pooled_pad[:, :, 128:192], 0.0))
        e.tensor_scalar(out=qtmp[0:64, 0:512], in0=pb[6][0:64, :],
                        scalar1=1.0 / 64.0, scalar2=None, op0=Alu.mult)
        e.tensor_scalar(out=qtmp[0:64, 512:1024], in0=pb[7][0:64, :],
                        scalar1=1.0 / 64.0, scalar2=None, op0=Alu.mult)
        e.drain()
        cV.tag(e.tensor_copy(xnb[0:64, 0, :], qtmp[0:64, :]))
        ck("pooled_lm")
        w(e, "wsc_mm1", cT)
        cV.tag(e.tensor_copy(srcp[0:1, 0:3], pb[6][0:1, 0:3]))
        ck("wsc_row")
        w(e, "wsc_mm2", cT)
        e.tensor_scalar(out=wsc[:, 3:6], in0=pb[6][:, 0:3],
                        scalar1=1.0 / (D * INNER), scalar2=1e-5,
                        op0=Alu.mult, op1=Alu.max)
        e.drain()
        e.reciprocal(out=wsc[:, 0:1], in_=wsc[:, 3:4])
        e.reciprocal(out=wsc[:, 1:2], in_=wsc[:, 4:5])
        cV.tag(e.reciprocal(out=wsc[:, 2:3], in_=wsc[:, 5:6]))
        ck("wsc_fin")
        w(e, "t_ptr", cT)
        e.tensor_copy(pooled_pad[:, 0:4, 64:128],
                      pb[0][:, :].rearrange("p (a n) -> p a n", a=4)[:, :, 0:64])
        cV.tag(e.tensor_copy(pooled_pad[:, 4:8, 64:128],
                             pb[1][:, :].rearrange("p (a n) -> p a n", a=4)[:, :, 0:64]))
        ck("ppad")
        w(e, "t_rkv1", cT)
        e.tensor_tensor(out=ltmp[0:64, :], in0=pb[2][0:64, :],
                        in1=rkvb_t[0:64, :], op=Alu.add)
        e.drain()
        e.tensor_copy(rkv1[0:64, :], ltmp[0:64, :])
        cV.tag(e.memset(rkv1[64:128, :], 0.0))
        ck("rkv1")
        w(e, "t_rkv2", cT)
        e.tensor_tensor(out=ltmp[64:128, :], in0=pb[3][64:128, :],
                        in1=rkvb_t[64:128, :], op=Alu.add)
        e.drain()
        e.tensor_copy(rkv2[64:128, :], ltmp[64:128, :])
        cV.tag(e.memset(rkv2[0:64, :], 0.0))
        ck("rkv2")
        w(e, "t_rkT", cT)
        e.tensor_copy(rk_fm[:, 0, :], pb[4][:, 0:64])
        cV.tag(e.tensor_copy(rk_fm[:, 1, :], pb[5][:, 0:64]))
        ck("rk")
        for oc in range(2):
            w(e, "t_rq_%d" % oc, cT)
            for n in range(4):
                inst = e.tensor_scalar(
                    out=rq_fm[:, oc, 512 * n:512 * n + 512],
                    in0=pb[4 * oc + n][:, :],
                    scalar1=rqb_t[:, oc:oc + 1], scalar2=None, op0=Alu.add)
                if n == 3:
                    cV.tag(inst)
            ck("v_rq_%d" % oc)
        for tt in range(NT):
            e.tensor_scalar(out=gsc[:, tt:tt + 1], in0=sxr[:, tt:tt + 1],
                            scalar1=wsc[:, 3:4], scalar2=None, op0=Alu.mult)
            inst = e.tensor_scalar(out=usc[:, tt:tt + 1], in0=sxr[:, tt:tt + 1],
                                   scalar1=wsc[:, 4:5], scalar2=None, op0=Alu.mult)
            if tt == NT - 1:
                cV.tag(inst)
        ck("scales")

    def pa2_tensor(e, ck, w):
        w(e, "pooled_lm", cV)
        w(e, "asums", cV)
        cT.tag(e.matmul(pb[6][0:1, 0:3], onesf[:, 0:1], stt[:, 8:11],
                        start=True, stop=True))
        ck("wsc_mm1")
        w(e, "wsc_row", cV)
        cT.tag(e.matmul(pb[6][:, 0:3], onesf[0:1, 1:129], srcp[0:1, 0:3],
                        start=True, stop=True))
        ck("wsc_mm2")
        for c in range(8):
            bnk = pb[0] if c < 4 else pb[1]
            inst = e.matmul(bnk[:, 128 * (c % 4):128 * (c % 4) + 128],
                            xnb[0:64, 0, 128 * c:128 * c + 128],
                            identb[0:64, :], start=True, stop=True)
            if c == 7:
                cT.tag(inst)
        ck("t_ptr")
        w(e, "ppad", cV)
        for kk in range(8):
            inst = e.matmul(pb[2][:, :], pooled_pad[:, kk, 64:192],
                            rkvw_b[:, kk, :], start=(kk == 0), stop=(kk == 7))
            if kk == 7:
                cT.tag(inst)
        ck("t_rkv1")
        for kk in range(8):
            inst = e.matmul(pb[3][:, :], pooled_pad[:, kk, 0:128],
                            rkvw_b[:, kk, :], start=(kk == 0), stop=(kk == 7))
            if kk == 7:
                cT.tag(inst)
        ck("t_rkv2")
        w(e, "rkv1", cV)
        e.matmul(pb[4][:, 0:64], rkv1[0:64, 0:128], identb[0:64, 0:64],
                 start=True, stop=True)
        cT.tag(e.matmul(pb[5][:, 0:64], rkv1[0:64, 128:256], identb[0:64, 0:64],
                        start=True, stop=True))
        ck("t_rkT")
        w(e, "rkv2", cV)
        w(e, "rk", cV)
        for oc in range(2):
            if oc == 1:
                w(e, "v_rq_0", cV)
            for n in range(4):
                for kk in range(8):
                    inst = e.matmul(pb[4 * oc + n][:, :],
                                    wqk_b[:, oc, kk, :],
                                    XN[:, kk, 128 + 512 * n:128 + 512 * n + 512],
                                    start=(kk == 0), stop=(kk == 7))
                    if kk == 7 and n == 3:
                        cT.tag(inst)
            ck("t_rq_%d" % oc)

    run_phase("pa2", [(blk.vector, pa2_vec), (blk.tensor, pa2_tensor)])
    dbg_dump([("dbg_rq", rq_fm), ("dbg_rk", rk_fm),
              ("dbg_rkv1", rkv1[:, :]), ("dbg_rkv2", rkv2[:, :]),
              ("dbg_pooled", pooled_pad)])

    if PHLIM < 4:
        es.close()
        return nc

    def pq_pre(e, ck, w):
        ck("d0")

    def pq_gp(e, ck, w):
        w3 = qkv_w.rearrange("(c p) n -> p c n", p=128)
        for oc in range(16):
            if oc >= 2:
                w(e, "t_wfree_%d" % (oc - 2), cT)
            dWp[oc % 2].tag(e.dma_start(out=wqk_b[:, oc % 2, :, :],
                               in_=w3[:, :, 128 * oc:128 * oc + 128]), 16)
        for vs in range(4):
            if vs >= 2:
                w(e, "t_v_%d_%d" % (vs - 2, 16), cT)
            else:
                w(e, "t_wfree_%d" % (14 + vs), cT)
            dWp[vs % 2].tag(e.dma_start(out=wv_b[:, vs % 2, :, :],
                               in_=w3[:, :, 2048 + 256 * vs:2048 + 256 * vs + 256]),
                   16)

    def pq_vec(e, ck, w):
        for oc in range(16):
            if oc < 8:
                for n in range(4):
                    w(e, "t_qk_%d_%d" % (oc, n), cT)
                    bnk = pb[n + 4 * (oc % 2)]
                    cV.tag(e.tensor_scalar(out=q_fm[:, oc, 512 * n:512 * n + 512],
                                           in0=bnk[:, :],
                                           scalar1=qkb_t[:, oc:oc + 1],
                                           scalar2=None, op0=Alu.add))
            else:
                ko = oc - 8
                for n in range(5):
                    w(e, "t_k_%d_%d" % (ko, n), cT)
                    bnk = pb[(n % 4) + 4 * (oc % 2)]
                    c0 = 512 * n
                    wd = 512 if n < 4 else 128
                    cV.tag(e.tensor_scalar(out=k_fm[:, ko, c0:c0 + wd],
                                           in0=bnk[:, 0:wd],
                                           scalar1=qkb_t[:, oc:oc + 1],
                                           scalar2=None, op0=Alu.add))
                    ck("v_kev_%d_%d" % (ko, n))
            ck("v_evac_%d" % oc)
        for vs in range(4):
            for tt in range(17):
                w(e, "t_v_%d_%d" % (vs, tt), cT)
                bnk = pb[(tt % 4) + 4 * (vs % 2)]
                cV.tag(e.tensor_tensor(out=v_tm[:, tt, 256 * vs:256 * vs + 256],
                                       in0=bnk[:, 0:256],
                                       in1=vb_t[:, 256 * vs:256 * vs + 256],
                                       op=Alu.add))
                ck("v_vev_%d_%d" % (vs, tt))

    def pq_tensor(e, ck, w):
        base = CK.get(("pq", "d0"))
        for oc in range(16):
            if base is not None and not isinstance(e, _StubEng):
                c = dWp[oc % 2]
                e.wait_ge(c.h, base[c.name] + 16 * (oc // 2 + 1))
            if oc < 8:
                for n in range(4):
                    if oc >= 2:
                        w(e, "v_evac_%d" % (oc - 2), cV)
                    bnk = pb[n + 4 * (oc % 2)]
                    for kk in range(8):
                        inst = e.matmul(bnk[:, :], wqk_b[:, oc % 2, kk, :],
                                        XN[:, kk, 128 + 512 * n:128 + 512 * n + 512],
                                        start=(kk == 0), stop=(kk == 7))
                        if kk == 7:
                            cT.tag(inst)
                    ck("t_qk_%d_%d" % (oc, n))
            else:
                ko = oc - 8
                for n in range(5):
                    if oc >= 2:
                        w(e, "v_evac_%d" % (oc - 2), cV)
                    if n == 4:
                        w(e, "v_kev_%d_0" % ko, cV)
                    bnk = pb[(n % 4) + 4 * (oc % 2)]
                    c0 = 512 * n
                    wd = 512 if n < 4 else 128
                    for kk in range(8):
                        inst = e.matmul(bnk[:, 0:wd], wqk_b[:, oc % 2, kk, :],
                                        XN[:, kk, c0:c0 + wd],
                                        start=(kk == 0), stop=(kk == 7))
                        if kk == 7:
                            cT.tag(inst)
                    ck("t_k_%d_%d" % (ko, n))
            ck("t_wfree_%d" % oc)
        for vs in range(4):
            if base is not None and not isinstance(e, _StubEng):
                c = dWp[vs % 2]
                e.wait_ge(c.h, base[c.name] + 16 * 8 + 16 * (vs // 2 + 1))
            if vs < 2:
                w(e, "v_evac_%d" % (14 + vs), cV)
            for tt in range(17):
                if tt >= 4:
                    w(e, "v_vev_%d_%d" % (vs, tt - 4), cV)
                elif vs >= 2:
                    w(e, "v_vev_%d_%d" % (vs - 2, 13 + tt), cV)
                bnk = pb[(tt % 4) + 4 * (vs % 2)]
                for kk in range(8):
                    inst = e.matmul(bnk[:, 0:256], XN[:, kk, 128 * tt:128 * tt + 128],
                                    wv_b[:, vs % 2, kk, :],
                                    start=(kk == 0), stop=(kk == 7))
                    if kk == 7:
                        cT.tag(inst)
                ck("t_v_%d_%d" % (vs, tt))

    run_phase("pq", [
        (blk.gpsimd, pq_pre), (blk.gpsimd, pq_gp),
        (blk.vector, pq_vec), (blk.tensor, pq_tensor),
    ])
    dbg_dump([("dbg_q", q_fm), ("dbg_k", k_fm), ("dbg_v", v_tm)])

    if PHLIM < 5:
        es.close()
        return nc

    def plat_vec(e, ck, w):
        e.memset(qz[0:64, 1, :], 0.0)
        e.memset(qz[64:128, 0, :], 0.0)
        nit = 2 * NT
        for it in range(nit + 2):
            nstt = 0
            if it < nit:
                lc, qt = divmod(it, NT)
                if qt == 0:
                    if lc >= 1:
                        w(e, "t_sc_%d" % (lc * NT - 1), cT)
                    e.tensor_copy(qz[0:64, 0, :], rq_fm[0:64, lc, :])
                    cV.tag(e.tensor_copy(qz[64:128, 1, :], rq_fm[64:128, lc, :]))
                    ck("v_qz_%d" % lc)
                bs = 4 * (it % 2)
                w(e, "t_sc_%d" % it, cT)
                e.scalar_tensor_tensor(
                    out=pb[bs][:, 0:64], in0=pb[bs][:, 0:64], scalar=0.125,
                    in1=latmask_t[:, qt, :], op0=Alu.mult, op1=Alu.add)
                nstt = cV.tag(e.scalar_tensor_tensor(
                    out=pb[bs][:, 64:128], in0=pb[bs][:, 64:128], scalar=0.125,
                    in1=latmask_t[:, qt, :], op0=Alu.mult, op1=Alu.add))
            nrcp = 0
            if it >= 2:
                p2 = it - 2
                p = p2 % 2
                w(e, "s_exp_%d" % p2, cS)
                e.reciprocal(out=srcp[:, 2 * p:2 * p + 1],
                             in_=ssum[:, 2 * p:2 * p + 1])
                nrcp = cV.tag(e.reciprocal(out=srcp[:, 2 * p + 1:2 * p + 2],
                                           in_=ssum[:, 2 * p + 1:2 * p + 2]))
            if it < nit:
                e.wait_ge(cV.h, nstt)
                cV.tag(e.reduce_max(smax[:, it % 2:it % 2 + 1],
                                    pb[4 * (it % 2)][:, 0:128],
                                    axis=AX.X, negate=True))
                ck("v_max_%d" % it)
            if it >= 2:
                p2 = it - 2
                p = p2 % 2
                p4 = p2 % 4
                sc2 = latbin_t[:, 0:1] if (p2 % NT) == 0 else 1.0
                e.wait_ge(cV.h, nrcp)
                e.tensor_scalar(out=pe_sb[:, p4, 0:64], in0=pe_sb[:, p4, 0:64],
                                scalar1=srcp[:, 2 * p:2 * p + 1], scalar2=sc2,
                                op0=Alu.mult, op1=Alu.mult)
                cV.tag(e.tensor_scalar(out=pe_sb[:, p4, 64:128],
                                       in0=pe_sb[:, p4, 64:128],
                                       scalar1=srcp[:, 2 * p + 1:2 * p + 2],
                                       scalar2=sc2, op0=Alu.mult, op1=Alu.mult))
                ck("v_pe_%d" % p2)
                w(e, "t_tr_%d" % p2, cT)
                cV.tag(e.tensor_copy(pt_sb[:, p, 0:128],
                                     pb[4 * (p2 % 2) + 1][:, 0:128]))
                ck("v_pt_%d" % p2)

    def plat_scalar(e, ck, w):
        nit = 2 * NT
        for it in range(nit + 4):
            if it < nit:
                p4 = it % 4
                w(e, "v_max_%d" % it, cV)
                if it >= 4:
                    w(e, "t_tr_%d" % (it - 4), cT)
                e.activation(out=pe_sb[:, p4, 0:64], in_=pb[4 * (it % 2)][:, 0:64],
                             func=AFT.Exp, bias=smax[:, it % 2:it % 2 + 1],
                             scale=1.0, accum_out=ssum[:, 2 * (it % 2):2 * (it % 2) + 1])
                cS.tag(e.activation(out=pe_sb[:, p4, 64:128],
                                    in_=pb[4 * (it % 2)][:, 64:128],
                                    func=AFT.Exp, bias=smax[:, it % 2:it % 2 + 1],
                                    scale=1.0,
                                    accum_out=ssum[:, 2 * (it % 2) + 1:2 * (it % 2) + 2]))
                ck("s_exp_%d" % it)
            if it >= 4:
                p4o = it - 4
                lc, qt = divmod(p4o, NT)
                bs = 4 * (p4o % 2)
                w(e, "t_av_%d" % p4o, cT)
                e.copy(latout[0:64, lc, 128 * qt:128 * qt + 128],
                       pb[bs + 2][0:64, 0:128])
                cS.tag(e.copy(latout[64:128, lc, 128 * qt:128 * qt + 128],
                              pb[bs + 2][64:128, 128:256]))
                ck("s_out_%d" % p4o)

    def plat_tensor(e, ck, w):
        nit = 2 * NT

        def emit_sc(e, it):
            lc, qt = divmod(it, NT)
            bs = 4 * (it % 2)
            if qt == 0:
                w(e, "v_qz_%d" % lc, cV)
            if it >= 2:
                w(e, "s_exp_%d" % (it - 2), cS)
            for sub in range(2):
                inst = e.matmul(pb[bs][:, 64 * sub:64 * sub + 64],
                                qz[:, sub, 128 * qt:128 * qt + 128],
                                rk_fm[:, lc, :], start=True, stop=True)
                if sub == 1:
                    cT.tag(inst)
            ck("t_sc_%d" % it)

        emit_sc(e, 0)
        for it in range(nit + 3):
            if 0 <= it - 2 < nit:
                p2 = it - 2
                w(e, "v_pe_%d" % p2, cV)
                if p2 >= 2:
                    w(e, "v_pt_%d" % (p2 - 2), cV)
                cT.tag(e.matmul(pb[4 * (p2 % 2) + 1][:, 0:128],
                                pe_sb[:, p2 % 4, 0:128],
                                identb[:, :], start=True, stop=True))
                ck("t_tr_%d" % p2)
            if 0 <= it - 3 < nit:
                p3 = it - 3
                lc = p3 // NT
                bs = 4 * (p3 % 2)
                w(e, "v_pt_%d" % p3, cV)
                if p3 >= 2:
                    w(e, "s_out_%d" % (p3 - 2), cS)
                e.matmul(pb[bs + 2][:, 0:128],
                         rkv1[:, LD + 128 * lc:LD + 128 * lc + 128],
                         pt_sb[:, p3 % 2, 0:128], start=True, stop=True)
                cT.tag(e.matmul(pb[bs + 2][:, 128:256],
                                rkv2[:, LD + 128 * lc:LD + 128 * lc + 128],
                                pt_sb[:, p3 % 2, 0:128], start=True, stop=True))
                ck("t_av_%d" % p3)
            if it + 1 < nit:
                emit_sc(e, it + 1)

    run_phase("plat", [
        (blk.vector, plat_vec), (blk.scalar, plat_scalar),
        (blk.tensor, plat_tensor),
    ])
    dbg_dump([("dbg_lat", latout)])

    if PHLIM < 6:
        es.close()
        return nc

    def ploc_vec(e, ck, w):
        nit = 8 * NT
        for it in range(nit + 2):
            nstt = 0
            if it < nit:
                c, qt = divmod(it, NT)
                if qt == 0:
                    if c >= 1:
                        w(e, "t_sc_%d" % (c * NT - 1), cT)
                    e.tensor_copy(qz[0:64, 0, :], q_fm[0:64, c, :])
                    cV.tag(e.tensor_copy(qz[64:128, 1, :], q_fm[64:128, c, :]))
                    ck("v_qz_%d" % c)
                bs = 3 * (it % 2)
                w(e, "t_sc_%d" % it, cT)
                mk = lmask_t[:, 0, :] if qt == 0 else lmask_t[:, 1, :]
                nstt = cV.tag(e.scalar_tensor_tensor(
                    out=pb[bs][:, :], in0=pb[bs][:, :], scalar=0.125,
                    in1=mk, op0=Alu.mult, op1=Alu.add))
            nrcp = 0
            if it >= 2:
                p2 = it - 2
                p = p2 % 2
                w(e, "s_exp_%d" % p2, cS)
                e.reciprocal(out=srcp[:, 2 * p:2 * p + 1],
                             in_=ssum[:, 2 * p:2 * p + 1])
                nrcp = cV.tag(e.reciprocal(out=srcp[:, 2 * p + 1:2 * p + 2],
                                           in_=ssum[:, 2 * p + 1:2 * p + 2]))
            if it < nit:
                e.wait_ge(cV.h, nstt)
                cV.tag(e.reduce_max(smax[:, it % 2:it % 2 + 1],
                                    pb[3 * (it % 2)][:, :],
                                    axis=AX.X, negate=True))
                ck("v_max_%d" % it)
            if it >= 2:
                p2 = it - 2
                p = p2 % 2
                p4 = p2 % 4
                sc2 = lbin_t[:, 0:1] if (p2 % NT) == 0 else 1.0
                e.wait_ge(cV.h, nrcp)
                e.tensor_scalar(out=pe_sb[:, p4, 0:256], in0=pe_sb[:, p4, 0:256],
                                scalar1=srcp[:, 2 * p:2 * p + 1], scalar2=sc2,
                                op0=Alu.mult, op1=Alu.mult)
                cV.tag(e.tensor_scalar(out=pe_sb[:, p4, 256:512],
                                       in0=pe_sb[:, p4, 256:512],
                                       scalar1=srcp[:, 2 * p + 1:2 * p + 2],
                                       scalar2=sc2, op0=Alu.mult, op1=Alu.mult))
                ck("v_pe_%d" % p2)
                w(e, "t_tr_%d" % p2, cT)
                cV.tag(e.tensor_copy(pt_sb[:, p, :], pb[3 * (p2 % 2) + 1][:, :]))
                ck("v_pt_%d" % p2)

    def ploc_scalar(e, ck, w):
        nit = 8 * NT
        for it in range(nit + 4):
            if it < nit:
                p4 = it % 4
                bs = 3 * (it % 2)
                w(e, "v_max_%d" % it, cV)
                if it >= 4:
                    w(e, "t_tr_%d" % (it - 4), cT)
                e.activation(out=pe_sb[:, p4, 0:256], in_=pb[bs][:, 0:256],
                             func=AFT.Exp, bias=smax[:, it % 2:it % 2 + 1],
                             scale=1.0,
                             accum_out=ssum[:, 2 * (it % 2):2 * (it % 2) + 1])
                cS.tag(e.activation(out=pe_sb[:, p4, 256:512],
                                    in_=pb[bs][:, 256:512],
                                    func=AFT.Exp, bias=smax[:, it % 2:it % 2 + 1],
                                    scale=1.0,
                                    accum_out=ssum[:, 2 * (it % 2) + 1:2 * (it % 2) + 2]))
                ck("s_exp_%d" % it)
            if it >= 4:
                p4o = it - 4
                c, qt = divmod(p4o, NT)
                bs2 = 3 * (p4o % 2) + 2
                w(e, "t_av_%d" % p4o, cT)
                e.copy(local_fm[0:64, c, 128 * qt:128 * qt + 128],
                       pb[bs2][0:64, 0:128])
                cS.tag(e.copy(local_fm[64:128, c, 128 * qt:128 * qt + 128],
                              pb[bs2][64:128, 128:256]))
                ck("s_out_%d" % p4o)

    def ploc_tensor(e, ck, w):
        nit = 8 * NT

        def emit_sc(e, it):
            c, qt = divmod(it, NT)
            bs = 3 * (it % 2)
            if qt == 0:
                w(e, "v_qz_%d" % c, cV)
            if it >= 2:
                w(e, "s_exp_%d" % (it - 2), cS)
            for sub in range(2):
                inst = e.matmul(pb[bs][:, 256 * sub:256 * sub + 256],
                                qz[:, sub, 128 * qt:128 * qt + 128],
                                k_fm[:, c, 128 * qt:128 * qt + 256],
                                start=True, stop=True)
                if sub == 1:
                    cT.tag(inst)
            ck("t_sc_%d" % it)

        emit_sc(e, 0)
        for it in range(nit + 3):
            if 0 <= it - 2 < nit:
                p2 = it - 2
                w(e, "v_pe_%d" % p2, cV)
                if p2 >= 2:
                    w(e, "v_pt_%d" % (p2 - 2), cV)
                for q4 in range(4):
                    inst = e.matmul(pb[3 * (p2 % 2) + 1][:, 128 * q4:128 * q4 + 128],
                                    pe_sb[:, p2 % 4, 128 * q4:128 * q4 + 128],
                                    identb[:, :], start=True, stop=True)
                    if q4 == 3:
                        cT.tag(inst)
                ck("t_tr_%d" % p2)
            if 0 <= it - 3 < nit:
                p3 = it - 3
                c, qt = divmod(p3, NT)
                bs = 3 * (p3 % 2)
                w(e, "v_pt_%d" % p3, cV)
                if p3 >= 2:
                    w(e, "s_out_%d" % (p3 - 2), cS)
                for sub in range(2):
                    for kc in range(2):
                        inst = e.matmul(
                            pb[bs + 2][:, 128 * sub:128 * sub + 128],
                            v_tm[:, qt + kc, 128 * c:128 * c + 128],
                            pt_sb[:, p3 % 2, 256 * sub + 128 * kc:256 * sub + 128 * kc + 128],
                            start=(kc == 0), stop=(kc == 1))
                        if sub == 1 and kc == 1:
                            cT.tag(inst)
                ck("t_av_%d" % p3)
            if it + 1 < nit:
                emit_sc(e, it + 1)

    run_phase("ploc", [
        (blk.vector, ploc_vec), (blk.scalar, ploc_scalar),
        (blk.tensor, ploc_tensor),
    ])
    dbg_dump([("dbg_loc", local_fm)])

    if PHLIM < 7:
        es.close()
        return nc

    def prout_gp(e, ck, w):
        ro3 = rout_w.rearrange("(c p) n -> p c n", p=128)
        dW.tag(e.dma_start(out=routw_b[:, :, :], in_=ro3), 16)
        e.wait_ge(dW.h, dW.n)

    def prout_vec(e, ck, w):
        for g in range(32):
            oc, n = divmod(g, 4)
            w(e, "t_ro_%d" % g, cT)
            cV.tag(e.tensor_scalar(out=remote[:, oc, 512 * n:512 * n + 512],
                                   in0=pb[g % 8][:, :],
                                   scalar1=routb_t[:, oc:oc + 1],
                                   scalar2=None, op0=Alu.add))
            ck("v_ro_%d" % g)

    def prout_tensor(e, ck, w):
        rw = routw_b[:, :, :].rearrange("p a n -> p (a n)")
        for g in range(32):
            oc, n = divmod(g, 4)
            if g >= 8:
                w(e, "v_ro_%d" % (g - 8), cV)
            for lc in range(2):
                inst = e.matmul(pb[g % 8][:, :],
                                rw[:, 1024 * lc + 128 * oc:1024 * lc + 128 * oc + 128],
                                latout[:, lc, 512 * n:512 * n + 512],
                                start=(lc == 0), stop=(lc == 1))
                if lc == 1:
                    cT.tag(inst)
            ck("t_ro_%d" % g)

    run_phase("prout0", [(blk.gpsimd, prout_gp)])
    run_phase("prout", [
        (blk.vector, prout_vec), (blk.tensor, prout_tensor),
    ])
    dbg_dump([("dbg_rem", remote)])

    def pg_vec(e, ck, w):
        cV.tag(e.memset(gate_sb[:, :], 0.0))
        ck("gz")
        for c in range(8):
            for n in range(4):
                g = 4 * c + n
                w(e, "t_bc_%d" % g, cT)
                sl = slice(512 * n, 512 * n + 512)
                e.tensor_tensor(out=gdiff[:, :], in0=local_fm[:, c, sl],
                                in1=remote[:, c, sl], op=Alu.subtract)
                e.drain()
                e.scalar_tensor_tensor(out=gtmp[:, :], in0=pb[4 + g % 4][:, :],
                                       scalar=1.0, in1=gdiff[:, :],
                                       op0=Alu.mult, op1=Alu.mult)
                e.drain()
                cV.tag(e.tensor_tensor(out=mixed[:, c, sl], in0=gtmp[:, :],
                                       in1=remote[:, c, sl], op=Alu.add))
                ck("v_mix_%d" % g)

    def pg_scalar(e, ck, w):
        for c in range(8):
            for n in range(4):
                g = 4 * c + n
                w(e, "t_lg_%d" % g, cT)
                if g >= 1:
                    w(e, "t_bc_%d" % (g - 1), cT)
                cS.tag(e.activation(out=gate_sb[0:2, 0:512],
                                    in_=pb[g % 4][0:2, :], func=AFT.Sigmoid))
                ck("s_sig_%d" % g)

    def pg_tensor(e, ck, w):
        w(e, "gz", cV)
        for c in range(8):
            for n in range(4):
                g = 4 * c + n
                sl = slice(512 * n, 512 * n + 512)
                if g >= 4:
                    w(e, "s_sig_%d" % (g - 4), cS)
                e.matmul(pb[g % 4][0:2, :], glgr_b[:, 0:2], local_fm[:, c, sl],
                         start=True, stop=False)
                cT.tag(e.matmul(pb[g % 4][0:2, :], glgr_b[:, 2:4],
                                remote[:, c, sl], start=False, stop=True))
                ck("t_lg_%d" % g)
                w(e, "s_sig_%d" % g, cS)
                if g >= 4:
                    w(e, "v_mix_%d" % (g - 4), cV)
                cT.tag(e.matmul(pb[4 + g % 4][:, :], sel2_b[:, :],
                                gate_sb[:, 0:512], start=True, stop=True))
                ck("t_bc_%d" % g)

    run_phase("pg", [
        (blk.vector, pg_vec), (blk.scalar, pg_scalar), (blk.tensor, pg_tensor),
    ])
    dbg_dump([("dbg_mix", mixed)])

    if PHLIM < 8:
        es.close()
        return nc

    def po_gp0(e, ck, w):
        ow3 = out_w.rearrange("(c p) n -> p c n", p=128)
        dW.tag(e.dma_start(out=outw_b[:, :, :], in_=ow3), 16)
        e.wait_ge(dW.h, dW.n)

    def po_vec(e, ck, w):
        for tt in range(NT):
            w(e, "t_o_%d" % tt, cT)
            if tt >= 2:
                w(e, "g_at_%d" % (tt - 2), dGp[tt % 2])
            b0, b1 = pb[2 * (tt % 2)], pb[2 * (tt % 2) + 1]
            e.tensor_tensor(out=attn_buf[:, tt % 2, 0:512], in0=b0[:, :],
                            in1=outb_t[:, 0:512], op=Alu.add)
            cV.tag(e.tensor_tensor(out=attn_buf[:, tt % 2, 512:1024],
                                   in0=b1[:, :], in1=outb_t[:, 512:1024],
                                   op=Alu.add))
            ck("v_o_%d" % tt)

    def po_tensor(e, ck, w):
        for tt in range(NT):
            if tt >= 2:
                w(e, "v_o_%d" % (tt - 2), cV)
            b0, b1 = pb[2 * (tt % 2)], pb[2 * (tt % 2) + 1]
            for kk in range(8):
                e.matmul(b0[:, :], mixed[:, kk, 128 * tt:128 * tt + 128],
                         outw_b[:, kk, 0:512], start=(kk == 0), stop=(kk == 7))
            for kk in range(8):
                inst = e.matmul(b1[:, :], mixed[:, kk, 128 * tt:128 * tt + 128],
                                outw_b[:, kk, 512:1024],
                                start=(kk == 0), stop=(kk == 7))
                if kk == 7:
                    cT.tag(inst)
            ck("t_o_%d" % tt)

    def po_gp(e, ck, w):
        for tt in range(NT):
            w(e, "v_o_%d" % tt, cV)
            dGp[tt % 2].tag(e.dma_start(out=attn_scr[tt, :, :],
                                        in_=attn_buf[:, tt % 2, :]), 16)
            ck("g_at_%d" % tt)

    run_phase("po0", [(blk.gpsimd, po_gp0)])
    run_phase("po", [
        (blk.vector, po_vec), (blk.tensor, po_tensor), (blk.gpsimd, po_gp),
    ])

    if PHLIM < 9:
        es.close()
        return nc

    def px_pre(e, ck, w):
        ck("d0")

    def px_sync(e, ck, w):
        for tt in range(NT):
            if tt >= 2:
                w(e, "t_x_%d" % (tt - 2), cT)
            dSp[tt % 2].tag(e.dma_start(out=xql[:, tt % 2, :],
                                        in_=xq_scr[tt, :, :]), 16)

    def px_tensor(e, ck, w):
        base = CK.get(("px", "d0"))
        for tt in range(NT):
            if base is not None and not isinstance(e, _StubEng):
                c = dSp[tt % 2]
                e.wait_ge(c.h, base[c.name] + 16 * (tt // 2 + 1))
            if tt >= 2:
                w(e, "v_x_%d" % (tt - 2), cV)
            b0, b1 = pb[2 * (tt % 2)], pb[2 * (tt % 2) + 1]
            for c in range(8):
                bnk = b0 if c < 4 else b1
                inst = e.matmul(bnk[:, 128 * (c % 4):128 * (c % 4) + 128],
                                xql[:, tt % 2, 128 * c:128 * c + 128],
                                identb[:, :], start=True, stop=True)
                if c == 7:
                    cT.tag(inst)
            ck("t_x_%d" % tt)

    def px_vec(e, ck, w):
        for tt in range(NT):
            w(e, "t_x_%d" % tt, cT)
            b0, b1 = pb[2 * (tt % 2)], pb[2 * (tt % 2) + 1]
            e.tensor_copy(XQ[:, 0:4, 128 * tt:128 * tt + 128],
                          b0[:, :].rearrange("p (a n) -> p a n", a=4))
            cV.tag(e.tensor_copy(XQ[:, 4:8, 128 * tt:128 * tt + 128],
                                 b1[:, :].rearrange("p (a n) -> p a n", a=4)))
            ck("v_x_%d" % tt)

    run_phase("px", [
        (blk.sync, px_pre), (blk.sync, px_sync), (blk.tensor, px_tensor),
        (blk.vector, px_vec),
    ])
    dbg_dump([("dbg_xq", XQ)])

    if PHLIM < 10:
        es.close()
        return nc

    def pgu_pre(e, ck, w):
        ck("d0")

    def pgu_sync(e, ck, w):
        wg3 = wgT.rearrange("(c p) n -> p c n", p=128)
        wu3 = wuT.rearrange("(c p) n -> p c n", p=128)
        for nt in range(8):
            for mi, w3 in ((0, wg3), (1, wu3)):
                idx = 2 * nt + mi
                if idx >= 2:
                    w(e, "s_rw_%d" % (idx - 2), cS)
                dSp[idx % 2].tag(e.dma_start(out=wq32[:, idx % 2, :, :],
                                   in_=w3[:, :, 512 * nt:512 * nt + 512]), 16)

    def pgu_scalar(e, ck, w):
        base = CK.get(("pgu", "d0"))

        def s_round(e, idx):
            mi = idx % 2
            if base is not None and not isinstance(e, _StubEng):
                c = dSp[idx % 2]
                e.wait_ge(c.h, base[c.name] + 16 * (idx // 2 + 1))
            if idx >= 1:
                w(e, "v_wq_%d" % (idx - 1), cV)
            cS.tag(e.activation(
                out=wtmp[:, :, :].rearrange("p a n -> p (a n)"),
                in_=wq32[:, idx % 2, :, :].rearrange("p a n -> p (a n)"),
                func=AFT.Copy, scale=wsc[:, mi:mi + 1], bias=C_RND))
            ck("s_rw_%d" % idx)

        s_round(e, 0)
        s_round(e, 1)
        for nt in range(8):
            for tt in range(NT):
                it = NT * nt + tt
                if nt + 1 < 8 and tt == 2:
                    s_round(e, 2 * (nt + 1))
                if nt + 1 < 8 and tt == 8:
                    s_round(e, 2 * (nt + 1) + 1)
                w(e, "t_g_%d" % it, cT)
                if it >= 1:
                    w(e, "v_h_%d" % (it - 1), cV)
                cS.tag(e.activation(out=sg_sb[:, :], in_=pb[tt % 2][:, :],
                                    func=AFT.Silu, scale=gsc[:, tt:tt + 1]))
                ck("s_silu_%d" % it)

    def pgu_vec(e, ck, w):
        def v_clip(e, idx, half):
            nt2, mi = divmod(idx, 2)
            dstW = wgq if mi == 0 else wuq
            sl = slice(4096 * half, 4096 * half + 4096)
            wt_f = wtmp[:, :, :].rearrange("p a n -> p (a n)")
            if half == 0:
                w(e, "s_rw_%d" % idx, cS)
                e.tensor_scalar(out=wt_f[:, sl], in0=wt_f[:, sl],
                                scalar1=C_RND, scalar2=1.0,
                                op0=Alu.subtract, op1=Alu.min)
            else:
                e.tensor_scalar(out=wt_f[:, sl], in0=wt_f[:, sl],
                                scalar1=C_RND, scalar2=1.0,
                                op0=Alu.subtract, op1=Alu.min)
                if nt2 >= 2:
                    w(e, "t_u_%d" % (NT * (nt2 - 2) + NT - 1), cT)
                cV.tag(e.tensor_scalar(
                    out=dstW[:, nt2 % 2, :, 0:256].rearrange("p a n -> p (a n)"),
                    in0=wt_f[:, 0:4096].rearrange(
                        "p (a n) -> p a n", a=8)[:, :, 0:256].rearrange(
                        "p a n -> p a n").rearrange("p a n -> p (a n)"),
                    scalar1=-1.0, scalar2=None, op0=Alu.max))
                cV.tag(e.tensor_scalar(
                    out=dstW[:, nt2 % 2, :, 256:512].rearrange("p a n -> p (a n)"),
                    in0=wt_f[:, 4096:8192].rearrange(
                        "p (a n) -> p a n", a=8)[:, :, 0:256].rearrange(
                        "p a n -> p (a n)"),
                    scalar1=-1.0, scalar2=None, op0=Alu.max))
                ck("v_wq_%d" % idx)

        pass

    def pgu_vec2(e, ck, w):
        wt_f = wtmp[:, :, :].rearrange("p a n -> p (a n)")

        def v_clip1(e, idx):
            w(e, "s_rw_%d" % idx, cS)
            nc1 = cV.tag(e.tensor_scalar(out=wt_f, in0=wt_f,
                                         scalar1=C_RND, scalar2=1.0,
                                         op0=Alu.subtract, op1=Alu.min))
            return nc1

        def v_clip2(e, idx, nc1):
            nt2, mi = divmod(idx, 2)
            dstW = wgq if mi == 0 else wuq
            if nt2 >= 2:
                w(e, "t_u_%d" % (NT * (nt2 - 2) + NT - 1), cT)
            e.wait_ge(cV.h, nc1)
            cV.tag(e.tensor_scalar(
                out=dstW[:, nt2 % 2, :, :].rearrange("p a n -> p (a n)"),
                in0=wt_f, scalar1=-1.0, scalar2=None, op0=Alu.max))
            ck("v_wq_%d" % idx)

        nc1 = v_clip1(e, 0)
        v_clip2(e, 0, nc1)
        nc1 = v_clip1(e, 1)
        v_clip2(e, 1, nc1)
        for nt in range(8):
            pend = None
            for tt in range(NT):
                it = NT * nt + tt
                if nt + 1 < 8:
                    if tt == 4:
                        pend = (2 * (nt + 1), v_clip1(e, 2 * (nt + 1)))
                    elif tt == 6 and pend is not None:
                        v_clip2(e, pend[0], pend[1])
                        pend = None
                    elif tt == 10:
                        pend = (2 * (nt + 1) + 1, v_clip1(e, 2 * (nt + 1) + 1))
                    elif tt == 12 and pend is not None:
                        v_clip2(e, pend[0], pend[1])
                        pend = None
                w(e, "s_silu_%d" % it, cS)
                w(e, "t_u_%d" % it, cT)
                g4 = it // 4
                if g4 >= 2 and it % 4 == 0:
                    w(e, "g_h_%d" % (g4 - 2), dGp[g4 % 2])
                cV.tag(e.scalar_tensor_tensor(
                    out=hwb[:, g4 % 2, it % 4, :], in0=pb[2 + tt % 2][:, :],
                    scalar=usc[:, tt:tt + 1], in1=sg_sb[:, :],
                    op0=Alu.mult, op1=Alu.mult))
                ck("v_h_%d" % it)

    def pgu_tensor(e, ck, w):
        for nt in range(8):
            w(e, "v_wq_%d" % (2 * nt + 1), cV)
            for tt in range(NT):
                it = NT * nt + tt
                if it >= 2:
                    w(e, "s_silu_%d" % (it - 2), cS)
                    w(e, "v_h_%d" % (it - 2), cV)
                bg = pb[tt % 2]
                bu = pb[2 + tt % 2]
                for kk in range(8):
                    inst = e.matmul(bg[:, :], XQ[:, kk, 128 * tt:128 * tt + 128],
                                    wgq[:, nt % 2, kk, :],
                                    start=(kk == 0), stop=(kk == 7))
                    if kk == 7:
                        cT.tag(inst)
                        ck("t_g_%d" % it)
                    inst = e.matmul(bu[:, :], XQ[:, kk, 128 * tt:128 * tt + 128],
                                    wuq[:, nt % 2, kk, :],
                                    start=(kk == 0), stop=(kk == 7))
                    if kk == 7:
                        cT.tag(inst)
                        ck("t_u_%d" % it)

    def pgu_gp(e, ck, w):
        for nt in range(8):
            for g4l in range(4):
                g4 = 4 * nt + g4l
                w(e, "v_h_%d" % (NT * nt + 4 * g4l + 3), cV)
                dGp[g4 % 2].tag(e.dma_start(
                    out=h_scr[4 * g4l:4 * g4l + 4, :, nt, :].rearrange(
                        "a p n -> p a n"),
                    in_=hwb[:, g4 % 2, :, :]), 16)
                ck("g_h_%d" % g4)

    run_phase("pgu", [
        (blk.sync, pgu_pre), (blk.sync, pgu_sync), (blk.scalar, pgu_scalar),
        (blk.vector, pgu_vec2), (blk.tensor, pgu_tensor), (blk.gpsimd, pgu_gp),
    ])

    if PHLIM < 11:
        es.close()
        return nc

    def pwd_pre(e, ck, w):
        ck("d0")

    def pwd_sync(e, ck, w):
        wd3 = wdT.rearrange("(c p) n -> c p n", p=128)
        for c in range(32):
            if c >= 2:
                w(e, "s_rd_%d" % (c - 2), cS)
            dSp[c % 2].tag(e.dma_start(out=wdf[:, c % 2, :], in_=wd3[c, :, :]),
                           16)

    def pwd_scalar(e, ck, w):
        base = CK.get(("pwd", "d0"))
        for c in range(32):
            if base is not None and not isinstance(e, _StubEng):
                ctr = dSp[c % 2]
                e.wait_ge(ctr.h, base[ctr.name] + 16 * (c // 2 + 1))
            if c >= 1:
                w(e, "v_wd1_%d" % (c - 1), cV)
            cS.tag(e.activation(out=wdt_r[:, c % 2, :], in_=wdf[:, c % 2, :],
                                func=AFT.Copy, scale=wsc[:, 2:3], bias=C_RND))
            ck("s_rd_%d" % c)

    def pwd_vec(e, ck, w):
        for c in range(32):
            w(e, "s_rd_%d" % c, cS)
            cV.tag(e.tensor_scalar(out=wdt_r[:, c % 2, :], in0=wdt_r[:, c % 2, :],
                                   scalar1=C_RND, scalar2=1.0,
                                   op0=Alu.subtract, op1=Alu.min))
            ck("v_wd1_%d" % c)
            e.drain()
            cV.tag(e.tensor_scalar(out=wdq2[:, c, :], in0=wdt_r[:, c % 2, :],
                                   scalar1=-1.0, scalar2=None, op0=Alu.max))
            ck("v_wd2_%d" % c)

    run_phase("pwd", [
        (blk.sync, pwd_pre), (blk.sync, pwd_sync), (blk.scalar, pwd_scalar),
        (blk.vector, pwd_vec),
    ])

    def pd_pre(e, ck, w):
        ck("d0")

    def pd_sync(e, ck, w):
        for tt in range(NT):
            if tt >= 2:
                w(e, "v_fin_%d" % (tt - 2), cV)
            dSp[tt % 2].tag(e.dma_start(
                out=hrb[:, tt % 2, :],
                in_=h_scr[tt, :, :, :].rearrange("p a n -> p (a n)")), 16)
            dSp[tt % 2].tag(e.dma_start(out=atl[:, tt % 2, :],
                                        in_=attn_scr[tt, :, :]), 16)
            dSp[tt % 2].tag(e.dma_start(out=xres[:, tt % 2, :],
                                        in_=x_own[128 * tt:128 * tt + 128, :]), 16)

    def pd_vec(e, ck, w):
        base = CK.get(("pd", "d0"))
        for tt in range(NT):
            if base is not None and not isinstance(e, _StubEng):
                c = dSp[tt % 2]
                e.wait_ge(c.h, base[c.name] + 48 * (tt // 2 + 1))
            if tt >= 1:
                w(e, "s_rh_%d" % (tt - 1), cS)
            n0 = cV.tag(e.reduce_max(am[:, :], hrb[:, tt % 2, :], axis=AX.X,
                                     apply_absolute_value=True))
            e.wait_ge(cV.h, n0)
            n1 = cV.tag(e.tensor_scalar(out=t1[:, :], in0=am[:, :], scalar1=1e-5,
                                        scalar2=None, op0=Alu.max))
            e.wait_ge(cV.h, n1)
            n2 = cV.tag(e.reciprocal(out=t3[:, :], in_=t1[:, :]))
            e.wait_ge(cV.h, n2)
            cV.tag(e.tensor_scalar(out=sh_t[:, tt % 2:tt % 2 + 1], in0=t3[:, :],
                                   scalar1=127.0, scalar2=None, op0=Alu.mult))
            ck("v_sh_%d" % tt)
            cV.tag(e.tensor_scalar(out=dsc_t[:, tt % 2:tt % 2 + 1], in0=t1[:, :],
                                   scalar1=wsc[:, 5:6], scalar2=1.0 / 127.0,
                                   op0=Alu.mult, op1=Alu.mult))
            ck("v_dsc_%d" % tt)
            w(e, "s_rh_%d" % tt, cS)
            if tt >= 1:
                w(e, "t_dn_%d" % (tt - 1), cT)
            cV.tag(e.tensor_scalar(out=xh_tm[:, :], in0=tq_sb[:, :],
                                   scalar1=C_RND, scalar2=None,
                                   op0=Alu.subtract))
            ck("v_xh_%d" % tt)
            for rg in range(8):
                w(e, "t_tr_%d_%d" % (tt, rg), cT)
                bnk = pb[4 + rg % 4]
                cV.tag(e.tensor_copy(
                    xh_fm[:, 4 * rg:4 * rg + 4, :].rearrange("p a n -> p (a n)"),
                    bnk[:, :]))
                ck("v_ev_%d_%d" % (tt, rg))
            w(e, "t_dn_%d" % tt, cT)
            if tt >= 2:
                w(e, "g_y_%d" % (tt - 2), dGp[tt % 2])
            bd0, bd1 = pb[tt % 2], pb[2 + tt % 2]
            e.scalar_tensor_tensor(out=obuf[:, tt % 2, 0:512], in0=bd0[:, :],
                                   scalar=dsc_t[:, tt % 2:tt % 2 + 1],
                                   in1=atl[:, tt % 2, 0:512],
                                   op0=Alu.mult, op1=Alu.add)
            nst = cV.tag(e.scalar_tensor_tensor(
                out=obuf[:, tt % 2, 512:1024], in0=bd1[:, :],
                scalar=dsc_t[:, tt % 2:tt % 2 + 1],
                in1=atl[:, tt % 2, 512:1024],
                op0=Alu.mult, op1=Alu.add))
            e.wait_ge(cV.h, nst)
            e.tensor_tensor(out=obuf[:, tt % 2, 0:512], in0=obuf[:, tt % 2, 0:512],
                            in1=xres[:, tt % 2, 0:512], op=Alu.add)
            cV.tag(e.tensor_tensor(out=obuf[:, tt % 2, 512:1024],
                                   in0=obuf[:, tt % 2, 512:1024],
                                   in1=xres[:, tt % 2, 512:1024], op=Alu.add))
            ck("v_fin_%d" % tt)

    def pd_scalar(e, ck, w):
        for tt in range(NT):
            w(e, "v_sh_%d" % tt, cV)
            if tt >= 1:
                w(e, "v_xh_%d" % (tt - 1), cV)
            cS.tag(e.activation(out=tq_sb[:, :], in_=hrb[:, tt % 2, :],
                                func=AFT.Copy, scale=sh_t[:, tt % 2:tt % 2 + 1],
                                bias=C_RND))
            ck("s_rh_%d" % tt)

    def pd_tensor(e, ck, w):
        for tt in range(NT):
            w(e, "v_xh_%d" % tt, cV)
            for rg in range(8):
                if rg >= 4:
                    w(e, "v_ev_%d_%d" % (tt, rg - 4), cV)
                elif tt >= 1:
                    w(e, "v_ev_%d_%d" % (tt - 1, rg + 4), cV)
                bnk = pb[4 + rg % 4]
                for u in range(4):
                    kc = 4 * rg + u
                    inst = e.matmul(bnk[:, 128 * u:128 * u + 128],
                                    xh_tm[:, 128 * kc:128 * kc + 128],
                                    identb[:, :], start=True, stop=True)
                    if u == 3:
                        cT.tag(inst)
                ck("t_tr_%d_%d" % (tt, rg))
            w(e, "v_ev_%d_%d" % (tt, 7), cV)
            if tt >= 2:
                w(e, "v_fin_%d" % (tt - 2), cV)
            bd0, bd1 = pb[tt % 2], pb[2 + tt % 2]
            for kk in range(32):
                e.matmul(bd0[:, :], xh_fm[:, kk, :], wdq2[:, kk, 0:512],
                         start=(kk == 0), stop=(kk == 31))
                inst = e.matmul(bd1[:, :], xh_fm[:, kk, :], wdq2[:, kk, 512:1024],
                                start=(kk == 0), stop=(kk == 31))
                if kk == 31:
                    cT.tag(inst)
            ck("t_dn_%d" % tt)

    def pd_gp(e, ck, w):
        for tt in range(NT):
            w(e, "v_fin_%d" % tt, cV)
            dGp[tt % 2].tag(e.dma_start(out=y[128 * tt:128 * tt + 128, :],
                                        in_=obuf[:, tt % 2, :]), 16)
            ck("g_y_%d" % tt)
        e.wait_ge(dG.h, dG.n)
        e.wait_ge(dGb.h, dGb.n)

    run_phase("pd", [
        (blk.sync, pd_pre), (blk.sync, pd_sync), (blk.vector, pd_vec),
        (blk.scalar, pd_scalar), (blk.tensor, pd_tensor), (blk.gpsimd, pd_gp),
    ])

    es.close()
    return nc


def _masks_for_core(h):
    r = np.arange(128)
    c = np.arange(256)
    band = (c[None, :] >= r[:, None]) & (c[None, :] <= r[:, None] + 127)
    gen = np.where(band, 0.0, NEG).astype(np.float32)
    if h == 0:
        first = np.where(band & (c[None, :] >= 128), 0.0, NEG).astype(np.float32)
        lb = np.ones((128, 1), np.float32)
        lb[0, 0] = 0.0
        latb = np.ones((128, 1), np.float32)
        latb[0:64, 0] = 0.0
    else:
        first = gen.copy()
        lb = np.ones((128, 1), np.float32)
        latb = np.ones((128, 1), np.float32)
    lm = np.stack([np.concatenate([first, first], axis=1),
                   np.concatenate([gen, gen], axis=1)], axis=0)

    r0 = 2048 * h
    own_c0 = r0 // 64
    oth_c0 = (2048 - r0) // 64
    slot_chunk = np.concatenate([own_c0 + np.arange(32), oth_c0 + np.arange(32)])
    latm = np.empty((16, 128, 64), np.float32)
    for i in range(16):
        qchunk = (r0 + 128 * i + np.arange(128)) // 64
        allowed = slot_chunk[None, :] < qchunk[:, None]
        latm[i] = np.where(allowed, 0.0, NEG).astype(np.float32)
    return lm, latm, lb, latb


def _build_inputs(inputs):
    f = np.float32
    x = np.asarray(inputs["L0_attn_x"], f)
    qkv_w = np.ascontiguousarray(np.asarray(inputs["L0_attn_qkv_w"], f))
    qkv_b = np.asarray(inputs["L0_attn_qkv_b"], f)
    rq_w = np.ascontiguousarray(np.asarray(inputs["L0_attn_rq_w"], f))
    rq_b = np.asarray(inputs["L0_attn_rq_b"], f)
    rkv_w = np.ascontiguousarray(np.asarray(inputs["L0_attn_rkv_w"], f))
    rkv_b = np.asarray(inputs["L0_attn_rkv_b"], f)
    rout_w = np.ascontiguousarray(np.asarray(inputs["L0_attn_rout_w"], f))
    rout_b = np.asarray(inputs["L0_attn_rout_b"], f)
    out_w = np.ascontiguousarray(np.asarray(inputs["L0_attn_out_w"], f))
    out_b = np.asarray(inputs["L0_attn_out_b"], f)
    gl_w = np.asarray(inputs["L0_attn_gl_w"], f)
    gr_w = np.asarray(inputs["L0_attn_gr_w"], f)
    w_gate = np.asarray(inputs["L0_ffn_w_gate"], f)
    w_up = np.asarray(inputs["L0_ffn_w_up"], f)
    w_down = np.asarray(inputs["L0_ffn_w_down"], f)

    wgT = np.ascontiguousarray(w_gate.T)
    wuT = np.ascontiguousarray(w_up.T)
    wdT = np.ascontiguousarray(w_down.T)

    qkb_fm = np.ascontiguousarray(qkv_b[:2048].reshape(16, 128).T)
    vb_bc = np.ascontiguousarray(np.broadcast_to(qkv_b[2048:], (128, 1024)))
    rqb_fm = np.ascontiguousarray(rq_b.reshape(2, 128).T)
    rkvb_bc = np.ascontiguousarray(np.broadcast_to(rkv_b, (128, 512)))
    routb_fm = np.ascontiguousarray(rout_b.reshape(8, 128).T)
    outb_bc = np.ascontiguousarray(np.broadcast_to(out_b, (128, 1024)))

    glgr = np.zeros((128, 4), f)
    glgr[0:64, 0] = gl_w[:, 0]
    glgr[64:128, 1] = gl_w[:, 0]
    glgr[0:64, 2] = gr_w[:, 0]
    glgr[64:128, 3] = gr_w[:, 0]
    sel2 = np.zeros((128, 128), f)
    sel2[0, 0:64] = 1.0
    sel2[1, 64:128] = 1.0

    identb = np.eye(128, dtype=f).astype(ml_dtypes.bfloat16)
    poolz = np.zeros((128, 196), f)
    poolz[0:64, 64] = 1.0
    poolz[64:128, 65] = 1.0
    poolz = poolz.astype(ml_dtypes.bfloat16)

    shared = dict(qkv_w=qkv_w, rq_w=rq_w, rkv_w=rkv_w, rout_w=rout_w,
                  out_w=out_w, wgT=wgT, wuT=wuT, wdT=wdT,
                  qkb_fm=qkb_fm, vb_bc=vb_bc, rqb_fm=rqb_fm, rkvb_bc=rkvb_bc,
                  routb_fm=routb_fm, outb_bc=outb_bc,
                  glgr_in=glgr, sel2_in=sel2, identb_in=identb,
                  poolz_in=poolz)

    in_maps = []
    for core in range(8):
        b, h = core // 2, core % 2
        r0 = 2048 * h
        lm, latm, lb, latb = _masks_for_core(h)
        m = dict(shared)
        m["x_own"] = np.ascontiguousarray(x[b, r0:r0 + 2048])
        if h == 0:
            m["x_halo"] = np.zeros((128, 1024), f)
        else:
            m["x_halo"] = np.ascontiguousarray(x[b, r0 - 128:r0])
        m["x_other"] = np.ascontiguousarray(x[b, 2048 - r0:4096 - r0])
        m["lmask"] = lm
        m["latmask"] = latm
        m["lbin"] = lb
        m["latbin"] = latb
        in_maps.append(m)
    return in_maps


_nc_cache = [None]


def kernel(**inputs):
    in_maps = _build_inputs(inputs)
    if _nc_cache[0] is None:
        _nc_cache[0] = build_nc()
    nc = _nc_cache[0]
    res = run_bass_kernel_spmd(nc, in_maps, core_ids=list(range(8)))
    out = np.empty((B, S, D), np.float32)
    for core in range(8):
        b, h = core // 2, core % 2
        out[b, 2048 * h:2048 * h + 2048] = res.results[core]["y"]
    return out


# version 17
# speedup vs baseline: 1.0298x; 1.0298x over previous
import numpy as np
import ml_dtypes
from contextlib import ExitStack

import concourse.bass as bass
import concourse.mybir as mybir
from concourse.bass_utils import run_bass_kernel_spmd
from concourse.alu_op_type import AluOpType as Alu

F32 = mybir.dt.float32
BF16 = mybir.dt.bfloat16
AFT = mybir.ActivationFunctionType
AX = mybir.AxisListType

D = 1024
S = 4096
B = 4
NH = 16
HD = 64
WIN = 128
LD = 256
LH = 4
NLAT = 64
INNER = 4096
T = 2048
NT = 16
NEG = -1.0e9
C_RND = 12582912.0
EPS = 1e-5

import os
PHLIM = int(os.environ.get("KPHLIM", "99"))
DBG = os.environ.get("KDBG", "0") == "1"

ARENA0 = 23552
ARENA_CAP = 229376


class Ctr:
    def __init__(self, es, nc, name):
        self.h = es.enter_context(nc.semaphore(name))
        self.n = 0
        self.name = name

    def tag(self, inst, k=1):
        if k:
            if not isinstance(inst, _StubInst) and inst is not None:
                inst.then_inc(self.h, k)
            self.n += k
        return self.n


class _StubInst:
    def then_inc(self, sem, k):
        return self


class _StubEng:
    def __getattr__(self, name):
        if name == "wait_ge":
            return lambda *a, **kw: None
        return lambda *a, **kw: _StubInst()


def build_nc():
    nc = bass.Bass("TRN2", target_bir_lowering=False)

    def din(name, shape, dt=F32):
        return nc.dram_tensor(name, list(shape), dt, kind="ExternalInput").ap()

    x_own = din("x_own", [T, D])
    x_halo = din("x_halo", [WIN, D])
    x_other = din("x_other", [T, D])
    qkv_w = din("qkv_w", [D, 3 * D])
    rq_w = din("rq_w", [D, LD])
    rkv_w = din("rkv_w", [D, 2 * LD])
    rout_w = din("rout_w", [LD, D])
    out_w = din("out_w", [D, D])
    wgT = din("wgT", [D, INNER])
    wuT = din("wuT", [D, INNER])
    wdT = din("wdT", [INNER, D])
    qkb_fm = din("qkb_fm", [128, 16])
    vb_bc = din("vb_bc", [128, D])
    rqb_fm = din("rqb_fm", [128, 2])
    rkvb_bc = din("rkvb_bc", [128, 2 * LD])
    routb_fm = din("routb_fm", [128, 8])
    outb_bc = din("outb_bc", [128, D])
    glgr_in = din("glgr_in", [128, 4])
    sel2_in = din("sel2_in", [128, 128])
    identb_in = din("identb_in", [128, 128], BF16)
    poolz_in = din("poolz_in", [128, 196], BF16)
    lmask = din("lmask", [2, 128, 512])
    latmask = din("latmask", [16, 128, 64])
    lbin = din("lbin", [128, 1])
    latbin = din("latbin", [128, 1])

    y = nc.dram_tensor("y", [T, D], F32, kind="ExternalOutput").ap()
    xq_scr = nc.dram_tensor("xq_scr", [NT, 128, D], BF16, kind="Internal").ap()
    attn_scr = nc.dram_tensor("attn_scr", [NT, 128, D], F32, kind="Internal").ap()
    h_scr = nc.dram_tensor("h_scr", [NT, 128, 8, 512], F32, kind="Internal").ap()

    dbg = {}
    if DBG:
        for nm, shape in [
            ("dbg_xn", [128, 8 * 2176]), ("dbg_sxr", [128, 16]),
            ("dbg_wsc", [128, 8]), ("dbg_rq", [128, 2 * 2048]),
            ("dbg_rk", [128, 2 * 64]), ("dbg_rkv1", [128, 512]),
            ("dbg_rkv2", [128, 512]), ("dbg_q", [128, 8 * 2048]),
            ("dbg_k", [128, 8 * 2176]), ("dbg_v", [128, 17 * 1024]),
            ("dbg_lat", [128, 2 * 2048]), ("dbg_loc", [128, 8 * 2048]),
            ("dbg_rem", [128, 8 * 2048]), ("dbg_mix", [128, 8 * 2048]),
            ("dbg_xq", [128, 8 * 2048]), ("dbg_pooled", [128, 8 * 192]),
        ]:
            dbg[nm] = nc.dram_tensor(nm, shape, F32, kind="ExternalOutput").ap()

    cursor = [0]

    def AT(name, shape, dt, off=None):
        nbytes = int(np.prod(shape[1:])) * (2 if dt == BF16 else 4)
        if off is None:
            off = ((cursor[0] + 63) // 64) * 64
        end = off + nbytes
        cursor[0] = max(cursor[0], end)
        assert end <= ARENA_CAP - ARENA0, (name, off, nbytes, end)
        t = nc.alloc_sbuf_tensor_at(name, list(shape), dt, offset=ARENA0 + off)
        return t.ap()

    identb = AT("identb", [128, 128], BF16)
    poolz = AT("poolz", [128, 196], BF16)
    lmask_t = AT("lmask_t", [128, 2, 512], F32)
    latmask_t = AT("latmask_t", [128, 16, 64], F32)
    lbin_t = AT("lbin_t", [128, 1], F32)
    latbin_t = AT("latbin_t", [128, 1], F32)
    qkb_t = AT("qkb_t", [128, 16], F32)
    vb_t = AT("vb_t", [128, D], F32)
    rqb_t = AT("rqb_t", [128, 2], F32)
    rkvb_t = AT("rkvb_t", [128, 2 * LD], F32)
    routb_t = AT("routb_t", [128, 8], F32)
    outb_t = AT("outb_t", [128, D], F32)
    glgr_f = AT("glgr_f", [128, 4], F32)
    glgr_b = AT("glgr_b", [128, 4], BF16)
    sel2_f = AT("sel2_f", [128, 128], F32)
    sel2_b = AT("sel2_b", [128, 128], BF16)
    onesf = AT("onesf", [128, 130], F32)
    sxr = AT("sxr", [128, NT], F32)
    gsc = AT("gsc", [128, NT], F32)
    usc = AT("usc", [128, NT], F32)
    wsc = AT("wsc", [128, 8], F32)
    absS = AT("absS", [128, 48], F32)
    stt = AT("stt", [128, 16], F32)
    mv = AT("mv", [128, 1], F32)
    t1 = AT("t1", [128, 1], F32)
    t2 = AT("t2", [128, 1], F32)
    t3 = AT("t3", [128, 1], F32)
    t4 = AT("t4", [128, 1], F32)
    rr = AT("rr", [128, 1], F32)
    r0 = AT("r0", [128, 1], F32)
    s_t = AT("s_t", [128, 2], F32)
    am = AT("am", [128, 1], F32)
    smax = AT("smax", [128, 8], F32)
    ssum = AT("ssum", [128, 8], F32)
    srcp = AT("srcp", [128, 8], F32)
    sh_t = AT("sh_t", [128, 2], F32)
    dsc_t = AT("dsc_t", [128, 2], F32)
    PERS_END = cursor[0]
    BIG0 = ((PERS_END + 63) // 64) * 64
    assert BIG0 <= 27648, BIG0

    XN = AT("XN", [128, 8, 2176], BF16, BIG0)
    Q0 = BIG0 + 34816
    q_fm = AT("q_fm", [128, 8, T], BF16, Q0)
    k_fm = AT("k_fm", [128, 8, 2176], BF16, BIG0 + 67584)
    V0 = BIG0 + 102400
    v_tm = AT("v_tm", [128, 17, D], BF16, V0)
    R0 = BIG0 + 137216
    rq_fm = AT("rq_fm", [128, 2, T], BF16, R0)
    xbuf = AT("xbuf", [128, 2, D], F32, Q0)
    xnf = AT("xnf", [128, D], F32, Q0 + 8192)
    xnb = AT("xnb", [128, 2, D], BF16, Q0 + 12288)
    xqb = AT("xqb", [128, 2, D], BF16, Q0 + 16384)
    qtmp = AT("qtmp", [128, D], F32, Q0 + 20480)
    sqd = AT("sqd", [128, 2, D], F32, Q0 + 24576)
    absb = AT("absb", [128, 2, 2048], F32, Q0)
    absd = AT("absd", [128, 2, 2048], F32, Q0 + 16384)
    outw_b = AT("outw_b", [128, 8, D], BF16, V0)
    attn_buf = AT("attn_buf", [128, 2, D], F32, V0 + 16384)
    gate_sb = AT("gate_sb", [128, 512], BF16, R0)
    gdiff = AT("gdiff", [128, 512], BF16, R0 + 1024)
    gtmp = AT("gtmp", [128, 512], BF16, R0 + 2048)
    TR0 = BIG0 + 145408
    wqk_b = AT("wqk_b", [128, 2, 8, 128], BF16, TR0)
    wv_b = AT("wv_b", [128, 2, 8, 256], BF16, TR0 + 4096)
    rkvw_b = AT("rkvw_b", [128, 8, 512], BF16, TR0 + 4096)
    routw_b = AT("routw_b", [128, 2, D], BF16, TR0 + 4096)
    rkv1 = AT("rkv1", [128, 2 * LD], BF16, TR0 + 12288)
    rkv2 = AT("rkv2", [128, 2 * LD], BF16, TR0 + 13312)
    rk_fm = AT("rk_fm", [128, 2, NLAT], BF16, TR0 + 14336)
    ltmp = AT("ltmp", [128, 512], F32, TR0 + 19776)
    qz = AT("qz", [128, 2, T], BF16, TR0 + 16704)
    pooled_pad = AT("pooled_pad", [128, 8, 192], BF16, TR0 + 16704)
    pe_sb = AT("pe_sb", [128, 4, 512], BF16, TR0 + 24896)
    pt_sb = AT("pt_sb", [128, 2, 512], BF16, TR0 + 14656)
    latout = AT("latout", [128, 2, T], BF16, TR0 + 28992)
    local_fm = AT("local_fm", [128, 8, T], BF16, BIG0)
    remote = AT("remote", [128, 8, T], BF16, Q0)
    mixed = AT("mixed", [128, 8, T], BF16, BIG0 + 67584)

    XQ = AT("XQ", [128, 8, T], BF16, BIG0)
    xql = AT("xql", [128, 2, D], BF16, BIG0 + 32768)
    wgq = AT("wgq", [128, 2, 8, 512], BF16, BIG0 + 36864)
    wuq = AT("wuq", [128, 2, 8, 512], BF16, BIG0 + 53248)
    sg_sb = AT("sg_sb", [128, 512], F32, BIG0 + 69632)
    hwb = AT("hwb", [128, 2, 4, 512], F32, BIG0 + 71680)
    wq32 = AT("wq32", [128, 2, 8, 512], F32, BIG0 + 88064)
    wtmp = AT("wtmp", [128, 8, 512], F32, BIG0 + 120832)
    wdq2 = AT("wdq2", [128, 32, D], BF16, BIG0 + 36864)
    hrb = AT("hrb", [128, 2, INNER], F32, BIG0)
    tq_sb = AT("tq_sb", [128, INNER], F32, BIG0 + 102400)
    xh_tm = AT("xh_tm", [128, INNER], BF16, BIG0 + 118784)
    xh_fm = AT("xh_fm", [128, 32, 128], BF16, BIG0 + 126976)
    atl = AT("atl", [128, 2, D], F32, BIG0 + 135168)
    xres = AT("xres", [128, 2, D], F32, BIG0 + 143360)
    obuf = AT("obuf", [128, 2, D], F32, BIG0 + 151552)
    wdf = AT("wdf", [128, 2, D], F32, BIG0 + 159744)
    wdt_r = AT("wdt_r", [128, 2, D], F32, BIG0 + 167936)
    xh_tm2 = AT("xh_tm2", [128, 2, INNER], BF16, BIG0 + 159744)
    assert BIG0 + 180544 <= 205824, BIG0

    pb = [nc.alloc_psum_tensor("pb%d" % i, [128, 512], F32).ap() for i in range(8)]

    es = ExitStack()
    dSp = [Ctr(es, nc, "dSa"), Ctr(es, nc, "dSb")]
    dGp = [Ctr(es, nc, "dGa"), Ctr(es, nc, "dGb")]
    dWp = [Ctr(es, nc, "dWa"), Ctr(es, nc, "dWb")]
    dS, dSb = dSp
    dG, dGb = dGp
    dW, dWb = dWp
    cV = Ctr(es, nc, "cV")
    cS = Ctr(es, nc, "cS")
    cT = Ctr(es, nc, "cT")
    cG = Ctr(es, nc, "cG")
    ALLC = dSp + dGp + dWp + [cV, cS, cT, cG]

    blk = es.enter_context(nc.Block())

    def full_barrier():
        sn = [(c, c.n) for c in ALLC]

        def w_all(e):
            for c, v in sn:
                if v > 0:
                    e.wait_ge(c.h, v)
        blk.sync(w_all)
        blk.vector(w_all)
        blk.scalar(w_all)
        blk.tensor(w_all)
        blk.gpsimd(w_all)

    CK = {}

    def run_phase(name, sections):
        base = {c.name: c.n for c in ALLC}
        mode = ["count"]

        def ck(label):
            if mode[0] == "count":
                CK[(name, label)] = {c.name: c.n for c in ALLC}

        def w(e, label, ctr):
            if mode[0] == "count":
                return
            key = (name, label)
            if key in CK:
                v = CK[key][ctr.name]
                if v > 0:
                    e.wait_ge(ctr.h, v)

        stub = _StubEng()
        for _, gen in sections:
            gen(stub, ck, w)
        after = {c.name: c.n for c in ALLC}
        for c in ALLC:
            c.n = base[c.name]
        mode[0] = "real"
        for meth, gen in sections:
            meth(lambda e, g=gen: g(e, ck, w))
        for c in ALLC:
            assert c.n == after[c.name], (name, c.name, c.n, after[c.name])
        full_barrier()

    def dbg_dump(pairs):
        if not DBG:
            return

        def g(e):
            for nm, src in pairs:
                if nm in dbg:
                    sflat = src if len(src.shape) == 2 else src.rearrange(
                        "p a n -> p (a n)")
                    dG.tag(e.dma_start(out=dbg[nm], in_=sflat), 16)
            e.wait_ge(dG.h, dG.n)
        blk.gpsimd(g)
        full_barrier()

    def p0_sync(e, ck, w):
        p0i = [0]
        for dst, src in [
            (identb[:, :], identb_in[:, :]),
            (poolz[:, :], poolz_in[:, :]),
            (lmask_t[:, :, :], lmask[:, :, :].rearrange("a p n -> p a n")),
            (latmask_t[:, :, :], latmask[:, :, :].rearrange("a p n -> p a n")),
            (lbin_t[:, :], lbin[:, :]), (latbin_t[:, :], latbin[:, :]),
            (qkb_t[:, :], qkb_fm[:, :]), (vb_t[:, :], vb_bc[:, :]),
            (rqb_t[:, :], rqb_fm[:, :]), (rkvb_t[:, :], rkvb_bc[:, :]),
            (routb_t[:, :], routb_fm[:, :]), (outb_t[:, :], outb_bc[:, :]),
            (glgr_f[:, :], glgr_in[:, :]), (sel2_f[:, :], sel2_in[:, :]),
        ]:
            dSp[p0i[0] % 2].tag(e.dma_start(out=dst, in_=src), 16)
            p0i[0] += 1
        ck("loads")

    def p0_vec(e, ck, w):
        w(e, "loads", dS)
        w(e, "loads", dSb)
        e.tensor_copy(glgr_b[:, :], glgr_f[:, :])
        e.tensor_copy(sel2_b[:, :], sel2_f[:, :])
        e.memset(onesf[:, 0:1], 1.0)
        e.memset(onesf[0:1, 1:130], 1.0)
        cV.tag(e.memset(wsc[:, :], 0.0))

    run_phase("p0", [(blk.sync, p0_sync), (blk.vector, p0_vec)])

    if PHLIM < 1:
        es.close()
        return nc

    def pabs_pre(e, ck, w):
        ck("d0")

    def pabs_sync(e, ck, w):
        idx = 0
        wg3 = wgT.rearrange("(c p) n -> p c n", p=128)
        wu3 = wuT.rearrange("(c p) n -> p c n", p=128)
        wd3 = wdT.rearrange("(c p) n -> p c n", p=128)
        for mat in range(3):
            for sl in range(16):
                w(e, "sq_%d" % (idx - 2), cS)
                if mat == 0:
                    src = wg3[:, :, 256 * sl:256 * sl + 256]
                elif mat == 1:
                    src = wu3[:, :, 256 * sl:256 * sl + 256]
                else:
                    src = wd3[:, 2 * sl:2 * sl + 2, :]
                dSp[idx % 2].tag(e.dma_start(
                    out=absb[:, idx % 2, :].rearrange(
                        "p (a n) -> p a n", a=src.shape[1]),
                    in_=src), 16)
                idx += 1

    def pabs_scalar(e, ck, w):
        for idx in range(48):
            if idx == 0:
                w(e, "d0", dS)
            base = CK.get(("pabs", "d0"))
            if base is not None and not isinstance(e, _StubEng):
                c = dSp[idx % 2]
                e.wait_ge(c.h, base[c.name] + 16 * (idx // 2 + 1))
            cS.tag(e.activation(out=absd[:, idx % 2, :], in_=absb[:, idx % 2, :],
                                func=AFT.Abs, accum_out=absS[:, idx:idx + 1]))
            ck("sq_%d" % idx)

    def pabs_vec(e, ck, w):
        w(e, "sq_47", cS)
        e.reduce_sum(stt[:, 0:1], absS[:, 0:16], axis=AX.X)
        e.reduce_sum(stt[:, 1:2], absS[:, 16:32], axis=AX.X)
        cV.tag(e.reduce_sum(stt[:, 2:3], absS[:, 32:48], axis=AX.X))
        ck("psums")
        w(e, "mm1", cT)
        cV.tag(e.tensor_copy(ltmp[0:1, 0:3], pb[4][0:1, 0:3]))
        ck("rowcp")
        w(e, "mm2", cT)
        e.tensor_scalar(out=wsc[:, 3:6], in0=pb[5][:, 0:3],
                        scalar1=1.0 / (D * INNER), scalar2=1e-5,
                        op0=Alu.mult, op1=Alu.max)
        e.drain()
        e.reciprocal(out=wsc[:, 0:1], in_=wsc[:, 3:4])
        e.reciprocal(out=wsc[:, 1:2], in_=wsc[:, 4:5])
        cV.tag(e.reciprocal(out=wsc[:, 2:3], in_=wsc[:, 5:6]))
        ck("wsc")

    def pabs_tensor(e, ck, w):
        w(e, "psums", cV)
        cT.tag(e.matmul(pb[4][0:1, 0:3], onesf[:, 0:1], stt[:, 0:3],
                        start=True, stop=True))
        ck("mm1")
        w(e, "rowcp", cV)
        cT.tag(e.matmul(pb[5][:, 0:3], onesf[0:1, 1:129], ltmp[0:1, 0:3],
                        start=True, stop=True))
        ck("mm2")

    run_phase("pabs", [
        (blk.sync, pabs_pre), (blk.sync, pabs_sync),
        (blk.scalar, pabs_scalar), (blk.vector, pabs_vec),
        (blk.tensor, pabs_tensor),
    ])
    dbg_dump([("dbg_wsc", wsc[:, :])])

    if PHLIM < 2:
        es.close()
        return nc

    tiles = [("halo", 0)] + [("own", i) for i in range(NT)] + \
            [("other", i) for i in range(NT)]
    n_pool = 2 * NT

    def a_src(kind, i):
        if kind == "halo":
            return x_halo[:, :]
        if kind == "own":
            return x_own[128 * i:128 * i + 128, :]
        return x_other[128 * i:128 * i + 128, :]

    def pa_pre(e, ck, w):
        ck("d0")

    def pa_sync(e, ck, w):
        for j, (kind, i) in enumerate(tiles):
            if j >= 2:
                w(e, "v_xnb_%d" % (j - 2), cV)
                w(e, "s_sq_%d" % (j - 2), cS)
            dSp[j % 2].tag(e.dma_start(out=xbuf[:, j % 2, :],
                                       in_=a_src(kind, i)), 16)

    def pa_vec(e, ck, w):
        base = CK.get(("pa", "d0"))
        last_own = None
        for j, (kind, i) in enumerate(tiles):
            if base is not None and not isinstance(e, _StubEng):
                c = dSp[j % 2]
                e.wait_ge(c.h, base[c.name] + 16 * (j // 2 + 1))
            e.reduce_sum(stt[:, 0:1], xbuf[:, j % 2, :], axis=AX.X)
            w(e, "s_sq_%d" % j, cS)
            e.drain()
            e.tensor_scalar(out=mv[:, :], in0=stt[:, 0:1], scalar1=1.0 / D,
                            scalar2=None, op0=Alu.mult)
            e.tensor_scalar(out=t1[:, :], in0=stt[:, 1:2], scalar1=1.0 / D,
                            scalar2=EPS, op0=Alu.mult, op1=Alu.add)
            e.drain()
            e.tensor_scalar(out=t2[:, :], in0=mv[:, :], scalar1=mv[:, :],
                            scalar2=None, op0=Alu.mult)
            e.drain()
            cV.tag(e.tensor_tensor(out=t1[:, :], in0=t1[:, :], in1=t2[:, :],
                                   op=Alu.subtract))
            ck("v_var_%d" % j)
            w(e, "s_sqrt_%d" % j, cS)
            cV.tag(e.reciprocal(out=r0[:, :], in_=s_t[:, 0:1]))
            ck("v_r0_%d" % j)
            e.drain()
            e.tensor_scalar(out=t3[:, :], in0=r0[:, :], scalar1=r0[:, :],
                            scalar2=None, op0=Alu.mult)
            e.drain()
            e.tensor_scalar(out=t3[:, :], in0=t3[:, :], scalar1=t1[:, :],
                            scalar2=None, op0=Alu.mult)
            e.drain()
            e.tensor_scalar(out=t3[:, :], in0=t3[:, :], scalar1=-0.5,
                            scalar2=1.5, op0=Alu.mult, op1=Alu.add)
            e.drain()
            e.tensor_scalar(out=rr[:, :], in0=r0[:, :], scalar1=t3[:, :],
                            scalar2=None, op0=Alu.mult)
            e.drain()
            e.tensor_scalar(out=t4[:, :], in0=mv[:, :], scalar1=rr[:, :],
                            scalar2=-1.0, op0=Alu.mult, op1=Alu.mult)
            e.drain()
            if j >= 2:
                w(e, "t_done_%d" % (j - 2), cT)
            if kind == "own":
                if last_own is not None:
                    w(e, "s_rnd_%d" % last_own, cS)
                e.tensor_scalar(out=xnf[:, :], in0=xbuf[:, j % 2, :],
                                scalar1=rr[:, :], scalar2=t4[:, :],
                                op0=Alu.mult, op1=Alu.add)
                e.drain()
                cV.tag(e.tensor_copy(xnb[:, j % 2, :], xnf[:, :]))
                ck("v_xnb_%d" % j)
                e.reduce_max(am[:, :], xnf[:, :], axis=AX.X,
                             apply_absolute_value=True)
                e.drain()
                e.tensor_scalar(out=am[:, :], in0=am[:, :], scalar1=1e-5,
                                scalar2=None, op0=Alu.max)
                e.drain()
                e.reciprocal(out=t3[:, :], in_=am[:, :])
                e.drain()
                cV.tag(e.tensor_scalar(out=s_t[:, 1:2], in0=t3[:, :],
                                       scalar1=127.0, scalar2=None, op0=Alu.mult))
                ck("v_st_%d" % j)
                e.drain()
                e.reciprocal(out=sxr[:, i:i + 1], in_=s_t[:, 1:2])
                w(e, "s_rnd_%d" % j, cS)
                if i >= 2:
                    w(e, "g_xq_%d" % (i - 2), dGp[i % 2])
                cV.tag(e.tensor_scalar(out=xqb[:, i % 2, :], in0=qtmp[:, :],
                                       scalar1=C_RND, scalar2=None,
                                       op0=Alu.subtract))
                ck("v_xq_%d" % j)
                last_own = j
            else:
                cV.tag(e.tensor_scalar(out=xnb[:, j % 2, :], in0=xbuf[:, j % 2, :],
                                       scalar1=rr[:, :], scalar2=t4[:, :],
                                       op0=Alu.mult, op1=Alu.add))
                ck("v_xnb_%d" % j)
            if kind != "other":
                w(e, "t_tr_%d" % j, cT)
                c0 = 0 if kind == "halo" else 128 + 128 * i
                bA, bB = pb[2 * (j % 2)], pb[2 * (j % 2) + 1]
                e.tensor_copy(XN[:, 0:4, c0:c0 + 128],
                              bA[:, :].rearrange("p (a n) -> p a n", a=4))
                cV.tag(e.tensor_copy(XN[:, 4:8, c0:c0 + 128],
                                     bB[:, :].rearrange("p (a n) -> p a n", a=4)))
            ck("v_evac_%d" % j)

    def pa_scalar(e, ck, w):
        base = CK.get(("pa", "d0"))
        last_own = None
        for j, (kind, i) in enumerate(tiles):
            if base is not None and not isinstance(e, _StubEng):
                c = dSp[j % 2]
                e.wait_ge(c.h, base[c.name] + 16 * (j // 2 + 1))
            if j >= 1:
                w(e, "v_var_%d" % (j - 1), cV)
            cS.tag(e.activation(out=sqd[:, j % 2, :], in_=xbuf[:, j % 2, :],
                                func=AFT.Square, accum_out=stt[:, 1:2]))
            ck("s_sq_%d" % j)
            w(e, "v_var_%d" % j, cV)
            if j >= 1:
                w(e, "v_r0_%d" % (j - 1), cV)
            cS.tag(e.activation(out=s_t[:, 0:1], in_=t1[:, :], func=AFT.Sqrt))
            ck("s_sqrt_%d" % j)
            if kind == "own":
                w(e, "v_st_%d" % j, cV)
                if last_own is not None:
                    w(e, "v_xq_%d" % last_own, cV)
                cS.tag(e.activation(out=qtmp[:, :], in_=xnf[:, :], func=AFT.Copy,
                                    scale=s_t[:, 1:2], bias=C_RND))
                ck("s_rnd_%d" % j)
                last_own = j

    def pa_tensor(e, ck, w):
        pool_seq = 0
        for j, (kind, i) in enumerate(tiles):
            w(e, "v_xnb_%d" % j, cV)
            if kind != "other":
                if j >= 2:
                    w(e, "v_evac_%d" % (j - 2), cV)
                bA, bB = pb[2 * (j % 2)], pb[2 * (j % 2) + 1]
                for c in range(8):
                    bnk = bA if c < 4 else bB
                    inst = e.matmul(bnk[:, 128 * (c % 4):128 * (c % 4) + 128],
                                    xnb[:, j % 2, 128 * c:128 * c + 128],
                                    identb[:, :], start=True, stop=True)
                    if c == 7:
                        cT.tag(inst)
                ck("t_tr_%d" % j)
            if kind != "halo":
                if kind == "own":
                    z = poolz[:, 64 - 2 * i:192 - 2 * i]
                else:
                    z = poolz[:, 32 - 2 * i:160 - 2 * i]
                first = pool_seq == 0
                last = pool_seq == n_pool - 1
                e.matmul(pb[6][:, :], z, xnb[:, j % 2, 0:512],
                         start=first, stop=last)
                cT.tag(e.matmul(pb[7][:, :], z, xnb[:, j % 2, 512:1024],
                                start=first, stop=last))
                pool_seq += 1
            ck("t_done_%d" % j)

    def pa_gp(e, ck, w):
        for j, (kind, i) in enumerate(tiles):
            if kind == "own":
                w(e, "v_xq_%d" % j, cV)
                dGp[i % 2].tag(e.dma_start(out=xq_scr[i, :, :],
                                           in_=xqb[:, i % 2, :]), 16)
                ck("g_xq_%d" % i)

    run_phase("pa", [
        (blk.sync, pa_pre), (blk.sync, pa_sync), (blk.vector, pa_vec),
        (blk.scalar, pa_scalar), (blk.tensor, pa_tensor), (blk.gpsimd, pa_gp),
    ])
    dbg_dump([("dbg_xn", XN), ("dbg_sxr", sxr[:, :])])

    if PHLIM < 3:
        es.close()
        return nc

    def pa2a_gp(e, ck, w):
        rkv3 = rkv_w.rearrange("(c p) n -> p c n", p=128)
        rq3 = rq_w.rearrange("(c p) n -> p c n", p=128)
        dW.tag(e.dma_start(out=rkvw_b[:, :, :], in_=rkv3), 16)
        dWb.tag(e.dma_start(out=wqk_b[:, 0, :, :], in_=rq3[:, :, 0:128]), 16)
        dW.tag(e.dma_start(out=wqk_b[:, 1, :, :], in_=rq3[:, :, 128:256]), 16)
        e.wait_ge(dW.h, dW.n)
        e.wait_ge(dWb.h, dWb.n)

    run_phase("pa2a", [(blk.gpsimd, pa2a_gp)])

    def pa2_vec(e, ck, w):
        e.memset(pooled_pad[:, :, 0:64], 0.0)
        cV.tag(e.memset(pooled_pad[:, :, 128:192], 0.0))
        e.tensor_scalar(out=qtmp[0:64, 0:512], in0=pb[6][0:64, :],
                        scalar1=1.0 / 64.0, scalar2=None, op0=Alu.mult)
        e.tensor_scalar(out=qtmp[0:64, 512:1024], in0=pb[7][0:64, :],
                        scalar1=1.0 / 64.0, scalar2=None, op0=Alu.mult)
        e.drain()
        cV.tag(e.tensor_copy(xnb[0:64, 0, :], qtmp[0:64, :]))
        ck("pooled_lm")
        w(e, "t_ptr", cT)
        e.tensor_copy(pooled_pad[:, 0:4, 64:128],
                      pb[0][:, :].rearrange("p (a n) -> p a n", a=4)[:, :, 0:64])
        cV.tag(e.tensor_copy(pooled_pad[:, 4:8, 64:128],
                             pb[1][:, :].rearrange("p (a n) -> p a n", a=4)[:, :, 0:64]))
        ck("ppad")
        w(e, "t_rkv1", cT)
        e.tensor_tensor(out=ltmp[0:64, :], in0=pb[2][0:64, :],
                        in1=rkvb_t[0:64, :], op=Alu.add)
        e.drain()
        e.tensor_copy(rkv1[0:64, :], ltmp[0:64, :])
        cV.tag(e.memset(rkv1[64:128, :], 0.0))
        ck("rkv1")
        w(e, "t_rkv2", cT)
        e.tensor_tensor(out=ltmp[64:128, :], in0=pb[3][64:128, :],
                        in1=rkvb_t[64:128, :], op=Alu.add)
        e.drain()
        e.tensor_copy(rkv2[64:128, :], ltmp[64:128, :])
        cV.tag(e.memset(rkv2[0:64, :], 0.0))
        ck("rkv2")
        w(e, "t_rkT", cT)
        e.tensor_copy(rk_fm[:, 0, :], pb[4][:, 0:64])
        cV.tag(e.tensor_copy(rk_fm[:, 1, :], pb[5][:, 0:64]))
        ck("rk")
        for oc in range(2):
            w(e, "t_rq_%d" % oc, cT)
            for n in range(4):
                inst = e.tensor_scalar(
                    out=rq_fm[:, oc, 512 * n:512 * n + 512],
                    in0=pb[4 * oc + n][:, :],
                    scalar1=rqb_t[:, oc:oc + 1], scalar2=None, op0=Alu.add)
                if n == 3:
                    cV.tag(inst)
            ck("v_rq_%d" % oc)
        for tt in range(NT):
            e.tensor_scalar(out=gsc[:, tt:tt + 1], in0=sxr[:, tt:tt + 1],
                            scalar1=wsc[:, 3:4], scalar2=None, op0=Alu.mult)
            inst = e.tensor_scalar(out=usc[:, tt:tt + 1], in0=sxr[:, tt:tt + 1],
                                   scalar1=wsc[:, 4:5], scalar2=None, op0=Alu.mult)
            if tt == NT - 1:
                cV.tag(inst)
        ck("scales")

    def pa2_tensor(e, ck, w):
        w(e, "pooled_lm", cV)
        for c in range(8):
            bnk = pb[0] if c < 4 else pb[1]
            inst = e.matmul(bnk[:, 128 * (c % 4):128 * (c % 4) + 128],
                            xnb[0:64, 0, 128 * c:128 * c + 128],
                            identb[0:64, :], start=True, stop=True)
            if c == 7:
                cT.tag(inst)
        ck("t_ptr")
        w(e, "ppad", cV)
        for kk in range(8):
            inst = e.matmul(pb[2][:, :], pooled_pad[:, kk, 64:192],
                            rkvw_b[:, kk, :], start=(kk == 0), stop=(kk == 7))
            if kk == 7:
                cT.tag(inst)
        ck("t_rkv1")
        for kk in range(8):
            inst = e.matmul(pb[3][:, :], pooled_pad[:, kk, 0:128],
                            rkvw_b[:, kk, :], start=(kk == 0), stop=(kk == 7))
            if kk == 7:
                cT.tag(inst)
        ck("t_rkv2")
        w(e, "rkv1", cV)
        e.matmul(pb[4][:, 0:64], rkv1[0:64, 0:128], identb[0:64, 0:64],
                 start=True, stop=True)
        cT.tag(e.matmul(pb[5][:, 0:64], rkv1[0:64, 128:256], identb[0:64, 0:64],
                        start=True, stop=True))
        ck("t_rkT")
        w(e, "rkv2", cV)
        w(e, "rk", cV)
        for oc in range(2):
            if oc == 1:
                w(e, "v_rq_0", cV)
            for n in range(4):
                for kk in range(8):
                    inst = e.matmul(pb[4 * oc + n][:, :],
                                    wqk_b[:, oc, kk, :],
                                    XN[:, kk, 128 + 512 * n:128 + 512 * n + 512],
                                    start=(kk == 0), stop=(kk == 7))
                    if kk == 7 and n == 3:
                        cT.tag(inst)
            ck("t_rq_%d" % oc)

    run_phase("pa2", [(blk.vector, pa2_vec), (blk.tensor, pa2_tensor)])
    dbg_dump([("dbg_rq", rq_fm), ("dbg_rk", rk_fm),
              ("dbg_rkv1", rkv1[:, :]), ("dbg_rkv2", rkv2[:, :]),
              ("dbg_pooled", pooled_pad)])

    if PHLIM < 4:
        es.close()
        return nc

    def pq_pre(e, ck, w):
        ck("d0")

    def pq_gp(e, ck, w):
        w3 = qkv_w.rearrange("(c p) n -> p c n", p=128)
        for oc in range(16):
            if oc >= 2:
                w(e, "t_wfree_%d" % (oc - 2), cT)
            dWp[oc % 2].tag(e.dma_start(out=wqk_b[:, oc % 2, :, :],
                               in_=w3[:, :, 128 * oc:128 * oc + 128]), 16)
        for vs in range(4):
            if vs >= 2:
                w(e, "t_v_%d_%d" % (vs - 2, 16), cT)
            else:
                w(e, "t_wfree_%d" % (14 + vs), cT)
            dWp[vs % 2].tag(e.dma_start(out=wv_b[:, vs % 2, :, :],
                               in_=w3[:, :, 2048 + 256 * vs:2048 + 256 * vs + 256]),
                   16)

    def pq_vec(e, ck, w):
        for oc in range(16):
            if oc < 8:
                for n in range(4):
                    w(e, "t_qk_%d_%d" % (oc, n), cT)
                    bnk = pb[n + 4 * (oc % 2)]
                    cV.tag(e.tensor_scalar(out=q_fm[:, oc, 512 * n:512 * n + 512],
                                           in0=bnk[:, :],
                                           scalar1=qkb_t[:, oc:oc + 1],
                                           scalar2=None, op0=Alu.add))
            else:
                ko = oc - 8
                for n in range(5):
                    w(e, "t_k_%d_%d" % (ko, n), cT)
                    bnk = pb[(n % 4) + 4 * (oc % 2)]
                    c0 = 512 * n
                    wd = 512 if n < 4 else 128
                    cV.tag(e.tensor_scalar(out=k_fm[:, ko, c0:c0 + wd],
                                           in0=bnk[:, 0:wd],
                                           scalar1=qkb_t[:, oc:oc + 1],
                                           scalar2=None, op0=Alu.add))
                    ck("v_kev_%d_%d" % (ko, n))
            ck("v_evac_%d" % oc)
        for vs in range(4):
            for tt in range(17):
                w(e, "t_v_%d_%d" % (vs, tt), cT)
                bnk = pb[(tt % 4) + 4 * (vs % 2)]
                cV.tag(e.tensor_tensor(out=v_tm[:, tt, 256 * vs:256 * vs + 256],
                                       in0=bnk[:, 0:256],
                                       in1=vb_t[:, 256 * vs:256 * vs + 256],
                                       op=Alu.add))
                ck("v_vev_%d_%d" % (vs, tt))

    def pq_tensor(e, ck, w):
        base = CK.get(("pq", "d0"))
        for oc in range(16):
            if base is not None and not isinstance(e, _StubEng):
                c = dWp[oc % 2]
                e.wait_ge(c.h, base[c.name] + 16 * (oc // 2 + 1))
            if oc < 8:
                for n in range(4):
                    if oc >= 2:
                        w(e, "v_evac_%d" % (oc - 2), cV)
                    bnk = pb[n + 4 * (oc % 2)]
                    for kk in range(8):
                        inst = e.matmul(bnk[:, :], wqk_b[:, oc % 2, kk, :],
                                        XN[:, kk, 128 + 512 * n:128 + 512 * n + 512],
                                        start=(kk == 0), stop=(kk == 7))
                        if kk == 7:
                            cT.tag(inst)
                    ck("t_qk_%d_%d" % (oc, n))
            else:
                ko = oc - 8
                for n in range(5):
                    if oc >= 2:
                        w(e, "v_evac_%d" % (oc - 2), cV)
                    if n == 4:
                        w(e, "v_kev_%d_0" % ko, cV)
                    bnk = pb[(n % 4) + 4 * (oc % 2)]
                    c0 = 512 * n
                    wd = 512 if n < 4 else 128
                    for kk in range(8):
                        inst = e.matmul(bnk[:, 0:wd], wqk_b[:, oc % 2, kk, :],
                                        XN[:, kk, c0:c0 + wd],
                                        start=(kk == 0), stop=(kk == 7))
                        if kk == 7:
                            cT.tag(inst)
                    ck("t_k_%d_%d" % (ko, n))
            ck("t_wfree_%d" % oc)
        for vs in range(4):
            if base is not None and not isinstance(e, _StubEng):
                c = dWp[vs % 2]
                e.wait_ge(c.h, base[c.name] + 16 * 8 + 16 * (vs // 2 + 1))
            if vs < 2:
                w(e, "v_evac_%d" % (14 + vs), cV)
            for tt in range(17):
                if tt >= 4:
                    w(e, "v_vev_%d_%d" % (vs, tt - 4), cV)
                elif vs >= 2:
                    w(e, "v_vev_%d_%d" % (vs - 2, 13 + tt), cV)
                bnk = pb[(tt % 4) + 4 * (vs % 2)]
                for kk in range(8):
                    inst = e.matmul(bnk[:, 0:256], XN[:, kk, 128 * tt:128 * tt + 128],
                                    wv_b[:, vs % 2, kk, :],
                                    start=(kk == 0), stop=(kk == 7))
                    if kk == 7:
                        cT.tag(inst)
                ck("t_v_%d_%d" % (vs, tt))

    run_phase("pq", [
        (blk.gpsimd, pq_pre), (blk.gpsimd, pq_gp),
        (blk.vector, pq_vec), (blk.tensor, pq_tensor),
    ])
    dbg_dump([("dbg_q", q_fm), ("dbg_k", k_fm), ("dbg_v", v_tm)])

    if PHLIM < 5:
        es.close()
        return nc

    def plat_vec(e, ck, w):
        e.memset(qz[0:64, 1, :], 0.0)
        e.memset(qz[64:128, 0, :], 0.0)
        nit = 2 * NT
        for it in range(nit + 2):
            nstt = 0
            if it < nit:
                lc, qt = divmod(it, NT)
                if qt == 0:
                    if lc >= 1:
                        w(e, "t_sc_%d" % (lc * NT - 1), cT)
                    e.tensor_copy(qz[0:64, 0, :], rq_fm[0:64, lc, :])
                    cV.tag(e.tensor_copy(qz[64:128, 1, :], rq_fm[64:128, lc, :]))
                    ck("v_qz_%d" % lc)
                bs = 4 * (it % 2)
                w(e, "t_sc_%d" % it, cT)
                e.scalar_tensor_tensor(
                    out=pb[bs][:, 0:64], in0=pb[bs][:, 0:64], scalar=0.125,
                    in1=latmask_t[:, qt, :], op0=Alu.mult, op1=Alu.add)
                nstt = cV.tag(e.scalar_tensor_tensor(
                    out=pb[bs][:, 64:128], in0=pb[bs][:, 64:128], scalar=0.125,
                    in1=latmask_t[:, qt, :], op0=Alu.mult, op1=Alu.add))
            nrcp = 0
            if it >= 2:
                p2 = it - 2
                p = p2 % 2
                w(e, "s_exp_%d" % p2, cS)
                e.reciprocal(out=srcp[:, 2 * p:2 * p + 1],
                             in_=ssum[:, 2 * p:2 * p + 1])
                nrcp = cV.tag(e.reciprocal(out=srcp[:, 2 * p + 1:2 * p + 2],
                                           in_=ssum[:, 2 * p + 1:2 * p + 2]))
            if it < nit:
                e.wait_ge(cV.h, nstt)
                cV.tag(e.reduce_max(smax[:, it % 2:it % 2 + 1],
                                    pb[4 * (it % 2)][:, 0:128],
                                    axis=AX.X, negate=True))
                ck("v_max_%d" % it)
            if it >= 2:
                p2 = it - 2
                p = p2 % 2
                p4 = p2 % 4
                sc2 = latbin_t[:, 0:1] if (p2 % NT) == 0 else 1.0
                e.wait_ge(cV.h, nrcp)
                e.tensor_scalar(out=pe_sb[:, p4, 0:64], in0=pe_sb[:, p4, 0:64],
                                scalar1=srcp[:, 2 * p:2 * p + 1], scalar2=sc2,
                                op0=Alu.mult, op1=Alu.mult)
                cV.tag(e.tensor_scalar(out=pe_sb[:, p4, 64:128],
                                       in0=pe_sb[:, p4, 64:128],
                                       scalar1=srcp[:, 2 * p + 1:2 * p + 2],
                                       scalar2=sc2, op0=Alu.mult, op1=Alu.mult))
                ck("v_pe_%d" % p2)
                w(e, "t_tr_%d" % p2, cT)
                cV.tag(e.tensor_copy(pt_sb[:, p, 0:128],
                                     pb[4 * (p2 % 2) + 1][:, 0:128]))
                ck("v_pt_%d" % p2)

    def plat_scalar(e, ck, w):
        nit = 2 * NT
        for it in range(nit + 4):
            if it < nit:
                p4 = it % 4
                w(e, "v_max_%d" % it, cV)
                if it >= 4:
                    w(e, "t_tr_%d" % (it - 4), cT)
                e.activation(out=pe_sb[:, p4, 0:64], in_=pb[4 * (it % 2)][:, 0:64],
                             func=AFT.Exp, bias=smax[:, it % 2:it % 2 + 1],
                             scale=1.0, accum_out=ssum[:, 2 * (it % 2):2 * (it % 2) + 1])
                cS.tag(e.activation(out=pe_sb[:, p4, 64:128],
                                    in_=pb[4 * (it % 2)][:, 64:128],
                                    func=AFT.Exp, bias=smax[:, it % 2:it % 2 + 1],
                                    scale=1.0,
                                    accum_out=ssum[:, 2 * (it % 2) + 1:2 * (it % 2) + 2]))
                ck("s_exp_%d" % it)
            if it >= 4:
                p4o = it - 4
                lc, qt = divmod(p4o, NT)
                bs = 4 * (p4o % 2)
                w(e, "t_av_%d" % p4o, cT)
                e.copy(latout[0:64, lc, 128 * qt:128 * qt + 128],
                       pb[bs + 2][0:64, 0:128])
                cS.tag(e.copy(latout[64:128, lc, 128 * qt:128 * qt + 128],
                              pb[bs + 2][64:128, 128:256]))
                ck("s_out_%d" % p4o)

    def plat_tensor(e, ck, w):
        nit = 2 * NT

        def emit_sc(e, it):
            lc, qt = divmod(it, NT)
            bs = 4 * (it % 2)
            if qt == 0:
                w(e, "v_qz_%d" % lc, cV)
            if it >= 2:
                w(e, "s_exp_%d" % (it - 2), cS)
            for sub in range(2):
                inst = e.matmul(pb[bs][:, 64 * sub:64 * sub + 64],
                                qz[:, sub, 128 * qt:128 * qt + 128],
                                rk_fm[:, lc, :], start=True, stop=True)
                if sub == 1:
                    cT.tag(inst)
            ck("t_sc_%d" % it)

        emit_sc(e, 0)
        for it in range(nit + 3):
            if 0 <= it - 2 < nit:
                p2 = it - 2
                w(e, "v_pe_%d" % p2, cV)
                if p2 >= 2:
                    w(e, "v_pt_%d" % (p2 - 2), cV)
                cT.tag(e.matmul(pb[4 * (p2 % 2) + 1][:, 0:128],
                                pe_sb[:, p2 % 4, 0:128],
                                identb[:, :], start=True, stop=True))
                ck("t_tr_%d" % p2)
            if 0 <= it - 3 < nit:
                p3 = it - 3
                lc = p3 // NT
                bs = 4 * (p3 % 2)
                w(e, "v_pt_%d" % p3, cV)
                if p3 >= 2:
                    w(e, "s_out_%d" % (p3 - 2), cS)
                e.matmul(pb[bs + 2][:, 0:128],
                         rkv1[:, LD + 128 * lc:LD + 128 * lc + 128],
                         pt_sb[:, p3 % 2, 0:128], start=True, stop=True)
                cT.tag(e.matmul(pb[bs + 2][:, 128:256],
                                rkv2[:, LD + 128 * lc:LD + 128 * lc + 128],
                                pt_sb[:, p3 % 2, 0:128], start=True, stop=True))
                ck("t_av_%d" % p3)
            if it + 1 < nit:
                emit_sc(e, it + 1)

    run_phase("plat", [
        (blk.vector, plat_vec), (blk.scalar, plat_scalar),
        (blk.tensor, plat_tensor),
    ])
    dbg_dump([("dbg_lat", latout)])

    if PHLIM < 6:
        es.close()
        return nc

    def ploc_vec(e, ck, w):
        nit = 8 * NT
        for it in range(nit + 2):
            nstt = 0
            if it < nit:
                c, qt = divmod(it, NT)
                if qt == 0:
                    if c >= 1:
                        w(e, "t_sc_%d" % (c * NT - 1), cT)
                    e.tensor_copy(qz[0:64, 0, :], q_fm[0:64, c, :])
                    cV.tag(e.tensor_copy(qz[64:128, 1, :], q_fm[64:128, c, :]))
                    ck("v_qz_%d" % c)
                bs = 3 * (it % 2)
                w(e, "t_sc_%d" % it, cT)
                mk = lmask_t[:, 0, :] if qt == 0 else lmask_t[:, 1, :]
                nstt = cV.tag(e.scalar_tensor_tensor(
                    out=pb[bs][:, :], in0=pb[bs][:, :], scalar=0.125,
                    in1=mk, op0=Alu.mult, op1=Alu.add))
            nrcp = 0
            if it >= 2:
                p2 = it - 2
                p = p2 % 2
                w(e, "s_exp_%d" % p2, cS)
                e.reciprocal(out=srcp[:, 2 * p:2 * p + 1],
                             in_=ssum[:, 2 * p:2 * p + 1])
                nrcp = cV.tag(e.reciprocal(out=srcp[:, 2 * p + 1:2 * p + 2],
                                           in_=ssum[:, 2 * p + 1:2 * p + 2]))
            if it < nit:
                e.wait_ge(cV.h, nstt)
                cV.tag(e.reduce_max(smax[:, it % 2:it % 2 + 1],
                                    pb[3 * (it % 2)][:, :],
                                    axis=AX.X, negate=True))
                ck("v_max_%d" % it)
            if it >= 2:
                p2 = it - 2
                p = p2 % 2
                p4 = p2 % 4
                sc2 = lbin_t[:, 0:1] if (p2 % NT) == 0 else 1.0
                e.wait_ge(cV.h, nrcp)
                e.tensor_scalar(out=pe_sb[:, p4, 0:256], in0=pe_sb[:, p4, 0:256],
                                scalar1=srcp[:, 2 * p:2 * p + 1], scalar2=sc2,
                                op0=Alu.mult, op1=Alu.mult)
                cV.tag(e.tensor_scalar(out=pe_sb[:, p4, 256:512],
                                       in0=pe_sb[:, p4, 256:512],
                                       scalar1=srcp[:, 2 * p + 1:2 * p + 2],
                                       scalar2=sc2, op0=Alu.mult, op1=Alu.mult))
                ck("v_pe_%d" % p2)
                w(e, "t_tr_%d" % p2, cT)
                cV.tag(e.tensor_copy(pt_sb[:, p, :], pb[3 * (p2 % 2) + 1][:, :]))
                ck("v_pt_%d" % p2)

    def ploc_scalar(e, ck, w):
        nit = 8 * NT
        for it in range(nit + 4):
            if it < nit:
                p4 = it % 4
                bs = 3 * (it % 2)
                w(e, "v_max_%d" % it, cV)
                if it >= 4:
                    w(e, "t_tr_%d" % (it - 4), cT)
                e.activation(out=pe_sb[:, p4, 0:256], in_=pb[bs][:, 0:256],
                             func=AFT.Exp, bias=smax[:, it % 2:it % 2 + 1],
                             scale=1.0,
                             accum_out=ssum[:, 2 * (it % 2):2 * (it % 2) + 1])
                cS.tag(e.activation(out=pe_sb[:, p4, 256:512],
                                    in_=pb[bs][:, 256:512],
                                    func=AFT.Exp, bias=smax[:, it % 2:it % 2 + 1],
                                    scale=1.0,
                                    accum_out=ssum[:, 2 * (it % 2) + 1:2 * (it % 2) + 2]))
                ck("s_exp_%d" % it)
            if it >= 4:
                p4o = it - 4
                c, qt = divmod(p4o, NT)
                bs2 = 3 * (p4o % 2) + 2
                w(e, "t_av_%d" % p4o, cT)
                e.copy(local_fm[0:64, c, 128 * qt:128 * qt + 128],
                       pb[bs2][0:64, 0:128])
                cS.tag(e.copy(local_fm[64:128, c, 128 * qt:128 * qt + 128],
                              pb[bs2][64:128, 128:256]))
                ck("s_out_%d" % p4o)

    def ploc_tensor(e, ck, w):
        nit = 8 * NT

        def emit_sc(e, it):
            c, qt = divmod(it, NT)
            bs = 3 * (it % 2)
            if qt == 0:
                w(e, "v_qz_%d" % c, cV)
            if it >= 2:
                w(e, "s_exp_%d" % (it - 2), cS)
            for sub in range(2):
                inst = e.matmul(pb[bs][:, 256 * sub:256 * sub + 256],
                                qz[:, sub, 128 * qt:128 * qt + 128],
                                k_fm[:, c, 128 * qt:128 * qt + 256],
                                start=True, stop=True)
                if sub == 1:
                    cT.tag(inst)
            ck("t_sc_%d" % it)

        emit_sc(e, 0)
        for it in range(nit + 3):
            if 0 <= it - 2 < nit:
                p2 = it - 2
                w(e, "v_pe_%d" % p2, cV)
                if p2 >= 2:
                    w(e, "v_pt_%d" % (p2 - 2), cV)
                for q4 in range(4):
                    inst = e.matmul(pb[3 * (p2 % 2) + 1][:, 128 * q4:128 * q4 + 128],
                                    pe_sb[:, p2 % 4, 128 * q4:128 * q4 + 128],
                                    identb[:, :], start=True, stop=True)
                    if q4 == 3:
                        cT.tag(inst)
                ck("t_tr_%d" % p2)
            if 0 <= it - 3 < nit:
                p3 = it - 3
                c, qt = divmod(p3, NT)
                bs = 3 * (p3 % 2)
                w(e, "v_pt_%d" % p3, cV)
                if p3 >= 2:
                    w(e, "s_out_%d" % (p3 - 2), cS)
                for sub in range(2):
                    for kc in range(2):
                        inst = e.matmul(
                            pb[bs + 2][:, 128 * sub:128 * sub + 128],
                            v_tm[:, qt + kc, 128 * c:128 * c + 128],
                            pt_sb[:, p3 % 2, 256 * sub + 128 * kc:256 * sub + 128 * kc + 128],
                            start=(kc == 0), stop=(kc == 1))
                        if sub == 1 and kc == 1:
                            cT.tag(inst)
                ck("t_av_%d" % p3)
            if it + 1 < nit:
                emit_sc(e, it + 1)

    run_phase("ploc", [
        (blk.vector, ploc_vec), (blk.scalar, ploc_scalar),
        (blk.tensor, ploc_tensor),
    ])
    dbg_dump([("dbg_loc", local_fm)])

    if PHLIM < 7:
        es.close()
        return nc

    def prout_gp(e, ck, w):
        ro3 = rout_w.rearrange("(c p) n -> p c n", p=128)
        dW.tag(e.dma_start(out=routw_b[:, :, :], in_=ro3), 16)
        e.wait_ge(dW.h, dW.n)

    def prout_vec(e, ck, w):
        for g in range(32):
            oc, n = divmod(g, 4)
            w(e, "t_ro_%d" % g, cT)
            cV.tag(e.tensor_scalar(out=remote[:, oc, 512 * n:512 * n + 512],
                                   in0=pb[g % 8][:, :],
                                   scalar1=routb_t[:, oc:oc + 1],
                                   scalar2=None, op0=Alu.add))
            ck("v_ro_%d" % g)

    def prout_tensor(e, ck, w):
        rw = routw_b[:, :, :].rearrange("p a n -> p (a n)")
        for g in range(32):
            oc, n = divmod(g, 4)
            if g >= 8:
                w(e, "v_ro_%d" % (g - 8), cV)
            for lc in range(2):
                inst = e.matmul(pb[g % 8][:, :],
                                rw[:, 1024 * lc + 128 * oc:1024 * lc + 128 * oc + 128],
                                latout[:, lc, 512 * n:512 * n + 512],
                                start=(lc == 0), stop=(lc == 1))
                if lc == 1:
                    cT.tag(inst)
            ck("t_ro_%d" % g)

    run_phase("prout0", [(blk.gpsimd, prout_gp)])
    run_phase("prout", [
        (blk.vector, prout_vec), (blk.tensor, prout_tensor),
    ])
    dbg_dump([("dbg_rem", remote)])

    def pg_vec(e, ck, w):
        cV.tag(e.memset(gate_sb[:, :], 0.0))
        ck("gz")
        for c in range(8):
            for n in range(4):
                g = 4 * c + n
                w(e, "t_bc_%d" % g, cT)
                sl = slice(512 * n, 512 * n + 512)
                e.tensor_tensor(out=gdiff[:, :], in0=local_fm[:, c, sl],
                                in1=remote[:, c, sl], op=Alu.subtract)
                e.drain()
                e.scalar_tensor_tensor(out=gtmp[:, :], in0=pb[4 + g % 4][:, :],
                                       scalar=1.0, in1=gdiff[:, :],
                                       op0=Alu.mult, op1=Alu.mult)
                e.drain()
                cV.tag(e.tensor_tensor(out=mixed[:, c, sl], in0=gtmp[:, :],
                                       in1=remote[:, c, sl], op=Alu.add))
                ck("v_mix_%d" % g)

    def pg_scalar(e, ck, w):
        for c in range(8):
            for n in range(4):
                g = 4 * c + n
                w(e, "t_lg_%d" % g, cT)
                if g >= 1:
                    w(e, "t_bc_%d" % (g - 1), cT)
                cS.tag(e.activation(out=gate_sb[0:2, 0:512],
                                    in_=pb[g % 4][0:2, :], func=AFT.Sigmoid))
                ck("s_sig_%d" % g)

    def pg_tensor(e, ck, w):
        w(e, "gz", cV)
        for c in range(8):
            for n in range(4):
                g = 4 * c + n
                sl = slice(512 * n, 512 * n + 512)
                if g >= 4:
                    w(e, "s_sig_%d" % (g - 4), cS)
                e.matmul(pb[g % 4][0:2, :], glgr_b[:, 0:2], local_fm[:, c, sl],
                         start=True, stop=False)
                cT.tag(e.matmul(pb[g % 4][0:2, :], glgr_b[:, 2:4],
                                remote[:, c, sl], start=False, stop=True))
                ck("t_lg_%d" % g)
                w(e, "s_sig_%d" % g, cS)
                if g >= 4:
                    w(e, "v_mix_%d" % (g - 4), cV)
                cT.tag(e.matmul(pb[4 + g % 4][:, :], sel2_b[:, :],
                                gate_sb[:, 0:512], start=True, stop=True))
                ck("t_bc_%d" % g)

    run_phase("pg", [
        (blk.vector, pg_vec), (blk.scalar, pg_scalar), (blk.tensor, pg_tensor),
    ])
    dbg_dump([("dbg_mix", mixed)])

    if PHLIM < 8:
        es.close()
        return nc

    def po_gp0(e, ck, w):
        ow3 = out_w.rearrange("(c p) n -> p c n", p=128)
        dW.tag(e.dma_start(out=outw_b[:, :, :], in_=ow3), 16)
        e.wait_ge(dW.h, dW.n)

    def po_vec(e, ck, w):
        for tt in range(NT):
            w(e, "t_o_%d" % tt, cT)
            if tt >= 2:
                w(e, "g_at_%d" % (tt - 2), dGp[tt % 2])
            b0, b1 = pb[2 * (tt % 2)], pb[2 * (tt % 2) + 1]
            e.tensor_tensor(out=attn_buf[:, tt % 2, 0:512], in0=b0[:, :],
                            in1=outb_t[:, 0:512], op=Alu.add)
            cV.tag(e.tensor_tensor(out=attn_buf[:, tt % 2, 512:1024],
                                   in0=b1[:, :], in1=outb_t[:, 512:1024],
                                   op=Alu.add))
            ck("v_o_%d" % tt)

    def po_tensor(e, ck, w):
        for tt in range(NT):
            if tt >= 2:
                w(e, "v_o_%d" % (tt - 2), cV)
            b0, b1 = pb[2 * (tt % 2)], pb[2 * (tt % 2) + 1]
            for kk in range(8):
                e.matmul(b0[:, :], mixed[:, kk, 128 * tt:128 * tt + 128],
                         outw_b[:, kk, 0:512], start=(kk == 0), stop=(kk == 7))
            for kk in range(8):
                inst = e.matmul(b1[:, :], mixed[:, kk, 128 * tt:128 * tt + 128],
                                outw_b[:, kk, 512:1024],
                                start=(kk == 0), stop=(kk == 7))
                if kk == 7:
                    cT.tag(inst)
            ck("t_o_%d" % tt)

    def po_gp(e, ck, w):
        for tt in range(NT):
            w(e, "v_o_%d" % tt, cV)
            dGp[tt % 2].tag(e.dma_start(out=attn_scr[tt, :, :],
                                        in_=attn_buf[:, tt % 2, :]), 16)
            ck("g_at_%d" % tt)

    run_phase("po0", [(blk.gpsimd, po_gp0)])
    run_phase("po", [
        (blk.vector, po_vec), (blk.tensor, po_tensor), (blk.gpsimd, po_gp),
    ])

    if PHLIM < 9:
        es.close()
        return nc

    def px_pre(e, ck, w):
        ck("d0")

    def px_sync(e, ck, w):
        for tt in range(NT):
            if tt >= 2:
                w(e, "t_x_%d" % (tt - 2), cT)
            dSp[tt % 2].tag(e.dma_start(out=xql[:, tt % 2, :],
                                        in_=xq_scr[tt, :, :]), 16)

    def px_tensor(e, ck, w):
        base = CK.get(("px", "d0"))
        for tt in range(NT):
            if base is not None and not isinstance(e, _StubEng):
                c = dSp[tt % 2]
                e.wait_ge(c.h, base[c.name] + 16 * (tt // 2 + 1))
            if tt >= 2:
                w(e, "v_x_%d" % (tt - 2), cV)
            b0, b1 = pb[2 * (tt % 2)], pb[2 * (tt % 2) + 1]
            for c in range(8):
                bnk = b0 if c < 4 else b1
                inst = e.matmul(bnk[:, 128 * (c % 4):128 * (c % 4) + 128],
                                xql[:, tt % 2, 128 * c:128 * c + 128],
                                identb[:, :], start=True, stop=True)
                if c == 7:
                    cT.tag(inst)
            ck("t_x_%d" % tt)

    def px_vec(e, ck, w):
        for tt in range(NT):
            w(e, "t_x_%d" % tt, cT)
            b0, b1 = pb[2 * (tt % 2)], pb[2 * (tt % 2) + 1]
            e.tensor_copy(XQ[:, 0:4, 128 * tt:128 * tt + 128],
                          b0[:, :].rearrange("p (a n) -> p a n", a=4))
            cV.tag(e.tensor_copy(XQ[:, 4:8, 128 * tt:128 * tt + 128],
                                 b1[:, :].rearrange("p (a n) -> p a n", a=4)))
            ck("v_x_%d" % tt)

    run_phase("px", [
        (blk.sync, px_pre), (blk.sync, px_sync), (blk.tensor, px_tensor),
        (blk.vector, px_vec),
    ])
    dbg_dump([("dbg_xq", XQ)])

    if PHLIM < 10:
        es.close()
        return nc

    def pgu_pre(e, ck, w):
        ck("d0")

    def pgu_sync(e, ck, w):
        wg3 = wgT.rearrange("(c p) n -> p c n", p=128)
        wu3 = wuT.rearrange("(c p) n -> p c n", p=128)
        for nt in range(8):
            for mi, w3 in ((0, wg3), (1, wu3)):
                idx = 2 * nt + mi
                if idx >= 2:
                    w(e, "s_rw_%d" % (idx - 2), cS)
                dSp[idx % 2].tag(e.dma_start(out=wq32[:, idx % 2, :, :],
                                   in_=w3[:, :, 512 * nt:512 * nt + 512]), 16)

    def pgu_scalar(e, ck, w):
        base = CK.get(("pgu", "d0"))

        def s_round(e, idx):
            mi = idx % 2
            if base is not None and not isinstance(e, _StubEng):
                c = dSp[idx % 2]
                e.wait_ge(c.h, base[c.name] + 16 * (idx // 2 + 1))
            if idx >= 1:
                w(e, "v_wq_%d" % (idx - 1), cV)
            cS.tag(e.activation(
                out=wtmp[:, :, :].rearrange("p a n -> p (a n)"),
                in_=wq32[:, idx % 2, :, :].rearrange("p a n -> p (a n)"),
                func=AFT.Copy, scale=wsc[:, mi:mi + 1], bias=C_RND))
            ck("s_rw_%d" % idx)

        s_round(e, 0)
        s_round(e, 1)
        for nt in range(8):
            for tt in range(NT):
                it = NT * nt + tt
                if nt + 1 < 8 and tt == 2:
                    s_round(e, 2 * (nt + 1))
                if nt + 1 < 8 and tt == 8:
                    s_round(e, 2 * (nt + 1) + 1)
                w(e, "t_g_%d" % it, cT)
                if it >= 1:
                    w(e, "v_h_%d" % (it - 1), cV)
                cS.tag(e.activation(out=sg_sb[:, :], in_=pb[tt % 2][:, :],
                                    func=AFT.Silu, scale=gsc[:, tt:tt + 1]))
                ck("s_silu_%d" % it)

    def pgu_vec(e, ck, w):
        def v_clip(e, idx, half):
            nt2, mi = divmod(idx, 2)
            dstW = wgq if mi == 0 else wuq
            sl = slice(4096 * half, 4096 * half + 4096)
            wt_f = wtmp[:, :, :].rearrange("p a n -> p (a n)")
            if half == 0:
                w(e, "s_rw_%d" % idx, cS)
                e.tensor_scalar(out=wt_f[:, sl], in0=wt_f[:, sl],
                                scalar1=C_RND, scalar2=1.0,
                                op0=Alu.subtract, op1=Alu.min)
            else:
                e.tensor_scalar(out=wt_f[:, sl], in0=wt_f[:, sl],
                                scalar1=C_RND, scalar2=1.0,
                                op0=Alu.subtract, op1=Alu.min)
                if nt2 >= 2:
                    w(e, "t_u_%d" % (NT * (nt2 - 2) + NT - 1), cT)
                cV.tag(e.tensor_scalar(
                    out=dstW[:, nt2 % 2, :, 0:256].rearrange("p a n -> p (a n)"),
                    in0=wt_f[:, 0:4096].rearrange(
                        "p (a n) -> p a n", a=8)[:, :, 0:256].rearrange(
                        "p a n -> p a n").rearrange("p a n -> p (a n)"),
                    scalar1=-1.0, scalar2=None, op0=Alu.max))
                cV.tag(e.tensor_scalar(
                    out=dstW[:, nt2 % 2, :, 256:512].rearrange("p a n -> p (a n)"),
                    in0=wt_f[:, 4096:8192].rearrange(
                        "p (a n) -> p a n", a=8)[:, :, 0:256].rearrange(
                        "p a n -> p (a n)"),
                    scalar1=-1.0, scalar2=None, op0=Alu.max))
                ck("v_wq_%d" % idx)

        pass

    def pgu_vec2(e, ck, w):
        wt_f = wtmp[:, :, :].rearrange("p a n -> p (a n)")

        def v_clip1(e, idx):
            w(e, "s_rw_%d" % idx, cS)
            nc1 = cV.tag(e.tensor_scalar(out=wt_f, in0=wt_f,
                                         scalar1=C_RND, scalar2=1.0,
                                         op0=Alu.subtract, op1=Alu.min))
            return nc1

        def v_clip2(e, idx, nc1):
            nt2, mi = divmod(idx, 2)
            dstW = wgq if mi == 0 else wuq
            if nt2 >= 2:
                w(e, "t_u_%d" % (NT * (nt2 - 2) + NT - 1), cT)
            e.wait_ge(cV.h, nc1)
            cV.tag(e.tensor_scalar(
                out=dstW[:, nt2 % 2, :, :].rearrange("p a n -> p (a n)"),
                in0=wt_f, scalar1=-1.0, scalar2=None, op0=Alu.max))
            ck("v_wq_%d" % idx)

        nc1 = v_clip1(e, 0)
        v_clip2(e, 0, nc1)
        nc1 = v_clip1(e, 1)
        v_clip2(e, 1, nc1)
        for nt in range(8):
            pend = None
            for tt in range(NT):
                it = NT * nt + tt
                if nt + 1 < 8:
                    if tt == 4:
                        pend = (2 * (nt + 1), v_clip1(e, 2 * (nt + 1)))
                    elif tt == 6 and pend is not None:
                        v_clip2(e, pend[0], pend[1])
                        pend = None
                    elif tt == 10:
                        pend = (2 * (nt + 1) + 1, v_clip1(e, 2 * (nt + 1) + 1))
                    elif tt == 12 and pend is not None:
                        v_clip2(e, pend[0], pend[1])
                        pend = None
                w(e, "s_silu_%d" % it, cS)
                w(e, "t_u_%d" % it, cT)
                g4 = it // 4
                if g4 >= 2 and it % 4 == 0:
                    w(e, "g_h_%d" % (g4 - 2), dGp[g4 % 2])
                cV.tag(e.scalar_tensor_tensor(
                    out=hwb[:, g4 % 2, it % 4, :], in0=pb[2 + tt % 2][:, :],
                    scalar=usc[:, tt:tt + 1], in1=sg_sb[:, :],
                    op0=Alu.mult, op1=Alu.mult))
                ck("v_h_%d" % it)

    def pgu_tensor(e, ck, w):
        for nt in range(8):
            w(e, "v_wq_%d" % (2 * nt + 1), cV)
            for tt in range(NT):
                it = NT * nt + tt
                if it >= 2:
                    w(e, "s_silu_%d" % (it - 2), cS)
                    w(e, "v_h_%d" % (it - 2), cV)
                bg = pb[tt % 2]
                bu = pb[2 + tt % 2]
                for kk in range(8):
                    inst = e.matmul(bg[:, :], XQ[:, kk, 128 * tt:128 * tt + 128],
                                    wgq[:, nt % 2, kk, :],
                                    start=(kk == 0), stop=(kk == 7))
                    if kk == 7:
                        cT.tag(inst)
                        ck("t_g_%d" % it)
                    inst = e.matmul(bu[:, :], XQ[:, kk, 128 * tt:128 * tt + 128],
                                    wuq[:, nt % 2, kk, :],
                                    start=(kk == 0), stop=(kk == 7))
                    if kk == 7:
                        cT.tag(inst)
                        ck("t_u_%d" % it)

    def pgu_gp(e, ck, w):
        for nt in range(8):
            for g4l in range(4):
                g4 = 4 * nt + g4l
                w(e, "v_h_%d" % (NT * nt + 4 * g4l + 3), cV)
                dGp[g4 % 2].tag(e.dma_start(
                    out=h_scr[4 * g4l:4 * g4l + 4, :, nt, :].rearrange(
                        "a p n -> p a n"),
                    in_=hwb[:, g4 % 2, :, :]), 16)
                ck("g_h_%d" % g4)

    run_phase("pgu", [
        (blk.sync, pgu_pre), (blk.sync, pgu_sync), (blk.scalar, pgu_scalar),
        (blk.vector, pgu_vec2), (blk.tensor, pgu_tensor), (blk.gpsimd, pgu_gp),
    ])

    if PHLIM < 11:
        es.close()
        return nc

    def pwd_pre(e, ck, w):
        ck("d0")

    def pwd_sync(e, ck, w):
        wd3 = wdT.rearrange("(c p) n -> c p n", p=128)
        for c in range(32):
            if c >= 2:
                w(e, "s_rd_%d" % (c - 2), cS)
            dSp[c % 2].tag(e.dma_start(out=wdf[:, c % 2, :], in_=wd3[c, :, :]),
                           16)

    def pwd_scalar(e, ck, w):
        base = CK.get(("pwd", "d0"))
        for c in range(32):
            if base is not None and not isinstance(e, _StubEng):
                ctr = dSp[c % 2]
                e.wait_ge(ctr.h, base[ctr.name] + 16 * (c // 2 + 1))
            if c >= 1:
                w(e, "v_wd1_%d" % (c - 1), cV)
            cS.tag(e.activation(out=wdt_r[:, c % 2, :], in_=wdf[:, c % 2, :],
                                func=AFT.Copy, scale=wsc[:, 2:3], bias=C_RND))
            ck("s_rd_%d" % c)

    def pwd_vec(e, ck, w):
        for c in range(32):
            w(e, "s_rd_%d" % c, cS)
            cV.tag(e.tensor_scalar(out=wdt_r[:, c % 2, :], in0=wdt_r[:, c % 2, :],
                                   scalar1=C_RND, scalar2=1.0,
                                   op0=Alu.subtract, op1=Alu.min))
            ck("v_wd1_%d" % c)
            e.drain()
            cV.tag(e.tensor_scalar(out=wdq2[:, c, :], in0=wdt_r[:, c % 2, :],
                                   scalar1=-1.0, scalar2=None, op0=Alu.max))
            ck("v_wd2_%d" % c)

    run_phase("pwd", [
        (blk.sync, pwd_pre), (blk.sync, pwd_sync), (blk.scalar, pwd_scalar),
        (blk.vector, pwd_vec),
    ])

    def pd_pre(e, ck, w):
        ck("d0")

    def pd_sync(e, ck, w):
        for tt in range(NT):
            if tt >= 2:
                w(e, "v_fin_%d" % (tt - 2), cV)
            dSp[tt % 2].tag(e.dma_start(
                out=hrb[:, tt % 2, :],
                in_=h_scr[tt, :, :, :].rearrange("p a n -> p (a n)")), 16)
            dSp[tt % 2].tag(e.dma_start(out=atl[:, tt % 2, :],
                                        in_=attn_scr[tt, :, :]), 16)
            dSp[tt % 2].tag(e.dma_start(out=xres[:, tt % 2, :],
                                        in_=x_own[128 * tt:128 * tt + 128, :]), 16)

    def pd_vec(e, ck, w):
        base = CK.get(("pd", "d0"))
        for tt in range(NT + 1):
            if tt < NT:
                if base is not None and not isinstance(e, _StubEng):
                    c = dSp[tt % 2]
                    e.wait_ge(c.h, base[c.name] + 48 * (tt // 2 + 1))
                if tt >= 1:
                    w(e, "s_rh_%d" % (tt - 1), cS)
                n0 = cV.tag(e.reduce_max(am[:, :], hrb[:, tt % 2, :], axis=AX.X,
                                         apply_absolute_value=True))
                e.wait_ge(cV.h, n0)
                n1 = cV.tag(e.tensor_scalar(out=t1[:, :], in0=am[:, :],
                                            scalar1=1e-5, scalar2=None,
                                            op0=Alu.max))
                e.wait_ge(cV.h, n1)
                n2 = cV.tag(e.reciprocal(out=t3[:, :], in_=t1[:, :]))
                e.wait_ge(cV.h, n2)
                cV.tag(e.tensor_scalar(out=sh_t[:, tt % 2:tt % 2 + 1],
                                       in0=t3[:, :], scalar1=127.0,
                                       scalar2=None, op0=Alu.mult))
                ck("v_sh_%d" % tt)
                cV.tag(e.tensor_scalar(out=dsc_t[:, tt % 2:tt % 2 + 1],
                                       in0=t1[:, :], scalar1=wsc[:, 5:6],
                                       scalar2=1.0 / 127.0,
                                       op0=Alu.mult, op1=Alu.mult))
                ck("v_dsc_%d" % tt)
                w(e, "s_rh_%d" % tt, cS)
                if tt >= 2:
                    w(e, "t_tr_%d_%d" % (tt - 2, 7), cT)
                cV.tag(e.tensor_scalar(out=xh_tm2[:, tt % 2, :], in0=tq_sb[:, :],
                                       scalar1=C_RND, scalar2=None,
                                       op0=Alu.subtract))
                ck("v_xh_%d" % tt)
                for rg in range(8):
                    w(e, "t_tr_%d_%d" % (tt, rg), cT)
                    bnk = pb[4 + rg % 4]
                    cV.tag(e.tensor_copy(
                        xh_fm[:, 4 * rg:4 * rg + 4, :].rearrange(
                            "p a n -> p (a n)"),
                        bnk[:, :]))
                    ck("v_ev_%d_%d" % (tt, rg))
            if tt >= 1:
                pt_ = tt - 1
                pp = pt_ % 2
                w(e, "t_dn_%d" % pt_, cT)
                if pt_ >= 2:
                    w(e, "g_y_%d" % (pt_ - 2), dGp[pt_ % 2])
                bd0, bd1 = pb[pt_ % 2], pb[2 + pt_ % 2]
                e.scalar_tensor_tensor(out=obuf[:, pp, 0:512], in0=bd0[:, :],
                                       scalar=dsc_t[:, pp:pp + 1],
                                       in1=atl[:, pp, 0:512],
                                       op0=Alu.mult, op1=Alu.add)
                nst = cV.tag(e.scalar_tensor_tensor(
                    out=obuf[:, pp, 512:1024], in0=bd1[:, :],
                    scalar=dsc_t[:, pp:pp + 1],
                    in1=atl[:, pp, 512:1024],
                    op0=Alu.mult, op1=Alu.add))
                e.wait_ge(cV.h, nst)
                e.tensor_tensor(out=obuf[:, pp, 0:512], in0=obuf[:, pp, 0:512],
                                in1=xres[:, pp, 0:512], op=Alu.add)
                cV.tag(e.tensor_tensor(out=obuf[:, pp, 512:1024],
                                       in0=obuf[:, pp, 512:1024],
                                       in1=xres[:, pp, 512:1024], op=Alu.add))
                ck("v_fin_%d" % pt_)

    def pd_scalar(e, ck, w):
        for tt in range(NT):
            w(e, "v_sh_%d" % tt, cV)
            if tt >= 1:
                w(e, "v_xh_%d" % (tt - 1), cV)
            cS.tag(e.activation(out=tq_sb[:, :], in_=hrb[:, tt % 2, :],
                                func=AFT.Copy, scale=sh_t[:, tt % 2:tt % 2 + 1],
                                bias=C_RND))
            ck("s_rh_%d" % tt)

    def pd_tensor(e, ck, w):
        for tt in range(NT):
            w(e, "v_xh_%d" % tt, cV)
            for rg in range(8):
                if rg >= 4:
                    w(e, "v_ev_%d_%d" % (tt, rg - 4), cV)
                elif tt >= 1:
                    w(e, "v_ev_%d_%d" % (tt - 1, rg + 4), cV)
                bnk = pb[4 + rg % 4]
                for u in range(4):
                    kc = 4 * rg + u
                    inst = e.matmul(bnk[:, 128 * u:128 * u + 128],
                                    xh_tm2[:, tt % 2, 128 * kc:128 * kc + 128],
                                    identb[:, :], start=True, stop=True)
                    if u == 3:
                        cT.tag(inst)
                ck("t_tr_%d_%d" % (tt, rg))
            w(e, "v_ev_%d_%d" % (tt, 7), cV)
            if tt >= 2:
                w(e, "v_fin_%d" % (tt - 2), cV)
            bd0, bd1 = pb[tt % 2], pb[2 + tt % 2]
            for kk in range(32):
                e.matmul(bd0[:, :], xh_fm[:, kk, :], wdq2[:, kk, 0:512],
                         start=(kk == 0), stop=(kk == 31))
                inst = e.matmul(bd1[:, :], xh_fm[:, kk, :], wdq2[:, kk, 512:1024],
                                start=(kk == 0), stop=(kk == 31))
                if kk == 31:
                    cT.tag(inst)
            ck("t_dn_%d" % tt)

    def pd_gp(e, ck, w):
        for tt in range(NT):
            w(e, "v_fin_%d" % tt, cV)
            dGp[tt % 2].tag(e.dma_start(out=y[128 * tt:128 * tt + 128, :],
                                        in_=obuf[:, tt % 2, :]), 16)
            ck("g_y_%d" % tt)
        e.wait_ge(dG.h, dG.n)
        e.wait_ge(dGb.h, dGb.n)

    run_phase("pd", [
        (blk.sync, pd_pre), (blk.sync, pd_sync), (blk.vector, pd_vec),
        (blk.scalar, pd_scalar), (blk.tensor, pd_tensor), (blk.gpsimd, pd_gp),
    ])

    es.close()
    return nc


def _masks_for_core(h):
    r = np.arange(128)
    c = np.arange(256)
    band = (c[None, :] >= r[:, None]) & (c[None, :] <= r[:, None] + 127)
    gen = np.where(band, 0.0, NEG).astype(np.float32)
    if h == 0:
        first = np.where(band & (c[None, :] >= 128), 0.0, NEG).astype(np.float32)
        lb = np.ones((128, 1), np.float32)
        lb[0, 0] = 0.0
        latb = np.ones((128, 1), np.float32)
        latb[0:64, 0] = 0.0
    else:
        first = gen.copy()
        lb = np.ones((128, 1), np.float32)
        latb = np.ones((128, 1), np.float32)
    lm = np.stack([np.concatenate([first, first], axis=1),
                   np.concatenate([gen, gen], axis=1)], axis=0)

    r0 = 2048 * h
    own_c0 = r0 // 64
    oth_c0 = (2048 - r0) // 64
    slot_chunk = np.concatenate([own_c0 + np.arange(32), oth_c0 + np.arange(32)])
    latm = np.empty((16, 128, 64), np.float32)
    for i in range(16):
        qchunk = (r0 + 128 * i + np.arange(128)) // 64
        allowed = slot_chunk[None, :] < qchunk[:, None]
        latm[i] = np.where(allowed, 0.0, NEG).astype(np.float32)
    return lm, latm, lb, latb


def _build_inputs(inputs):
    f = np.float32
    x = np.asarray(inputs["L0_attn_x"], f)
    qkv_w = np.ascontiguousarray(np.asarray(inputs["L0_attn_qkv_w"], f))
    qkv_b = np.asarray(inputs["L0_attn_qkv_b"], f)
    rq_w = np.ascontiguousarray(np.asarray(inputs["L0_attn_rq_w"], f))
    rq_b = np.asarray(inputs["L0_attn_rq_b"], f)
    rkv_w = np.ascontiguousarray(np.asarray(inputs["L0_attn_rkv_w"], f))
    rkv_b = np.asarray(inputs["L0_attn_rkv_b"], f)
    rout_w = np.ascontiguousarray(np.asarray(inputs["L0_attn_rout_w"], f))
    rout_b = np.asarray(inputs["L0_attn_rout_b"], f)
    out_w = np.ascontiguousarray(np.asarray(inputs["L0_attn_out_w"], f))
    out_b = np.asarray(inputs["L0_attn_out_b"], f)
    gl_w = np.asarray(inputs["L0_attn_gl_w"], f)
    gr_w = np.asarray(inputs["L0_attn_gr_w"], f)
    w_gate = np.asarray(inputs["L0_ffn_w_gate"], f)
    w_up = np.asarray(inputs["L0_ffn_w_up"], f)
    w_down = np.asarray(inputs["L0_ffn_w_down"], f)

    wgT = np.ascontiguousarray(w_gate.T)
    wuT = np.ascontiguousarray(w_up.T)
    wdT = np.ascontiguousarray(w_down.T)

    qkb_fm = np.ascontiguousarray(qkv_b[:2048].reshape(16, 128).T)
    vb_bc = np.ascontiguousarray(np.broadcast_to(qkv_b[2048:], (128, 1024)))
    rqb_fm = np.ascontiguousarray(rq_b.reshape(2, 128).T)
    rkvb_bc = np.ascontiguousarray(np.broadcast_to(rkv_b, (128, 512)))
    routb_fm = np.ascontiguousarray(rout_b.reshape(8, 128).T)
    outb_bc = np.ascontiguousarray(np.broadcast_to(out_b, (128, 1024)))

    glgr = np.zeros((128, 4), f)
    glgr[0:64, 0] = gl_w[:, 0]
    glgr[64:128, 1] = gl_w[:, 0]
    glgr[0:64, 2] = gr_w[:, 0]
    glgr[64:128, 3] = gr_w[:, 0]
    sel2 = np.zeros((128, 128), f)
    sel2[0, 0:64] = 1.0
    sel2[1, 64:128] = 1.0

    identb = np.eye(128, dtype=f).astype(ml_dtypes.bfloat16)
    poolz = np.zeros((128, 196), f)
    poolz[0:64, 64] = 1.0
    poolz[64:128, 65] = 1.0
    poolz = poolz.astype(ml_dtypes.bfloat16)

    shared = dict(qkv_w=qkv_w, rq_w=rq_w, rkv_w=rkv_w, rout_w=rout_w,
                  out_w=out_w, wgT=wgT, wuT=wuT, wdT=wdT,
                  qkb_fm=qkb_fm, vb_bc=vb_bc, rqb_fm=rqb_fm, rkvb_bc=rkvb_bc,
                  routb_fm=routb_fm, outb_bc=outb_bc,
                  glgr_in=glgr, sel2_in=sel2, identb_in=identb,
                  poolz_in=poolz)

    in_maps = []
    for core in range(8):
        b, h = core // 2, core % 2
        r0 = 2048 * h
        lm, latm, lb, latb = _masks_for_core(h)
        m = dict(shared)
        m["x_own"] = np.ascontiguousarray(x[b, r0:r0 + 2048])
        if h == 0:
            m["x_halo"] = np.zeros((128, 1024), f)
        else:
            m["x_halo"] = np.ascontiguousarray(x[b, r0 - 128:r0])
        m["x_other"] = np.ascontiguousarray(x[b, 2048 - r0:4096 - r0])
        m["lmask"] = lm
        m["latmask"] = latm
        m["lbin"] = lb
        m["latbin"] = latb
        in_maps.append(m)
    return in_maps


_nc_cache = [None]


def kernel(**inputs):
    in_maps = _build_inputs(inputs)
    if _nc_cache[0] is None:
        _nc_cache[0] = build_nc()
    nc = _nc_cache[0]
    res = run_bass_kernel_spmd(nc, in_maps, core_ids=list(range(8)))
    out = np.empty((B, S, D), np.float32)
    for core in range(8):
        b, h = core // 2, core % 2
        out[b, 2048 * h:2048 * h + 2048] = res.results[core]["y"]
    return out


# version 18
# speedup vs baseline: 1.0472x; 1.0169x over previous
import numpy as np
import ml_dtypes
from contextlib import ExitStack

import concourse.bass as bass
import concourse.mybir as mybir
from concourse.bass_utils import run_bass_kernel_spmd
from concourse.alu_op_type import AluOpType as Alu

F32 = mybir.dt.float32
BF16 = mybir.dt.bfloat16
AFT = mybir.ActivationFunctionType
AX = mybir.AxisListType

D = 1024
S = 4096
B = 4
NH = 16
HD = 64
WIN = 128
LD = 256
LH = 4
NLAT = 64
INNER = 4096
T = 2048
NT = 16
NEG = -1.0e9
C_RND = 12582912.0
EPS = 1e-5

import os
PHLIM = int(os.environ.get("KPHLIM", "99"))
DBG = os.environ.get("KDBG", "0") == "1"

ARENA0 = 23552
ARENA_CAP = 229376


class Ctr:
    def __init__(self, es, nc, name):
        self.h = es.enter_context(nc.semaphore(name))
        self.n = 0
        self.name = name

    def tag(self, inst, k=1):
        if k:
            if not isinstance(inst, _StubInst) and inst is not None:
                inst.then_inc(self.h, k)
            self.n += k
        return self.n


class _StubInst:
    def then_inc(self, sem, k):
        return self


class _StubEng:
    def __getattr__(self, name):
        if name == "wait_ge":
            return lambda *a, **kw: None
        return lambda *a, **kw: _StubInst()


def build_nc():
    nc = bass.Bass("TRN2", target_bir_lowering=False)

    def din(name, shape, dt=F32):
        return nc.dram_tensor(name, list(shape), dt, kind="ExternalInput").ap()

    x_own = din("x_own", [T, D])
    x_halo = din("x_halo", [WIN, D])
    x_other = din("x_other", [T, D])
    qkv_w = din("qkv_w", [D, 3 * D])
    rq_w = din("rq_w", [D, LD])
    rkv_w = din("rkv_w", [D, 2 * LD])
    rout_w = din("rout_w", [LD, D])
    out_w = din("out_w", [D, D])
    wgT = din("wgT", [D, INNER])
    wuT = din("wuT", [D, INNER])
    wdT = din("wdT", [INNER, D])
    qkb_fm = din("qkb_fm", [128, 16])
    vb_bc = din("vb_bc", [128, D])
    rqb_fm = din("rqb_fm", [128, 2])
    rkvb_bc = din("rkvb_bc", [128, 2 * LD])
    routb_fm = din("routb_fm", [128, 8])
    outb_bc = din("outb_bc", [128, D])
    glgr_in = din("glgr_in", [128, 4])
    sel2_in = din("sel2_in", [128, 128])
    identb_in = din("identb_in", [128, 128], BF16)
    poolz_in = din("poolz_in", [128, 196], BF16)
    lmask = din("lmask", [2, 128, 512])
    latmask = din("latmask", [16, 128, 64])
    lbin = din("lbin", [128, 1])
    latbin = din("latbin", [128, 1])

    y = nc.dram_tensor("y", [T, D], F32, kind="ExternalOutput").ap()
    xq_scr = nc.dram_tensor("xq_scr", [NT, 128, D], BF16, kind="Internal").ap()
    attn_scr = nc.dram_tensor("attn_scr", [NT, 128, D], F32, kind="Internal").ap()
    h_scr = nc.dram_tensor("h_scr", [NT, 128, 8, 512], F32, kind="Internal").ap()

    dbg = {}
    if DBG:
        for nm, shape in [
            ("dbg_xn", [128, 8 * 2176]), ("dbg_sxr", [128, 16]),
            ("dbg_wsc", [128, 8]), ("dbg_rq", [128, 2 * 2048]),
            ("dbg_rk", [128, 2 * 64]), ("dbg_rkv1", [128, 512]),
            ("dbg_rkv2", [128, 512]), ("dbg_q", [128, 8 * 2048]),
            ("dbg_k", [128, 8 * 2176]), ("dbg_v", [128, 17 * 1024]),
            ("dbg_lat", [128, 2 * 2048]), ("dbg_loc", [128, 8 * 2048]),
            ("dbg_rem", [128, 8 * 2048]), ("dbg_mix", [128, 8 * 2048]),
            ("dbg_xq", [128, 8 * 2048]), ("dbg_pooled", [128, 8 * 192]),
        ]:
            dbg[nm] = nc.dram_tensor(nm, shape, F32, kind="ExternalOutput").ap()

    cursor = [0]

    def AT(name, shape, dt, off=None):
        nbytes = int(np.prod(shape[1:])) * (2 if dt == BF16 else 4)
        if off is None:
            off = ((cursor[0] + 63) // 64) * 64
        end = off + nbytes
        cursor[0] = max(cursor[0], end)
        assert end <= ARENA_CAP - ARENA0, (name, off, nbytes, end)
        t = nc.alloc_sbuf_tensor_at(name, list(shape), dt, offset=ARENA0 + off)
        return t.ap()

    identb = AT("identb", [128, 128], BF16)
    poolz = AT("poolz", [128, 196], BF16)
    lmask_t = AT("lmask_t", [128, 2, 512], F32)
    latmask_t = AT("latmask_t", [128, 16, 64], F32)
    lbin_t = AT("lbin_t", [128, 1], F32)
    latbin_t = AT("latbin_t", [128, 1], F32)
    qkb_t = AT("qkb_t", [128, 16], F32)
    vb_t = AT("vb_t", [128, D], F32)
    rqb_t = AT("rqb_t", [128, 2], F32)
    rkvb_t = AT("rkvb_t", [128, 2 * LD], F32)
    routb_t = AT("routb_t", [128, 8], F32)
    outb_t = AT("outb_t", [128, D], F32)
    glgr_f = AT("glgr_f", [128, 4], F32)
    glgr_b = AT("glgr_b", [128, 4], BF16)
    sel2_f = AT("sel2_f", [128, 128], F32)
    sel2_b = AT("sel2_b", [128, 128], BF16)
    onesf = AT("onesf", [128, 130], F32)
    sxr = AT("sxr", [128, NT], F32)
    gsc = AT("gsc", [128, NT], F32)
    usc = AT("usc", [128, NT], F32)
    wsc = AT("wsc", [128, 8], F32)
    absS = AT("absS", [128, 48], F32)
    stt = AT("stt", [128, 16], F32)
    mv = AT("mv", [128, 1], F32)
    t1 = AT("t1", [128, 1], F32)
    t2 = AT("t2", [128, 1], F32)
    t3 = AT("t3", [128, 1], F32)
    t4 = AT("t4", [128, 1], F32)
    rr = AT("rr", [128, 1], F32)
    r0 = AT("r0", [128, 1], F32)
    s_t = AT("s_t", [128, 2], F32)
    am = AT("am", [128, 1], F32)
    smax = AT("smax", [128, 8], F32)
    ssum = AT("ssum", [128, 8], F32)
    srcp = AT("srcp", [128, 8], F32)
    sh_t = AT("sh_t", [128, 2], F32)
    hmax = AT("hmax", [128, 128], F32)
    dsc_t = AT("dsc_t", [128, 2], F32)
    PERS_END = cursor[0]
    BIG0 = ((PERS_END + 63) // 64) * 64
    assert BIG0 <= 27648, BIG0

    XN = AT("XN", [128, 8, 2176], BF16, BIG0)
    Q0 = BIG0 + 34816
    q_fm = AT("q_fm", [128, 8, T], BF16, Q0)
    k_fm = AT("k_fm", [128, 8, 2176], BF16, BIG0 + 67584)
    V0 = BIG0 + 102400
    v_tm = AT("v_tm", [128, 17, D], BF16, V0)
    R0 = BIG0 + 137216
    rq_fm = AT("rq_fm", [128, 2, T], BF16, R0)
    xbuf = AT("xbuf", [128, 2, D], F32, Q0)
    xnf = AT("xnf", [128, D], F32, Q0 + 8192)
    xnb = AT("xnb", [128, 2, D], BF16, Q0 + 12288)
    xqb = AT("xqb", [128, 2, D], BF16, Q0 + 16384)
    qtmp = AT("qtmp", [128, D], F32, Q0 + 20480)
    sqd = AT("sqd", [128, 2, D], F32, Q0 + 24576)
    absb = AT("absb", [128, 2, 2048], F32, Q0)
    absd = AT("absd", [128, 2, 2048], F32, Q0 + 16384)
    outw_b = AT("outw_b", [128, 8, D], BF16, V0)
    attn_buf = AT("attn_buf", [128, 2, D], F32, V0 + 16384)
    gate_sb = AT("gate_sb", [128, 512], BF16, R0)
    gdiff = AT("gdiff", [128, 512], BF16, R0 + 1024)
    gtmp = AT("gtmp", [128, 512], BF16, R0 + 2048)
    TR0 = BIG0 + 145408
    wqk_b = AT("wqk_b", [128, 2, 8, 128], BF16, TR0)
    wv_b = AT("wv_b", [128, 2, 8, 256], BF16, TR0 + 4096)
    rkvw_b = AT("rkvw_b", [128, 8, 512], BF16, TR0 + 4096)
    routw_b = AT("routw_b", [128, 2, D], BF16, TR0 + 4096)
    rkv1 = AT("rkv1", [128, 2 * LD], BF16, TR0 + 12288)
    rkv2 = AT("rkv2", [128, 2 * LD], BF16, TR0 + 13312)
    rk_fm = AT("rk_fm", [128, 2, NLAT], BF16, TR0 + 14336)
    ltmp = AT("ltmp", [128, 512], F32, TR0 + 19776)
    qz = AT("qz", [128, 2, T], BF16, TR0 + 16704)
    pooled_pad = AT("pooled_pad", [128, 8, 192], BF16, TR0 + 16704)
    pe_sb = AT("pe_sb", [128, 4, 512], BF16, TR0 + 24896)
    pt_sb = AT("pt_sb", [128, 2, 512], BF16, TR0 + 14656)
    latout = AT("latout", [128, 2, T], BF16, TR0 + 28992)
    local_fm = AT("local_fm", [128, 8, T], BF16, BIG0)
    remote = AT("remote", [128, 8, T], BF16, Q0)
    mixed = AT("mixed", [128, 8, T], BF16, BIG0 + 67584)

    XQ = AT("XQ", [128, 8, T], BF16, BIG0)
    xql = AT("xql", [128, 2, D], BF16, BIG0 + 32768)
    wgq = AT("wgq", [128, 2, 8, 512], BF16, BIG0 + 36864)
    wuq = AT("wuq", [128, 2, 8, 512], BF16, BIG0 + 53248)
    sg_sb = AT("sg_sb", [128, 512], F32, BIG0 + 69632)
    hwb = AT("hwb", [128, 2, 4, 512], F32, BIG0 + 71680)
    wq32 = AT("wq32", [128, 2, 8, 512], F32, BIG0 + 88064)
    wtmp = AT("wtmp", [128, 8, 512], F32, BIG0 + 120832)
    wdq2 = AT("wdq2", [128, 32, D], BF16, BIG0 + 36864)
    hrb = AT("hrb", [128, 2, INNER], F32, BIG0)
    tq_sb = AT("tq_sb", [128, INNER], F32, BIG0 + 102400)
    xh_tm = AT("xh_tm", [128, INNER], BF16, BIG0 + 118784)
    xh_fm = AT("xh_fm", [128, 32, 128], BF16, BIG0 + 126976)
    atl = AT("atl", [128, 2, D], F32, BIG0 + 135168)
    xres = AT("xres", [128, 2, D], F32, BIG0 + 143360)
    obuf = AT("obuf", [128, 2, D], F32, BIG0 + 151552)
    wdf = AT("wdf", [128, 2, D], F32, BIG0 + 159744)
    wdt_r = AT("wdt_r", [128, 2, D], F32, BIG0 + 167936)
    xh_tm2 = AT("xh_tm2", [128, 2, INNER], BF16, BIG0 + 159744)
    assert BIG0 + 180544 <= 205824, BIG0

    pb = [nc.alloc_psum_tensor("pb%d" % i, [128, 512], F32).ap() for i in range(8)]

    es = ExitStack()
    dSp = [Ctr(es, nc, "dSa"), Ctr(es, nc, "dSb")]
    dGp = [Ctr(es, nc, "dGa"), Ctr(es, nc, "dGb")]
    dWp = [Ctr(es, nc, "dWa"), Ctr(es, nc, "dWb")]
    dS, dSb = dSp
    dG, dGb = dGp
    dW, dWb = dWp
    cV = Ctr(es, nc, "cV")
    cS = Ctr(es, nc, "cS")
    cT = Ctr(es, nc, "cT")
    cG = Ctr(es, nc, "cG")
    ALLC = dSp + dGp + dWp + [cV, cS, cT, cG]

    blk = es.enter_context(nc.Block())

    def full_barrier():
        sn = [(c, c.n) for c in ALLC]

        def w_all(e):
            for c, v in sn:
                if v > 0:
                    e.wait_ge(c.h, v)
        blk.sync(w_all)
        blk.vector(w_all)
        blk.scalar(w_all)
        blk.tensor(w_all)
        blk.gpsimd(w_all)

    CK = {}

    def run_phase(name, sections):
        base = {c.name: c.n for c in ALLC}
        mode = ["count"]

        def ck(label):
            if mode[0] == "count":
                CK[(name, label)] = {c.name: c.n for c in ALLC}

        def w(e, label, ctr):
            if mode[0] == "count":
                return
            key = (name, label)
            if key in CK:
                v = CK[key][ctr.name]
                if v > 0:
                    e.wait_ge(ctr.h, v)

        stub = _StubEng()
        for _, gen in sections:
            gen(stub, ck, w)
        after = {c.name: c.n for c in ALLC}
        for c in ALLC:
            c.n = base[c.name]
        mode[0] = "real"
        for meth, gen in sections:
            meth(lambda e, g=gen: g(e, ck, w))
        for c in ALLC:
            assert c.n == after[c.name], (name, c.name, c.n, after[c.name])
        full_barrier()

    def dbg_dump(pairs):
        if not DBG:
            return

        def g(e):
            for nm, src in pairs:
                if nm in dbg:
                    sflat = src if len(src.shape) == 2 else src.rearrange(
                        "p a n -> p (a n)")
                    dG.tag(e.dma_start(out=dbg[nm], in_=sflat), 16)
            e.wait_ge(dG.h, dG.n)
        blk.gpsimd(g)
        full_barrier()

    def p0_sync(e, ck, w):
        p0i = [0]
        for dst, src in [
            (identb[:, :], identb_in[:, :]),
            (poolz[:, :], poolz_in[:, :]),
            (lmask_t[:, :, :], lmask[:, :, :].rearrange("a p n -> p a n")),
            (latmask_t[:, :, :], latmask[:, :, :].rearrange("a p n -> p a n")),
            (lbin_t[:, :], lbin[:, :]), (latbin_t[:, :], latbin[:, :]),
            (qkb_t[:, :], qkb_fm[:, :]), (vb_t[:, :], vb_bc[:, :]),
            (rqb_t[:, :], rqb_fm[:, :]), (rkvb_t[:, :], rkvb_bc[:, :]),
            (routb_t[:, :], routb_fm[:, :]), (outb_t[:, :], outb_bc[:, :]),
            (glgr_f[:, :], glgr_in[:, :]), (sel2_f[:, :], sel2_in[:, :]),
        ]:
            dSp[p0i[0] % 2].tag(e.dma_start(out=dst, in_=src), 16)
            p0i[0] += 1
        ck("loads")

    def p0_vec(e, ck, w):
        w(e, "loads", dS)
        w(e, "loads", dSb)
        e.tensor_copy(glgr_b[:, :], glgr_f[:, :])
        e.tensor_copy(sel2_b[:, :], sel2_f[:, :])
        e.memset(onesf[:, 0:1], 1.0)
        e.memset(onesf[0:1, 1:130], 1.0)
        cV.tag(e.memset(wsc[:, :], 0.0))

    run_phase("p0", [(blk.sync, p0_sync), (blk.vector, p0_vec)])

    if PHLIM < 1:
        es.close()
        return nc

    def pabs_pre(e, ck, w):
        ck("d0")

    def pabs_sync(e, ck, w):
        idx = 0
        wg3 = wgT.rearrange("(c p) n -> p c n", p=128)
        wu3 = wuT.rearrange("(c p) n -> p c n", p=128)
        wd3 = wdT.rearrange("(c p) n -> p c n", p=128)
        for mat in range(3):
            for sl in range(16):
                w(e, "sq_%d" % (idx - 2), cS)
                if mat == 0:
                    src = wg3[:, :, 256 * sl:256 * sl + 256]
                elif mat == 1:
                    src = wu3[:, :, 256 * sl:256 * sl + 256]
                else:
                    src = wd3[:, 2 * sl:2 * sl + 2, :]
                dSp[idx % 2].tag(e.dma_start(
                    out=absb[:, idx % 2, :].rearrange(
                        "p (a n) -> p a n", a=src.shape[1]),
                    in_=src), 16)
                idx += 1

    def pabs_scalar(e, ck, w):
        for idx in range(48):
            if idx == 0:
                w(e, "d0", dS)
            base = CK.get(("pabs", "d0"))
            if base is not None and not isinstance(e, _StubEng):
                c = dSp[idx % 2]
                e.wait_ge(c.h, base[c.name] + 16 * (idx // 2 + 1))
            cS.tag(e.activation(out=absd[:, idx % 2, :], in_=absb[:, idx % 2, :],
                                func=AFT.Abs, accum_out=absS[:, idx:idx + 1]))
            ck("sq_%d" % idx)

    def pabs_vec(e, ck, w):
        w(e, "sq_47", cS)
        e.reduce_sum(stt[:, 0:1], absS[:, 0:16], axis=AX.X)
        e.reduce_sum(stt[:, 1:2], absS[:, 16:32], axis=AX.X)
        cV.tag(e.reduce_sum(stt[:, 2:3], absS[:, 32:48], axis=AX.X))
        ck("psums")
        w(e, "mm1", cT)
        cV.tag(e.tensor_copy(ltmp[0:1, 0:3], pb[4][0:1, 0:3]))
        ck("rowcp")
        w(e, "mm2", cT)
        e.tensor_scalar(out=wsc[:, 3:6], in0=pb[5][:, 0:3],
                        scalar1=1.0 / (D * INNER), scalar2=1e-5,
                        op0=Alu.mult, op1=Alu.max)
        e.drain()
        e.reciprocal(out=wsc[:, 0:1], in_=wsc[:, 3:4])
        e.reciprocal(out=wsc[:, 1:2], in_=wsc[:, 4:5])
        cV.tag(e.reciprocal(out=wsc[:, 2:3], in_=wsc[:, 5:6]))
        ck("wsc")

    def pabs_tensor(e, ck, w):
        w(e, "psums", cV)
        cT.tag(e.matmul(pb[4][0:1, 0:3], onesf[:, 0:1], stt[:, 0:3],
                        start=True, stop=True))
        ck("mm1")
        w(e, "rowcp", cV)
        cT.tag(e.matmul(pb[5][:, 0:3], onesf[0:1, 1:129], ltmp[0:1, 0:3],
                        start=True, stop=True))
        ck("mm2")

    run_phase("pabs", [
        (blk.sync, pabs_pre), (blk.sync, pabs_sync),
        (blk.scalar, pabs_scalar), (blk.vector, pabs_vec),
        (blk.tensor, pabs_tensor),
    ])
    dbg_dump([("dbg_wsc", wsc[:, :])])

    if PHLIM < 2:
        es.close()
        return nc

    tiles = [("halo", 0)] + [("own", i) for i in range(NT)] + \
            [("other", i) for i in range(NT)]
    n_pool = 2 * NT

    def a_src(kind, i):
        if kind == "halo":
            return x_halo[:, :]
        if kind == "own":
            return x_own[128 * i:128 * i + 128, :]
        return x_other[128 * i:128 * i + 128, :]

    def pa_pre(e, ck, w):
        ck("d0")

    def pa_sync(e, ck, w):
        for j, (kind, i) in enumerate(tiles):
            if j >= 2:
                w(e, "v_xnb_%d" % (j - 2), cV)
                w(e, "s_sq_%d" % (j - 2), cS)
            dSp[j % 2].tag(e.dma_start(out=xbuf[:, j % 2, :],
                                       in_=a_src(kind, i)), 16)

    def pa_vec(e, ck, w):
        base = CK.get(("pa", "d0"))
        last_own = None
        for j, (kind, i) in enumerate(tiles):
            if base is not None and not isinstance(e, _StubEng):
                c = dSp[j % 2]
                e.wait_ge(c.h, base[c.name] + 16 * (j // 2 + 1))
            e.reduce_sum(stt[:, 0:1], xbuf[:, j % 2, :], axis=AX.X)
            w(e, "s_sq_%d" % j, cS)
            e.drain()
            e.tensor_scalar(out=mv[:, :], in0=stt[:, 0:1], scalar1=1.0 / D,
                            scalar2=None, op0=Alu.mult)
            e.tensor_scalar(out=t1[:, :], in0=stt[:, 1:2], scalar1=1.0 / D,
                            scalar2=EPS, op0=Alu.mult, op1=Alu.add)
            e.drain()
            e.tensor_scalar(out=t2[:, :], in0=mv[:, :], scalar1=mv[:, :],
                            scalar2=None, op0=Alu.mult)
            e.drain()
            cV.tag(e.tensor_tensor(out=t1[:, :], in0=t1[:, :], in1=t2[:, :],
                                   op=Alu.subtract))
            ck("v_var_%d" % j)
            w(e, "s_sqrt_%d" % j, cS)
            cV.tag(e.reciprocal(out=r0[:, :], in_=s_t[:, 0:1]))
            ck("v_r0_%d" % j)
            e.drain()
            e.tensor_scalar(out=t3[:, :], in0=r0[:, :], scalar1=r0[:, :],
                            scalar2=None, op0=Alu.mult)
            e.drain()
            e.tensor_scalar(out=t3[:, :], in0=t3[:, :], scalar1=t1[:, :],
                            scalar2=None, op0=Alu.mult)
            e.drain()
            e.tensor_scalar(out=t3[:, :], in0=t3[:, :], scalar1=-0.5,
                            scalar2=1.5, op0=Alu.mult, op1=Alu.add)
            e.drain()
            e.tensor_scalar(out=rr[:, :], in0=r0[:, :], scalar1=t3[:, :],
                            scalar2=None, op0=Alu.mult)
            e.drain()
            e.tensor_scalar(out=t4[:, :], in0=mv[:, :], scalar1=rr[:, :],
                            scalar2=-1.0, op0=Alu.mult, op1=Alu.mult)
            e.drain()
            if j >= 2:
                w(e, "t_done_%d" % (j - 2), cT)
            if kind == "own":
                if last_own is not None:
                    w(e, "s_rnd_%d" % last_own, cS)
                e.tensor_scalar(out=xnf[:, :], in0=xbuf[:, j % 2, :],
                                scalar1=rr[:, :], scalar2=t4[:, :],
                                op0=Alu.mult, op1=Alu.add)
                e.drain()
                cV.tag(e.tensor_copy(xnb[:, j % 2, :], xnf[:, :]))
                ck("v_xnb_%d" % j)
                e.reduce_max(am[:, :], xnf[:, :], axis=AX.X,
                             apply_absolute_value=True)
                e.drain()
                e.tensor_scalar(out=am[:, :], in0=am[:, :], scalar1=1e-5,
                                scalar2=None, op0=Alu.max)
                e.drain()
                e.reciprocal(out=t3[:, :], in_=am[:, :])
                e.drain()
                cV.tag(e.tensor_scalar(out=s_t[:, 1:2], in0=t3[:, :],
                                       scalar1=127.0, scalar2=None, op0=Alu.mult))
                ck("v_st_%d" % j)
                e.drain()
                e.reciprocal(out=sxr[:, i:i + 1], in_=s_t[:, 1:2])
                w(e, "s_rnd_%d" % j, cS)
                if i >= 2:
                    w(e, "g_xq_%d" % (i - 2), dGp[i % 2])
                cV.tag(e.tensor_scalar(out=xqb[:, i % 2, :], in0=qtmp[:, :],
                                       scalar1=C_RND, scalar2=None,
                                       op0=Alu.subtract))
                ck("v_xq_%d" % j)
                last_own = j
            else:
                cV.tag(e.tensor_scalar(out=xnb[:, j % 2, :], in0=xbuf[:, j % 2, :],
                                       scalar1=rr[:, :], scalar2=t4[:, :],
                                       op0=Alu.mult, op1=Alu.add))
                ck("v_xnb_%d" % j)
            if kind != "other":
                w(e, "t_tr_%d" % j, cT)
                c0 = 0 if kind == "halo" else 128 + 128 * i
                bA, bB = pb[2 * (j % 2)], pb[2 * (j % 2) + 1]
                e.tensor_copy(XN[:, 0:4, c0:c0 + 128],
                              bA[:, :].rearrange("p (a n) -> p a n", a=4))
                cV.tag(e.tensor_copy(XN[:, 4:8, c0:c0 + 128],
                                     bB[:, :].rearrange("p (a n) -> p a n", a=4)))
            ck("v_evac_%d" % j)

    def pa_scalar(e, ck, w):
        base = CK.get(("pa", "d0"))
        last_own = None
        for j, (kind, i) in enumerate(tiles):
            if base is not None and not isinstance(e, _StubEng):
                c = dSp[j % 2]
                e.wait_ge(c.h, base[c.name] + 16 * (j // 2 + 1))
            if j >= 1:
                w(e, "v_var_%d" % (j - 1), cV)
            cS.tag(e.activation(out=sqd[:, j % 2, :], in_=xbuf[:, j % 2, :],
                                func=AFT.Square, accum_out=stt[:, 1:2]))
            ck("s_sq_%d" % j)
            w(e, "v_var_%d" % j, cV)
            if j >= 1:
                w(e, "v_r0_%d" % (j - 1), cV)
            cS.tag(e.activation(out=s_t[:, 0:1], in_=t1[:, :], func=AFT.Sqrt))
            ck("s_sqrt_%d" % j)
            if kind == "own":
                w(e, "v_st_%d" % j, cV)
                if last_own is not None:
                    w(e, "v_xq_%d" % last_own, cV)
                cS.tag(e.activation(out=qtmp[:, :], in_=xnf[:, :], func=AFT.Copy,
                                    scale=s_t[:, 1:2], bias=C_RND))
                ck("s_rnd_%d" % j)
                last_own = j

    def pa_tensor(e, ck, w):
        pool_seq = 0
        for j, (kind, i) in enumerate(tiles):
            w(e, "v_xnb_%d" % j, cV)
            if kind != "other":
                if j >= 2:
                    w(e, "v_evac_%d" % (j - 2), cV)
                bA, bB = pb[2 * (j % 2)], pb[2 * (j % 2) + 1]
                for c in range(8):
                    bnk = bA if c < 4 else bB
                    inst = e.matmul(bnk[:, 128 * (c % 4):128 * (c % 4) + 128],
                                    xnb[:, j % 2, 128 * c:128 * c + 128],
                                    identb[:, :], start=True, stop=True)
                    if c == 7:
                        cT.tag(inst)
                ck("t_tr_%d" % j)
            if kind != "halo":
                if kind == "own":
                    z = poolz[:, 64 - 2 * i:192 - 2 * i]
                else:
                    z = poolz[:, 32 - 2 * i:160 - 2 * i]
                first = pool_seq == 0
                last = pool_seq == n_pool - 1
                e.matmul(pb[6][:, :], z, xnb[:, j % 2, 0:512],
                         start=first, stop=last)
                cT.tag(e.matmul(pb[7][:, :], z, xnb[:, j % 2, 512:1024],
                                start=first, stop=last))
                pool_seq += 1
            ck("t_done_%d" % j)

    def pa_gp(e, ck, w):
        for j, (kind, i) in enumerate(tiles):
            if kind == "own":
                w(e, "v_xq_%d" % j, cV)
                dGp[i % 2].tag(e.dma_start(out=xq_scr[i, :, :],
                                           in_=xqb[:, i % 2, :]), 16)
                ck("g_xq_%d" % i)

    run_phase("pa", [
        (blk.sync, pa_pre), (blk.sync, pa_sync), (blk.vector, pa_vec),
        (blk.scalar, pa_scalar), (blk.tensor, pa_tensor), (blk.gpsimd, pa_gp),
    ])
    dbg_dump([("dbg_xn", XN), ("dbg_sxr", sxr[:, :])])

    if PHLIM < 3:
        es.close()
        return nc

    def pa2a_gp(e, ck, w):
        rkv3 = rkv_w.rearrange("(c p) n -> p c n", p=128)
        rq3 = rq_w.rearrange("(c p) n -> p c n", p=128)
        dW.tag(e.dma_start(out=rkvw_b[:, :, :], in_=rkv3), 16)
        dWb.tag(e.dma_start(out=wqk_b[:, 0, :, :], in_=rq3[:, :, 0:128]), 16)
        dW.tag(e.dma_start(out=wqk_b[:, 1, :, :], in_=rq3[:, :, 128:256]), 16)
        e.wait_ge(dW.h, dW.n)
        e.wait_ge(dWb.h, dWb.n)

    run_phase("pa2a", [(blk.gpsimd, pa2a_gp)])

    def pa2_vec(e, ck, w):
        e.memset(pooled_pad[:, :, 0:64], 0.0)
        cV.tag(e.memset(pooled_pad[:, :, 128:192], 0.0))
        e.tensor_scalar(out=qtmp[0:64, 0:512], in0=pb[6][0:64, :],
                        scalar1=1.0 / 64.0, scalar2=None, op0=Alu.mult)
        e.tensor_scalar(out=qtmp[0:64, 512:1024], in0=pb[7][0:64, :],
                        scalar1=1.0 / 64.0, scalar2=None, op0=Alu.mult)
        e.drain()
        cV.tag(e.tensor_copy(xnb[0:64, 0, :], qtmp[0:64, :]))
        ck("pooled_lm")
        w(e, "t_ptr", cT)
        e.tensor_copy(pooled_pad[:, 0:4, 64:128],
                      pb[0][:, :].rearrange("p (a n) -> p a n", a=4)[:, :, 0:64])
        cV.tag(e.tensor_copy(pooled_pad[:, 4:8, 64:128],
                             pb[1][:, :].rearrange("p (a n) -> p a n", a=4)[:, :, 0:64]))
        ck("ppad")
        w(e, "t_rkv1", cT)
        e.tensor_tensor(out=ltmp[0:64, :], in0=pb[2][0:64, :],
                        in1=rkvb_t[0:64, :], op=Alu.add)
        e.drain()
        e.tensor_copy(rkv1[0:64, :], ltmp[0:64, :])
        cV.tag(e.memset(rkv1[64:128, :], 0.0))
        ck("rkv1")
        w(e, "t_rkv2", cT)
        e.tensor_tensor(out=ltmp[64:128, :], in0=pb[3][64:128, :],
                        in1=rkvb_t[64:128, :], op=Alu.add)
        e.drain()
        e.tensor_copy(rkv2[64:128, :], ltmp[64:128, :])
        cV.tag(e.memset(rkv2[0:64, :], 0.0))
        ck("rkv2")
        w(e, "t_rkT", cT)
        e.tensor_copy(rk_fm[:, 0, :], pb[4][:, 0:64])
        cV.tag(e.tensor_copy(rk_fm[:, 1, :], pb[5][:, 0:64]))
        ck("rk")
        for oc in range(2):
            w(e, "t_rq_%d" % oc, cT)
            for n in range(4):
                inst = e.tensor_scalar(
                    out=rq_fm[:, oc, 512 * n:512 * n + 512],
                    in0=pb[4 * oc + n][:, :],
                    scalar1=rqb_t[:, oc:oc + 1], scalar2=None, op0=Alu.add)
                if n == 3:
                    cV.tag(inst)
            ck("v_rq_%d" % oc)
        for tt in range(NT):
            e.tensor_scalar(out=gsc[:, tt:tt + 1], in0=sxr[:, tt:tt + 1],
                            scalar1=wsc[:, 3:4], scalar2=None, op0=Alu.mult)
            inst = e.tensor_scalar(out=usc[:, tt:tt + 1], in0=sxr[:, tt:tt + 1],
                                   scalar1=wsc[:, 4:5], scalar2=None, op0=Alu.mult)
            if tt == NT - 1:
                cV.tag(inst)
        ck("scales")

    def pa2_tensor(e, ck, w):
        w(e, "pooled_lm", cV)
        for c in range(8):
            bnk = pb[0] if c < 4 else pb[1]
            inst = e.matmul(bnk[:, 128 * (c % 4):128 * (c % 4) + 128],
                            xnb[0:64, 0, 128 * c:128 * c + 128],
                            identb[0:64, :], start=True, stop=True)
            if c == 7:
                cT.tag(inst)
        ck("t_ptr")
        w(e, "ppad", cV)
        for kk in range(8):
            inst = e.matmul(pb[2][:, :], pooled_pad[:, kk, 64:192],
                            rkvw_b[:, kk, :], start=(kk == 0), stop=(kk == 7))
            if kk == 7:
                cT.tag(inst)
        ck("t_rkv1")
        for kk in range(8):
            inst = e.matmul(pb[3][:, :], pooled_pad[:, kk, 0:128],
                            rkvw_b[:, kk, :], start=(kk == 0), stop=(kk == 7))
            if kk == 7:
                cT.tag(inst)
        ck("t_rkv2")
        w(e, "rkv1", cV)
        e.matmul(pb[4][:, 0:64], rkv1[0:64, 0:128], identb[0:64, 0:64],
                 start=True, stop=True)
        cT.tag(e.matmul(pb[5][:, 0:64], rkv1[0:64, 128:256], identb[0:64, 0:64],
                        start=True, stop=True))
        ck("t_rkT")
        w(e, "rkv2", cV)
        w(e, "rk", cV)
        for oc in range(2):
            if oc == 1:
                w(e, "v_rq_0", cV)
            for n in range(4):
                for kk in range(8):
                    inst = e.matmul(pb[4 * oc + n][:, :],
                                    wqk_b[:, oc, kk, :],
                                    XN[:, kk, 128 + 512 * n:128 + 512 * n + 512],
                                    start=(kk == 0), stop=(kk == 7))
                    if kk == 7 and n == 3:
                        cT.tag(inst)
            ck("t_rq_%d" % oc)

    run_phase("pa2", [(blk.vector, pa2_vec), (blk.tensor, pa2_tensor)])
    dbg_dump([("dbg_rq", rq_fm), ("dbg_rk", rk_fm),
              ("dbg_rkv1", rkv1[:, :]), ("dbg_rkv2", rkv2[:, :]),
              ("dbg_pooled", pooled_pad)])

    if PHLIM < 4:
        es.close()
        return nc

    def pq_pre(e, ck, w):
        ck("d0")

    def pq_gp(e, ck, w):
        w3 = qkv_w.rearrange("(c p) n -> p c n", p=128)
        for oc in range(16):
            if oc >= 2:
                w(e, "t_wfree_%d" % (oc - 2), cT)
            dWp[oc % 2].tag(e.dma_start(out=wqk_b[:, oc % 2, :, :],
                               in_=w3[:, :, 128 * oc:128 * oc + 128]), 16)
        for vs in range(4):
            if vs >= 2:
                w(e, "t_v_%d_%d" % (vs - 2, 16), cT)
            else:
                w(e, "t_wfree_%d" % (14 + vs), cT)
            dWp[vs % 2].tag(e.dma_start(out=wv_b[:, vs % 2, :, :],
                               in_=w3[:, :, 2048 + 256 * vs:2048 + 256 * vs + 256]),
                   16)

    def pq_vec(e, ck, w):
        for oc in range(16):
            if oc < 8:
                for n in range(4):
                    w(e, "t_qk_%d_%d" % (oc, n), cT)
                    bnk = pb[n + 4 * (oc % 2)]
                    cV.tag(e.tensor_scalar(out=q_fm[:, oc, 512 * n:512 * n + 512],
                                           in0=bnk[:, :],
                                           scalar1=qkb_t[:, oc:oc + 1],
                                           scalar2=None, op0=Alu.add))
            else:
                ko = oc - 8
                for n in range(5):
                    w(e, "t_k_%d_%d" % (ko, n), cT)
                    bnk = pb[(n % 4) + 4 * (oc % 2)]
                    c0 = 512 * n
                    wd = 512 if n < 4 else 128
                    cV.tag(e.tensor_scalar(out=k_fm[:, ko, c0:c0 + wd],
                                           in0=bnk[:, 0:wd],
                                           scalar1=qkb_t[:, oc:oc + 1],
                                           scalar2=None, op0=Alu.add))
                    ck("v_kev_%d_%d" % (ko, n))
            ck("v_evac_%d" % oc)
        for vs in range(4):
            for tt in range(17):
                w(e, "t_v_%d_%d" % (vs, tt), cT)
                bnk = pb[(tt % 4) + 4 * (vs % 2)]
                cV.tag(e.tensor_tensor(out=v_tm[:, tt, 256 * vs:256 * vs + 256],
                                       in0=bnk[:, 0:256],
                                       in1=vb_t[:, 256 * vs:256 * vs + 256],
                                       op=Alu.add))
                ck("v_vev_%d_%d" % (vs, tt))

    def pq_tensor(e, ck, w):
        base = CK.get(("pq", "d0"))
        for oc in range(16):
            if base is not None and not isinstance(e, _StubEng):
                c = dWp[oc % 2]
                e.wait_ge(c.h, base[c.name] + 16 * (oc // 2 + 1))
            if oc < 8:
                for n in range(4):
                    if oc >= 2:
                        w(e, "v_evac_%d" % (oc - 2), cV)
                    bnk = pb[n + 4 * (oc % 2)]
                    for kk in range(8):
                        inst = e.matmul(bnk[:, :], wqk_b[:, oc % 2, kk, :],
                                        XN[:, kk, 128 + 512 * n:128 + 512 * n + 512],
                                        start=(kk == 0), stop=(kk == 7))
                        if kk == 7:
                            cT.tag(inst)
                    ck("t_qk_%d_%d" % (oc, n))
            else:
                ko = oc - 8
                for n in range(5):
                    if oc >= 2:
                        w(e, "v_evac_%d" % (oc - 2), cV)
                    if n == 4:
                        w(e, "v_kev_%d_0" % ko, cV)
                    bnk = pb[(n % 4) + 4 * (oc % 2)]
                    c0 = 512 * n
                    wd = 512 if n < 4 else 128
                    for kk in range(8):
                        inst = e.matmul(bnk[:, 0:wd], wqk_b[:, oc % 2, kk, :],
                                        XN[:, kk, c0:c0 + wd],
                                        start=(kk == 0), stop=(kk == 7))
                        if kk == 7:
                            cT.tag(inst)
                    ck("t_k_%d_%d" % (ko, n))
            ck("t_wfree_%d" % oc)
        for vs in range(4):
            if base is not None and not isinstance(e, _StubEng):
                c = dWp[vs % 2]
                e.wait_ge(c.h, base[c.name] + 16 * 8 + 16 * (vs // 2 + 1))
            if vs < 2:
                w(e, "v_evac_%d" % (14 + vs), cV)
            for tt in range(17):
                if tt >= 4:
                    w(e, "v_vev_%d_%d" % (vs, tt - 4), cV)
                elif vs >= 2:
                    w(e, "v_vev_%d_%d" % (vs - 2, 13 + tt), cV)
                bnk = pb[(tt % 4) + 4 * (vs % 2)]
                for kk in range(8):
                    inst = e.matmul(bnk[:, 0:256], XN[:, kk, 128 * tt:128 * tt + 128],
                                    wv_b[:, vs % 2, kk, :],
                                    start=(kk == 0), stop=(kk == 7))
                    if kk == 7:
                        cT.tag(inst)
                ck("t_v_%d_%d" % (vs, tt))

    run_phase("pq", [
        (blk.gpsimd, pq_pre), (blk.gpsimd, pq_gp),
        (blk.vector, pq_vec), (blk.tensor, pq_tensor),
    ])
    dbg_dump([("dbg_q", q_fm), ("dbg_k", k_fm), ("dbg_v", v_tm)])

    if PHLIM < 5:
        es.close()
        return nc

    def plat_vec(e, ck, w):
        e.memset(qz[0:64, 1, :], 0.0)
        e.memset(qz[64:128, 0, :], 0.0)
        nit = 2 * NT
        for it in range(nit + 2):
            nstt = 0
            if it < nit:
                lc, qt = divmod(it, NT)
                if qt == 0:
                    if lc >= 1:
                        w(e, "t_sc_%d" % (lc * NT - 1), cT)
                    e.tensor_copy(qz[0:64, 0, :], rq_fm[0:64, lc, :])
                    cV.tag(e.tensor_copy(qz[64:128, 1, :], rq_fm[64:128, lc, :]))
                    ck("v_qz_%d" % lc)
                bs = 4 * (it % 2)
                w(e, "t_sc_%d" % it, cT)
                e.scalar_tensor_tensor(
                    out=pb[bs][:, 0:64], in0=pb[bs][:, 0:64], scalar=0.125,
                    in1=latmask_t[:, qt, :], op0=Alu.mult, op1=Alu.add)
                nstt = cV.tag(e.scalar_tensor_tensor(
                    out=pb[bs][:, 64:128], in0=pb[bs][:, 64:128], scalar=0.125,
                    in1=latmask_t[:, qt, :], op0=Alu.mult, op1=Alu.add))
            nrcp = 0
            if it >= 2:
                p2 = it - 2
                p = p2 % 2
                w(e, "s_exp_%d" % p2, cS)
                e.reciprocal(out=srcp[:, 2 * p:2 * p + 1],
                             in_=ssum[:, 2 * p:2 * p + 1])
                nrcp = cV.tag(e.reciprocal(out=srcp[:, 2 * p + 1:2 * p + 2],
                                           in_=ssum[:, 2 * p + 1:2 * p + 2]))
            if it < nit:
                e.wait_ge(cV.h, nstt)
                cV.tag(e.reduce_max(smax[:, it % 2:it % 2 + 1],
                                    pb[4 * (it % 2)][:, 0:128],
                                    axis=AX.X, negate=True))
                ck("v_max_%d" % it)
            if it >= 2:
                p2 = it - 2
                p = p2 % 2
                p4 = p2 % 4
                sc2 = latbin_t[:, 0:1] if (p2 % NT) == 0 else 1.0
                e.wait_ge(cV.h, nrcp)
                e.tensor_scalar(out=pe_sb[:, p4, 0:64], in0=pe_sb[:, p4, 0:64],
                                scalar1=srcp[:, 2 * p:2 * p + 1], scalar2=sc2,
                                op0=Alu.mult, op1=Alu.mult)
                cV.tag(e.tensor_scalar(out=pe_sb[:, p4, 64:128],
                                       in0=pe_sb[:, p4, 64:128],
                                       scalar1=srcp[:, 2 * p + 1:2 * p + 2],
                                       scalar2=sc2, op0=Alu.mult, op1=Alu.mult))
                ck("v_pe_%d" % p2)
                w(e, "t_tr_%d" % p2, cT)
                cV.tag(e.tensor_copy(pt_sb[:, p, 0:128],
                                     pb[4 * (p2 % 2) + 1][:, 0:128]))
                ck("v_pt_%d" % p2)

    def plat_scalar(e, ck, w):
        nit = 2 * NT
        for it in range(nit + 4):
            if it < nit:
                p4 = it % 4
                w(e, "v_max_%d" % it, cV)
                if it >= 4:
                    w(e, "t_tr_%d" % (it - 4), cT)
                e.activation(out=pe_sb[:, p4, 0:64], in_=pb[4 * (it % 2)][:, 0:64],
                             func=AFT.Exp, bias=smax[:, it % 2:it % 2 + 1],
                             scale=1.0, accum_out=ssum[:, 2 * (it % 2):2 * (it % 2) + 1])
                cS.tag(e.activation(out=pe_sb[:, p4, 64:128],
                                    in_=pb[4 * (it % 2)][:, 64:128],
                                    func=AFT.Exp, bias=smax[:, it % 2:it % 2 + 1],
                                    scale=1.0,
                                    accum_out=ssum[:, 2 * (it % 2) + 1:2 * (it % 2) + 2]))
                ck("s_exp_%d" % it)
            if it >= 4:
                p4o = it - 4
                lc, qt = divmod(p4o, NT)
                bs = 4 * (p4o % 2)
                w(e, "t_av_%d" % p4o, cT)
                e.copy(latout[0:64, lc, 128 * qt:128 * qt + 128],
                       pb[bs + 2][0:64, 0:128])
                cS.tag(e.copy(latout[64:128, lc, 128 * qt:128 * qt + 128],
                              pb[bs + 2][64:128, 128:256]))
                ck("s_out_%d" % p4o)

    def plat_tensor(e, ck, w):
        nit = 2 * NT

        def emit_sc(e, it):
            lc, qt = divmod(it, NT)
            bs = 4 * (it % 2)
            if qt == 0:
                w(e, "v_qz_%d" % lc, cV)
            if it >= 2:
                w(e, "s_exp_%d" % (it - 2), cS)
            for sub in range(2):
                inst = e.matmul(pb[bs][:, 64 * sub:64 * sub + 64],
                                qz[:, sub, 128 * qt:128 * qt + 128],
                                rk_fm[:, lc, :], start=True, stop=True)
                if sub == 1:
                    cT.tag(inst)
            ck("t_sc_%d" % it)

        emit_sc(e, 0)
        for it in range(nit + 3):
            if 0 <= it - 2 < nit:
                p2 = it - 2
                w(e, "v_pe_%d" % p2, cV)
                if p2 >= 2:
                    w(e, "v_pt_%d" % (p2 - 2), cV)
                cT.tag(e.matmul(pb[4 * (p2 % 2) + 1][:, 0:128],
                                pe_sb[:, p2 % 4, 0:128],
                                identb[:, :], start=True, stop=True))
                ck("t_tr_%d" % p2)
            if 0 <= it - 3 < nit:
                p3 = it - 3
                lc = p3 // NT
                bs = 4 * (p3 % 2)
                w(e, "v_pt_%d" % p3, cV)
                if p3 >= 2:
                    w(e, "s_out_%d" % (p3 - 2), cS)
                e.matmul(pb[bs + 2][:, 0:128],
                         rkv1[:, LD + 128 * lc:LD + 128 * lc + 128],
                         pt_sb[:, p3 % 2, 0:128], start=True, stop=True)
                cT.tag(e.matmul(pb[bs + 2][:, 128:256],
                                rkv2[:, LD + 128 * lc:LD + 128 * lc + 128],
                                pt_sb[:, p3 % 2, 0:128], start=True, stop=True))
                ck("t_av_%d" % p3)
            if it + 1 < nit:
                emit_sc(e, it + 1)

    run_phase("plat", [
        (blk.vector, plat_vec), (blk.scalar, plat_scalar),
        (blk.tensor, plat_tensor),
    ])
    dbg_dump([("dbg_lat", latout)])

    if PHLIM < 6:
        es.close()
        return nc

    def ploc_vec(e, ck, w):
        nit = 8 * NT
        for it in range(nit + 2):
            nstt = 0
            if it < nit:
                c, qt = divmod(it, NT)
                if qt == 0:
                    if c >= 1:
                        w(e, "t_sc_%d" % (c * NT - 1), cT)
                    e.tensor_copy(qz[0:64, 0, :], q_fm[0:64, c, :])
                    cV.tag(e.tensor_copy(qz[64:128, 1, :], q_fm[64:128, c, :]))
                    ck("v_qz_%d" % c)
                bs = 3 * (it % 2)
                w(e, "t_sc_%d" % it, cT)
                mk = lmask_t[:, 0, :] if qt == 0 else lmask_t[:, 1, :]
                nstt = cV.tag(e.scalar_tensor_tensor(
                    out=pb[bs][:, :], in0=pb[bs][:, :], scalar=0.125,
                    in1=mk, op0=Alu.mult, op1=Alu.add))
            nrcp = 0
            if it >= 2:
                p2 = it - 2
                p = p2 % 2
                w(e, "s_exp_%d" % p2, cS)
                e.reciprocal(out=srcp[:, 2 * p:2 * p + 1],
                             in_=ssum[:, 2 * p:2 * p + 1])
                nrcp = cV.tag(e.reciprocal(out=srcp[:, 2 * p + 1:2 * p + 2],
                                           in_=ssum[:, 2 * p + 1:2 * p + 2]))
            if it < nit:
                e.wait_ge(cV.h, nstt)
                cV.tag(e.reduce_max(smax[:, it % 2:it % 2 + 1],
                                    pb[3 * (it % 2)][:, :],
                                    axis=AX.X, negate=True))
                ck("v_max_%d" % it)
            if it >= 2:
                p2 = it - 2
                p = p2 % 2
                p4 = p2 % 4
                sc2 = lbin_t[:, 0:1] if (p2 % NT) == 0 else 1.0
                e.wait_ge(cV.h, nrcp)
                e.tensor_scalar(out=pe_sb[:, p4, 0:256], in0=pe_sb[:, p4, 0:256],
                                scalar1=srcp[:, 2 * p:2 * p + 1], scalar2=sc2,
                                op0=Alu.mult, op1=Alu.mult)
                cV.tag(e.tensor_scalar(out=pe_sb[:, p4, 256:512],
                                       in0=pe_sb[:, p4, 256:512],
                                       scalar1=srcp[:, 2 * p + 1:2 * p + 2],
                                       scalar2=sc2, op0=Alu.mult, op1=Alu.mult))
                ck("v_pe_%d" % p2)
                w(e, "t_tr_%d" % p2, cT)
                cV.tag(e.tensor_copy(pt_sb[:, p, :], pb[3 * (p2 % 2) + 1][:, :]))
                ck("v_pt_%d" % p2)

    def ploc_scalar(e, ck, w):
        nit = 8 * NT
        for it in range(nit + 4):
            if it < nit:
                p4 = it % 4
                bs = 3 * (it % 2)
                w(e, "v_max_%d" % it, cV)
                if it >= 4:
                    w(e, "t_tr_%d" % (it - 4), cT)
                e.activation(out=pe_sb[:, p4, 0:256], in_=pb[bs][:, 0:256],
                             func=AFT.Exp, bias=smax[:, it % 2:it % 2 + 1],
                             scale=1.0,
                             accum_out=ssum[:, 2 * (it % 2):2 * (it % 2) + 1])
                cS.tag(e.activation(out=pe_sb[:, p4, 256:512],
                                    in_=pb[bs][:, 256:512],
                                    func=AFT.Exp, bias=smax[:, it % 2:it % 2 + 1],
                                    scale=1.0,
                                    accum_out=ssum[:, 2 * (it % 2) + 1:2 * (it % 2) + 2]))
                ck("s_exp_%d" % it)
            if it >= 4:
                p4o = it - 4
                c, qt = divmod(p4o, NT)
                bs2 = 3 * (p4o % 2) + 2
                w(e, "t_av_%d" % p4o, cT)
                e.copy(local_fm[0:64, c, 128 * qt:128 * qt + 128],
                       pb[bs2][0:64, 0:128])
                cS.tag(e.copy(local_fm[64:128, c, 128 * qt:128 * qt + 128],
                              pb[bs2][64:128, 128:256]))
                ck("s_out_%d" % p4o)

    def ploc_tensor(e, ck, w):
        nit = 8 * NT

        def emit_sc(e, it):
            c, qt = divmod(it, NT)
            bs = 3 * (it % 2)
            if qt == 0:
                w(e, "v_qz_%d" % c, cV)
            if it >= 2:
                w(e, "s_exp_%d" % (it - 2), cS)
            for sub in range(2):
                inst = e.matmul(pb[bs][:, 256 * sub:256 * sub + 256],
                                qz[:, sub, 128 * qt:128 * qt + 128],
                                k_fm[:, c, 128 * qt:128 * qt + 256],
                                start=True, stop=True)
                if sub == 1:
                    cT.tag(inst)
            ck("t_sc_%d" % it)

        emit_sc(e, 0)
        for it in range(nit + 3):
            if 0 <= it - 2 < nit:
                p2 = it - 2
                w(e, "v_pe_%d" % p2, cV)
                if p2 >= 2:
                    w(e, "v_pt_%d" % (p2 - 2), cV)
                for q4 in range(4):
                    inst = e.matmul(pb[3 * (p2 % 2) + 1][:, 128 * q4:128 * q4 + 128],
                                    pe_sb[:, p2 % 4, 128 * q4:128 * q4 + 128],
                                    identb[:, :], start=True, stop=True)
                    if q4 == 3:
                        cT.tag(inst)
                ck("t_tr_%d" % p2)
            if 0 <= it - 3 < nit:
                p3 = it - 3
                c, qt = divmod(p3, NT)
                bs = 3 * (p3 % 2)
                w(e, "v_pt_%d" % p3, cV)
                if p3 >= 2:
                    w(e, "s_out_%d" % (p3 - 2), cS)
                for sub in range(2):
                    for kc in range(2):
                        inst = e.matmul(
                            pb[bs + 2][:, 128 * sub:128 * sub + 128],
                            v_tm[:, qt + kc, 128 * c:128 * c + 128],
                            pt_sb[:, p3 % 2, 256 * sub + 128 * kc:256 * sub + 128 * kc + 128],
                            start=(kc == 0), stop=(kc == 1))
                        if sub == 1 and kc == 1:
                            cT.tag(inst)
                ck("t_av_%d" % p3)
            if it + 1 < nit:
                emit_sc(e, it + 1)

    run_phase("ploc", [
        (blk.vector, ploc_vec), (blk.scalar, ploc_scalar),
        (blk.tensor, ploc_tensor),
    ])
    dbg_dump([("dbg_loc", local_fm)])

    if PHLIM < 7:
        es.close()
        return nc

    def prout_gp(e, ck, w):
        ro3 = rout_w.rearrange("(c p) n -> p c n", p=128)
        dW.tag(e.dma_start(out=routw_b[:, :, :], in_=ro3), 16)
        e.wait_ge(dW.h, dW.n)

    def prout_vec(e, ck, w):
        for g in range(32):
            oc, n = divmod(g, 4)
            w(e, "t_ro_%d" % g, cT)
            cV.tag(e.tensor_scalar(out=remote[:, oc, 512 * n:512 * n + 512],
                                   in0=pb[g % 8][:, :],
                                   scalar1=routb_t[:, oc:oc + 1],
                                   scalar2=None, op0=Alu.add))
            ck("v_ro_%d" % g)

    def prout_tensor(e, ck, w):
        rw = routw_b[:, :, :].rearrange("p a n -> p (a n)")
        for g in range(32):
            oc, n = divmod(g, 4)
            if g >= 8:
                w(e, "v_ro_%d" % (g - 8), cV)
            for lc in range(2):
                inst = e.matmul(pb[g % 8][:, :],
                                rw[:, 1024 * lc + 128 * oc:1024 * lc + 128 * oc + 128],
                                latout[:, lc, 512 * n:512 * n + 512],
                                start=(lc == 0), stop=(lc == 1))
                if lc == 1:
                    cT.tag(inst)
            ck("t_ro_%d" % g)

    run_phase("prout0", [(blk.gpsimd, prout_gp)])
    run_phase("prout", [
        (blk.vector, prout_vec), (blk.tensor, prout_tensor),
    ])
    dbg_dump([("dbg_rem", remote)])

    def pg_vec(e, ck, w):
        cV.tag(e.memset(gate_sb[:, :], 0.0))
        ck("gz")
        for c in range(8):
            for n in range(4):
                g = 4 * c + n
                w(e, "t_bc_%d" % g, cT)
                sl = slice(512 * n, 512 * n + 512)
                e.tensor_tensor(out=gdiff[:, :], in0=local_fm[:, c, sl],
                                in1=remote[:, c, sl], op=Alu.subtract)
                e.drain()
                e.scalar_tensor_tensor(out=gtmp[:, :], in0=pb[4 + g % 4][:, :],
                                       scalar=1.0, in1=gdiff[:, :],
                                       op0=Alu.mult, op1=Alu.mult)
                e.drain()
                cV.tag(e.tensor_tensor(out=mixed[:, c, sl], in0=gtmp[:, :],
                                       in1=remote[:, c, sl], op=Alu.add))
                ck("v_mix_%d" % g)

    def pg_scalar(e, ck, w):
        for c in range(8):
            for n in range(4):
                g = 4 * c + n
                w(e, "t_lg_%d" % g, cT)
                if g >= 1:
                    w(e, "t_bc_%d" % (g - 1), cT)
                cS.tag(e.activation(out=gate_sb[0:2, 0:512],
                                    in_=pb[g % 4][0:2, :], func=AFT.Sigmoid))
                ck("s_sig_%d" % g)

    def pg_tensor(e, ck, w):
        w(e, "gz", cV)
        for c in range(8):
            for n in range(4):
                g = 4 * c + n
                sl = slice(512 * n, 512 * n + 512)
                if g >= 4:
                    w(e, "s_sig_%d" % (g - 4), cS)
                e.matmul(pb[g % 4][0:2, :], glgr_b[:, 0:2], local_fm[:, c, sl],
                         start=True, stop=False)
                cT.tag(e.matmul(pb[g % 4][0:2, :], glgr_b[:, 2:4],
                                remote[:, c, sl], start=False, stop=True))
                ck("t_lg_%d" % g)
                w(e, "s_sig_%d" % g, cS)
                if g >= 4:
                    w(e, "v_mix_%d" % (g - 4), cV)
                cT.tag(e.matmul(pb[4 + g % 4][:, :], sel2_b[:, :],
                                gate_sb[:, 0:512], start=True, stop=True))
                ck("t_bc_%d" % g)

    run_phase("pg", [
        (blk.vector, pg_vec), (blk.scalar, pg_scalar), (blk.tensor, pg_tensor),
    ])
    dbg_dump([("dbg_mix", mixed)])

    if PHLIM < 8:
        es.close()
        return nc

    def po_gp0(e, ck, w):
        ow3 = out_w.rearrange("(c p) n -> p c n", p=128)
        dW.tag(e.dma_start(out=outw_b[:, :, :], in_=ow3), 16)
        e.wait_ge(dW.h, dW.n)

    def po_vec(e, ck, w):
        for tt in range(NT):
            w(e, "t_o_%d" % tt, cT)
            if tt >= 2:
                w(e, "g_at_%d" % (tt - 2), dGp[tt % 2])
            b0, b1 = pb[2 * (tt % 2)], pb[2 * (tt % 2) + 1]
            e.tensor_tensor(out=attn_buf[:, tt % 2, 0:512], in0=b0[:, :],
                            in1=outb_t[:, 0:512], op=Alu.add)
            cV.tag(e.tensor_tensor(out=attn_buf[:, tt % 2, 512:1024],
                                   in0=b1[:, :], in1=outb_t[:, 512:1024],
                                   op=Alu.add))
            ck("v_o_%d" % tt)

    def po_tensor(e, ck, w):
        for tt in range(NT):
            if tt >= 2:
                w(e, "v_o_%d" % (tt - 2), cV)
            b0, b1 = pb[2 * (tt % 2)], pb[2 * (tt % 2) + 1]
            for kk in range(8):
                e.matmul(b0[:, :], mixed[:, kk, 128 * tt:128 * tt + 128],
                         outw_b[:, kk, 0:512], start=(kk == 0), stop=(kk == 7))
            for kk in range(8):
                inst = e.matmul(b1[:, :], mixed[:, kk, 128 * tt:128 * tt + 128],
                                outw_b[:, kk, 512:1024],
                                start=(kk == 0), stop=(kk == 7))
                if kk == 7:
                    cT.tag(inst)
            ck("t_o_%d" % tt)

    def po_gp(e, ck, w):
        for tt in range(NT):
            w(e, "v_o_%d" % tt, cV)
            dGp[tt % 2].tag(e.dma_start(out=attn_scr[tt, :, :],
                                        in_=attn_buf[:, tt % 2, :]), 16)
            ck("g_at_%d" % tt)

    run_phase("po0", [(blk.gpsimd, po_gp0)])
    run_phase("po", [
        (blk.vector, po_vec), (blk.tensor, po_tensor), (blk.gpsimd, po_gp),
    ])

    if PHLIM < 9:
        es.close()
        return nc

    def px_pre(e, ck, w):
        ck("d0")

    def px_sync(e, ck, w):
        for tt in range(NT):
            if tt >= 2:
                w(e, "t_x_%d" % (tt - 2), cT)
            dSp[tt % 2].tag(e.dma_start(out=xql[:, tt % 2, :],
                                        in_=xq_scr[tt, :, :]), 16)

    def px_tensor(e, ck, w):
        base = CK.get(("px", "d0"))
        for tt in range(NT):
            if base is not None and not isinstance(e, _StubEng):
                c = dSp[tt % 2]
                e.wait_ge(c.h, base[c.name] + 16 * (tt // 2 + 1))
            if tt >= 2:
                w(e, "v_x_%d" % (tt - 2), cV)
            b0, b1 = pb[2 * (tt % 2)], pb[2 * (tt % 2) + 1]
            for c in range(8):
                bnk = b0 if c < 4 else b1
                inst = e.matmul(bnk[:, 128 * (c % 4):128 * (c % 4) + 128],
                                xql[:, tt % 2, 128 * c:128 * c + 128],
                                identb[:, :], start=True, stop=True)
                if c == 7:
                    cT.tag(inst)
            ck("t_x_%d" % tt)

    def px_vec(e, ck, w):
        for tt in range(NT):
            w(e, "t_x_%d" % tt, cT)
            b0, b1 = pb[2 * (tt % 2)], pb[2 * (tt % 2) + 1]
            e.tensor_copy(XQ[:, 0:4, 128 * tt:128 * tt + 128],
                          b0[:, :].rearrange("p (a n) -> p a n", a=4))
            cV.tag(e.tensor_copy(XQ[:, 4:8, 128 * tt:128 * tt + 128],
                                 b1[:, :].rearrange("p (a n) -> p a n", a=4)))
            ck("v_x_%d" % tt)

    run_phase("px", [
        (blk.sync, px_pre), (blk.sync, px_sync), (blk.tensor, px_tensor),
        (blk.vector, px_vec),
    ])
    dbg_dump([("dbg_xq", XQ)])

    if PHLIM < 10:
        es.close()
        return nc

    def pgu_pre(e, ck, w):
        ck("d0")

    def pgu_sync(e, ck, w):
        wg3 = wgT.rearrange("(c p) n -> p c n", p=128)
        wu3 = wuT.rearrange("(c p) n -> p c n", p=128)
        for nt in range(8):
            for mi, w3 in ((0, wg3), (1, wu3)):
                idx = 2 * nt + mi
                if idx >= 2:
                    w(e, "s_rw_%d" % (idx - 2), cS)
                dSp[idx % 2].tag(e.dma_start(out=wq32[:, idx % 2, :, :],
                                   in_=w3[:, :, 512 * nt:512 * nt + 512]), 16)

    def pgu_scalar(e, ck, w):
        base = CK.get(("pgu", "d0"))

        def s_round(e, idx):
            mi = idx % 2
            if base is not None and not isinstance(e, _StubEng):
                c = dSp[idx % 2]
                e.wait_ge(c.h, base[c.name] + 16 * (idx // 2 + 1))
            if idx >= 1:
                w(e, "v_wq_%d" % (idx - 1), cV)
            cS.tag(e.activation(
                out=wtmp[:, :, :].rearrange("p a n -> p (a n)"),
                in_=wq32[:, idx % 2, :, :].rearrange("p a n -> p (a n)"),
                func=AFT.Copy, scale=wsc[:, mi:mi + 1], bias=C_RND))
            ck("s_rw_%d" % idx)

        s_round(e, 0)
        s_round(e, 1)
        for nt in range(8):
            for tt in range(NT):
                it = NT * nt + tt
                if nt + 1 < 8 and tt == 2:
                    s_round(e, 2 * (nt + 1))
                if nt + 1 < 8 and tt == 8:
                    s_round(e, 2 * (nt + 1) + 1)
                w(e, "t_g_%d" % it, cT)
                if it >= 1:
                    w(e, "v_h_%d" % (it - 1), cV)
                cS.tag(e.activation(out=sg_sb[:, :], in_=pb[tt % 2][:, :],
                                    func=AFT.Silu, scale=gsc[:, tt:tt + 1]))
                ck("s_silu_%d" % it)

    def pgu_vec(e, ck, w):
        def v_clip(e, idx, half):
            nt2, mi = divmod(idx, 2)
            dstW = wgq if mi == 0 else wuq
            sl = slice(4096 * half, 4096 * half + 4096)
            wt_f = wtmp[:, :, :].rearrange("p a n -> p (a n)")
            if half == 0:
                w(e, "s_rw_%d" % idx, cS)
                e.tensor_scalar(out=wt_f[:, sl], in0=wt_f[:, sl],
                                scalar1=C_RND, scalar2=1.0,
                                op0=Alu.subtract, op1=Alu.min)
            else:
                e.tensor_scalar(out=wt_f[:, sl], in0=wt_f[:, sl],
                                scalar1=C_RND, scalar2=1.0,
                                op0=Alu.subtract, op1=Alu.min)
                if nt2 >= 2:
                    w(e, "t_u_%d" % (NT * (nt2 - 2) + NT - 1), cT)
                cV.tag(e.tensor_scalar(
                    out=dstW[:, nt2 % 2, :, 0:256].rearrange("p a n -> p (a n)"),
                    in0=wt_f[:, 0:4096].rearrange(
                        "p (a n) -> p a n", a=8)[:, :, 0:256].rearrange(
                        "p a n -> p a n").rearrange("p a n -> p (a n)"),
                    scalar1=-1.0, scalar2=None, op0=Alu.max))
                cV.tag(e.tensor_scalar(
                    out=dstW[:, nt2 % 2, :, 256:512].rearrange("p a n -> p (a n)"),
                    in0=wt_f[:, 4096:8192].rearrange(
                        "p (a n) -> p a n", a=8)[:, :, 0:256].rearrange(
                        "p a n -> p (a n)"),
                    scalar1=-1.0, scalar2=None, op0=Alu.max))
                ck("v_wq_%d" % idx)

        pass

    def pgu_vec2(e, ck, w):
        wt_f = wtmp[:, :, :].rearrange("p a n -> p (a n)")

        def v_clip1(e, idx):
            w(e, "s_rw_%d" % idx, cS)
            nc1 = cV.tag(e.tensor_scalar(out=wt_f, in0=wt_f,
                                         scalar1=C_RND, scalar2=1.0,
                                         op0=Alu.subtract, op1=Alu.min))
            return nc1

        def v_clip2(e, idx, nc1):
            nt2, mi = divmod(idx, 2)
            dstW = wgq if mi == 0 else wuq
            if nt2 >= 2:
                w(e, "t_u_%d" % (NT * (nt2 - 2) + NT - 1), cT)
            e.wait_ge(cV.h, nc1)
            cV.tag(e.tensor_scalar(
                out=dstW[:, nt2 % 2, :, :].rearrange("p a n -> p (a n)"),
                in0=wt_f, scalar1=-1.0, scalar2=None, op0=Alu.max))
            ck("v_wq_%d" % idx)

        nc1 = v_clip1(e, 0)
        v_clip2(e, 0, nc1)
        nc1 = v_clip1(e, 1)
        v_clip2(e, 1, nc1)
        for nt in range(8):
            pend = None
            for tt in range(NT):
                it = NT * nt + tt
                if nt + 1 < 8:
                    if tt == 4:
                        pend = (2 * (nt + 1), v_clip1(e, 2 * (nt + 1)))
                    elif tt == 6 and pend is not None:
                        v_clip2(e, pend[0], pend[1])
                        pend = None
                    elif tt == 10:
                        pend = (2 * (nt + 1) + 1, v_clip1(e, 2 * (nt + 1) + 1))
                    elif tt == 12 and pend is not None:
                        v_clip2(e, pend[0], pend[1])
                        pend = None
                w(e, "s_silu_%d" % it, cS)
                w(e, "t_u_%d" % it, cT)
                g4 = it // 4
                if g4 >= 2 and it % 4 == 0:
                    w(e, "g_h_%d" % (g4 - 2), dGp[g4 % 2])
                nh = cV.tag(e.scalar_tensor_tensor(
                    out=hwb[:, g4 % 2, it % 4, :], in0=pb[2 + tt % 2][:, :],
                    scalar=usc[:, tt:tt + 1], in1=sg_sb[:, :],
                    op0=Alu.mult, op1=Alu.mult))
                e.wait_ge(cV.h, nh)
                cV.tag(e.reduce_max(hmax[:, 8 * tt + nt:8 * tt + nt + 1],
                                    hwb[:, g4 % 2, it % 4, :], axis=AX.X,
                                    apply_absolute_value=True))
                ck("v_h_%d" % it)

    def pgu_tensor(e, ck, w):
        for nt in range(8):
            w(e, "v_wq_%d" % (2 * nt + 1), cV)
            for tt in range(NT):
                it = NT * nt + tt
                if it >= 2:
                    w(e, "s_silu_%d" % (it - 2), cS)
                    w(e, "v_h_%d" % (it - 2), cV)
                bg = pb[tt % 2]
                bu = pb[2 + tt % 2]
                for kk in range(8):
                    inst = e.matmul(bg[:, :], XQ[:, kk, 128 * tt:128 * tt + 128],
                                    wgq[:, nt % 2, kk, :],
                                    start=(kk == 0), stop=(kk == 7))
                    if kk == 7:
                        cT.tag(inst)
                        ck("t_g_%d" % it)
                    inst = e.matmul(bu[:, :], XQ[:, kk, 128 * tt:128 * tt + 128],
                                    wuq[:, nt % 2, kk, :],
                                    start=(kk == 0), stop=(kk == 7))
                    if kk == 7:
                        cT.tag(inst)
                        ck("t_u_%d" % it)

    def pgu_gp(e, ck, w):
        for nt in range(8):
            for g4l in range(4):
                g4 = 4 * nt + g4l
                w(e, "v_h_%d" % (NT * nt + 4 * g4l + 3), cV)
                dGp[g4 % 2].tag(e.dma_start(
                    out=h_scr[4 * g4l:4 * g4l + 4, :, nt, :].rearrange(
                        "a p n -> p a n"),
                    in_=hwb[:, g4 % 2, :, :]), 16)
                ck("g_h_%d" % g4)

    run_phase("pgu", [
        (blk.sync, pgu_pre), (blk.sync, pgu_sync), (blk.scalar, pgu_scalar),
        (blk.vector, pgu_vec2), (blk.tensor, pgu_tensor), (blk.gpsimd, pgu_gp),
    ])

    if PHLIM < 11:
        es.close()
        return nc

    def pwd_pre(e, ck, w):
        ck("d0")

    def pwd_sync(e, ck, w):
        wd3 = wdT.rearrange("(c p) n -> c p n", p=128)
        for c in range(32):
            if c >= 2:
                w(e, "s_rd_%d" % (c - 2), cS)
            dSp[c % 2].tag(e.dma_start(out=wdf[:, c % 2, :], in_=wd3[c, :, :]),
                           16)

    def pwd_scalar(e, ck, w):
        base = CK.get(("pwd", "d0"))
        for c in range(32):
            if base is not None and not isinstance(e, _StubEng):
                ctr = dSp[c % 2]
                e.wait_ge(ctr.h, base[ctr.name] + 16 * (c // 2 + 1))
            if c >= 1:
                w(e, "v_wd1_%d" % (c - 1), cV)
            cS.tag(e.activation(out=wdt_r[:, c % 2, :], in_=wdf[:, c % 2, :],
                                func=AFT.Copy, scale=wsc[:, 2:3], bias=C_RND))
            ck("s_rd_%d" % c)

    def pwd_vec(e, ck, w):
        for c in range(32):
            w(e, "s_rd_%d" % c, cS)
            cV.tag(e.tensor_scalar(out=wdt_r[:, c % 2, :], in0=wdt_r[:, c % 2, :],
                                   scalar1=C_RND, scalar2=1.0,
                                   op0=Alu.subtract, op1=Alu.min))
            ck("v_wd1_%d" % c)
            e.drain()
            cV.tag(e.tensor_scalar(out=wdq2[:, c, :], in0=wdt_r[:, c % 2, :],
                                   scalar1=-1.0, scalar2=None, op0=Alu.max))
            ck("v_wd2_%d" % c)

    run_phase("pwd", [
        (blk.sync, pwd_pre), (blk.sync, pwd_sync), (blk.scalar, pwd_scalar),
        (blk.vector, pwd_vec),
    ])

    def pd_pre(e, ck, w):
        ck("d0")

    def pd_sync(e, ck, w):
        for tt in range(NT):
            if tt >= 2:
                w(e, "v_fin_%d" % (tt - 2), cV)
            dSp[tt % 2].tag(e.dma_start(
                out=hrb[:, tt % 2, :],
                in_=h_scr[tt, :, :, :].rearrange("p a n -> p (a n)")), 16)
            dSp[tt % 2].tag(e.dma_start(out=atl[:, tt % 2, :],
                                        in_=attn_scr[tt, :, :]), 16)
            dSp[tt % 2].tag(e.dma_start(out=xres[:, tt % 2, :],
                                        in_=x_own[128 * tt:128 * tt + 128, :]), 16)

    def pd_vec(e, ck, w):
        base = CK.get(("pd", "d0"))
        for tt in range(NT + 1):
            if tt < NT:
                if base is not None and not isinstance(e, _StubEng):
                    c = dSp[tt % 2]
                    e.wait_ge(c.h, base[c.name] + 48 * (tt // 2 + 1))
                if tt >= 1:
                    w(e, "s_rh_%d" % (tt - 1), cS)
                n0 = cV.tag(e.reduce_max(am[:, :], hmax[:, 8 * tt:8 * tt + 8],
                                         axis=AX.X))
                e.wait_ge(cV.h, n0)
                n1 = cV.tag(e.tensor_scalar(out=t1[:, :], in0=am[:, :],
                                            scalar1=1e-5, scalar2=None,
                                            op0=Alu.max))
                e.wait_ge(cV.h, n1)
                n2 = cV.tag(e.reciprocal(out=t3[:, :], in_=t1[:, :]))
                e.wait_ge(cV.h, n2)
                cV.tag(e.tensor_scalar(out=sh_t[:, tt % 2:tt % 2 + 1],
                                       in0=t3[:, :], scalar1=127.0,
                                       scalar2=None, op0=Alu.mult))
                ck("v_sh_%d" % tt)
                cV.tag(e.tensor_scalar(out=dsc_t[:, tt % 2:tt % 2 + 1],
                                       in0=t1[:, :], scalar1=wsc[:, 5:6],
                                       scalar2=1.0 / 127.0,
                                       op0=Alu.mult, op1=Alu.mult))
                ck("v_dsc_%d" % tt)
                w(e, "s_rh_%d" % tt, cS)
                if tt >= 2:
                    w(e, "t_tr_%d_%d" % (tt - 2, 7), cT)
                cV.tag(e.tensor_scalar(out=xh_tm2[:, tt % 2, :], in0=tq_sb[:, :],
                                       scalar1=C_RND, scalar2=None,
                                       op0=Alu.subtract))
                ck("v_xh_%d" % tt)
                for rg in range(8):
                    w(e, "t_tr_%d_%d" % (tt, rg), cT)
                    bnk = pb[4 + rg % 4]
                    cV.tag(e.tensor_copy(
                        xh_fm[:, 4 * rg:4 * rg + 4, :].rearrange(
                            "p a n -> p (a n)"),
                        bnk[:, :]))
                    ck("v_ev_%d_%d" % (tt, rg))
            if tt >= 1:
                pt_ = tt - 1
                pp = pt_ % 2
                w(e, "t_dn_%d" % pt_, cT)
                if pt_ >= 2:
                    w(e, "g_y_%d" % (pt_ - 2), dGp[pt_ % 2])
                bd0, bd1 = pb[pt_ % 2], pb[2 + pt_ % 2]
                e.scalar_tensor_tensor(out=obuf[:, pp, 0:512], in0=bd0[:, :],
                                       scalar=dsc_t[:, pp:pp + 1],
                                       in1=atl[:, pp, 0:512],
                                       op0=Alu.mult, op1=Alu.add)
                nst = cV.tag(e.scalar_tensor_tensor(
                    out=obuf[:, pp, 512:1024], in0=bd1[:, :],
                    scalar=dsc_t[:, pp:pp + 1],
                    in1=atl[:, pp, 512:1024],
                    op0=Alu.mult, op1=Alu.add))
                e.wait_ge(cV.h, nst)
                e.tensor_tensor(out=obuf[:, pp, 0:512], in0=obuf[:, pp, 0:512],
                                in1=xres[:, pp, 0:512], op=Alu.add)
                cV.tag(e.tensor_tensor(out=obuf[:, pp, 512:1024],
                                       in0=obuf[:, pp, 512:1024],
                                       in1=xres[:, pp, 512:1024], op=Alu.add))
                ck("v_fin_%d" % pt_)

    def pd_scalar(e, ck, w):
        for tt in range(NT):
            w(e, "v_sh_%d" % tt, cV)
            if tt >= 1:
                w(e, "v_xh_%d" % (tt - 1), cV)
            cS.tag(e.activation(out=tq_sb[:, :], in_=hrb[:, tt % 2, :],
                                func=AFT.Copy, scale=sh_t[:, tt % 2:tt % 2 + 1],
                                bias=C_RND))
            ck("s_rh_%d" % tt)

    def pd_tensor(e, ck, w):
        for tt in range(NT):
            w(e, "v_xh_%d" % tt, cV)
            for rg in range(8):
                if rg >= 4:
                    w(e, "v_ev_%d_%d" % (tt, rg - 4), cV)
                elif tt >= 1:
                    w(e, "v_ev_%d_%d" % (tt - 1, rg + 4), cV)
                bnk = pb[4 + rg % 4]
                for u in range(4):
                    kc = 4 * rg + u
                    inst = e.matmul(bnk[:, 128 * u:128 * u + 128],
                                    xh_tm2[:, tt % 2, 128 * kc:128 * kc + 128],
                                    identb[:, :], start=True, stop=True)
                    if u == 3:
                        cT.tag(inst)
                ck("t_tr_%d_%d" % (tt, rg))
            w(e, "v_ev_%d_%d" % (tt, 7), cV)
            if tt >= 2:
                w(e, "v_fin_%d" % (tt - 2), cV)
            bd0, bd1 = pb[tt % 2], pb[2 + tt % 2]
            for kk in range(32):
                e.matmul(bd0[:, :], xh_fm[:, kk, :], wdq2[:, kk, 0:512],
                         start=(kk == 0), stop=(kk == 31))
                inst = e.matmul(bd1[:, :], xh_fm[:, kk, :], wdq2[:, kk, 512:1024],
                                start=(kk == 0), stop=(kk == 31))
                if kk == 31:
                    cT.tag(inst)
            ck("t_dn_%d" % tt)

    def pd_gp(e, ck, w):
        for tt in range(NT):
            w(e, "v_fin_%d" % tt, cV)
            dGp[tt % 2].tag(e.dma_start(out=y[128 * tt:128 * tt + 128, :],
                                        in_=obuf[:, tt % 2, :]), 16)
            ck("g_y_%d" % tt)
        e.wait_ge(dG.h, dG.n)
        e.wait_ge(dGb.h, dGb.n)

    run_phase("pd", [
        (blk.sync, pd_pre), (blk.sync, pd_sync), (blk.vector, pd_vec),
        (blk.scalar, pd_scalar), (blk.tensor, pd_tensor), (blk.gpsimd, pd_gp),
    ])

    es.close()
    return nc


def _masks_for_core(h):
    r = np.arange(128)
    c = np.arange(256)
    band = (c[None, :] >= r[:, None]) & (c[None, :] <= r[:, None] + 127)
    gen = np.where(band, 0.0, NEG).astype(np.float32)
    if h == 0:
        first = np.where(band & (c[None, :] >= 128), 0.0, NEG).astype(np.float32)
        lb = np.ones((128, 1), np.float32)
        lb[0, 0] = 0.0
        latb = np.ones((128, 1), np.float32)
        latb[0:64, 0] = 0.0
    else:
        first = gen.copy()
        lb = np.ones((128, 1), np.float32)
        latb = np.ones((128, 1), np.float32)
    lm = np.stack([np.concatenate([first, first], axis=1),
                   np.concatenate([gen, gen], axis=1)], axis=0)

    r0 = 2048 * h
    own_c0 = r0 // 64
    oth_c0 = (2048 - r0) // 64
    slot_chunk = np.concatenate([own_c0 + np.arange(32), oth_c0 + np.arange(32)])
    latm = np.empty((16, 128, 64), np.float32)
    for i in range(16):
        qchunk = (r0 + 128 * i + np.arange(128)) // 64
        allowed = slot_chunk[None, :] < qchunk[:, None]
        latm[i] = np.where(allowed, 0.0, NEG).astype(np.float32)
    return lm, latm, lb, latb


def _build_inputs(inputs):
    f = np.float32
    x = np.asarray(inputs["L0_attn_x"], f)
    qkv_w = np.ascontiguousarray(np.asarray(inputs["L0_attn_qkv_w"], f))
    qkv_b = np.asarray(inputs["L0_attn_qkv_b"], f)
    rq_w = np.ascontiguousarray(np.asarray(inputs["L0_attn_rq_w"], f))
    rq_b = np.asarray(inputs["L0_attn_rq_b"], f)
    rkv_w = np.ascontiguousarray(np.asarray(inputs["L0_attn_rkv_w"], f))
    rkv_b = np.asarray(inputs["L0_attn_rkv_b"], f)
    rout_w = np.ascontiguousarray(np.asarray(inputs["L0_attn_rout_w"], f))
    rout_b = np.asarray(inputs["L0_attn_rout_b"], f)
    out_w = np.ascontiguousarray(np.asarray(inputs["L0_attn_out_w"], f))
    out_b = np.asarray(inputs["L0_attn_out_b"], f)
    gl_w = np.asarray(inputs["L0_attn_gl_w"], f)
    gr_w = np.asarray(inputs["L0_attn_gr_w"], f)
    w_gate = np.asarray(inputs["L0_ffn_w_gate"], f)
    w_up = np.asarray(inputs["L0_ffn_w_up"], f)
    w_down = np.asarray(inputs["L0_ffn_w_down"], f)

    wgT = np.ascontiguousarray(w_gate.T)
    wuT = np.ascontiguousarray(w_up.T)
    wdT = np.ascontiguousarray(w_down.T)

    qkb_fm = np.ascontiguousarray(qkv_b[:2048].reshape(16, 128).T)
    vb_bc = np.ascontiguousarray(np.broadcast_to(qkv_b[2048:], (128, 1024)))
    rqb_fm = np.ascontiguousarray(rq_b.reshape(2, 128).T)
    rkvb_bc = np.ascontiguousarray(np.broadcast_to(rkv_b, (128, 512)))
    routb_fm = np.ascontiguousarray(rout_b.reshape(8, 128).T)
    outb_bc = np.ascontiguousarray(np.broadcast_to(out_b, (128, 1024)))

    glgr = np.zeros((128, 4), f)
    glgr[0:64, 0] = gl_w[:, 0]
    glgr[64:128, 1] = gl_w[:, 0]
    glgr[0:64, 2] = gr_w[:, 0]
    glgr[64:128, 3] = gr_w[:, 0]
    sel2 = np.zeros((128, 128), f)
    sel2[0, 0:64] = 1.0
    sel2[1, 64:128] = 1.0

    identb = np.eye(128, dtype=f).astype(ml_dtypes.bfloat16)
    poolz = np.zeros((128, 196), f)
    poolz[0:64, 64] = 1.0
    poolz[64:128, 65] = 1.0
    poolz = poolz.astype(ml_dtypes.bfloat16)

    shared = dict(qkv_w=qkv_w, rq_w=rq_w, rkv_w=rkv_w, rout_w=rout_w,
                  out_w=out_w, wgT=wgT, wuT=wuT, wdT=wdT,
                  qkb_fm=qkb_fm, vb_bc=vb_bc, rqb_fm=rqb_fm, rkvb_bc=rkvb_bc,
                  routb_fm=routb_fm, outb_bc=outb_bc,
                  glgr_in=glgr, sel2_in=sel2, identb_in=identb,
                  poolz_in=poolz)

    in_maps = []
    for core in range(8):
        b, h = core // 2, core % 2
        r0 = 2048 * h
        lm, latm, lb, latb = _masks_for_core(h)
        m = dict(shared)
        m["x_own"] = np.ascontiguousarray(x[b, r0:r0 + 2048])
        if h == 0:
            m["x_halo"] = np.zeros((128, 1024), f)
        else:
            m["x_halo"] = np.ascontiguousarray(x[b, r0 - 128:r0])
        m["x_other"] = np.ascontiguousarray(x[b, 2048 - r0:4096 - r0])
        m["lmask"] = lm
        m["latmask"] = latm
        m["lbin"] = lb
        m["latbin"] = latb
        in_maps.append(m)
    return in_maps


_nc_cache = [None]


def kernel(**inputs):
    in_maps = _build_inputs(inputs)
    if _nc_cache[0] is None:
        _nc_cache[0] = build_nc()
    nc = _nc_cache[0]
    res = run_bass_kernel_spmd(nc, in_maps, core_ids=list(range(8)))
    out = np.empty((B, S, D), np.float32)
    for core in range(8):
        b, h = core // 2, core % 2
        out[b, 2048 * h:2048 * h + 2048] = res.results[core]["y"]
    return out


# version 19
# speedup vs baseline: 1.1488x; 1.0970x over previous
import numpy as np
import ml_dtypes
from contextlib import ExitStack

import concourse.bass as bass
import concourse.mybir as mybir
from concourse.bass_utils import run_bass_kernel_spmd
from concourse.alu_op_type import AluOpType as Alu

F32 = mybir.dt.float32
BF16 = mybir.dt.bfloat16
AFT = mybir.ActivationFunctionType
AX = mybir.AxisListType

D = 1024
S = 4096
B = 4
NH = 16
HD = 64
WIN = 128
LD = 256
LH = 4
NLAT = 64
INNER = 4096
T = 2048
NT = 16
NEG = -1.0e9
C_RND = 12582912.0
EPS = 1e-5

import os
PHLIM = int(os.environ.get("KPHLIM", "99"))
DBG = os.environ.get("KDBG", "0") == "1"

ARENA0 = 23552
ARENA_CAP = 229376


class Ctr:
    def __init__(self, es, nc, name):
        self.h = es.enter_context(nc.semaphore(name))
        self.n = 0
        self.name = name

    def tag(self, inst, k=1):
        if k:
            if not isinstance(inst, _StubInst) and inst is not None:
                inst.then_inc(self.h, k)
            self.n += k
        return self.n


class _StubInst:
    def then_inc(self, sem, k):
        return self


class _StubEng:
    def __getattr__(self, name):
        if name == "wait_ge":
            return lambda *a, **kw: None
        return lambda *a, **kw: _StubInst()


def build_nc():
    nc = bass.Bass("TRN2", target_bir_lowering=False)

    def din(name, shape, dt=F32):
        return nc.dram_tensor(name, list(shape), dt, kind="ExternalInput").ap()

    x_own = din("x_own", [T, D])
    x_halo = din("x_halo", [WIN, D])
    x_other = din("x_other", [T, D])
    qkv_w = din("qkv_w", [D, 3 * D])
    rq_w = din("rq_w", [D, LD])
    rkv_w = din("rkv_w", [D, 2 * LD])
    rout_w = din("rout_w", [LD, D])
    out_w = din("out_w", [D, D])
    wgT = din("wgT", [D, INNER])
    wuT = din("wuT", [D, INNER])
    wdT = din("wdT", [INNER, D])
    qkb_fm = din("qkb_fm", [128, 16])
    vb_bc = din("vb_bc", [128, D])
    rqb_fm = din("rqb_fm", [128, 2])
    rkvb_bc = din("rkvb_bc", [128, 2 * LD])
    routb_fm = din("routb_fm", [128, 8])
    outb_bc = din("outb_bc", [128, D])
    glgr_in = din("glgr_in", [128, 4])
    sel2_in = din("sel2_in", [128, 128])
    identb_in = din("identb_in", [128, 128], BF16)
    poolz_in = din("poolz_in", [128, 196], BF16)
    lmask = din("lmask", [2, 128, 512])
    latmask = din("latmask", [16, 128, 64])
    lbin = din("lbin", [128, 1])
    latbin = din("latbin", [128, 1])

    y = nc.dram_tensor("y", [T, D], F32, kind="ExternalOutput").ap()
    xq_scr = nc.dram_tensor("xq_scr", [NT, 128, D], BF16, kind="Internal").ap()
    attn_scr = nc.dram_tensor("attn_scr", [NT, 128, D], F32, kind="Internal").ap()
    h_scr = nc.dram_tensor("h_scr", [NT, 128, 8, 512], F32, kind="Internal").ap()

    dbg = {}
    if DBG:
        for nm, shape in [
            ("dbg_xn", [128, 8 * 2176]), ("dbg_sxr", [128, 16]),
            ("dbg_wsc", [128, 8]), ("dbg_rq", [128, 2 * 2048]),
            ("dbg_rk", [128, 2 * 64]), ("dbg_rkv1", [128, 512]),
            ("dbg_rkv2", [128, 512]), ("dbg_q", [128, 8 * 2048]),
            ("dbg_k", [128, 8 * 2176]), ("dbg_v", [128, 17 * 1024]),
            ("dbg_lat", [128, 2 * 2048]), ("dbg_loc", [128, 8 * 2048]),
            ("dbg_rem", [128, 8 * 2048]), ("dbg_mix", [128, 8 * 2048]),
            ("dbg_xq", [128, 8 * 2048]), ("dbg_pooled", [128, 8 * 192]),
        ]:
            dbg[nm] = nc.dram_tensor(nm, shape, F32, kind="ExternalOutput").ap()

    cursor = [0]

    def AT(name, shape, dt, off=None):
        nbytes = int(np.prod(shape[1:])) * (2 if dt == BF16 else 4)
        if off is None:
            off = ((cursor[0] + 63) // 64) * 64
        end = off + nbytes
        cursor[0] = max(cursor[0], end)
        assert end <= ARENA_CAP - ARENA0, (name, off, nbytes, end)
        t = nc.alloc_sbuf_tensor_at(name, list(shape), dt, offset=ARENA0 + off)
        return t.ap()

    identb = AT("identb", [128, 128], BF16)
    poolz = AT("poolz", [128, 196], BF16)
    lmask_t = AT("lmask_t", [128, 2, 512], F32)
    latmask_t = AT("latmask_t", [128, 16, 64], F32)
    lbin_t = AT("lbin_t", [128, 1], F32)
    latbin_t = AT("latbin_t", [128, 1], F32)
    qkb_t = AT("qkb_t", [128, 16], F32)
    vb_t = AT("vb_t", [128, D], F32)
    rqb_t = AT("rqb_t", [128, 2], F32)
    rkvb_t = AT("rkvb_t", [128, 2 * LD], F32)
    routb_t = AT("routb_t", [128, 8], F32)
    outb_t = AT("outb_t", [128, D], F32)
    glgr_f = AT("glgr_f", [128, 4], F32)
    glgr_b = AT("glgr_b", [128, 4], BF16)
    sel2_f = AT("sel2_f", [128, 128], F32)
    sel2_b = AT("sel2_b", [128, 128], BF16)
    onesf = AT("onesf", [128, 130], F32)
    sxr = AT("sxr", [128, NT], F32)
    gsc = AT("gsc", [128, NT], F32)
    usc = AT("usc", [128, NT], F32)
    wsc = AT("wsc", [128, 8], F32)
    absS = AT("absS", [128, 96], F32)
    stt = AT("stt", [128, 16], F32)
    mv = AT("mv", [128, 1], F32)
    t1 = AT("t1", [128, 1], F32)
    t2 = AT("t2", [128, 1], F32)
    t3 = AT("t3", [128, 1], F32)
    t4 = AT("t4", [128, 1], F32)
    rr = AT("rr", [128, 1], F32)
    r0 = AT("r0", [128, 1], F32)
    s_t = AT("s_t", [128, 2], F32)
    am = AT("am", [128, 1], F32)
    smax = AT("smax", [128, 8], F32)
    ssum = AT("ssum", [128, 8], F32)
    srcp = AT("srcp", [128, 8], F32)
    sh_t = AT("sh_t", [128, 2], F32)
    hmax = AT("hmax", [128, 128], F32)
    dsc_t = AT("dsc_t", [128, 2], F32)
    PERS_END = cursor[0]
    BIG0 = ((PERS_END + 63) // 64) * 64
    assert BIG0 <= 27648, BIG0

    XN = AT("XN", [128, 8, 2176], BF16, BIG0)
    Q0 = BIG0 + 34816
    q_fm = AT("q_fm", [128, 8, T], BF16, Q0)
    k_fm = AT("k_fm", [128, 8, 2176], BF16, BIG0 + 67584)
    V0 = BIG0 + 102400
    v_tm = AT("v_tm", [128, 17, D], BF16, V0)
    R0 = BIG0 + 137216
    rq_fm = AT("rq_fm", [128, 2, T], BF16, R0)
    xbuf = AT("xbuf", [128, 2, D], F32, Q0)
    xnf = AT("xnf", [128, D], F32, Q0 + 8192)
    xnb = AT("xnb", [128, 2, D], BF16, Q0 + 12288)
    xqb = AT("xqb", [128, 2, D], BF16, Q0 + 16384)
    qtmp = AT("qtmp", [128, D], F32, Q0 + 20480)
    sqd = AT("sqd", [128, 2, D], F32, Q0 + 24576)
    outw_b = AT("outw_b", [128, 8, D], BF16, V0)
    attn_buf = AT("attn_buf", [128, 2, D], F32, V0 + 16384)
    gate_sb = AT("gate_sb", [128, 512], BF16, R0)
    gdiff = AT("gdiff", [128, 512], BF16, R0 + 1024)
    gtmp = AT("gtmp", [128, 512], BF16, R0 + 2048)
    TR0 = BIG0 + 145408
    wqk_b = AT("wqk_b", [128, 2, 8, 128], BF16, TR0)
    wv_b = AT("wv_b", [128, 2, 8, 256], BF16, TR0 + 4096)
    rkvw_b = AT("rkvw_b", [128, 8, 512], BF16, TR0 + 4096)
    routw_b = AT("routw_b", [128, 2, D], BF16, TR0 + 4096)
    rkv1 = AT("rkv1", [128, 2 * LD], BF16, TR0 + 12288)
    rkv2 = AT("rkv2", [128, 2 * LD], BF16, TR0 + 13312)
    rk_fm = AT("rk_fm", [128, 2, NLAT], BF16, TR0 + 14336)
    ltmp = AT("ltmp", [128, 512], F32, TR0 + 19776)
    qz = AT("qz", [128, 2, T], BF16, TR0 + 16704)
    pooled_pad = AT("pooled_pad", [128, 8, 192], BF16, TR0 + 16704)
    pe_sb = AT("pe_sb", [128, 4, 512], BF16, TR0 + 24896)
    pt_sb = AT("pt_sb", [128, 2, 512], BF16, TR0 + 14656)
    latout = AT("latout", [128, 2, T], BF16, TR0 + 28992)
    absb = AT("absb", [128, 2, 1024], F32, TR0 + 4096)
    absd = AT("absd", [128, 2, 1024], BF16, TR0)
    local_fm = AT("local_fm", [128, 8, T], BF16, BIG0)
    remote = AT("remote", [128, 8, T], BF16, Q0)
    mixed = AT("mixed", [128, 8, T], BF16, BIG0 + 67584)

    XQ = AT("XQ", [128, 8, T], BF16, BIG0)
    xql = AT("xql", [128, 2, D], BF16, BIG0 + 32768)
    wgq = AT("wgq", [128, 2, 8, 512], BF16, BIG0 + 36864)
    wuq = AT("wuq", [128, 2, 8, 512], BF16, BIG0 + 53248)
    sg_sb = AT("sg_sb", [128, 512], F32, BIG0 + 69632)
    hwb = AT("hwb", [128, 2, 4, 512], F32, BIG0 + 71680)
    wq32 = AT("wq32", [128, 2, 8, 512], F32, BIG0 + 88064)
    wtmp = AT("wtmp", [128, 8, 512], F32, BIG0 + 120832)
    wdq2 = AT("wdq2", [128, 32, D], BF16, BIG0 + 36864)
    hrb = AT("hrb", [128, 2, INNER], F32, BIG0)
    tq_sb = AT("tq_sb", [128, INNER], F32, BIG0 + 102400)
    xh_tm = AT("xh_tm", [128, INNER], BF16, BIG0 + 118784)
    xh_fm = AT("xh_fm", [128, 32, 128], BF16, BIG0 + 126976)
    atl = AT("atl", [128, 2, D], F32, BIG0 + 135168)
    xres = AT("xres", [128, 2, D], F32, BIG0 + 143360)
    obuf = AT("obuf", [128, 2, D], F32, BIG0 + 151552)
    wdf = AT("wdf", [128, 2, D], F32, BIG0 + 159744)
    wdt_r = AT("wdt_r", [128, 2, D], F32, BIG0 + 167936)
    xh_tm2 = AT("xh_tm2", [128, 2, INNER], BF16, BIG0 + 159744)
    assert BIG0 + 180544 <= 205824, BIG0

    pb = [nc.alloc_psum_tensor("pb%d" % i, [128, 512], F32).ap() for i in range(8)]

    es = ExitStack()
    dSp = [Ctr(es, nc, "dSa"), Ctr(es, nc, "dSb")]
    dGp = [Ctr(es, nc, "dGa"), Ctr(es, nc, "dGb")]
    dWp = [Ctr(es, nc, "dWa"), Ctr(es, nc, "dWb")]
    dAp = [Ctr(es, nc, "dAa"), Ctr(es, nc, "dAb")]
    dS, dSb = dSp
    dG, dGb = dGp
    dW, dWb = dWp
    cV = Ctr(es, nc, "cV")
    cS = Ctr(es, nc, "cS")
    cT = Ctr(es, nc, "cT")
    cG = Ctr(es, nc, "cG")
    ALLC = dSp + dGp + dWp + dAp + [cV, cS, cT, cG]

    blk = es.enter_context(nc.Block())

    def full_barrier():
        sn = [(c, c.n) for c in ALLC]

        def w_all(e):
            for c, v in sn:
                if v > 0:
                    e.wait_ge(c.h, v)
        blk.sync(w_all)
        blk.vector(w_all)
        blk.scalar(w_all)
        blk.tensor(w_all)
        blk.gpsimd(w_all)

    CK = {}

    def run_phase(name, sections):
        base = {c.name: c.n for c in ALLC}
        mode = ["count"]

        def ck(label):
            if mode[0] == "count":
                CK[(name, label)] = {c.name: c.n for c in ALLC}

        def w(e, label, ctr):
            if mode[0] == "count":
                return
            key = (name, label)
            if key in CK:
                v = CK[key][ctr.name]
                if v > 0:
                    e.wait_ge(ctr.h, v)

        stub = _StubEng()
        for _, gen in sections:
            gen(stub, ck, w)
        after = {c.name: c.n for c in ALLC}
        for c in ALLC:
            c.n = base[c.name]
        mode[0] = "real"
        for meth, gen in sections:
            meth(lambda e, g=gen: g(e, ck, w))
        for c in ALLC:
            assert c.n == after[c.name], (name, c.name, c.n, after[c.name])
        full_barrier()

    def dbg_dump(pairs):
        if not DBG:
            return

        def g(e):
            for nm, src in pairs:
                if nm in dbg:
                    sflat = src if len(src.shape) == 2 else src.rearrange(
                        "p a n -> p (a n)")
                    dG.tag(e.dma_start(out=dbg[nm], in_=sflat), 16)
            e.wait_ge(dG.h, dG.n)
        blk.gpsimd(g)
        full_barrier()

    def p0_sync(e, ck, w):
        p0i = [0]
        for dst, src in [
            (identb[:, :], identb_in[:, :]),
            (poolz[:, :], poolz_in[:, :]),
            (lmask_t[:, :, :], lmask[:, :, :].rearrange("a p n -> p a n")),
            (latmask_t[:, :, :], latmask[:, :, :].rearrange("a p n -> p a n")),
            (lbin_t[:, :], lbin[:, :]), (latbin_t[:, :], latbin[:, :]),
            (qkb_t[:, :], qkb_fm[:, :]), (vb_t[:, :], vb_bc[:, :]),
            (rqb_t[:, :], rqb_fm[:, :]), (rkvb_t[:, :], rkvb_bc[:, :]),
            (routb_t[:, :], routb_fm[:, :]), (outb_t[:, :], outb_bc[:, :]),
            (glgr_f[:, :], glgr_in[:, :]), (sel2_f[:, :], sel2_in[:, :]),
        ]:
            dSp[p0i[0] % 2].tag(e.dma_start(out=dst, in_=src), 16)
            p0i[0] += 1
        ck("loads")

    def p0_vec(e, ck, w):
        w(e, "loads", dS)
        w(e, "loads", dSb)
        e.tensor_copy(glgr_b[:, :], glgr_f[:, :])
        e.tensor_copy(sel2_b[:, :], sel2_f[:, :])
        e.memset(onesf[:, 0:1], 1.0)
        e.memset(onesf[0:1, 1:130], 1.0)
        cV.tag(e.memset(wsc[:, :], 0.0))

    run_phase("p0", [(blk.sync, p0_sync), (blk.vector, p0_vec)])

    if PHLIM < 1:
        es.close()
        return nc

    if PHLIM < 2:
        es.close()
        return nc

    tiles = [("halo", 0)] + [("own", i) for i in range(NT)] + \
            [("other", i) for i in range(NT)]
    n_pool = 2 * NT

    def a_src(kind, i):
        if kind == "halo":
            return x_halo[:, :]
        if kind == "own":
            return x_own[128 * i:128 * i + 128, :]
        return x_other[128 * i:128 * i + 128, :]

    def pa_pre(e, ck, w):
        ck("d0")

    def pa_sync(e, ck, w):
        for j, (kind, i) in enumerate(tiles):
            if j >= 2:
                w(e, "v_xnb_%d" % (j - 2), cV)
                w(e, "s_sq_%d" % (j - 2), cS)
            dSp[j % 2].tag(e.dma_start(out=xbuf[:, j % 2, :],
                                       in_=a_src(kind, i)), 16)

    def pa_vec(e, ck, w):
        base = CK.get(("pa", "d0"))
        last_own = None
        for j, (kind, i) in enumerate(tiles):
            if base is not None and not isinstance(e, _StubEng):
                c = dSp[j % 2]
                e.wait_ge(c.h, base[c.name] + 16 * (j // 2 + 1))
            e.reduce_sum(stt[:, 0:1], xbuf[:, j % 2, :], axis=AX.X)
            w(e, "s_sq_%d" % j, cS)
            e.drain()
            e.tensor_scalar(out=mv[:, :], in0=stt[:, 0:1], scalar1=1.0 / D,
                            scalar2=None, op0=Alu.mult)
            e.tensor_scalar(out=t1[:, :], in0=stt[:, 1:2], scalar1=1.0 / D,
                            scalar2=EPS, op0=Alu.mult, op1=Alu.add)
            e.drain()
            e.tensor_scalar(out=t2[:, :], in0=mv[:, :], scalar1=mv[:, :],
                            scalar2=None, op0=Alu.mult)
            e.drain()
            cV.tag(e.tensor_tensor(out=t1[:, :], in0=t1[:, :], in1=t2[:, :],
                                   op=Alu.subtract))
            ck("v_var_%d" % j)
            w(e, "s_sqrt_%d" % j, cS)
            cV.tag(e.reciprocal(out=r0[:, :], in_=s_t[:, 0:1]))
            ck("v_r0_%d" % j)
            e.drain()
            e.tensor_scalar(out=t3[:, :], in0=r0[:, :], scalar1=r0[:, :],
                            scalar2=None, op0=Alu.mult)
            e.drain()
            e.tensor_scalar(out=t3[:, :], in0=t3[:, :], scalar1=t1[:, :],
                            scalar2=None, op0=Alu.mult)
            e.drain()
            e.tensor_scalar(out=t3[:, :], in0=t3[:, :], scalar1=-0.5,
                            scalar2=1.5, op0=Alu.mult, op1=Alu.add)
            e.drain()
            e.tensor_scalar(out=rr[:, :], in0=r0[:, :], scalar1=t3[:, :],
                            scalar2=None, op0=Alu.mult)
            e.drain()
            e.tensor_scalar(out=t4[:, :], in0=mv[:, :], scalar1=rr[:, :],
                            scalar2=-1.0, op0=Alu.mult, op1=Alu.mult)
            e.drain()
            if j >= 2:
                w(e, "t_done_%d" % (j - 2), cT)
            if kind == "own":
                if last_own is not None:
                    w(e, "s_rnd_%d" % last_own, cS)
                e.tensor_scalar(out=xnf[:, :], in0=xbuf[:, j % 2, :],
                                scalar1=rr[:, :], scalar2=t4[:, :],
                                op0=Alu.mult, op1=Alu.add)
                e.drain()
                cV.tag(e.tensor_copy(xnb[:, j % 2, :], xnf[:, :]))
                ck("v_xnb_%d" % j)
                e.reduce_max(am[:, :], xnf[:, :], axis=AX.X,
                             apply_absolute_value=True)
                e.drain()
                e.tensor_scalar(out=am[:, :], in0=am[:, :], scalar1=1e-5,
                                scalar2=None, op0=Alu.max)
                e.drain()
                e.reciprocal(out=t3[:, :], in_=am[:, :])
                e.drain()
                cV.tag(e.tensor_scalar(out=s_t[:, 1:2], in0=t3[:, :],
                                       scalar1=127.0, scalar2=None, op0=Alu.mult))
                ck("v_st_%d" % j)
                e.drain()
                e.reciprocal(out=sxr[:, i:i + 1], in_=s_t[:, 1:2])
                w(e, "s_rnd_%d" % j, cS)
                if i >= 2:
                    w(e, "g_xq_%d" % (i - 2), dGp[i % 2])
                cV.tag(e.tensor_scalar(out=xqb[:, i % 2, :], in0=qtmp[:, :],
                                       scalar1=C_RND, scalar2=None,
                                       op0=Alu.subtract))
                ck("v_xq_%d" % j)
                last_own = j
            else:
                cV.tag(e.tensor_scalar(out=xnb[:, j % 2, :], in0=xbuf[:, j % 2, :],
                                       scalar1=rr[:, :], scalar2=t4[:, :],
                                       op0=Alu.mult, op1=Alu.add))
                ck("v_xnb_%d" % j)
            if kind != "other":
                w(e, "t_tr_%d" % j, cT)
                c0 = 0 if kind == "halo" else 128 + 128 * i
                bA, bB = pb[2 * (j % 2)], pb[2 * (j % 2) + 1]
                e.tensor_copy(XN[:, 0:4, c0:c0 + 128],
                              bA[:, :].rearrange("p (a n) -> p a n", a=4))
                cV.tag(e.tensor_copy(XN[:, 4:8, c0:c0 + 128],
                                     bB[:, :].rearrange("p (a n) -> p a n", a=4)))
            ck("v_evac_%d" % j)

    def pa_scalar(e, ck, w):
        base = CK.get(("pa", "d0"))
        last_own = None
        for j, (kind, i) in enumerate(tiles):
            if base is not None and not isinstance(e, _StubEng):
                c = dSp[j % 2]
                e.wait_ge(c.h, base[c.name] + 16 * (j // 2 + 1))
            if j >= 1:
                w(e, "v_var_%d" % (j - 1), cV)
            cS.tag(e.activation(out=sqd[:, j % 2, :], in_=xbuf[:, j % 2, :],
                                func=AFT.Square, accum_out=stt[:, 1:2]))
            ck("s_sq_%d" % j)
            w(e, "v_var_%d" % j, cV)
            if j >= 1:
                w(e, "v_r0_%d" % (j - 1), cV)
            cS.tag(e.activation(out=s_t[:, 0:1], in_=t1[:, :], func=AFT.Sqrt))
            ck("s_sqrt_%d" % j)
            if kind == "own":
                w(e, "v_st_%d" % j, cV)
                if last_own is not None:
                    w(e, "v_xq_%d" % last_own, cV)
                cS.tag(e.activation(out=qtmp[:, :], in_=xnf[:, :], func=AFT.Copy,
                                    scale=s_t[:, 1:2], bias=C_RND))
                ck("s_rnd_%d" % j)
                last_own = j

    def pa_tensor(e, ck, w):
        pool_seq = 0
        for j, (kind, i) in enumerate(tiles):
            w(e, "v_xnb_%d" % j, cV)
            if kind != "other":
                if j >= 2:
                    w(e, "v_evac_%d" % (j - 2), cV)
                bA, bB = pb[2 * (j % 2)], pb[2 * (j % 2) + 1]
                for c in range(8):
                    bnk = bA if c < 4 else bB
                    inst = e.matmul(bnk[:, 128 * (c % 4):128 * (c % 4) + 128],
                                    xnb[:, j % 2, 128 * c:128 * c + 128],
                                    identb[:, :], start=True, stop=True)
                    if c == 7:
                        cT.tag(inst)
                ck("t_tr_%d" % j)
            if kind != "halo":
                if kind == "own":
                    z = poolz[:, 64 - 2 * i:192 - 2 * i]
                else:
                    z = poolz[:, 32 - 2 * i:160 - 2 * i]
                first = pool_seq == 0
                last = pool_seq == n_pool - 1
                e.matmul(pb[6][:, :], z, xnb[:, j % 2, 0:512],
                         start=first, stop=last)
                cT.tag(e.matmul(pb[7][:, :], z, xnb[:, j % 2, 512:1024],
                                start=first, stop=last))
                pool_seq += 1
            ck("t_done_%d" % j)

    def pa_gp(e, ck, w):
        for j, (kind, i) in enumerate(tiles):
            if kind == "own":
                w(e, "v_xq_%d" % j, cV)
                dGp[i % 2].tag(e.dma_start(out=xq_scr[i, :, :],
                                           in_=xqb[:, i % 2, :]), 16)
                ck("g_xq_%d" % i)

    run_phase("pa", [
        (blk.sync, pa_pre), (blk.sync, pa_sync), (blk.vector, pa_vec),
        (blk.scalar, pa_scalar), (blk.tensor, pa_tensor), (blk.gpsimd, pa_gp),
    ])
    dbg_dump([("dbg_xn", XN), ("dbg_sxr", sxr[:, :])])

    if PHLIM < 3:
        es.close()
        return nc

    def pa2a_gp(e, ck, w):
        rkv3 = rkv_w.rearrange("(c p) n -> p c n", p=128)
        rq3 = rq_w.rearrange("(c p) n -> p c n", p=128)
        dW.tag(e.dma_start(out=rkvw_b[:, :, :], in_=rkv3), 16)
        dWb.tag(e.dma_start(out=wqk_b[:, 0, :, :], in_=rq3[:, :, 0:128]), 16)
        dW.tag(e.dma_start(out=wqk_b[:, 1, :, :], in_=rq3[:, :, 128:256]), 16)
        e.wait_ge(dW.h, dW.n)
        e.wait_ge(dWb.h, dWb.n)

    run_phase("pa2a", [(blk.gpsimd, pa2a_gp)])

    def pa2_vec(e, ck, w):
        e.memset(pooled_pad[:, :, 0:64], 0.0)
        cV.tag(e.memset(pooled_pad[:, :, 128:192], 0.0))
        e.tensor_scalar(out=qtmp[0:64, 0:512], in0=pb[6][0:64, :],
                        scalar1=1.0 / 64.0, scalar2=None, op0=Alu.mult)
        e.tensor_scalar(out=qtmp[0:64, 512:1024], in0=pb[7][0:64, :],
                        scalar1=1.0 / 64.0, scalar2=None, op0=Alu.mult)
        e.drain()
        cV.tag(e.tensor_copy(xnb[0:64, 0, :], qtmp[0:64, :]))
        ck("pooled_lm")
        w(e, "t_ptr", cT)
        e.tensor_copy(pooled_pad[:, 0:4, 64:128],
                      pb[0][:, :].rearrange("p (a n) -> p a n", a=4)[:, :, 0:64])
        cV.tag(e.tensor_copy(pooled_pad[:, 4:8, 64:128],
                             pb[1][:, :].rearrange("p (a n) -> p a n", a=4)[:, :, 0:64]))
        ck("ppad")
        w(e, "t_rkv1", cT)
        e.tensor_tensor(out=ltmp[0:64, :], in0=pb[2][0:64, :],
                        in1=rkvb_t[0:64, :], op=Alu.add)
        e.drain()
        e.tensor_copy(rkv1[0:64, :], ltmp[0:64, :])
        cV.tag(e.memset(rkv1[64:128, :], 0.0))
        ck("rkv1")
        w(e, "t_rkv2", cT)
        e.tensor_tensor(out=ltmp[64:128, :], in0=pb[3][64:128, :],
                        in1=rkvb_t[64:128, :], op=Alu.add)
        e.drain()
        e.tensor_copy(rkv2[64:128, :], ltmp[64:128, :])
        cV.tag(e.memset(rkv2[0:64, :], 0.0))
        ck("rkv2")
        w(e, "t_rkT", cT)
        e.tensor_copy(rk_fm[:, 0, :], pb[4][:, 0:64])
        cV.tag(e.tensor_copy(rk_fm[:, 1, :], pb[5][:, 0:64]))
        ck("rk")
        for oc in range(2):
            w(e, "t_rq_%d" % oc, cT)
            for n in range(4):
                inst = e.tensor_scalar(
                    out=rq_fm[:, oc, 512 * n:512 * n + 512],
                    in0=pb[4 * oc + n][:, :],
                    scalar1=rqb_t[:, oc:oc + 1], scalar2=None, op0=Alu.add)
                if n == 3:
                    cV.tag(inst)
            ck("v_rq_%d" % oc)


    def pa2_tensor(e, ck, w):
        w(e, "pooled_lm", cV)
        for c in range(8):
            bnk = pb[0] if c < 4 else pb[1]
            inst = e.matmul(bnk[:, 128 * (c % 4):128 * (c % 4) + 128],
                            xnb[0:64, 0, 128 * c:128 * c + 128],
                            identb[0:64, :], start=True, stop=True)
            if c == 7:
                cT.tag(inst)
        ck("t_ptr")
        w(e, "ppad", cV)
        for kk in range(8):
            inst = e.matmul(pb[2][:, :], pooled_pad[:, kk, 64:192],
                            rkvw_b[:, kk, :], start=(kk == 0), stop=(kk == 7))
            if kk == 7:
                cT.tag(inst)
        ck("t_rkv1")
        for kk in range(8):
            inst = e.matmul(pb[3][:, :], pooled_pad[:, kk, 0:128],
                            rkvw_b[:, kk, :], start=(kk == 0), stop=(kk == 7))
            if kk == 7:
                cT.tag(inst)
        ck("t_rkv2")
        w(e, "rkv1", cV)
        e.matmul(pb[4][:, 0:64], rkv1[0:64, 0:128], identb[0:64, 0:64],
                 start=True, stop=True)
        cT.tag(e.matmul(pb[5][:, 0:64], rkv1[0:64, 128:256], identb[0:64, 0:64],
                        start=True, stop=True))
        ck("t_rkT")
        w(e, "rkv2", cV)
        w(e, "rk", cV)
        for oc in range(2):
            if oc == 1:
                w(e, "v_rq_0", cV)
            for n in range(4):
                for kk in range(8):
                    inst = e.matmul(pb[4 * oc + n][:, :],
                                    wqk_b[:, oc, kk, :],
                                    XN[:, kk, 128 + 512 * n:128 + 512 * n + 512],
                                    start=(kk == 0), stop=(kk == 7))
                    if kk == 7 and n == 3:
                        cT.tag(inst)
            ck("t_rq_%d" % oc)

    run_phase("pa2", [(blk.vector, pa2_vec), (blk.tensor, pa2_tensor)])
    dbg_dump([("dbg_rq", rq_fm), ("dbg_rk", rk_fm),
              ("dbg_rkv1", rkv1[:, :]), ("dbg_rkv2", rkv2[:, :]),
              ("dbg_pooled", pooled_pad)])

    if PHLIM < 4:
        es.close()
        return nc

    def pq_pre(e, ck, w):
        ck("d0")

    def pq_gp(e, ck, w):
        w3 = qkv_w.rearrange("(c p) n -> p c n", p=128)
        for oc in range(16):
            if oc >= 2:
                w(e, "t_wfree_%d" % (oc - 2), cT)
            dWp[oc % 2].tag(e.dma_start(out=wqk_b[:, oc % 2, :, :],
                               in_=w3[:, :, 128 * oc:128 * oc + 128]), 16)
        for vs in range(4):
            if vs >= 2:
                w(e, "t_v_%d_%d" % (vs - 2, 16), cT)
            else:
                w(e, "t_wfree_%d" % (14 + vs), cT)
            dWp[vs % 2].tag(e.dma_start(out=wv_b[:, vs % 2, :, :],
                               in_=w3[:, :, 2048 + 256 * vs:2048 + 256 * vs + 256]),
                   16)

    def pq_vec(e, ck, w):
        for oc in range(16):
            if oc < 8:
                for n in range(4):
                    w(e, "t_qk_%d_%d" % (oc, n), cT)
                    bnk = pb[n + 4 * (oc % 2)]
                    cV.tag(e.tensor_scalar(out=q_fm[:, oc, 512 * n:512 * n + 512],
                                           in0=bnk[:, :],
                                           scalar1=qkb_t[:, oc:oc + 1],
                                           scalar2=None, op0=Alu.add))
            else:
                ko = oc - 8
                for n in range(5):
                    w(e, "t_k_%d_%d" % (ko, n), cT)
                    bnk = pb[(n % 4) + 4 * (oc % 2)]
                    c0 = 512 * n
                    wd = 512 if n < 4 else 128
                    cV.tag(e.tensor_scalar(out=k_fm[:, ko, c0:c0 + wd],
                                           in0=bnk[:, 0:wd],
                                           scalar1=qkb_t[:, oc:oc + 1],
                                           scalar2=None, op0=Alu.add))
                    ck("v_kev_%d_%d" % (ko, n))
            ck("v_evac_%d" % oc)
        for vs in range(4):
            for tt in range(17):
                w(e, "t_v_%d_%d" % (vs, tt), cT)
                bnk = pb[(tt % 4) + 4 * (vs % 2)]
                cV.tag(e.tensor_tensor(out=v_tm[:, tt, 256 * vs:256 * vs + 256],
                                       in0=bnk[:, 0:256],
                                       in1=vb_t[:, 256 * vs:256 * vs + 256],
                                       op=Alu.add))
                ck("v_vev_%d_%d" % (vs, tt))

    def pq_tensor(e, ck, w):
        base = CK.get(("pq", "d0"))
        for oc in range(16):
            if base is not None and not isinstance(e, _StubEng):
                c = dWp[oc % 2]
                e.wait_ge(c.h, base[c.name] + 16 * (oc // 2 + 1))
            if oc < 8:
                for n in range(4):
                    if oc >= 2:
                        w(e, "v_evac_%d" % (oc - 2), cV)
                    bnk = pb[n + 4 * (oc % 2)]
                    for kk in range(8):
                        inst = e.matmul(bnk[:, :], wqk_b[:, oc % 2, kk, :],
                                        XN[:, kk, 128 + 512 * n:128 + 512 * n + 512],
                                        start=(kk == 0), stop=(kk == 7))
                        if kk == 7:
                            cT.tag(inst)
                    ck("t_qk_%d_%d" % (oc, n))
            else:
                ko = oc - 8
                for n in range(5):
                    if oc >= 2:
                        w(e, "v_evac_%d" % (oc - 2), cV)
                    if n == 4:
                        w(e, "v_kev_%d_0" % ko, cV)
                    bnk = pb[(n % 4) + 4 * (oc % 2)]
                    c0 = 512 * n
                    wd = 512 if n < 4 else 128
                    for kk in range(8):
                        inst = e.matmul(bnk[:, 0:wd], wqk_b[:, oc % 2, kk, :],
                                        XN[:, kk, c0:c0 + wd],
                                        start=(kk == 0), stop=(kk == 7))
                        if kk == 7:
                            cT.tag(inst)
                    ck("t_k_%d_%d" % (ko, n))
            ck("t_wfree_%d" % oc)
        for vs in range(4):
            if base is not None and not isinstance(e, _StubEng):
                c = dWp[vs % 2]
                e.wait_ge(c.h, base[c.name] + 16 * 8 + 16 * (vs // 2 + 1))
            if vs < 2:
                w(e, "v_evac_%d" % (14 + vs), cV)
            for tt in range(17):
                if tt >= 4:
                    w(e, "v_vev_%d_%d" % (vs, tt - 4), cV)
                elif vs >= 2:
                    w(e, "v_vev_%d_%d" % (vs - 2, 13 + tt), cV)
                bnk = pb[(tt % 4) + 4 * (vs % 2)]
                for kk in range(8):
                    inst = e.matmul(bnk[:, 0:256], XN[:, kk, 128 * tt:128 * tt + 128],
                                    wv_b[:, vs % 2, kk, :],
                                    start=(kk == 0), stop=(kk == 7))
                    if kk == 7:
                        cT.tag(inst)
                ck("t_v_%d_%d" % (vs, tt))

    run_phase("pq", [
        (blk.gpsimd, pq_pre), (blk.gpsimd, pq_gp),
        (blk.vector, pq_vec), (blk.tensor, pq_tensor),
    ])
    dbg_dump([("dbg_q", q_fm), ("dbg_k", k_fm), ("dbg_v", v_tm)])

    if PHLIM < 5:
        es.close()
        return nc

    NSLAB = 96

    def abs_src(sl):
        wg3 = wgT.rearrange("(c p) n -> p c n", p=128)
        wu3 = wuT.rearrange("(c p) n -> p c n", p=128)
        wd3 = wdT.rearrange("(c p) n -> p c n", p=128)
        mat, k = divmod(sl, 32)
        if mat == 0:
            return wg3[:, :, 128 * k:128 * k + 128]
        if mat == 1:
            return wu3[:, :, 128 * k:128 * k + 128]
        return wd3[:, k:k + 1, :]

    def mk_abs_gens(phname, nit, sl0, sl1):
        nsl = sl1 - sl0

        def slabs_for(it):
            return range(sl0 + (nsl * it) // nit, sl0 + (nsl * (it + 1)) // nit)

        def gp_gen(e, ck, w):
            ck("a0")
            for it in range(nit):
                for sl in slabs_for(it):
                    if sl - 2 >= sl0:
                        w(e, "abs_%d" % (sl - 2), cS)
                    src = abs_src(sl)
                    dAp[sl % 2].tag(e.dma_start(
                        out=absb[:, sl % 2, :].rearrange(
                            "p (a n) -> p a n", a=src.shape[1]),
                        in_=src), 16)

        def s_abs(e, it, base, ck, w):
            for sl in slabs_for(it):
                if base is not None and not isinstance(e, _StubEng):
                    c = dAp[sl % 2]
                    e.wait_ge(c.h, base[c.name] + 16 * ((sl - sl0) // 2 + 1))
                cS.tag(e.activation(out=absd[:, sl % 2, :],
                                    in_=absb[:, sl % 2, :], func=AFT.Abs,
                                    accum_out=absS[:, sl:sl + 1]))
                ck("abs_%d" % sl)
        return gp_gen, s_abs, slabs_for

    def plat_vec(e, ck, w):
        e.memset(qz[0:64, 1, :], 0.0)
        e.memset(qz[64:128, 0, :], 0.0)
        nit = 2 * NT
        for it in range(nit + 2):
            nstt = 0
            if it < nit:
                lc, qt = divmod(it, NT)
                if qt == 0:
                    if lc >= 1:
                        w(e, "t_sc_%d" % (lc * NT - 1), cT)
                    e.tensor_copy(qz[0:64, 0, :], rq_fm[0:64, lc, :])
                    cV.tag(e.tensor_copy(qz[64:128, 1, :], rq_fm[64:128, lc, :]))
                    ck("v_qz_%d" % lc)
                bs = 4 * (it % 2)
                w(e, "t_sc_%d" % it, cT)
                e.scalar_tensor_tensor(
                    out=pb[bs][:, 0:64], in0=pb[bs][:, 0:64], scalar=0.125,
                    in1=latmask_t[:, qt, :], op0=Alu.mult, op1=Alu.add)
                nstt = cV.tag(e.scalar_tensor_tensor(
                    out=pb[bs][:, 64:128], in0=pb[bs][:, 64:128], scalar=0.125,
                    in1=latmask_t[:, qt, :], op0=Alu.mult, op1=Alu.add))
            nrcp = 0
            if it >= 2:
                p2 = it - 2
                p = p2 % 2
                w(e, "s_exp_%d" % p2, cS)
                e.reciprocal(out=srcp[:, 2 * p:2 * p + 1],
                             in_=ssum[:, 2 * p:2 * p + 1])
                nrcp = cV.tag(e.reciprocal(out=srcp[:, 2 * p + 1:2 * p + 2],
                                           in_=ssum[:, 2 * p + 1:2 * p + 2]))
            if it < nit:
                e.wait_ge(cV.h, nstt)
                cV.tag(e.reduce_max(smax[:, it % 2:it % 2 + 1],
                                    pb[4 * (it % 2)][:, 0:128],
                                    axis=AX.X, negate=True))
                ck("v_max_%d" % it)
            if it >= 2:
                p2 = it - 2
                p = p2 % 2
                p4 = p2 % 4
                sc2 = latbin_t[:, 0:1] if (p2 % NT) == 0 else 1.0
                e.wait_ge(cV.h, nrcp)
                e.tensor_scalar(out=pe_sb[:, p4, 0:64], in0=pe_sb[:, p4, 0:64],
                                scalar1=srcp[:, 2 * p:2 * p + 1], scalar2=sc2,
                                op0=Alu.mult, op1=Alu.mult)
                cV.tag(e.tensor_scalar(out=pe_sb[:, p4, 64:128],
                                       in0=pe_sb[:, p4, 64:128],
                                       scalar1=srcp[:, 2 * p + 1:2 * p + 2],
                                       scalar2=sc2, op0=Alu.mult, op1=Alu.mult))
                ck("v_pe_%d" % p2)
                w(e, "t_tr_%d" % p2, cT)
                cV.tag(e.tensor_copy(pt_sb[:, p, 0:128],
                                     pb[4 * (p2 % 2) + 1][:, 0:128]))
                ck("v_pt_%d" % p2)

    def plat_scalar(e, ck, w, hook=lambda it: None):
        nit = 2 * NT
        for it in range(nit + 4):
            hook(it)
            if it < nit:
                p4 = it % 4
                w(e, "v_max_%d" % it, cV)
                if it >= 4:
                    w(e, "t_tr_%d" % (it - 4), cT)
                e.activation(out=pe_sb[:, p4, 0:64], in_=pb[4 * (it % 2)][:, 0:64],
                             func=AFT.Exp, bias=smax[:, it % 2:it % 2 + 1],
                             scale=1.0, accum_out=ssum[:, 2 * (it % 2):2 * (it % 2) + 1])
                cS.tag(e.activation(out=pe_sb[:, p4, 64:128],
                                    in_=pb[4 * (it % 2)][:, 64:128],
                                    func=AFT.Exp, bias=smax[:, it % 2:it % 2 + 1],
                                    scale=1.0,
                                    accum_out=ssum[:, 2 * (it % 2) + 1:2 * (it % 2) + 2]))
                ck("s_exp_%d" % it)
            if it >= 4:
                p4o = it - 4
                lc, qt = divmod(p4o, NT)
                bs = 4 * (p4o % 2)
                w(e, "t_av_%d" % p4o, cT)
                e.copy(latout[0:64, lc, 128 * qt:128 * qt + 128],
                       pb[bs + 2][0:64, 0:128])
                cS.tag(e.copy(latout[64:128, lc, 128 * qt:128 * qt + 128],
                              pb[bs + 2][64:128, 128:256]))
                ck("s_out_%d" % p4o)

    def plat_tensor(e, ck, w):
        nit = 2 * NT

        def emit_sc(e, it):
            lc, qt = divmod(it, NT)
            bs = 4 * (it % 2)
            if qt == 0:
                w(e, "v_qz_%d" % lc, cV)
            if it >= 2:
                w(e, "s_exp_%d" % (it - 2), cS)
            for sub in range(2):
                inst = e.matmul(pb[bs][:, 64 * sub:64 * sub + 64],
                                qz[:, sub, 128 * qt:128 * qt + 128],
                                rk_fm[:, lc, :], start=True, stop=True)
                if sub == 1:
                    cT.tag(inst)
            ck("t_sc_%d" % it)

        emit_sc(e, 0)
        for it in range(nit + 3):
            if 0 <= it - 2 < nit:
                p2 = it - 2
                w(e, "v_pe_%d" % p2, cV)
                if p2 >= 2:
                    w(e, "v_pt_%d" % (p2 - 2), cV)
                cT.tag(e.matmul(pb[4 * (p2 % 2) + 1][:, 0:128],
                                pe_sb[:, p2 % 4, 0:128],
                                identb[:, :], start=True, stop=True))
                ck("t_tr_%d" % p2)
            if 0 <= it - 3 < nit:
                p3 = it - 3
                lc = p3 // NT
                bs = 4 * (p3 % 2)
                w(e, "v_pt_%d" % p3, cV)
                if p3 >= 2:
                    w(e, "s_out_%d" % (p3 - 2), cS)
                e.matmul(pb[bs + 2][:, 0:128],
                         rkv1[:, LD + 128 * lc:LD + 128 * lc + 128],
                         pt_sb[:, p3 % 2, 0:128], start=True, stop=True)
                cT.tag(e.matmul(pb[bs + 2][:, 128:256],
                                rkv2[:, LD + 128 * lc:LD + 128 * lc + 128],
                                pt_sb[:, p3 % 2, 0:128], start=True, stop=True))
                ck("t_av_%d" % p3)
            if it + 1 < nit:
                emit_sc(e, it + 1)

    plat_gp_abs, plat_s_abs, _plat_slabs = mk_abs_gens("plat", 2 * NT + 4, 0, 20)

    def plat_scalar_wrapped(e, ck, w):
        base = CK.get(("plat", "a0"))
        it2 = [0]

        def hook(it):
            plat_s_abs(e, it, base, ck, w)
        plat_scalar(e, ck, w, hook)

    run_phase("plat", [
        (blk.gpsimd, plat_gp_abs),
        (blk.vector, plat_vec), (blk.scalar, plat_scalar_wrapped),
        (blk.tensor, plat_tensor),
    ])
    dbg_dump([("dbg_lat", latout)])

    if PHLIM < 6:
        es.close()
        return nc

    def ploc_vec(e, ck, w):
        nit = 8 * NT
        for it in range(nit + 2):
            nstt = 0
            if it < nit:
                c, qt = divmod(it, NT)
                if qt == 0:
                    if c >= 1:
                        w(e, "t_sc_%d" % (c * NT - 1), cT)
                    e.tensor_copy(qz[0:64, 0, :], q_fm[0:64, c, :])
                    cV.tag(e.tensor_copy(qz[64:128, 1, :], q_fm[64:128, c, :]))
                    ck("v_qz_%d" % c)
                bs = 3 * (it % 2)
                w(e, "t_sc_%d" % it, cT)
                mk = lmask_t[:, 0, :] if qt == 0 else lmask_t[:, 1, :]
                nstt = cV.tag(e.scalar_tensor_tensor(
                    out=pb[bs][:, :], in0=pb[bs][:, :], scalar=0.125,
                    in1=mk, op0=Alu.mult, op1=Alu.add))
            nrcp = 0
            if it >= 2:
                p2 = it - 2
                p = p2 % 2
                w(e, "s_exp_%d" % p2, cS)
                e.reciprocal(out=srcp[:, 2 * p:2 * p + 1],
                             in_=ssum[:, 2 * p:2 * p + 1])
                nrcp = cV.tag(e.reciprocal(out=srcp[:, 2 * p + 1:2 * p + 2],
                                           in_=ssum[:, 2 * p + 1:2 * p + 2]))
            if it < nit:
                e.wait_ge(cV.h, nstt)
                cV.tag(e.reduce_max(smax[:, it % 2:it % 2 + 1],
                                    pb[3 * (it % 2)][:, :],
                                    axis=AX.X, negate=True))
                ck("v_max_%d" % it)
            if it >= 2:
                p2 = it - 2
                p = p2 % 2
                p4 = p2 % 4
                sc2 = lbin_t[:, 0:1] if (p2 % NT) == 0 else 1.0
                e.wait_ge(cV.h, nrcp)
                e.tensor_scalar(out=pe_sb[:, p4, 0:256], in0=pe_sb[:, p4, 0:256],
                                scalar1=srcp[:, 2 * p:2 * p + 1], scalar2=sc2,
                                op0=Alu.mult, op1=Alu.mult)
                cV.tag(e.tensor_scalar(out=pe_sb[:, p4, 256:512],
                                       in0=pe_sb[:, p4, 256:512],
                                       scalar1=srcp[:, 2 * p + 1:2 * p + 2],
                                       scalar2=sc2, op0=Alu.mult, op1=Alu.mult))
                ck("v_pe_%d" % p2)
                w(e, "t_tr_%d" % p2, cT)
                cV.tag(e.tensor_copy(pt_sb[:, p, :], pb[3 * (p2 % 2) + 1][:, :]))
                ck("v_pt_%d" % p2)

    def ploc_scalar(e, ck, w, hook=lambda it: None):
        nit = 8 * NT
        for it in range(nit + 4):
            hook(it)
            if it < nit:
                p4 = it % 4
                bs = 3 * (it % 2)
                w(e, "v_max_%d" % it, cV)
                if it >= 4:
                    w(e, "t_tr_%d" % (it - 4), cT)
                e.activation(out=pe_sb[:, p4, 0:256], in_=pb[bs][:, 0:256],
                             func=AFT.Exp, bias=smax[:, it % 2:it % 2 + 1],
                             scale=1.0,
                             accum_out=ssum[:, 2 * (it % 2):2 * (it % 2) + 1])
                cS.tag(e.activation(out=pe_sb[:, p4, 256:512],
                                    in_=pb[bs][:, 256:512],
                                    func=AFT.Exp, bias=smax[:, it % 2:it % 2 + 1],
                                    scale=1.0,
                                    accum_out=ssum[:, 2 * (it % 2) + 1:2 * (it % 2) + 2]))
                ck("s_exp_%d" % it)
            if it >= 4:
                p4o = it - 4
                c, qt = divmod(p4o, NT)
                bs2 = 3 * (p4o % 2) + 2
                w(e, "t_av_%d" % p4o, cT)
                e.copy(local_fm[0:64, c, 128 * qt:128 * qt + 128],
                       pb[bs2][0:64, 0:128])
                cS.tag(e.copy(local_fm[64:128, c, 128 * qt:128 * qt + 128],
                              pb[bs2][64:128, 128:256]))
                ck("s_out_%d" % p4o)

    def ploc_tensor(e, ck, w):
        nit = 8 * NT

        def emit_sc(e, it):
            c, qt = divmod(it, NT)
            bs = 3 * (it % 2)
            if qt == 0:
                w(e, "v_qz_%d" % c, cV)
            if it >= 2:
                w(e, "s_exp_%d" % (it - 2), cS)
            for sub in range(2):
                inst = e.matmul(pb[bs][:, 256 * sub:256 * sub + 256],
                                qz[:, sub, 128 * qt:128 * qt + 128],
                                k_fm[:, c, 128 * qt:128 * qt + 256],
                                start=True, stop=True)
                if sub == 1:
                    cT.tag(inst)
            ck("t_sc_%d" % it)

        emit_sc(e, 0)
        for it in range(nit + 3):
            if 0 <= it - 2 < nit:
                p2 = it - 2
                w(e, "v_pe_%d" % p2, cV)
                if p2 >= 2:
                    w(e, "v_pt_%d" % (p2 - 2), cV)
                for q4 in range(4):
                    inst = e.matmul(pb[3 * (p2 % 2) + 1][:, 128 * q4:128 * q4 + 128],
                                    pe_sb[:, p2 % 4, 128 * q4:128 * q4 + 128],
                                    identb[:, :], start=True, stop=True)
                    if q4 == 3:
                        cT.tag(inst)
                ck("t_tr_%d" % p2)
            if 0 <= it - 3 < nit:
                p3 = it - 3
                c, qt = divmod(p3, NT)
                bs = 3 * (p3 % 2)
                w(e, "v_pt_%d" % p3, cV)
                if p3 >= 2:
                    w(e, "s_out_%d" % (p3 - 2), cS)
                for sub in range(2):
                    for kc in range(2):
                        inst = e.matmul(
                            pb[bs + 2][:, 128 * sub:128 * sub + 128],
                            v_tm[:, qt + kc, 128 * c:128 * c + 128],
                            pt_sb[:, p3 % 2, 256 * sub + 128 * kc:256 * sub + 128 * kc + 128],
                            start=(kc == 0), stop=(kc == 1))
                        if sub == 1 and kc == 1:
                            cT.tag(inst)
                ck("t_av_%d" % p3)
            if it + 1 < nit:
                emit_sc(e, it + 1)

    ploc_gp_abs, ploc_s_abs, _ploc_slabs = mk_abs_gens("ploc", 8 * NT + 4, 20, 96)

    def ploc_scalar_wrapped(e, ck, w):
        base = CK.get(("ploc", "a0"))

        def hook(it):
            ploc_s_abs(e, it, base, ck, w)
        ploc_scalar(e, ck, w, hook)

    run_phase("ploc", [
        (blk.gpsimd, ploc_gp_abs),
        (blk.vector, ploc_vec), (blk.scalar, ploc_scalar_wrapped),
        (blk.tensor, ploc_tensor),
    ])
    dbg_dump([("dbg_loc", local_fm)])

    if PHLIM < 7:
        es.close()
        return nc

    def prout_gp(e, ck, w):
        ro3 = rout_w.rearrange("(c p) n -> p c n", p=128)
        dW.tag(e.dma_start(out=routw_b[:, :, :], in_=ro3), 16)
        e.wait_ge(dW.h, dW.n)

    def prout_vec(e, ck, w):
        e.reduce_sum(stt[:, 8:9], absS[:, 0:32], axis=AX.X)
        e.reduce_sum(stt[:, 9:10], absS[:, 32:64], axis=AX.X)
        cV.tag(e.reduce_sum(stt[:, 10:11], absS[:, 64:96], axis=AX.X))
        ck("asums")
        w(e, "wsc_mm1", cT)
        cV.tag(e.tensor_copy(srcp[0:1, 0:3], pb[0][0:1, 0:3]))
        ck("wsc_row")
        w(e, "wsc_mm2", cT)
        e.tensor_scalar(out=wsc[:, 3:6], in0=pb[0][:, 0:3],
                        scalar1=1.0 / (D * INNER), scalar2=1e-5,
                        op0=Alu.mult, op1=Alu.max)
        e.drain()
        e.reciprocal(out=wsc[:, 0:1], in_=wsc[:, 3:4])
        e.reciprocal(out=wsc[:, 1:2], in_=wsc[:, 4:5])
        nw = cV.tag(e.reciprocal(out=wsc[:, 2:3], in_=wsc[:, 5:6]))
        ck("wsc_fin")
        e.wait_ge(cV.h, nw)
        for tt in range(NT):
            e.tensor_scalar(out=gsc[:, tt:tt + 1], in0=sxr[:, tt:tt + 1],
                            scalar1=wsc[:, 3:4], scalar2=None, op0=Alu.mult)
            inst = e.tensor_scalar(out=usc[:, tt:tt + 1], in0=sxr[:, tt:tt + 1],
                                   scalar1=wsc[:, 4:5], scalar2=None,
                                   op0=Alu.mult)
            if tt == NT - 1:
                cV.tag(inst)
        ck("scales")
        for g in range(32):
            oc, n = divmod(g, 4)
            w(e, "t_ro_%d" % g, cT)
            cV.tag(e.tensor_scalar(out=remote[:, oc, 512 * n:512 * n + 512],
                                   in0=pb[g % 8][:, :],
                                   scalar1=routb_t[:, oc:oc + 1],
                                   scalar2=None, op0=Alu.add))
            ck("v_ro_%d" % g)

    def prout_tensor(e, ck, w):
        w(e, "asums", cV)
        cT.tag(e.matmul(pb[0][0:1, 0:3], onesf[:, 0:1], stt[:, 8:11],
                        start=True, stop=True))
        ck("wsc_mm1")
        w(e, "wsc_row", cV)
        cT.tag(e.matmul(pb[0][:, 0:3], onesf[0:1, 1:129], srcp[0:1, 0:3],
                        start=True, stop=True))
        ck("wsc_mm2")
        rw = routw_b[:, :, :].rearrange("p a n -> p (a n)")
        for g in range(32):
            oc, n = divmod(g, 4)
            if g == 0:
                w(e, "wsc_fin", cV)
            if g >= 8:
                w(e, "v_ro_%d" % (g - 8), cV)
            for lc in range(2):
                inst = e.matmul(pb[g % 8][:, :],
                                rw[:, 1024 * lc + 128 * oc:1024 * lc + 128 * oc + 128],
                                latout[:, lc, 512 * n:512 * n + 512],
                                start=(lc == 0), stop=(lc == 1))
                if lc == 1:
                    cT.tag(inst)
            ck("t_ro_%d" % g)

    run_phase("prout0", [(blk.gpsimd, prout_gp)])
    run_phase("prout", [
        (blk.vector, prout_vec), (blk.tensor, prout_tensor),
    ])
    dbg_dump([("dbg_rem", remote)])

    def pg_vec(e, ck, w):
        cV.tag(e.memset(gate_sb[:, :], 0.0))
        ck("gz")
        for c in range(8):
            for n in range(4):
                g = 4 * c + n
                w(e, "t_bc_%d" % g, cT)
                sl = slice(512 * n, 512 * n + 512)
                e.tensor_tensor(out=gdiff[:, :], in0=local_fm[:, c, sl],
                                in1=remote[:, c, sl], op=Alu.subtract)
                e.drain()
                e.scalar_tensor_tensor(out=gtmp[:, :], in0=pb[4 + g % 4][:, :],
                                       scalar=1.0, in1=gdiff[:, :],
                                       op0=Alu.mult, op1=Alu.mult)
                e.drain()
                cV.tag(e.tensor_tensor(out=mixed[:, c, sl], in0=gtmp[:, :],
                                       in1=remote[:, c, sl], op=Alu.add))
                ck("v_mix_%d" % g)

    def pg_scalar(e, ck, w):
        for c in range(8):
            for n in range(4):
                g = 4 * c + n
                w(e, "t_lg_%d" % g, cT)
                if g >= 1:
                    w(e, "t_bc_%d" % (g - 1), cT)
                cS.tag(e.activation(out=gate_sb[0:2, 0:512],
                                    in_=pb[g % 4][0:2, :], func=AFT.Sigmoid))
                ck("s_sig_%d" % g)

    def pg_tensor(e, ck, w):
        w(e, "gz", cV)
        for c in range(8):
            for n in range(4):
                g = 4 * c + n
                sl = slice(512 * n, 512 * n + 512)
                if g >= 4:
                    w(e, "s_sig_%d" % (g - 4), cS)
                e.matmul(pb[g % 4][0:2, :], glgr_b[:, 0:2], local_fm[:, c, sl],
                         start=True, stop=False)
                cT.tag(e.matmul(pb[g % 4][0:2, :], glgr_b[:, 2:4],
                                remote[:, c, sl], start=False, stop=True))
                ck("t_lg_%d" % g)
                w(e, "s_sig_%d" % g, cS)
                if g >= 4:
                    w(e, "v_mix_%d" % (g - 4), cV)
                cT.tag(e.matmul(pb[4 + g % 4][:, :], sel2_b[:, :],
                                gate_sb[:, 0:512], start=True, stop=True))
                ck("t_bc_%d" % g)

    run_phase("pg", [
        (blk.vector, pg_vec), (blk.scalar, pg_scalar), (blk.tensor, pg_tensor),
    ])
    dbg_dump([("dbg_mix", mixed)])

    if PHLIM < 8:
        es.close()
        return nc

    def po_gp0(e, ck, w):
        ow3 = out_w.rearrange("(c p) n -> p c n", p=128)
        dW.tag(e.dma_start(out=outw_b[:, :, :], in_=ow3), 16)
        e.wait_ge(dW.h, dW.n)

    def po_vec(e, ck, w):
        for tt in range(NT):
            w(e, "t_o_%d" % tt, cT)
            if tt >= 2:
                w(e, "g_at_%d" % (tt - 2), dGp[tt % 2])
            b0, b1 = pb[2 * (tt % 2)], pb[2 * (tt % 2) + 1]
            e.tensor_tensor(out=attn_buf[:, tt % 2, 0:512], in0=b0[:, :],
                            in1=outb_t[:, 0:512], op=Alu.add)
            cV.tag(e.tensor_tensor(out=attn_buf[:, tt % 2, 512:1024],
                                   in0=b1[:, :], in1=outb_t[:, 512:1024],
                                   op=Alu.add))
            ck("v_o_%d" % tt)

    def po_tensor(e, ck, w):
        for tt in range(NT):
            if tt >= 2:
                w(e, "v_o_%d" % (tt - 2), cV)
            b0, b1 = pb[2 * (tt % 2)], pb[2 * (tt % 2) + 1]
            for kk in range(8):
                e.matmul(b0[:, :], mixed[:, kk, 128 * tt:128 * tt + 128],
                         outw_b[:, kk, 0:512], start=(kk == 0), stop=(kk == 7))
            for kk in range(8):
                inst = e.matmul(b1[:, :], mixed[:, kk, 128 * tt:128 * tt + 128],
                                outw_b[:, kk, 512:1024],
                                start=(kk == 0), stop=(kk == 7))
                if kk == 7:
                    cT.tag(inst)
            ck("t_o_%d" % tt)

    def po_gp(e, ck, w):
        for tt in range(NT):
            w(e, "v_o_%d" % tt, cV)
            dGp[tt % 2].tag(e.dma_start(out=attn_scr[tt, :, :],
                                        in_=attn_buf[:, tt % 2, :]), 16)
            ck("g_at_%d" % tt)

    run_phase("po0", [(blk.gpsimd, po_gp0)])
    run_phase("po", [
        (blk.vector, po_vec), (blk.tensor, po_tensor), (blk.gpsimd, po_gp),
    ])

    if PHLIM < 9:
        es.close()
        return nc

    def px_pre(e, ck, w):
        ck("d0")

    def px_sync(e, ck, w):
        for tt in range(NT):
            if tt >= 2:
                w(e, "t_x_%d" % (tt - 2), cT)
            dSp[tt % 2].tag(e.dma_start(out=xql[:, tt % 2, :],
                                        in_=xq_scr[tt, :, :]), 16)

    def px_tensor(e, ck, w):
        base = CK.get(("px", "d0"))
        for tt in range(NT):
            if base is not None and not isinstance(e, _StubEng):
                c = dSp[tt % 2]
                e.wait_ge(c.h, base[c.name] + 16 * (tt // 2 + 1))
            if tt >= 2:
                w(e, "v_x_%d" % (tt - 2), cV)
            b0, b1 = pb[2 * (tt % 2)], pb[2 * (tt % 2) + 1]
            for c in range(8):
                bnk = b0 if c < 4 else b1
                inst = e.matmul(bnk[:, 128 * (c % 4):128 * (c % 4) + 128],
                                xql[:, tt % 2, 128 * c:128 * c + 128],
                                identb[:, :], start=True, stop=True)
                if c == 7:
                    cT.tag(inst)
            ck("t_x_%d" % tt)

    def px_vec(e, ck, w):
        for tt in range(NT):
            w(e, "t_x_%d" % tt, cT)
            b0, b1 = pb[2 * (tt % 2)], pb[2 * (tt % 2) + 1]
            e.tensor_copy(XQ[:, 0:4, 128 * tt:128 * tt + 128],
                          b0[:, :].rearrange("p (a n) -> p a n", a=4))
            cV.tag(e.tensor_copy(XQ[:, 4:8, 128 * tt:128 * tt + 128],
                                 b1[:, :].rearrange("p (a n) -> p a n", a=4)))
            ck("v_x_%d" % tt)

    run_phase("px", [
        (blk.sync, px_pre), (blk.sync, px_sync), (blk.tensor, px_tensor),
        (blk.vector, px_vec),
    ])
    dbg_dump([("dbg_xq", XQ)])

    if PHLIM < 10:
        es.close()
        return nc

    def pgu_pre(e, ck, w):
        ck("d0")

    def pgu_sync(e, ck, w):
        wg3 = wgT.rearrange("(c p) n -> p c n", p=128)
        wu3 = wuT.rearrange("(c p) n -> p c n", p=128)
        for nt in range(8):
            for mi, w3 in ((0, wg3), (1, wu3)):
                idx = 2 * nt + mi
                if idx >= 2:
                    w(e, "s_rw_%d" % (idx - 2), cS)
                dSp[idx % 2].tag(e.dma_start(out=wq32[:, idx % 2, :, :],
                                   in_=w3[:, :, 512 * nt:512 * nt + 512]), 16)

    def pgu_scalar(e, ck, w):
        base = CK.get(("pgu", "d0"))

        def s_round(e, idx):
            mi = idx % 2
            if base is not None and not isinstance(e, _StubEng):
                c = dSp[idx % 2]
                e.wait_ge(c.h, base[c.name] + 16 * (idx // 2 + 1))
            if idx >= 1:
                w(e, "v_wq_%d" % (idx - 1), cV)
            cS.tag(e.activation(
                out=wtmp[:, :, :].rearrange("p a n -> p (a n)"),
                in_=wq32[:, idx % 2, :, :].rearrange("p a n -> p (a n)"),
                func=AFT.Copy, scale=wsc[:, mi:mi + 1], bias=C_RND))
            ck("s_rw_%d" % idx)

        s_round(e, 0)
        s_round(e, 1)
        for nt in range(8):
            for tt in range(NT):
                it = NT * nt + tt
                if nt + 1 < 8 and tt == 2:
                    s_round(e, 2 * (nt + 1))
                if nt + 1 < 8 and tt == 8:
                    s_round(e, 2 * (nt + 1) + 1)
                w(e, "t_g_%d" % it, cT)
                if it >= 1:
                    w(e, "v_h_%d" % (it - 1), cV)
                cS.tag(e.activation(out=sg_sb[:, :], in_=pb[tt % 2][:, :],
                                    func=AFT.Silu, scale=gsc[:, tt:tt + 1]))
                ck("s_silu_%d" % it)

    def pgu_vec(e, ck, w):
        def v_clip(e, idx, half):
            nt2, mi = divmod(idx, 2)
            dstW = wgq if mi == 0 else wuq
            sl = slice(4096 * half, 4096 * half + 4096)
            wt_f = wtmp[:, :, :].rearrange("p a n -> p (a n)")
            if half == 0:
                w(e, "s_rw_%d" % idx, cS)
                e.tensor_scalar(out=wt_f[:, sl], in0=wt_f[:, sl],
                                scalar1=C_RND, scalar2=1.0,
                                op0=Alu.subtract, op1=Alu.min)
            else:
                e.tensor_scalar(out=wt_f[:, sl], in0=wt_f[:, sl],
                                scalar1=C_RND, scalar2=1.0,
                                op0=Alu.subtract, op1=Alu.min)
                if nt2 >= 2:
                    w(e, "t_u_%d" % (NT * (nt2 - 2) + NT - 1), cT)
                cV.tag(e.tensor_scalar(
                    out=dstW[:, nt2 % 2, :, 0:256].rearrange("p a n -> p (a n)"),
                    in0=wt_f[:, 0:4096].rearrange(
                        "p (a n) -> p a n", a=8)[:, :, 0:256].rearrange(
                        "p a n -> p a n").rearrange("p a n -> p (a n)"),
                    scalar1=-1.0, scalar2=None, op0=Alu.max))
                cV.tag(e.tensor_scalar(
                    out=dstW[:, nt2 % 2, :, 256:512].rearrange("p a n -> p (a n)"),
                    in0=wt_f[:, 4096:8192].rearrange(
                        "p (a n) -> p a n", a=8)[:, :, 0:256].rearrange(
                        "p a n -> p (a n)"),
                    scalar1=-1.0, scalar2=None, op0=Alu.max))
                ck("v_wq_%d" % idx)

        pass

    def pgu_vec2(e, ck, w):
        wt_f = wtmp[:, :, :].rearrange("p a n -> p (a n)")

        def v_clip1(e, idx):
            w(e, "s_rw_%d" % idx, cS)
            nc1 = cV.tag(e.tensor_scalar(out=wt_f, in0=wt_f,
                                         scalar1=C_RND, scalar2=1.0,
                                         op0=Alu.subtract, op1=Alu.min))
            return nc1

        def v_clip2(e, idx, nc1):
            nt2, mi = divmod(idx, 2)
            dstW = wgq if mi == 0 else wuq
            if nt2 >= 2:
                w(e, "t_u_%d" % (NT * (nt2 - 2) + NT - 1), cT)
            e.wait_ge(cV.h, nc1)
            cV.tag(e.tensor_scalar(
                out=dstW[:, nt2 % 2, :, :].rearrange("p a n -> p (a n)"),
                in0=wt_f, scalar1=-1.0, scalar2=None, op0=Alu.max))
            ck("v_wq_%d" % idx)

        nc1 = v_clip1(e, 0)
        v_clip2(e, 0, nc1)
        nc1 = v_clip1(e, 1)
        v_clip2(e, 1, nc1)
        for nt in range(8):
            pend = None
            for tt in range(NT):
                it = NT * nt + tt
                if nt + 1 < 8:
                    if tt == 4:
                        pend = (2 * (nt + 1), v_clip1(e, 2 * (nt + 1)))
                    elif tt == 6 and pend is not None:
                        v_clip2(e, pend[0], pend[1])
                        pend = None
                    elif tt == 10:
                        pend = (2 * (nt + 1) + 1, v_clip1(e, 2 * (nt + 1) + 1))
                    elif tt == 12 and pend is not None:
                        v_clip2(e, pend[0], pend[1])
                        pend = None
                w(e, "s_silu_%d" % it, cS)
                w(e, "t_u_%d" % it, cT)
                g4 = it // 4
                if g4 >= 2 and it % 4 == 0:
                    w(e, "g_h_%d" % (g4 - 2), dGp[g4 % 2])
                nh = cV.tag(e.scalar_tensor_tensor(
                    out=hwb[:, g4 % 2, it % 4, :], in0=pb[2 + tt % 2][:, :],
                    scalar=usc[:, tt:tt + 1], in1=sg_sb[:, :],
                    op0=Alu.mult, op1=Alu.mult))
                e.wait_ge(cV.h, nh)
                cV.tag(e.reduce_max(hmax[:, 8 * tt + nt:8 * tt + nt + 1],
                                    hwb[:, g4 % 2, it % 4, :], axis=AX.X,
                                    apply_absolute_value=True))
                ck("v_h_%d" % it)

    def pgu_tensor(e, ck, w):
        for nt in range(8):
            w(e, "v_wq_%d" % (2 * nt + 1), cV)
            for tt in range(NT):
                it = NT * nt + tt
                if it >= 2:
                    w(e, "s_silu_%d" % (it - 2), cS)
                    w(e, "v_h_%d" % (it - 2), cV)
                bg = pb[tt % 2]
                bu = pb[2 + tt % 2]
                for kk in range(8):
                    inst = e.matmul(bg[:, :], XQ[:, kk, 128 * tt:128 * tt + 128],
                                    wgq[:, nt % 2, kk, :],
                                    start=(kk == 0), stop=(kk == 7))
                    if kk == 7:
                        cT.tag(inst)
                        ck("t_g_%d" % it)
                    inst = e.matmul(bu[:, :], XQ[:, kk, 128 * tt:128 * tt + 128],
                                    wuq[:, nt % 2, kk, :],
                                    start=(kk == 0), stop=(kk == 7))
                    if kk == 7:
                        cT.tag(inst)
                        ck("t_u_%d" % it)

    def pgu_gp(e, ck, w):
        for nt in range(8):
            for g4l in range(4):
                g4 = 4 * nt + g4l
                w(e, "v_h_%d" % (NT * nt + 4 * g4l + 3), cV)
                dGp[g4 % 2].tag(e.dma_start(
                    out=h_scr[4 * g4l:4 * g4l + 4, :, nt, :].rearrange(
                        "a p n -> p a n"),
                    in_=hwb[:, g4 % 2, :, :]), 16)
                ck("g_h_%d" % g4)

    run_phase("pgu", [
        (blk.sync, pgu_pre), (blk.sync, pgu_sync), (blk.scalar, pgu_scalar),
        (blk.vector, pgu_vec2), (blk.tensor, pgu_tensor), (blk.gpsimd, pgu_gp),
    ])

    if PHLIM < 11:
        es.close()
        return nc

    def pwd_pre(e, ck, w):
        ck("d0")

    def pwd_sync(e, ck, w):
        wd3 = wdT.rearrange("(c p) n -> c p n", p=128)
        for c in range(32):
            if c >= 2:
                w(e, "s_rd_%d" % (c - 2), cS)
            dSp[c % 2].tag(e.dma_start(out=wdf[:, c % 2, :], in_=wd3[c, :, :]),
                           16)

    def pwd_scalar(e, ck, w):
        base = CK.get(("pwd", "d0"))
        for c in range(32):
            if base is not None and not isinstance(e, _StubEng):
                ctr = dSp[c % 2]
                e.wait_ge(ctr.h, base[ctr.name] + 16 * (c // 2 + 1))
            if c >= 1:
                w(e, "v_wd1_%d" % (c - 1), cV)
            cS.tag(e.activation(out=wdt_r[:, c % 2, :], in_=wdf[:, c % 2, :],
                                func=AFT.Copy, scale=wsc[:, 2:3], bias=C_RND))
            ck("s_rd_%d" % c)

    def pwd_vec(e, ck, w):
        for c in range(32):
            w(e, "s_rd_%d" % c, cS)
            cV.tag(e.tensor_scalar(out=wdt_r[:, c % 2, :], in0=wdt_r[:, c % 2, :],
                                   scalar1=C_RND, scalar2=1.0,
                                   op0=Alu.subtract, op1=Alu.min))
            ck("v_wd1_%d" % c)
            e.drain()
            cV.tag(e.tensor_scalar(out=wdq2[:, c, :], in0=wdt_r[:, c % 2, :],
                                   scalar1=-1.0, scalar2=None, op0=Alu.max))
            ck("v_wd2_%d" % c)

    run_phase("pwd", [
        (blk.sync, pwd_pre), (blk.sync, pwd_sync), (blk.scalar, pwd_scalar),
        (blk.vector, pwd_vec),
    ])

    def pd_pre(e, ck, w):
        ck("d0")

    def pd_sync(e, ck, w):
        for tt in range(NT):
            if tt >= 2:
                w(e, "v_fin_%d" % (tt - 2), cV)
            dSp[tt % 2].tag(e.dma_start(
                out=hrb[:, tt % 2, :],
                in_=h_scr[tt, :, :, :].rearrange("p a n -> p (a n)")), 16)
            dSp[tt % 2].tag(e.dma_start(out=atl[:, tt % 2, :],
                                        in_=attn_scr[tt, :, :]), 16)
            dSp[tt % 2].tag(e.dma_start(out=xres[:, tt % 2, :],
                                        in_=x_own[128 * tt:128 * tt + 128, :]), 16)

    def pd_vec(e, ck, w):
        base = CK.get(("pd", "d0"))
        for tt in range(NT + 1):
            if tt < NT:
                if base is not None and not isinstance(e, _StubEng):
                    c = dSp[tt % 2]
                    e.wait_ge(c.h, base[c.name] + 48 * (tt // 2 + 1))
                if tt >= 1:
                    w(e, "s_rh_%d" % (tt - 1), cS)
                n0 = cV.tag(e.reduce_max(am[:, :], hmax[:, 8 * tt:8 * tt + 8],
                                         axis=AX.X))
                e.wait_ge(cV.h, n0)
                n1 = cV.tag(e.tensor_scalar(out=t1[:, :], in0=am[:, :],
                                            scalar1=1e-5, scalar2=None,
                                            op0=Alu.max))
                e.wait_ge(cV.h, n1)
                n2 = cV.tag(e.reciprocal(out=t3[:, :], in_=t1[:, :]))
                e.wait_ge(cV.h, n2)
                cV.tag(e.tensor_scalar(out=sh_t[:, tt % 2:tt % 2 + 1],
                                       in0=t3[:, :], scalar1=127.0,
                                       scalar2=None, op0=Alu.mult))
                ck("v_sh_%d" % tt)
                cV.tag(e.tensor_scalar(out=dsc_t[:, tt % 2:tt % 2 + 1],
                                       in0=t1[:, :], scalar1=wsc[:, 5:6],
                                       scalar2=1.0 / 127.0,
                                       op0=Alu.mult, op1=Alu.mult))
                ck("v_dsc_%d" % tt)
                w(e, "s_rh_%d" % tt, cS)
                if tt >= 2:
                    w(e, "t_tr_%d_%d" % (tt - 2, 7), cT)
                cV.tag(e.tensor_scalar(out=xh_tm2[:, tt % 2, :], in0=tq_sb[:, :],
                                       scalar1=C_RND, scalar2=None,
                                       op0=Alu.subtract))
                ck("v_xh_%d" % tt)
                for rg in range(8):
                    w(e, "t_tr_%d_%d" % (tt, rg), cT)
                    bnk = pb[4 + rg % 4]
                    cV.tag(e.tensor_copy(
                        xh_fm[:, 4 * rg:4 * rg + 4, :].rearrange(
                            "p a n -> p (a n)"),
                        bnk[:, :]))
                    ck("v_ev_%d_%d" % (tt, rg))
            if tt >= 1:
                pt_ = tt - 1
                pp = pt_ % 2
                w(e, "t_dn_%d" % pt_, cT)
                if pt_ >= 2:
                    w(e, "g_y_%d" % (pt_ - 2), dGp[pt_ % 2])
                bd0, bd1 = pb[pt_ % 2], pb[2 + pt_ % 2]
                e.scalar_tensor_tensor(out=obuf[:, pp, 0:512], in0=bd0[:, :],
                                       scalar=dsc_t[:, pp:pp + 1],
                                       in1=atl[:, pp, 0:512],
                                       op0=Alu.mult, op1=Alu.add)
                nst = cV.tag(e.scalar_tensor_tensor(
                    out=obuf[:, pp, 512:1024], in0=bd1[:, :],
                    scalar=dsc_t[:, pp:pp + 1],
                    in1=atl[:, pp, 512:1024],
                    op0=Alu.mult, op1=Alu.add))
                e.wait_ge(cV.h, nst)
                e.tensor_tensor(out=obuf[:, pp, 0:512], in0=obuf[:, pp, 0:512],
                                in1=xres[:, pp, 0:512], op=Alu.add)
                cV.tag(e.tensor_tensor(out=obuf[:, pp, 512:1024],
                                       in0=obuf[:, pp, 512:1024],
                                       in1=xres[:, pp, 512:1024], op=Alu.add))
                ck("v_fin_%d" % pt_)

    def pd_scalar(e, ck, w):
        for tt in range(NT):
            w(e, "v_sh_%d" % tt, cV)
            if tt >= 1:
                w(e, "v_xh_%d" % (tt - 1), cV)
            cS.tag(e.activation(out=tq_sb[:, :], in_=hrb[:, tt % 2, :],
                                func=AFT.Copy, scale=sh_t[:, tt % 2:tt % 2 + 1],
                                bias=C_RND))
            ck("s_rh_%d" % tt)

    def pd_tensor(e, ck, w):
        for tt in range(NT):
            w(e, "v_xh_%d" % tt, cV)
            for rg in range(8):
                if rg >= 4:
                    w(e, "v_ev_%d_%d" % (tt, rg - 4), cV)
                elif tt >= 1:
                    w(e, "v_ev_%d_%d" % (tt - 1, rg + 4), cV)
                bnk = pb[4 + rg % 4]
                for u in range(4):
                    kc = 4 * rg + u
                    inst = e.matmul(bnk[:, 128 * u:128 * u + 128],
                                    xh_tm2[:, tt % 2, 128 * kc:128 * kc + 128],
                                    identb[:, :], start=True, stop=True)
                    if u == 3:
                        cT.tag(inst)
                ck("t_tr_%d_%d" % (tt, rg))
            w(e, "v_ev_%d_%d" % (tt, 7), cV)
            if tt >= 2:
                w(e, "v_fin_%d" % (tt - 2), cV)
            bd0, bd1 = pb[tt % 2], pb[2 + tt % 2]
            for kk in range(32):
                e.matmul(bd0[:, :], xh_fm[:, kk, :], wdq2[:, kk, 0:512],
                         start=(kk == 0), stop=(kk == 31))
                inst = e.matmul(bd1[:, :], xh_fm[:, kk, :], wdq2[:, kk, 512:1024],
                                start=(kk == 0), stop=(kk == 31))
                if kk == 31:
                    cT.tag(inst)
            ck("t_dn_%d" % tt)

    def pd_gp(e, ck, w):
        for tt in range(NT):
            w(e, "v_fin_%d" % tt, cV)
            dGp[tt % 2].tag(e.dma_start(out=y[128 * tt:128 * tt + 128, :],
                                        in_=obuf[:, tt % 2, :]), 16)
            ck("g_y_%d" % tt)
        e.wait_ge(dG.h, dG.n)
        e.wait_ge(dGb.h, dGb.n)

    run_phase("pd", [
        (blk.sync, pd_pre), (blk.sync, pd_sync), (blk.vector, pd_vec),
        (blk.scalar, pd_scalar), (blk.tensor, pd_tensor), (blk.gpsimd, pd_gp),
    ])

    es.close()
    return nc


def _masks_for_core(h):
    r = np.arange(128)
    c = np.arange(256)
    band = (c[None, :] >= r[:, None]) & (c[None, :] <= r[:, None] + 127)
    gen = np.where(band, 0.0, NEG).astype(np.float32)
    if h == 0:
        first = np.where(band & (c[None, :] >= 128), 0.0, NEG).astype(np.float32)
        lb = np.ones((128, 1), np.float32)
        lb[0, 0] = 0.0
        latb = np.ones((128, 1), np.float32)
        latb[0:64, 0] = 0.0
    else:
        first = gen.copy()
        lb = np.ones((128, 1), np.float32)
        latb = np.ones((128, 1), np.float32)
    lm = np.stack([np.concatenate([first, first], axis=1),
                   np.concatenate([gen, gen], axis=1)], axis=0)

    r0 = 2048 * h
    own_c0 = r0 // 64
    oth_c0 = (2048 - r0) // 64
    slot_chunk = np.concatenate([own_c0 + np.arange(32), oth_c0 + np.arange(32)])
    latm = np.empty((16, 128, 64), np.float32)
    for i in range(16):
        qchunk = (r0 + 128 * i + np.arange(128)) // 64
        allowed = slot_chunk[None, :] < qchunk[:, None]
        latm[i] = np.where(allowed, 0.0, NEG).astype(np.float32)
    return lm, latm, lb, latb


def _build_inputs(inputs):
    f = np.float32
    x = np.asarray(inputs["L0_attn_x"], f)
    qkv_w = np.ascontiguousarray(np.asarray(inputs["L0_attn_qkv_w"], f))
    qkv_b = np.asarray(inputs["L0_attn_qkv_b"], f)
    rq_w = np.ascontiguousarray(np.asarray(inputs["L0_attn_rq_w"], f))
    rq_b = np.asarray(inputs["L0_attn_rq_b"], f)
    rkv_w = np.ascontiguousarray(np.asarray(inputs["L0_attn_rkv_w"], f))
    rkv_b = np.asarray(inputs["L0_attn_rkv_b"], f)
    rout_w = np.ascontiguousarray(np.asarray(inputs["L0_attn_rout_w"], f))
    rout_b = np.asarray(inputs["L0_attn_rout_b"], f)
    out_w = np.ascontiguousarray(np.asarray(inputs["L0_attn_out_w"], f))
    out_b = np.asarray(inputs["L0_attn_out_b"], f)
    gl_w = np.asarray(inputs["L0_attn_gl_w"], f)
    gr_w = np.asarray(inputs["L0_attn_gr_w"], f)
    w_gate = np.asarray(inputs["L0_ffn_w_gate"], f)
    w_up = np.asarray(inputs["L0_ffn_w_up"], f)
    w_down = np.asarray(inputs["L0_ffn_w_down"], f)

    wgT = np.ascontiguousarray(w_gate.T)
    wuT = np.ascontiguousarray(w_up.T)
    wdT = np.ascontiguousarray(w_down.T)

    qkb_fm = np.ascontiguousarray(qkv_b[:2048].reshape(16, 128).T)
    vb_bc = np.ascontiguousarray(np.broadcast_to(qkv_b[2048:], (128, 1024)))
    rqb_fm = np.ascontiguousarray(rq_b.reshape(2, 128).T)
    rkvb_bc = np.ascontiguousarray(np.broadcast_to(rkv_b, (128, 512)))
    routb_fm = np.ascontiguousarray(rout_b.reshape(8, 128).T)
    outb_bc = np.ascontiguousarray(np.broadcast_to(out_b, (128, 1024)))

    glgr = np.zeros((128, 4), f)
    glgr[0:64, 0] = gl_w[:, 0]
    glgr[64:128, 1] = gl_w[:, 0]
    glgr[0:64, 2] = gr_w[:, 0]
    glgr[64:128, 3] = gr_w[:, 0]
    sel2 = np.zeros((128, 128), f)
    sel2[0, 0:64] = 1.0
    sel2[1, 64:128] = 1.0

    identb = np.eye(128, dtype=f).astype(ml_dtypes.bfloat16)
    poolz = np.zeros((128, 196), f)
    poolz[0:64, 64] = 1.0
    poolz[64:128, 65] = 1.0
    poolz = poolz.astype(ml_dtypes.bfloat16)

    shared = dict(qkv_w=qkv_w, rq_w=rq_w, rkv_w=rkv_w, rout_w=rout_w,
                  out_w=out_w, wgT=wgT, wuT=wuT, wdT=wdT,
                  qkb_fm=qkb_fm, vb_bc=vb_bc, rqb_fm=rqb_fm, rkvb_bc=rkvb_bc,
                  routb_fm=routb_fm, outb_bc=outb_bc,
                  glgr_in=glgr, sel2_in=sel2, identb_in=identb,
                  poolz_in=poolz)

    in_maps = []
    for core in range(8):
        b, h = core // 2, core % 2
        r0 = 2048 * h
        lm, latm, lb, latb = _masks_for_core(h)
        m = dict(shared)
        m["x_own"] = np.ascontiguousarray(x[b, r0:r0 + 2048])
        if h == 0:
            m["x_halo"] = np.zeros((128, 1024), f)
        else:
            m["x_halo"] = np.ascontiguousarray(x[b, r0 - 128:r0])
        m["x_other"] = np.ascontiguousarray(x[b, 2048 - r0:4096 - r0])
        m["lmask"] = lm
        m["latmask"] = latm
        m["lbin"] = lb
        m["latbin"] = latb
        in_maps.append(m)
    return in_maps


_nc_cache = [None]


def kernel(**inputs):
    in_maps = _build_inputs(inputs)
    if _nc_cache[0] is None:
        _nc_cache[0] = build_nc()
    nc = _nc_cache[0]
    res = run_bass_kernel_spmd(nc, in_maps, core_ids=list(range(8)))
    out = np.empty((B, S, D), np.float32)
    for core in range(8):
        b, h = core // 2, core % 2
        out[b, 2048 * h:2048 * h + 2048] = res.results[core]["y"]
    return out


# version 20 (final)
# speedup vs baseline: 1.1519x; 1.0027x over previous
import numpy as np
import ml_dtypes
from contextlib import ExitStack

import concourse.bass as bass
import concourse.mybir as mybir
from concourse.bass_utils import run_bass_kernel_spmd
from concourse.alu_op_type import AluOpType as Alu

F32 = mybir.dt.float32
BF16 = mybir.dt.bfloat16
AFT = mybir.ActivationFunctionType
AX = mybir.AxisListType

D = 1024
S = 4096
B = 4
NH = 16
HD = 64
WIN = 128
LD = 256
LH = 4
NLAT = 64
INNER = 4096
T = 2048
NT = 16
NEG = -1.0e9
C_RND = 12582912.0
EPS = 1e-5

import os
PHLIM = int(os.environ.get("KPHLIM", "99"))
DBG = os.environ.get("KDBG", "0") == "1"

ARENA0 = 23552
ARENA_CAP = 229376


class Ctr:
    def __init__(self, es, nc, name):
        self.h = es.enter_context(nc.semaphore(name))
        self.n = 0
        self.name = name

    def tag(self, inst, k=1):
        if k:
            if not isinstance(inst, _StubInst) and inst is not None:
                inst.then_inc(self.h, k)
            self.n += k
        return self.n


class _StubInst:
    def then_inc(self, sem, k):
        return self


class _StubEng:
    def __getattr__(self, name):
        if name == "wait_ge":
            return lambda *a, **kw: None
        return lambda *a, **kw: _StubInst()


def build_nc():
    nc = bass.Bass("TRN2", target_bir_lowering=False)

    def din(name, shape, dt=F32):
        return nc.dram_tensor(name, list(shape), dt, kind="ExternalInput").ap()

    x_own = din("x_own", [T, D])
    x_halo = din("x_halo", [WIN, D])
    x_other = din("x_other", [T, D])
    qkv_w = din("qkv_w", [D, 3 * D])
    rq_w = din("rq_w", [D, LD])
    rkv_w = din("rkv_w", [D, 2 * LD])
    rout_w = din("rout_w", [LD, D])
    out_w = din("out_w", [D, D])
    wgT = din("wgT", [D, INNER])
    wuT = din("wuT", [D, INNER])
    wdT = din("wdT", [INNER, D])
    qkb_fm = din("qkb_fm", [128, 16])
    vb_bc = din("vb_bc", [128, D])
    rqb_fm = din("rqb_fm", [128, 2])
    rkvb_bc = din("rkvb_bc", [128, 2 * LD])
    routb_fm = din("routb_fm", [128, 8])
    outb_bc = din("outb_bc", [128, D])
    glgr_in = din("glgr_in", [128, 4])
    sel2_in = din("sel2_in", [128, 128])
    identb_in = din("identb_in", [128, 128], BF16)
    poolz_in = din("poolz_in", [128, 196], BF16)
    lmask = din("lmask", [2, 128, 512])
    latmask = din("latmask", [16, 128, 64])
    lbin = din("lbin", [128, 1])
    latbin = din("latbin", [128, 1])

    y = nc.dram_tensor("y", [T, D], F32, kind="ExternalOutput").ap()
    xq_scr = nc.dram_tensor("xq_scr", [NT, 128, D], BF16, kind="Internal").ap()
    attn_scr = nc.dram_tensor("attn_scr", [NT, 128, D], F32, kind="Internal").ap()
    h_scr = nc.dram_tensor("h_scr", [NT, 128, 8, 512], F32, kind="Internal").ap()

    dbg = {}
    if DBG:
        for nm, shape in [
            ("dbg_xn", [128, 8 * 2176]), ("dbg_sxr", [128, 16]),
            ("dbg_wsc", [128, 8]), ("dbg_rq", [128, 2 * 2048]),
            ("dbg_rk", [128, 2 * 64]), ("dbg_rkv1", [128, 512]),
            ("dbg_rkv2", [128, 512]), ("dbg_q", [128, 8 * 2048]),
            ("dbg_k", [128, 8 * 2176]), ("dbg_v", [128, 17 * 1024]),
            ("dbg_lat", [128, 2 * 2048]), ("dbg_loc", [128, 8 * 2048]),
            ("dbg_rem", [128, 8 * 2048]), ("dbg_mix", [128, 8 * 2048]),
            ("dbg_xq", [128, 8 * 2048]), ("dbg_pooled", [128, 8 * 192]),
        ]:
            dbg[nm] = nc.dram_tensor(nm, shape, F32, kind="ExternalOutput").ap()

    cursor = [0]

    def AT(name, shape, dt, off=None):
        nbytes = int(np.prod(shape[1:])) * (2 if dt == BF16 else 4)
        if off is None:
            off = ((cursor[0] + 63) // 64) * 64
        end = off + nbytes
        cursor[0] = max(cursor[0], end)
        assert end <= ARENA_CAP - ARENA0, (name, off, nbytes, end)
        t = nc.alloc_sbuf_tensor_at(name, list(shape), dt, offset=ARENA0 + off)
        return t.ap()

    identb = AT("identb", [128, 128], BF16)
    poolz = AT("poolz", [128, 196], BF16)
    lmask_t = AT("lmask_t", [128, 2, 512], F32)
    latmask_t = AT("latmask_t", [128, 16, 64], F32)
    lbin_t = AT("lbin_t", [128, 1], F32)
    latbin_t = AT("latbin_t", [128, 1], F32)
    qkb_t = AT("qkb_t", [128, 16], F32)
    vb_t = AT("vb_t", [128, D], F32)
    rqb_t = AT("rqb_t", [128, 2], F32)
    rkvb_t = AT("rkvb_t", [128, 2 * LD], F32)
    routb_t = AT("routb_t", [128, 8], F32)
    outb_t = AT("outb_t", [128, D], F32)
    glgr_f = AT("glgr_f", [128, 4], F32)
    glgr_b = AT("glgr_b", [128, 4], BF16)
    sel2_f = AT("sel2_f", [128, 128], F32)
    sel2_b = AT("sel2_b", [128, 128], BF16)
    onesf = AT("onesf", [128, 130], F32)
    sxr = AT("sxr", [128, NT], F32)
    gsc = AT("gsc", [128, NT], F32)
    usc = AT("usc", [128, NT], F32)
    wsc = AT("wsc", [128, 8], F32)
    absS = AT("absS", [128, 96], F32)
    stt = AT("stt", [128, 16], F32)
    mv = AT("mv", [128, 1], F32)
    t1 = AT("t1", [128, 1], F32)
    t2 = AT("t2", [128, 1], F32)
    t3 = AT("t3", [128, 1], F32)
    t4 = AT("t4", [128, 1], F32)
    rr = AT("rr", [128, 1], F32)
    r0 = AT("r0", [128, 1], F32)
    s_t = AT("s_t", [128, 2], F32)
    am = AT("am", [128, 1], F32)
    smax = AT("smax", [128, 8], F32)
    ssum = AT("ssum", [128, 8], F32)
    srcp = AT("srcp", [128, 8], F32)
    sh_t = AT("sh_t", [128, 2], F32)
    hmax = AT("hmax", [128, 128], F32)
    dsc_t = AT("dsc_t", [128, 2], F32)
    PERS_END = cursor[0]
    BIG0 = ((PERS_END + 63) // 64) * 64
    assert BIG0 <= 27648, BIG0

    XN = AT("XN", [128, 8, 2176], BF16, BIG0)
    Q0 = BIG0 + 34816
    q_fm = AT("q_fm", [128, 8, T], BF16, Q0)
    k_fm = AT("k_fm", [128, 8, 2176], BF16, BIG0 + 67584)
    V0 = BIG0 + 102400
    v_tm = AT("v_tm", [128, 17, D], BF16, V0)
    R0 = BIG0 + 137216
    rq_fm = AT("rq_fm", [128, 2, T], BF16, R0)
    xbuf = AT("xbuf", [128, 2, D], F32, Q0)
    xnf = AT("xnf", [128, D], F32, Q0 + 8192)
    xnb = AT("xnb", [128, 2, D], BF16, Q0 + 12288)
    xqb = AT("xqb", [128, 2, D], BF16, Q0 + 16384)
    qtmp = AT("qtmp", [128, D], F32, Q0 + 20480)
    sqd = AT("sqd", [128, 2, D], F32, Q0 + 24576)
    outw_b = AT("outw_b", [128, 8, D], BF16, V0)
    attn_buf = AT("attn_buf", [128, 2, D], F32, V0 + 16384)
    gate_sb = AT("gate_sb", [128, 512], BF16, R0)
    gdiff = AT("gdiff", [128, 512], BF16, R0 + 1024)
    gtmp = AT("gtmp", [128, 512], BF16, R0 + 2048)
    TR0 = BIG0 + 145408
    wqk_b = AT("wqk_b", [128, 2, 8, 128], BF16, TR0)
    wv_b = AT("wv_b", [128, 2, 8, 256], BF16, TR0 + 4096)
    rkvw_b = AT("rkvw_b", [128, 8, 512], BF16, TR0 + 4096)
    routw_b = AT("routw_b", [128, 2, D], BF16, TR0 + 4096)
    rkv1 = AT("rkv1", [128, 2 * LD], BF16, TR0 + 12288)
    rkv2 = AT("rkv2", [128, 2 * LD], BF16, TR0 + 13312)
    rk_fm = AT("rk_fm", [128, 2, NLAT], BF16, TR0 + 14336)
    ltmp = AT("ltmp", [128, 512], F32, TR0 + 19776)
    qz = AT("qz", [128, 2, T], BF16, TR0 + 16704)
    pooled_pad = AT("pooled_pad", [128, 8, 192], BF16, TR0 + 16704)
    pe_sb = AT("pe_sb", [128, 4, 512], BF16, TR0 + 24896)
    pt_sb = AT("pt_sb", [128, 2, 512], BF16, TR0 + 14656)
    latout = AT("latout", [128, 2, T], BF16, TR0 + 28992)
    absb = AT("absb", [128, 2, 1024], F32, TR0 + 4096)
    absd = AT("absd", [128, 2, 1024], BF16, TR0)
    local_fm = AT("local_fm", [128, 8, T], BF16, BIG0)
    remote = AT("remote", [128, 8, T], BF16, Q0)
    mixed = AT("mixed", [128, 8, T], BF16, BIG0 + 67584)

    XQ = AT("XQ", [128, 8, T], BF16, BIG0)
    xql = AT("xql", [128, 2, D], BF16, BIG0 + 32768)
    wgq = AT("wgq", [128, 2, 8, 512], BF16, BIG0 + 36864)
    wuq = AT("wuq", [128, 2, 8, 512], BF16, BIG0 + 53248)
    sg_sb = AT("sg_sb", [128, 512], F32, BIG0 + 69632)
    hwb = AT("hwb", [128, 2, 4, 512], F32, BIG0 + 71680)
    wq32 = AT("wq32", [128, 2, 8, 512], F32, BIG0 + 88064)
    wtmp = AT("wtmp", [128, 8, 512], F32, BIG0 + 120832)
    wdq2 = AT("wdq2", [128, 32, D], BF16, BIG0 + 36864)
    hrb = AT("hrb", [128, 2, INNER], F32, BIG0)
    tq_sb = AT("tq_sb", [128, INNER], F32, BIG0 + 102400)
    xh_tm = AT("xh_tm", [128, INNER], BF16, BIG0 + 118784)
    xh_fm = AT("xh_fm", [128, 32, 128], BF16, BIG0 + 126976)
    atl = AT("atl", [128, 2, D], F32, BIG0 + 135168)
    xres = AT("xres", [128, 2, D], F32, BIG0 + 143360)
    obuf = AT("obuf", [128, 2, D], F32, BIG0 + 151552)
    wdf = AT("wdf", [128, 2, D], F32, BIG0 + 159744)
    wdt_r = AT("wdt_r", [128, 2, D], F32, BIG0 + 167936)
    xh_tm2 = AT("xh_tm2", [128, 2, INNER], BF16, BIG0 + 159744)
    assert BIG0 + 180544 <= 205824, BIG0

    pb = [nc.alloc_psum_tensor("pb%d" % i, [128, 512], F32).ap() for i in range(8)]

    es = ExitStack()
    dSp = [Ctr(es, nc, "dSa"), Ctr(es, nc, "dSb")]
    dGp = [Ctr(es, nc, "dGa"), Ctr(es, nc, "dGb")]
    dWp = [Ctr(es, nc, "dWa"), Ctr(es, nc, "dWb")]
    dAp = [Ctr(es, nc, "dAa"), Ctr(es, nc, "dAb")]
    dS, dSb = dSp
    dG, dGb = dGp
    dW, dWb = dWp
    cV = Ctr(es, nc, "cV")
    cS = Ctr(es, nc, "cS")
    cT = Ctr(es, nc, "cT")
    cG = Ctr(es, nc, "cG")
    ALLC = dSp + dGp + dWp + dAp + [cV, cS, cT, cG]

    blk = es.enter_context(nc.Block())

    def full_barrier():
        sn = [(c, c.n) for c in ALLC]

        def w_all(e):
            for c, v in sn:
                if v > 0:
                    e.wait_ge(c.h, v)
        blk.sync(w_all)
        blk.vector(w_all)
        blk.scalar(w_all)
        blk.tensor(w_all)
        blk.gpsimd(w_all)

    CK = {}

    def run_phase(name, sections):
        base = {c.name: c.n for c in ALLC}
        mode = ["count"]

        def ck(label):
            if mode[0] == "count":
                CK[(name, label)] = {c.name: c.n for c in ALLC}

        def w(e, label, ctr):
            if mode[0] == "count":
                return
            key = (name, label)
            if key in CK:
                v = CK[key][ctr.name]
                if v > 0:
                    e.wait_ge(ctr.h, v)

        stub = _StubEng()
        for _, gen in sections:
            gen(stub, ck, w)
        after = {c.name: c.n for c in ALLC}
        for c in ALLC:
            c.n = base[c.name]
        mode[0] = "real"
        for meth, gen in sections:
            meth(lambda e, g=gen: g(e, ck, w))
        for c in ALLC:
            assert c.n == after[c.name], (name, c.name, c.n, after[c.name])
        full_barrier()

    def dbg_dump(pairs):
        if not DBG:
            return

        def g(e):
            for nm, src in pairs:
                if nm in dbg:
                    sflat = src if len(src.shape) == 2 else src.rearrange(
                        "p a n -> p (a n)")
                    dG.tag(e.dma_start(out=dbg[nm], in_=sflat), 16)
            e.wait_ge(dG.h, dG.n)
        blk.gpsimd(g)
        full_barrier()

    def p0_sync(e, ck, w):
        p0i = [0]
        for dst, src in [
            (identb[:, :], identb_in[:, :]),
            (poolz[:, :], poolz_in[:, :]),
            (lmask_t[:, :, :], lmask[:, :, :].rearrange("a p n -> p a n")),
            (latmask_t[:, :, :], latmask[:, :, :].rearrange("a p n -> p a n")),
            (lbin_t[:, :], lbin[:, :]), (latbin_t[:, :], latbin[:, :]),
            (qkb_t[:, :], qkb_fm[:, :]), (vb_t[:, :], vb_bc[:, :]),
            (rqb_t[:, :], rqb_fm[:, :]), (rkvb_t[:, :], rkvb_bc[:, :]),
            (routb_t[:, :], routb_fm[:, :]), (outb_t[:, :], outb_bc[:, :]),
            (glgr_f[:, :], glgr_in[:, :]), (sel2_f[:, :], sel2_in[:, :]),
        ]:
            dSp[p0i[0] % 2].tag(e.dma_start(out=dst, in_=src), 16)
            p0i[0] += 1
        ck("loads")

    def p0_vec(e, ck, w):
        w(e, "loads", dS)
        w(e, "loads", dSb)
        e.tensor_copy(glgr_b[:, :], glgr_f[:, :])
        e.tensor_copy(sel2_b[:, :], sel2_f[:, :])
        e.memset(onesf[:, 0:1], 1.0)
        e.memset(onesf[0:1, 1:130], 1.0)
        cV.tag(e.memset(wsc[:, :], 0.0))

    run_phase("p0", [(blk.sync, p0_sync), (blk.vector, p0_vec)])

    if PHLIM < 1:
        es.close()
        return nc

    if PHLIM < 2:
        es.close()
        return nc

    tiles = [("halo", 0)] + [("own", i) for i in range(NT)] + \
            [("other", i) for i in range(NT)]
    n_pool = 2 * NT

    def a_src(kind, i):
        if kind == "halo":
            return x_halo[:, :]
        if kind == "own":
            return x_own[128 * i:128 * i + 128, :]
        return x_other[128 * i:128 * i + 128, :]

    def pa_pre(e, ck, w):
        ck("d0")

    def pa_sync(e, ck, w):
        for j, (kind, i) in enumerate(tiles):
            if j >= 2:
                w(e, "v_xnb_%d" % (j - 2), cV)
                w(e, "s_sq_%d" % (j - 2), cS)
            dSp[j % 2].tag(e.dma_start(out=xbuf[:, j % 2, :],
                                       in_=a_src(kind, i)), 16)

    def pa_vec(e, ck, w):
        base = CK.get(("pa", "d0"))
        last_own = None
        for j, (kind, i) in enumerate(tiles):
            if base is not None and not isinstance(e, _StubEng):
                c = dSp[j % 2]
                e.wait_ge(c.h, base[c.name] + 16 * (j // 2 + 1))
            e.reduce_sum(stt[:, 0:1], xbuf[:, j % 2, :], axis=AX.X)
            w(e, "s_sq_%d" % j, cS)
            e.drain()
            e.tensor_scalar(out=mv[:, :], in0=stt[:, 0:1], scalar1=1.0 / D,
                            scalar2=None, op0=Alu.mult)
            e.tensor_scalar(out=t1[:, :], in0=stt[:, 1:2], scalar1=1.0 / D,
                            scalar2=EPS, op0=Alu.mult, op1=Alu.add)
            e.drain()
            e.tensor_scalar(out=t2[:, :], in0=mv[:, :], scalar1=mv[:, :],
                            scalar2=None, op0=Alu.mult)
            e.drain()
            cV.tag(e.tensor_tensor(out=t1[:, :], in0=t1[:, :], in1=t2[:, :],
                                   op=Alu.subtract))
            ck("v_var_%d" % j)
            w(e, "s_sqrt_%d" % j, cS)
            cV.tag(e.reciprocal(out=r0[:, :], in_=s_t[:, 0:1]))
            ck("v_r0_%d" % j)
            e.drain()
            e.tensor_scalar(out=t3[:, :], in0=r0[:, :], scalar1=r0[:, :],
                            scalar2=None, op0=Alu.mult)
            e.drain()
            e.tensor_scalar(out=t3[:, :], in0=t3[:, :], scalar1=t1[:, :],
                            scalar2=None, op0=Alu.mult)
            e.drain()
            e.tensor_scalar(out=t3[:, :], in0=t3[:, :], scalar1=-0.5,
                            scalar2=1.5, op0=Alu.mult, op1=Alu.add)
            e.drain()
            e.tensor_scalar(out=rr[:, :], in0=r0[:, :], scalar1=t3[:, :],
                            scalar2=None, op0=Alu.mult)
            e.drain()
            e.tensor_scalar(out=t4[:, :], in0=mv[:, :], scalar1=rr[:, :],
                            scalar2=-1.0, op0=Alu.mult, op1=Alu.mult)
            e.drain()
            if j >= 2:
                w(e, "t_done_%d" % (j - 2), cT)
            if kind == "own":
                if last_own is not None:
                    w(e, "s_rnd_%d" % last_own, cS)
                e.tensor_scalar(out=xnf[:, :], in0=xbuf[:, j % 2, :],
                                scalar1=rr[:, :], scalar2=t4[:, :],
                                op0=Alu.mult, op1=Alu.add)
                e.drain()
                cV.tag(e.tensor_copy(xnb[:, j % 2, :], xnf[:, :]))
                ck("v_xnb_%d" % j)
                e.reduce_max(am[:, :], xnf[:, :], axis=AX.X,
                             apply_absolute_value=True)
                e.drain()
                e.tensor_scalar(out=am[:, :], in0=am[:, :], scalar1=1e-5,
                                scalar2=None, op0=Alu.max)
                e.drain()
                e.reciprocal(out=t3[:, :], in_=am[:, :])
                e.drain()
                cV.tag(e.tensor_scalar(out=s_t[:, 1:2], in0=t3[:, :],
                                       scalar1=127.0, scalar2=None, op0=Alu.mult))
                ck("v_st_%d" % j)
                e.drain()
                e.reciprocal(out=sxr[:, i:i + 1], in_=s_t[:, 1:2])
                w(e, "s_rnd_%d" % j, cS)
                if i >= 2:
                    w(e, "g_xq_%d" % (i - 2), dGp[i % 2])
                cV.tag(e.tensor_scalar(out=xqb[:, i % 2, :], in0=qtmp[:, :],
                                       scalar1=C_RND, scalar2=None,
                                       op0=Alu.subtract))
                ck("v_xq_%d" % j)
                last_own = j
            else:
                cV.tag(e.tensor_scalar(out=xnb[:, j % 2, :], in0=xbuf[:, j % 2, :],
                                       scalar1=rr[:, :], scalar2=t4[:, :],
                                       op0=Alu.mult, op1=Alu.add))
                ck("v_xnb_%d" % j)
            if kind != "other":
                w(e, "t_tr_%d" % j, cT)
                c0 = 0 if kind == "halo" else 128 + 128 * i
                bA, bB = pb[2 * (j % 2)], pb[2 * (j % 2) + 1]
                e.tensor_copy(XN[:, 0:4, c0:c0 + 128],
                              bA[:, :].rearrange("p (a n) -> p a n", a=4))
                cV.tag(e.tensor_copy(XN[:, 4:8, c0:c0 + 128],
                                     bB[:, :].rearrange("p (a n) -> p a n", a=4)))
            ck("v_evac_%d" % j)

    def pa_scalar(e, ck, w):
        base = CK.get(("pa", "d0"))
        last_own = None
        for j, (kind, i) in enumerate(tiles):
            if base is not None and not isinstance(e, _StubEng):
                c = dSp[j % 2]
                e.wait_ge(c.h, base[c.name] + 16 * (j // 2 + 1))
            if j >= 1:
                w(e, "v_var_%d" % (j - 1), cV)
            cS.tag(e.activation(out=sqd[:, j % 2, :], in_=xbuf[:, j % 2, :],
                                func=AFT.Square, accum_out=stt[:, 1:2]))
            ck("s_sq_%d" % j)
            w(e, "v_var_%d" % j, cV)
            if j >= 1:
                w(e, "v_r0_%d" % (j - 1), cV)
            cS.tag(e.activation(out=s_t[:, 0:1], in_=t1[:, :], func=AFT.Sqrt))
            ck("s_sqrt_%d" % j)
            if kind == "own":
                w(e, "v_st_%d" % j, cV)
                if last_own is not None:
                    w(e, "v_xq_%d" % last_own, cV)
                cS.tag(e.activation(out=qtmp[:, :], in_=xnf[:, :], func=AFT.Copy,
                                    scale=s_t[:, 1:2], bias=C_RND))
                ck("s_rnd_%d" % j)
                last_own = j

    def pa_tensor(e, ck, w):
        pool_seq = 0
        for j, (kind, i) in enumerate(tiles):
            w(e, "v_xnb_%d" % j, cV)
            if kind != "other":
                if j >= 2:
                    w(e, "v_evac_%d" % (j - 2), cV)
                bA, bB = pb[2 * (j % 2)], pb[2 * (j % 2) + 1]
                for c in range(8):
                    bnk = bA if c < 4 else bB
                    inst = e.matmul(bnk[:, 128 * (c % 4):128 * (c % 4) + 128],
                                    xnb[:, j % 2, 128 * c:128 * c + 128],
                                    identb[:, :], start=True, stop=True)
                    if c == 7:
                        cT.tag(inst)
                ck("t_tr_%d" % j)
            if kind != "halo":
                if kind == "own":
                    z = poolz[:, 64 - 2 * i:192 - 2 * i]
                else:
                    z = poolz[:, 32 - 2 * i:160 - 2 * i]
                first = pool_seq == 0
                last = pool_seq == n_pool - 1
                e.matmul(pb[6][:, :], z, xnb[:, j % 2, 0:512],
                         start=first, stop=last)
                cT.tag(e.matmul(pb[7][:, :], z, xnb[:, j % 2, 512:1024],
                                start=first, stop=last))
                pool_seq += 1
            ck("t_done_%d" % j)

    def pa_gp(e, ck, w):
        for j, (kind, i) in enumerate(tiles):
            if kind == "own":
                w(e, "v_xq_%d" % j, cV)
                dGp[i % 2].tag(e.dma_start(out=xq_scr[i, :, :],
                                           in_=xqb[:, i % 2, :]), 16)
                ck("g_xq_%d" % i)

    run_phase("pa", [
        (blk.sync, pa_pre), (blk.sync, pa_sync), (blk.vector, pa_vec),
        (blk.scalar, pa_scalar), (blk.tensor, pa_tensor), (blk.gpsimd, pa_gp),
    ])
    dbg_dump([("dbg_xn", XN), ("dbg_sxr", sxr[:, :])])

    if PHLIM < 3:
        es.close()
        return nc

    def pa2a_gp(e, ck, w):
        rkv3 = rkv_w.rearrange("(c p) n -> p c n", p=128)
        rq3 = rq_w.rearrange("(c p) n -> p c n", p=128)
        dW.tag(e.dma_start(out=rkvw_b[:, :, :], in_=rkv3), 16)
        dWb.tag(e.dma_start(out=wqk_b[:, 0, :, :], in_=rq3[:, :, 0:128]), 16)
        dW.tag(e.dma_start(out=wqk_b[:, 1, :, :], in_=rq3[:, :, 128:256]), 16)
        e.wait_ge(dW.h, dW.n)
        e.wait_ge(dWb.h, dWb.n)

    run_phase("pa2a", [(blk.gpsimd, pa2a_gp)])

    def pa2_vec(e, ck, w):
        e.memset(pooled_pad[:, :, 0:64], 0.0)
        cV.tag(e.memset(pooled_pad[:, :, 128:192], 0.0))
        e.tensor_scalar(out=qtmp[0:64, 0:512], in0=pb[6][0:64, :],
                        scalar1=1.0 / 64.0, scalar2=None, op0=Alu.mult)
        e.tensor_scalar(out=qtmp[0:64, 512:1024], in0=pb[7][0:64, :],
                        scalar1=1.0 / 64.0, scalar2=None, op0=Alu.mult)
        e.drain()
        cV.tag(e.tensor_copy(xnb[0:64, 0, :], qtmp[0:64, :]))
        ck("pooled_lm")
        w(e, "t_ptr", cT)
        e.tensor_copy(pooled_pad[:, 0:4, 64:128],
                      pb[0][:, :].rearrange("p (a n) -> p a n", a=4)[:, :, 0:64])
        cV.tag(e.tensor_copy(pooled_pad[:, 4:8, 64:128],
                             pb[1][:, :].rearrange("p (a n) -> p a n", a=4)[:, :, 0:64]))
        ck("ppad")
        w(e, "t_rkv1", cT)
        e.tensor_tensor(out=ltmp[0:64, :], in0=pb[2][0:64, :],
                        in1=rkvb_t[0:64, :], op=Alu.add)
        e.drain()
        e.tensor_copy(rkv1[0:64, :], ltmp[0:64, :])
        cV.tag(e.memset(rkv1[64:128, :], 0.0))
        ck("rkv1")
        w(e, "t_rkv2", cT)
        e.tensor_tensor(out=ltmp[64:128, :], in0=pb[3][64:128, :],
                        in1=rkvb_t[64:128, :], op=Alu.add)
        e.drain()
        e.tensor_copy(rkv2[64:128, :], ltmp[64:128, :])
        cV.tag(e.memset(rkv2[0:64, :], 0.0))
        ck("rkv2")
        w(e, "t_rkT", cT)
        e.tensor_copy(rk_fm[:, 0, :], pb[4][:, 0:64])
        cV.tag(e.tensor_copy(rk_fm[:, 1, :], pb[5][:, 0:64]))
        ck("rk")
        for oc in range(2):
            w(e, "t_rq_%d" % oc, cT)
            for n in range(4):
                inst = e.tensor_scalar(
                    out=rq_fm[:, oc, 512 * n:512 * n + 512],
                    in0=pb[4 * oc + n][:, :],
                    scalar1=rqb_t[:, oc:oc + 1], scalar2=None, op0=Alu.add)
                if n == 3:
                    cV.tag(inst)
            ck("v_rq_%d" % oc)


    def pa2_tensor(e, ck, w):
        w(e, "pooled_lm", cV)
        for c in range(8):
            bnk = pb[0] if c < 4 else pb[1]
            inst = e.matmul(bnk[:, 128 * (c % 4):128 * (c % 4) + 128],
                            xnb[0:64, 0, 128 * c:128 * c + 128],
                            identb[0:64, :], start=True, stop=True)
            if c == 7:
                cT.tag(inst)
        ck("t_ptr")
        w(e, "ppad", cV)
        for kk in range(8):
            inst = e.matmul(pb[2][:, :], pooled_pad[:, kk, 64:192],
                            rkvw_b[:, kk, :], start=(kk == 0), stop=(kk == 7))
            if kk == 7:
                cT.tag(inst)
        ck("t_rkv1")
        for kk in range(8):
            inst = e.matmul(pb[3][:, :], pooled_pad[:, kk, 0:128],
                            rkvw_b[:, kk, :], start=(kk == 0), stop=(kk == 7))
            if kk == 7:
                cT.tag(inst)
        ck("t_rkv2")
        w(e, "rkv1", cV)
        e.matmul(pb[4][:, 0:64], rkv1[0:64, 0:128], identb[0:64, 0:64],
                 start=True, stop=True)
        cT.tag(e.matmul(pb[5][:, 0:64], rkv1[0:64, 128:256], identb[0:64, 0:64],
                        start=True, stop=True))
        ck("t_rkT")
        w(e, "rkv2", cV)
        w(e, "rk", cV)
        for oc in range(2):
            if oc == 1:
                w(e, "v_rq_0", cV)
            for n in range(4):
                for kk in range(8):
                    inst = e.matmul(pb[4 * oc + n][:, :],
                                    wqk_b[:, oc, kk, :],
                                    XN[:, kk, 128 + 512 * n:128 + 512 * n + 512],
                                    start=(kk == 0), stop=(kk == 7))
                    if kk == 7 and n == 3:
                        cT.tag(inst)
            ck("t_rq_%d" % oc)

    run_phase("pa2", [(blk.vector, pa2_vec), (blk.tensor, pa2_tensor)])
    dbg_dump([("dbg_rq", rq_fm), ("dbg_rk", rk_fm),
              ("dbg_rkv1", rkv1[:, :]), ("dbg_rkv2", rkv2[:, :]),
              ("dbg_pooled", pooled_pad)])

    if PHLIM < 4:
        es.close()
        return nc

    def pq_pre(e, ck, w):
        ck("d0")

    def pq_gp(e, ck, w):
        w3 = qkv_w.rearrange("(c p) n -> p c n", p=128)
        for oc in range(16):
            if oc >= 2:
                w(e, "t_wfree_%d" % (oc - 2), cT)
            dWp[oc % 2].tag(e.dma_start(out=wqk_b[:, oc % 2, :, :],
                               in_=w3[:, :, 128 * oc:128 * oc + 128]), 16)
        for vs in range(4):
            if vs >= 2:
                w(e, "t_v_%d_%d" % (vs - 2, 16), cT)
            else:
                w(e, "t_wfree_%d" % (14 + vs), cT)
            dWp[vs % 2].tag(e.dma_start(out=wv_b[:, vs % 2, :, :],
                               in_=w3[:, :, 2048 + 256 * vs:2048 + 256 * vs + 256]),
                   16)

    def pq_vec(e, ck, w):
        for oc in range(16):
            if oc < 8:
                for n in range(4):
                    w(e, "t_qk_%d_%d" % (oc, n), cT)
                    bnk = pb[n + 4 * (oc % 2)]
                    cV.tag(e.tensor_scalar(out=q_fm[:, oc, 512 * n:512 * n + 512],
                                           in0=bnk[:, :],
                                           scalar1=qkb_t[:, oc:oc + 1],
                                           scalar2=None, op0=Alu.add))
            else:
                ko = oc - 8
                for n in range(5):
                    w(e, "t_k_%d_%d" % (ko, n), cT)
                    bnk = pb[(n % 4) + 4 * (oc % 2)]
                    c0 = 512 * n
                    wd = 512 if n < 4 else 128
                    cV.tag(e.tensor_scalar(out=k_fm[:, ko, c0:c0 + wd],
                                           in0=bnk[:, 0:wd],
                                           scalar1=qkb_t[:, oc:oc + 1],
                                           scalar2=None, op0=Alu.add))
                    ck("v_kev_%d_%d" % (ko, n))
            ck("v_evac_%d" % oc)
        for vs in range(4):
            for tt in range(17):
                w(e, "t_v_%d_%d" % (vs, tt), cT)
                bnk = pb[(tt % 4) + 4 * (vs % 2)]
                cV.tag(e.tensor_tensor(out=v_tm[:, tt, 256 * vs:256 * vs + 256],
                                       in0=bnk[:, 0:256],
                                       in1=vb_t[:, 256 * vs:256 * vs + 256],
                                       op=Alu.add))
                ck("v_vev_%d_%d" % (vs, tt))

    def pq_tensor(e, ck, w):
        base = CK.get(("pq", "d0"))
        for oc in range(16):
            if base is not None and not isinstance(e, _StubEng):
                c = dWp[oc % 2]
                e.wait_ge(c.h, base[c.name] + 16 * (oc // 2 + 1))
            if oc < 8:
                for n in range(4):
                    if oc >= 2:
                        w(e, "v_evac_%d" % (oc - 2), cV)
                    bnk = pb[n + 4 * (oc % 2)]
                    for kk in range(8):
                        inst = e.matmul(bnk[:, :], wqk_b[:, oc % 2, kk, :],
                                        XN[:, kk, 128 + 512 * n:128 + 512 * n + 512],
                                        start=(kk == 0), stop=(kk == 7))
                        if kk == 7:
                            cT.tag(inst)
                    ck("t_qk_%d_%d" % (oc, n))
            else:
                ko = oc - 8
                for n in range(5):
                    if oc >= 2:
                        w(e, "v_evac_%d" % (oc - 2), cV)
                    if n == 4:
                        w(e, "v_kev_%d_0" % ko, cV)
                    bnk = pb[(n % 4) + 4 * (oc % 2)]
                    c0 = 512 * n
                    wd = 512 if n < 4 else 128
                    for kk in range(8):
                        inst = e.matmul(bnk[:, 0:wd], wqk_b[:, oc % 2, kk, :],
                                        XN[:, kk, c0:c0 + wd],
                                        start=(kk == 0), stop=(kk == 7))
                        if kk == 7:
                            cT.tag(inst)
                    ck("t_k_%d_%d" % (ko, n))
            ck("t_wfree_%d" % oc)
        for vs in range(4):
            if base is not None and not isinstance(e, _StubEng):
                c = dWp[vs % 2]
                e.wait_ge(c.h, base[c.name] + 16 * 8 + 16 * (vs // 2 + 1))
            if vs < 2:
                w(e, "v_evac_%d" % (14 + vs), cV)
            for tt in range(17):
                if tt >= 4:
                    w(e, "v_vev_%d_%d" % (vs, tt - 4), cV)
                elif vs >= 2:
                    w(e, "v_vev_%d_%d" % (vs - 2, 13 + tt), cV)
                bnk = pb[(tt % 4) + 4 * (vs % 2)]
                for kk in range(8):
                    inst = e.matmul(bnk[:, 0:256], XN[:, kk, 128 * tt:128 * tt + 128],
                                    wv_b[:, vs % 2, kk, :],
                                    start=(kk == 0), stop=(kk == 7))
                    if kk == 7:
                        cT.tag(inst)
                ck("t_v_%d_%d" % (vs, tt))

    run_phase("pq", [
        (blk.gpsimd, pq_pre), (blk.gpsimd, pq_gp),
        (blk.vector, pq_vec), (blk.tensor, pq_tensor),
    ])
    dbg_dump([("dbg_q", q_fm), ("dbg_k", k_fm), ("dbg_v", v_tm)])

    if PHLIM < 5:
        es.close()
        return nc

    NSLAB = 96

    def abs_src(sl):
        wg3 = wgT.rearrange("(c p) n -> p c n", p=128)
        wu3 = wuT.rearrange("(c p) n -> p c n", p=128)
        wd3 = wdT.rearrange("(c p) n -> p c n", p=128)
        mat, k = divmod(sl, 32)
        if mat == 0:
            return wg3[:, :, 128 * k:128 * k + 128]
        if mat == 1:
            return wu3[:, :, 128 * k:128 * k + 128]
        return wd3[:, k:k + 1, :]

    def mk_abs_gens(phname, nit, sl0, sl1):
        nsl = sl1 - sl0

        def slabs_for(it):
            return range(sl0 + (nsl * it) // nit, sl0 + (nsl * (it + 1)) // nit)

        def gp_gen(e, ck, w):
            ck("a0")
            for it in range(nit):
                for sl in slabs_for(it):
                    if sl - 2 >= sl0:
                        w(e, "abs_%d" % (sl - 2), cS)
                    src = abs_src(sl)
                    dAp[sl % 2].tag(e.dma_start(
                        out=absb[:, sl % 2, :].rearrange(
                            "p (a n) -> p a n", a=src.shape[1]),
                        in_=src), 16)

        def s_abs(e, it, base, ck, w):
            for sl in slabs_for(it):
                if base is not None and not isinstance(e, _StubEng):
                    c = dAp[sl % 2]
                    e.wait_ge(c.h, base[c.name] + 16 * ((sl - sl0) // 2 + 1))
                cS.tag(e.activation(out=absd[:, sl % 2, :],
                                    in_=absb[:, sl % 2, :], func=AFT.Abs,
                                    accum_out=absS[:, sl:sl + 1]))
                ck("abs_%d" % sl)
        return gp_gen, s_abs, slabs_for

    def plat_vec(e, ck, w):
        e.memset(qz[0:64, 1, :], 0.0)
        e.memset(qz[64:128, 0, :], 0.0)
        nit = 2 * NT
        for it in range(nit + 2):
            nstt = 0
            if it < nit:
                lc, qt = divmod(it, NT)
                if qt == 0:
                    if lc >= 1:
                        w(e, "t_sc_%d" % (lc * NT - 1), cT)
                    e.tensor_copy(qz[0:64, 0, :], rq_fm[0:64, lc, :])
                    cV.tag(e.tensor_copy(qz[64:128, 1, :], rq_fm[64:128, lc, :]))
                    ck("v_qz_%d" % lc)
                bs = 4 * (it % 2)
                w(e, "t_sc_%d" % it, cT)
                e.scalar_tensor_tensor(
                    out=pb[bs][:, 0:64], in0=pb[bs][:, 0:64], scalar=0.125,
                    in1=latmask_t[:, qt, :], op0=Alu.mult, op1=Alu.add)
                nstt = cV.tag(e.scalar_tensor_tensor(
                    out=pb[bs][:, 64:128], in0=pb[bs][:, 64:128], scalar=0.125,
                    in1=latmask_t[:, qt, :], op0=Alu.mult, op1=Alu.add))
            nrcp = 0
            if it >= 2:
                p2 = it - 2
                p = p2 % 2
                w(e, "s_exp_%d" % p2, cS)
                e.reciprocal(out=srcp[:, 2 * p:2 * p + 1],
                             in_=ssum[:, 2 * p:2 * p + 1])
                nrcp = cV.tag(e.reciprocal(out=srcp[:, 2 * p + 1:2 * p + 2],
                                           in_=ssum[:, 2 * p + 1:2 * p + 2]))
            if it < nit:
                e.wait_ge(cV.h, nstt)
                cV.tag(e.reduce_max(smax[:, it % 2:it % 2 + 1],
                                    pb[4 * (it % 2)][:, 0:128],
                                    axis=AX.X, negate=True))
                ck("v_max_%d" % it)
            if it >= 2:
                p2 = it - 2
                p = p2 % 2
                p4 = p2 % 4
                sc2 = latbin_t[:, 0:1] if (p2 % NT) == 0 else 1.0
                e.wait_ge(cV.h, nrcp)
                e.tensor_scalar(out=pe_sb[:, p4, 0:64], in0=pe_sb[:, p4, 0:64],
                                scalar1=srcp[:, 2 * p:2 * p + 1], scalar2=sc2,
                                op0=Alu.mult, op1=Alu.mult)
                cV.tag(e.tensor_scalar(out=pe_sb[:, p4, 64:128],
                                       in0=pe_sb[:, p4, 64:128],
                                       scalar1=srcp[:, 2 * p + 1:2 * p + 2],
                                       scalar2=sc2, op0=Alu.mult, op1=Alu.mult))
                ck("v_pe_%d" % p2)
                w(e, "t_tr_%d" % p2, cT)
                cV.tag(e.tensor_copy(pt_sb[:, p, 0:128],
                                     pb[4 * (p2 % 2) + 1][:, 0:128]))
                ck("v_pt_%d" % p2)

    def plat_scalar(e, ck, w, hook=lambda it: None):
        nit = 2 * NT
        for it in range(nit + 4):
            hook(it)
            if it < nit:
                p4 = it % 4
                w(e, "v_max_%d" % it, cV)
                if it >= 4:
                    w(e, "t_tr_%d" % (it - 4), cT)
                e.activation(out=pe_sb[:, p4, 0:64], in_=pb[4 * (it % 2)][:, 0:64],
                             func=AFT.Exp, bias=smax[:, it % 2:it % 2 + 1],
                             scale=1.0, accum_out=ssum[:, 2 * (it % 2):2 * (it % 2) + 1])
                cS.tag(e.activation(out=pe_sb[:, p4, 64:128],
                                    in_=pb[4 * (it % 2)][:, 64:128],
                                    func=AFT.Exp, bias=smax[:, it % 2:it % 2 + 1],
                                    scale=1.0,
                                    accum_out=ssum[:, 2 * (it % 2) + 1:2 * (it % 2) + 2]))
                ck("s_exp_%d" % it)
            if it >= 4:
                p4o = it - 4
                lc, qt = divmod(p4o, NT)
                bs = 4 * (p4o % 2)
                w(e, "t_av_%d" % p4o, cT)
                e.copy(latout[0:64, lc, 128 * qt:128 * qt + 128],
                       pb[bs + 2][0:64, 0:128])
                cS.tag(e.copy(latout[64:128, lc, 128 * qt:128 * qt + 128],
                              pb[bs + 2][64:128, 128:256]))
                ck("s_out_%d" % p4o)

    def plat_tensor(e, ck, w):
        nit = 2 * NT

        def emit_sc(e, it):
            lc, qt = divmod(it, NT)
            bs = 4 * (it % 2)
            if qt == 0:
                w(e, "v_qz_%d" % lc, cV)
            if it >= 2:
                w(e, "s_exp_%d" % (it - 2), cS)
            for sub in range(2):
                inst = e.matmul(pb[bs][:, 64 * sub:64 * sub + 64],
                                qz[:, sub, 128 * qt:128 * qt + 128],
                                rk_fm[:, lc, :], start=True, stop=True)
                if sub == 1:
                    cT.tag(inst)
            ck("t_sc_%d" % it)

        emit_sc(e, 0)
        for it in range(nit + 3):
            if 0 <= it - 2 < nit:
                p2 = it - 2
                w(e, "v_pe_%d" % p2, cV)
                if p2 >= 2:
                    w(e, "v_pt_%d" % (p2 - 2), cV)
                cT.tag(e.matmul(pb[4 * (p2 % 2) + 1][:, 0:128],
                                pe_sb[:, p2 % 4, 0:128],
                                identb[:, :], start=True, stop=True))
                ck("t_tr_%d" % p2)
            if 0 <= it - 3 < nit:
                p3 = it - 3
                lc = p3 // NT
                bs = 4 * (p3 % 2)
                w(e, "v_pt_%d" % p3, cV)
                if p3 >= 2:
                    w(e, "s_out_%d" % (p3 - 2), cS)
                e.matmul(pb[bs + 2][:, 0:128],
                         rkv1[:, LD + 128 * lc:LD + 128 * lc + 128],
                         pt_sb[:, p3 % 2, 0:128], start=True, stop=True)
                cT.tag(e.matmul(pb[bs + 2][:, 128:256],
                                rkv2[:, LD + 128 * lc:LD + 128 * lc + 128],
                                pt_sb[:, p3 % 2, 0:128], start=True, stop=True))
                ck("t_av_%d" % p3)
            if it + 1 < nit:
                emit_sc(e, it + 1)

    plat_gp_abs, plat_s_abs, _plat_slabs = mk_abs_gens("plat", 2 * NT + 4, 0, 20)

    def plat_scalar_wrapped(e, ck, w):
        base = CK.get(("plat", "a0"))
        it2 = [0]

        def hook(it):
            plat_s_abs(e, it, base, ck, w)
        plat_scalar(e, ck, w, hook)

    run_phase("plat", [
        (blk.gpsimd, plat_gp_abs),
        (blk.vector, plat_vec), (blk.scalar, plat_scalar_wrapped),
        (blk.tensor, plat_tensor),
    ])
    dbg_dump([("dbg_lat", latout)])

    if PHLIM < 6:
        es.close()
        return nc

    def ploc_vec(e, ck, w):
        nit = 8 * NT
        for it in range(nit + 2):
            nstt = 0
            if it < nit:
                c, qt = divmod(it, NT)
                if qt == 0:
                    if c >= 1:
                        w(e, "t_sc_%d" % (c * NT - 1), cT)
                    e.tensor_copy(qz[0:64, 0, :], q_fm[0:64, c, :])
                    cV.tag(e.tensor_copy(qz[64:128, 1, :], q_fm[64:128, c, :]))
                    ck("v_qz_%d" % c)
                bs = 3 * (it % 2)
                w(e, "t_sc_%d" % it, cT)
                mk = lmask_t[:, 0, :] if qt == 0 else lmask_t[:, 1, :]
                nstt = cV.tag(e.scalar_tensor_tensor(
                    out=pb[bs][:, :], in0=pb[bs][:, :], scalar=0.125,
                    in1=mk, op0=Alu.mult, op1=Alu.add))
            nrcp = 0
            if it >= 2:
                p2 = it - 2
                p = p2 % 2
                w(e, "s_exp_%d" % p2, cS)
                e.reciprocal(out=srcp[:, 2 * p:2 * p + 1],
                             in_=ssum[:, 2 * p:2 * p + 1])
                nrcp = cV.tag(e.reciprocal(out=srcp[:, 2 * p + 1:2 * p + 2],
                                           in_=ssum[:, 2 * p + 1:2 * p + 2]))
            if it < nit:
                e.wait_ge(cV.h, nstt)
                cV.tag(e.reduce_max(smax[:, it % 2:it % 2 + 1],
                                    pb[3 * (it % 2)][:, :],
                                    axis=AX.X, negate=True))
                ck("v_max_%d" % it)
            if it >= 2:
                p2 = it - 2
                p = p2 % 2
                p4 = p2 % 4
                sc2 = lbin_t[:, 0:1] if (p2 % NT) == 0 else 1.0
                e.wait_ge(cV.h, nrcp)
                e.tensor_scalar(out=pe_sb[:, p4, 0:256], in0=pe_sb[:, p4, 0:256],
                                scalar1=srcp[:, 2 * p:2 * p + 1], scalar2=sc2,
                                op0=Alu.mult, op1=Alu.mult)
                cV.tag(e.tensor_scalar(out=pe_sb[:, p4, 256:512],
                                       in0=pe_sb[:, p4, 256:512],
                                       scalar1=srcp[:, 2 * p + 1:2 * p + 2],
                                       scalar2=sc2, op0=Alu.mult, op1=Alu.mult))
                ck("v_pe_%d" % p2)
                w(e, "t_tr_%d" % p2, cT)
                cV.tag(e.tensor_copy(pt_sb[:, p, :], pb[3 * (p2 % 2) + 1][:, :]))
                ck("v_pt_%d" % p2)

    def ploc_scalar(e, ck, w, hook=lambda it: None):
        nit = 8 * NT
        for it in range(nit + 4):
            hook(it)
            if it < nit:
                p4 = it % 4
                bs = 3 * (it % 2)
                w(e, "v_max_%d" % it, cV)
                if it >= 4:
                    w(e, "t_tr_%d" % (it - 4), cT)
                e.activation(out=pe_sb[:, p4, 0:256], in_=pb[bs][:, 0:256],
                             func=AFT.Exp, bias=smax[:, it % 2:it % 2 + 1],
                             scale=1.0,
                             accum_out=ssum[:, 2 * (it % 2):2 * (it % 2) + 1])
                cS.tag(e.activation(out=pe_sb[:, p4, 256:512],
                                    in_=pb[bs][:, 256:512],
                                    func=AFT.Exp, bias=smax[:, it % 2:it % 2 + 1],
                                    scale=1.0,
                                    accum_out=ssum[:, 2 * (it % 2) + 1:2 * (it % 2) + 2]))
                ck("s_exp_%d" % it)
            if it >= 4:
                p4o = it - 4
                c, qt = divmod(p4o, NT)
                bs2 = 3 * (p4o % 2) + 2
                w(e, "t_av_%d" % p4o, cT)
                e.copy(local_fm[0:64, c, 128 * qt:128 * qt + 128],
                       pb[bs2][0:64, 0:128])
                cS.tag(e.copy(local_fm[64:128, c, 128 * qt:128 * qt + 128],
                              pb[bs2][64:128, 128:256]))
                ck("s_out_%d" % p4o)

    def ploc_tensor(e, ck, w):
        nit = 8 * NT

        def emit_sc(e, it):
            c, qt = divmod(it, NT)
            bs = 3 * (it % 2)
            if qt == 0:
                w(e, "v_qz_%d" % c, cV)
            if it >= 2:
                w(e, "s_exp_%d" % (it - 2), cS)
            for sub in range(2):
                inst = e.matmul(pb[bs][:, 256 * sub:256 * sub + 256],
                                qz[:, sub, 128 * qt:128 * qt + 128],
                                k_fm[:, c, 128 * qt:128 * qt + 256],
                                start=True, stop=True)
                if sub == 1:
                    cT.tag(inst)
            ck("t_sc_%d" % it)

        emit_sc(e, 0)
        for it in range(nit + 3):
            if 0 <= it - 2 < nit:
                p2 = it - 2
                w(e, "v_pe_%d" % p2, cV)
                if p2 >= 2:
                    w(e, "v_pt_%d" % (p2 - 2), cV)
                for q4 in range(4):
                    inst = e.matmul(pb[3 * (p2 % 2) + 1][:, 128 * q4:128 * q4 + 128],
                                    pe_sb[:, p2 % 4, 128 * q4:128 * q4 + 128],
                                    identb[:, :], start=True, stop=True)
                    if q4 == 3:
                        cT.tag(inst)
                ck("t_tr_%d" % p2)
            if 0 <= it - 3 < nit:
                p3 = it - 3
                c, qt = divmod(p3, NT)
                bs = 3 * (p3 % 2)
                w(e, "v_pt_%d" % p3, cV)
                if p3 >= 2:
                    w(e, "s_out_%d" % (p3 - 2), cS)
                for sub in range(2):
                    for kc in range(2):
                        inst = e.matmul(
                            pb[bs + 2][:, 128 * sub:128 * sub + 128],
                            v_tm[:, qt + kc, 128 * c:128 * c + 128],
                            pt_sb[:, p3 % 2, 256 * sub + 128 * kc:256 * sub + 128 * kc + 128],
                            start=(kc == 0), stop=(kc == 1))
                        if sub == 1 and kc == 1:
                            cT.tag(inst)
                ck("t_av_%d" % p3)
            if it + 1 < nit:
                emit_sc(e, it + 1)

    ploc_gp_abs, ploc_s_abs, _ploc_slabs = mk_abs_gens("ploc", 8 * NT + 4, 20, 96)

    def ploc_scalar_wrapped(e, ck, w):
        base = CK.get(("ploc", "a0"))

        def hook(it):
            ploc_s_abs(e, it, base, ck, w)
        ploc_scalar(e, ck, w, hook)

    run_phase("ploc", [
        (blk.gpsimd, ploc_gp_abs),
        (blk.vector, ploc_vec), (blk.scalar, ploc_scalar_wrapped),
        (blk.tensor, ploc_tensor),
    ])
    dbg_dump([("dbg_loc", local_fm)])

    if PHLIM < 7:
        es.close()
        return nc

    def prout_gp(e, ck, w):
        ro3 = rout_w.rearrange("(c p) n -> p c n", p=128)
        dW.tag(e.dma_start(out=routw_b[:, :, :], in_=ro3), 16)
        e.wait_ge(dW.h, dW.n)

    def prout_vec(e, ck, w):
        e.reduce_sum(stt[:, 8:9], absS[:, 0:32], axis=AX.X)
        e.reduce_sum(stt[:, 9:10], absS[:, 32:64], axis=AX.X)
        cV.tag(e.reduce_sum(stt[:, 10:11], absS[:, 64:96], axis=AX.X))
        ck("asums")
        w(e, "wsc_mm1", cT)
        cV.tag(e.tensor_copy(srcp[0:1, 0:3], pb[0][0:1, 0:3]))
        ck("wsc_row")
        w(e, "wsc_mm2", cT)
        e.tensor_scalar(out=wsc[:, 3:6], in0=pb[0][:, 0:3],
                        scalar1=1.0 / (D * INNER), scalar2=1e-5,
                        op0=Alu.mult, op1=Alu.max)
        e.drain()
        e.reciprocal(out=wsc[:, 0:1], in_=wsc[:, 3:4])
        e.reciprocal(out=wsc[:, 1:2], in_=wsc[:, 4:5])
        nw = cV.tag(e.reciprocal(out=wsc[:, 2:3], in_=wsc[:, 5:6]))
        ck("wsc_fin")
        e.wait_ge(cV.h, nw)
        for tt in range(NT):
            e.tensor_scalar(out=gsc[:, tt:tt + 1], in0=sxr[:, tt:tt + 1],
                            scalar1=wsc[:, 3:4], scalar2=None, op0=Alu.mult)
            inst = e.tensor_scalar(out=usc[:, tt:tt + 1], in0=sxr[:, tt:tt + 1],
                                   scalar1=wsc[:, 4:5], scalar2=None,
                                   op0=Alu.mult)
            if tt == NT - 1:
                cV.tag(inst)
        ck("scales")
        for g in range(32):
            oc, n = divmod(g, 4)
            w(e, "t_ro_%d" % g, cT)
            cV.tag(e.tensor_scalar(out=remote[:, oc, 512 * n:512 * n + 512],
                                   in0=pb[g % 8][:, :],
                                   scalar1=routb_t[:, oc:oc + 1],
                                   scalar2=None, op0=Alu.add))
            ck("v_ro_%d" % g)

    def prout_tensor(e, ck, w):
        w(e, "asums", cV)
        cT.tag(e.matmul(pb[0][0:1, 0:3], onesf[:, 0:1], stt[:, 8:11],
                        start=True, stop=True))
        ck("wsc_mm1")
        w(e, "wsc_row", cV)
        cT.tag(e.matmul(pb[0][:, 0:3], onesf[0:1, 1:129], srcp[0:1, 0:3],
                        start=True, stop=True))
        ck("wsc_mm2")
        rw = routw_b[:, :, :].rearrange("p a n -> p (a n)")
        for g in range(32):
            oc, n = divmod(g, 4)
            if g == 0:
                w(e, "wsc_fin", cV)
            if g >= 8:
                w(e, "v_ro_%d" % (g - 8), cV)
            for lc in range(2):
                inst = e.matmul(pb[g % 8][:, :],
                                rw[:, 1024 * lc + 128 * oc:1024 * lc + 128 * oc + 128],
                                latout[:, lc, 512 * n:512 * n + 512],
                                start=(lc == 0), stop=(lc == 1))
                if lc == 1:
                    cT.tag(inst)
            ck("t_ro_%d" % g)

    run_phase("prout0", [(blk.gpsimd, prout_gp)])
    run_phase("prout", [
        (blk.vector, prout_vec), (blk.tensor, prout_tensor),
    ])
    dbg_dump([("dbg_rem", remote)])

    def pg_vec(e, ck, w):
        cV.tag(e.memset(gate_sb[:, :], 0.0))
        ck("gz")
        for c in range(8):
            for n in range(4):
                g = 4 * c + n
                w(e, "t_bc_%d" % g, cT)
                sl = slice(512 * n, 512 * n + 512)
                e.tensor_tensor(out=gdiff[:, :], in0=local_fm[:, c, sl],
                                in1=remote[:, c, sl], op=Alu.subtract)
                e.drain()
                e.scalar_tensor_tensor(out=gtmp[:, :], in0=pb[4 + g % 4][:, :],
                                       scalar=1.0, in1=gdiff[:, :],
                                       op0=Alu.mult, op1=Alu.mult)
                e.drain()
                cV.tag(e.tensor_tensor(out=mixed[:, c, sl], in0=gtmp[:, :],
                                       in1=remote[:, c, sl], op=Alu.add))
                ck("v_mix_%d" % g)

    def pg_scalar(e, ck, w):
        for c in range(8):
            for n in range(4):
                g = 4 * c + n
                w(e, "t_lg_%d" % g, cT)
                if g >= 1:
                    w(e, "t_bc_%d" % (g - 1), cT)
                cS.tag(e.activation(out=gate_sb[0:2, 0:512],
                                    in_=pb[g % 4][0:2, :], func=AFT.Sigmoid))
                ck("s_sig_%d" % g)

    def pg_tensor(e, ck, w):
        w(e, "gz", cV)
        for c in range(8):
            for n in range(4):
                g = 4 * c + n
                sl = slice(512 * n, 512 * n + 512)
                if g >= 4:
                    w(e, "s_sig_%d" % (g - 4), cS)
                e.matmul(pb[g % 4][0:2, :], glgr_b[:, 0:2], local_fm[:, c, sl],
                         start=True, stop=False)
                cT.tag(e.matmul(pb[g % 4][0:2, :], glgr_b[:, 2:4],
                                remote[:, c, sl], start=False, stop=True))
                ck("t_lg_%d" % g)
                w(e, "s_sig_%d" % g, cS)
                if g >= 4:
                    w(e, "v_mix_%d" % (g - 4), cV)
                cT.tag(e.matmul(pb[4 + g % 4][:, :], sel2_b[:, :],
                                gate_sb[:, 0:512], start=True, stop=True))
                ck("t_bc_%d" % g)

    run_phase("pg", [
        (blk.vector, pg_vec), (blk.scalar, pg_scalar), (blk.tensor, pg_tensor),
    ])
    dbg_dump([("dbg_mix", mixed)])

    if PHLIM < 8:
        es.close()
        return nc

    def po_gp0(e, ck, w):
        ow3 = out_w.rearrange("(c p) n -> p c n", p=128)
        dW.tag(e.dma_start(out=outw_b[:, :, :], in_=ow3), 16)
        e.wait_ge(dW.h, dW.n)

    def po_vec(e, ck, w):
        for tt in range(NT):
            w(e, "t_o_%d" % tt, cT)
            if tt >= 2:
                w(e, "g_at_%d" % (tt - 2), dGp[tt % 2])
            b0, b1 = pb[2 * (tt % 2)], pb[2 * (tt % 2) + 1]
            e.tensor_tensor(out=attn_buf[:, tt % 2, 0:512], in0=b0[:, :],
                            in1=outb_t[:, 0:512], op=Alu.add)
            cV.tag(e.tensor_tensor(out=attn_buf[:, tt % 2, 512:1024],
                                   in0=b1[:, :], in1=outb_t[:, 512:1024],
                                   op=Alu.add))
            ck("v_o_%d" % tt)

    def po_tensor(e, ck, w):
        for tt in range(NT):
            if tt >= 2:
                w(e, "v_o_%d" % (tt - 2), cV)
            b0, b1 = pb[2 * (tt % 2)], pb[2 * (tt % 2) + 1]
            for kk in range(8):
                e.matmul(b0[:, :], mixed[:, kk, 128 * tt:128 * tt + 128],
                         outw_b[:, kk, 0:512], start=(kk == 0), stop=(kk == 7))
            for kk in range(8):
                inst = e.matmul(b1[:, :], mixed[:, kk, 128 * tt:128 * tt + 128],
                                outw_b[:, kk, 512:1024],
                                start=(kk == 0), stop=(kk == 7))
                if kk == 7:
                    cT.tag(inst)
            ck("t_o_%d" % tt)

    def po_gp(e, ck, w):
        for tt in range(NT):
            w(e, "v_o_%d" % tt, cV)
            dGp[tt % 2].tag(e.dma_start(out=attn_scr[tt, :, :],
                                        in_=attn_buf[:, tt % 2, :]), 16)
            ck("g_at_%d" % tt)

    run_phase("po0", [(blk.gpsimd, po_gp0)])

    if PHLIM < 9:
        es.close()
        return nc

    def px_pre(e, ck, w):
        ck("d0")

    def px_sync(e, ck, w):
        for tt in range(NT):
            if tt >= 2:
                w(e, "t_x_%d" % (tt - 2), cT)
            dSp[tt % 2].tag(e.dma_start(out=xql[:, tt % 2, :],
                                        in_=xq_scr[tt, :, :]), 16)

    def px_tensor(e, ck, w):
        base = CK.get(("po", "d0"))
        for tt in range(NT):
            if base is not None and not isinstance(e, _StubEng):
                c = dSp[tt % 2]
                e.wait_ge(c.h, base[c.name] + 16 * (tt // 2 + 1))
            if tt >= 2:
                w(e, "v_x_%d" % (tt - 2), cV)
            b0, b1 = pb[4 + 2 * (tt % 2)], pb[5 + 2 * (tt % 2)]
            for c in range(8):
                bnk = b0 if c < 4 else b1
                inst = e.matmul(bnk[:, 128 * (c % 4):128 * (c % 4) + 128],
                                xql[:, tt % 2, 128 * c:128 * c + 128],
                                identb[:, :], start=True, stop=True)
                if c == 7:
                    cT.tag(inst)
            ck("t_x_%d" % tt)

    def px_vec(e, ck, w):
        for tt in range(NT):
            w(e, "t_x_%d" % tt, cT)
            b0, b1 = pb[4 + 2 * (tt % 2)], pb[5 + 2 * (tt % 2)]
            e.tensor_copy(XQ[:, 0:4, 128 * tt:128 * tt + 128],
                          b0[:, :].rearrange("p (a n) -> p a n", a=4))
            cV.tag(e.tensor_copy(XQ[:, 4:8, 128 * tt:128 * tt + 128],
                                 b1[:, :].rearrange("p (a n) -> p a n", a=4)))
            ck("v_x_%d" % tt)

    run_phase("po", [
        (blk.sync, px_pre), (blk.sync, px_sync),
        (blk.tensor, px_tensor), (blk.tensor, po_tensor),
        (blk.vector, px_vec), (blk.vector, po_vec),
        (blk.gpsimd, po_gp),
    ])
    dbg_dump([("dbg_xq", XQ)])

    if PHLIM < 10:
        es.close()
        return nc

    def pgu_pre(e, ck, w):
        ck("d0")

    def pgu_sync(e, ck, w):
        wg3 = wgT.rearrange("(c p) n -> p c n", p=128)
        wu3 = wuT.rearrange("(c p) n -> p c n", p=128)
        for nt in range(8):
            for mi, w3 in ((0, wg3), (1, wu3)):
                idx = 2 * nt + mi
                if idx >= 2:
                    w(e, "s_rw_%d" % (idx - 2), cS)
                dSp[idx % 2].tag(e.dma_start(out=wq32[:, idx % 2, :, :],
                                   in_=w3[:, :, 512 * nt:512 * nt + 512]), 16)

    def pgu_scalar(e, ck, w):
        base = CK.get(("pgu", "d0"))

        def s_round(e, idx):
            mi = idx % 2
            if base is not None and not isinstance(e, _StubEng):
                c = dSp[idx % 2]
                e.wait_ge(c.h, base[c.name] + 16 * (idx // 2 + 1))
            if idx >= 1:
                w(e, "v_wq_%d" % (idx - 1), cV)
            cS.tag(e.activation(
                out=wtmp[:, :, :].rearrange("p a n -> p (a n)"),
                in_=wq32[:, idx % 2, :, :].rearrange("p a n -> p (a n)"),
                func=AFT.Copy, scale=wsc[:, mi:mi + 1], bias=C_RND))
            ck("s_rw_%d" % idx)

        s_round(e, 0)
        s_round(e, 1)
        for nt in range(8):
            for tt in range(NT):
                it = NT * nt + tt
                if nt + 1 < 8 and tt == 2:
                    s_round(e, 2 * (nt + 1))
                if nt + 1 < 8 and tt == 8:
                    s_round(e, 2 * (nt + 1) + 1)
                w(e, "t_g_%d" % it, cT)
                if it >= 1:
                    w(e, "v_h_%d" % (it - 1), cV)
                cS.tag(e.activation(out=sg_sb[:, :], in_=pb[tt % 2][:, :],
                                    func=AFT.Silu, scale=gsc[:, tt:tt + 1]))
                ck("s_silu_%d" % it)

    def pgu_vec(e, ck, w):
        def v_clip(e, idx, half):
            nt2, mi = divmod(idx, 2)
            dstW = wgq if mi == 0 else wuq
            sl = slice(4096 * half, 4096 * half + 4096)
            wt_f = wtmp[:, :, :].rearrange("p a n -> p (a n)")
            if half == 0:
                w(e, "s_rw_%d" % idx, cS)
                e.tensor_scalar(out=wt_f[:, sl], in0=wt_f[:, sl],
                                scalar1=C_RND, scalar2=1.0,
                                op0=Alu.subtract, op1=Alu.min)
            else:
                e.tensor_scalar(out=wt_f[:, sl], in0=wt_f[:, sl],
                                scalar1=C_RND, scalar2=1.0,
                                op0=Alu.subtract, op1=Alu.min)
                if nt2 >= 2:
                    w(e, "t_u_%d" % (NT * (nt2 - 2) + NT - 1), cT)
                cV.tag(e.tensor_scalar(
                    out=dstW[:, nt2 % 2, :, 0:256].rearrange("p a n -> p (a n)"),
                    in0=wt_f[:, 0:4096].rearrange(
                        "p (a n) -> p a n", a=8)[:, :, 0:256].rearrange(
                        "p a n -> p a n").rearrange("p a n -> p (a n)"),
                    scalar1=-1.0, scalar2=None, op0=Alu.max))
                cV.tag(e.tensor_scalar(
                    out=dstW[:, nt2 % 2, :, 256:512].rearrange("p a n -> p (a n)"),
                    in0=wt_f[:, 4096:8192].rearrange(
                        "p (a n) -> p a n", a=8)[:, :, 0:256].rearrange(
                        "p a n -> p (a n)"),
                    scalar1=-1.0, scalar2=None, op0=Alu.max))
                ck("v_wq_%d" % idx)

        pass

    def pgu_vec2(e, ck, w):
        wt_f = wtmp[:, :, :].rearrange("p a n -> p (a n)")

        def v_clip1(e, idx):
            w(e, "s_rw_%d" % idx, cS)
            nc1 = cV.tag(e.tensor_scalar(out=wt_f, in0=wt_f,
                                         scalar1=C_RND, scalar2=1.0,
                                         op0=Alu.subtract, op1=Alu.min))
            return nc1

        def v_clip2(e, idx, nc1):
            nt2, mi = divmod(idx, 2)
            dstW = wgq if mi == 0 else wuq
            if nt2 >= 2:
                w(e, "t_u_%d" % (NT * (nt2 - 2) + NT - 1), cT)
            e.wait_ge(cV.h, nc1)
            cV.tag(e.tensor_scalar(
                out=dstW[:, nt2 % 2, :, :].rearrange("p a n -> p (a n)"),
                in0=wt_f, scalar1=-1.0, scalar2=None, op0=Alu.max))
            ck("v_wq_%d" % idx)

        nc1 = v_clip1(e, 0)
        v_clip2(e, 0, nc1)
        nc1 = v_clip1(e, 1)
        v_clip2(e, 1, nc1)
        for nt in range(8):
            pend = None
            for tt in range(NT):
                it = NT * nt + tt
                if nt + 1 < 8:
                    if tt == 4:
                        pend = (2 * (nt + 1), v_clip1(e, 2 * (nt + 1)))
                    elif tt == 6 and pend is not None:
                        v_clip2(e, pend[0], pend[1])
                        pend = None
                    elif tt == 10:
                        pend = (2 * (nt + 1) + 1, v_clip1(e, 2 * (nt + 1) + 1))
                    elif tt == 12 and pend is not None:
                        v_clip2(e, pend[0], pend[1])
                        pend = None
                w(e, "s_silu_%d" % it, cS)
                w(e, "t_u_%d" % it, cT)
                g4 = it // 4
                if g4 >= 2 and it % 4 == 0:
                    w(e, "g_h_%d" % (g4 - 2), dGp[g4 % 2])
                nh = cV.tag(e.scalar_tensor_tensor(
                    out=hwb[:, g4 % 2, it % 4, :], in0=pb[2 + tt % 2][:, :],
                    scalar=usc[:, tt:tt + 1], in1=sg_sb[:, :],
                    op0=Alu.mult, op1=Alu.mult))
                e.wait_ge(cV.h, nh)
                cV.tag(e.reduce_max(hmax[:, 8 * tt + nt:8 * tt + nt + 1],
                                    hwb[:, g4 % 2, it % 4, :], axis=AX.X,
                                    apply_absolute_value=True))
                ck("v_h_%d" % it)

    def pgu_tensor(e, ck, w):
        for nt in range(8):
            w(e, "v_wq_%d" % (2 * nt + 1), cV)
            for tt in range(NT):
                it = NT * nt + tt
                if it >= 2:
                    w(e, "s_silu_%d" % (it - 2), cS)
                    w(e, "v_h_%d" % (it - 2), cV)
                bg = pb[tt % 2]
                bu = pb[2 + tt % 2]
                for kk in range(8):
                    inst = e.matmul(bg[:, :], XQ[:, kk, 128 * tt:128 * tt + 128],
                                    wgq[:, nt % 2, kk, :],
                                    start=(kk == 0), stop=(kk == 7))
                    if kk == 7:
                        cT.tag(inst)
                        ck("t_g_%d" % it)
                    inst = e.matmul(bu[:, :], XQ[:, kk, 128 * tt:128 * tt + 128],
                                    wuq[:, nt % 2, kk, :],
                                    start=(kk == 0), stop=(kk == 7))
                    if kk == 7:
                        cT.tag(inst)
                        ck("t_u_%d" % it)

    def pgu_gp(e, ck, w):
        for nt in range(8):
            for g4l in range(4):
                g4 = 4 * nt + g4l
                w(e, "v_h_%d" % (NT * nt + 4 * g4l + 3), cV)
                dGp[g4 % 2].tag(e.dma_start(
                    out=h_scr[4 * g4l:4 * g4l + 4, :, nt, :].rearrange(
                        "a p n -> p a n"),
                    in_=hwb[:, g4 % 2, :, :]), 16)
                ck("g_h_%d" % g4)

    run_phase("pgu", [
        (blk.sync, pgu_pre), (blk.sync, pgu_sync), (blk.scalar, pgu_scalar),
        (blk.vector, pgu_vec2), (blk.tensor, pgu_tensor), (blk.gpsimd, pgu_gp),
    ])

    if PHLIM < 11:
        es.close()
        return nc

    def pwd_pre(e, ck, w):
        ck("d0")

    def pwd_sync(e, ck, w):
        wd3 = wdT.rearrange("(c p) n -> c p n", p=128)
        for c in range(32):
            if c >= 2:
                w(e, "s_rd_%d" % (c - 2), cS)
            dSp[c % 2].tag(e.dma_start(out=wdf[:, c % 2, :], in_=wd3[c, :, :]),
                           16)

    def pwd_scalar(e, ck, w):
        base = CK.get(("pwd", "d0"))
        for c in range(32):
            if base is not None and not isinstance(e, _StubEng):
                ctr = dSp[c % 2]
                e.wait_ge(ctr.h, base[ctr.name] + 16 * (c // 2 + 1))
            if c >= 1:
                w(e, "v_wd1_%d" % (c - 1), cV)
            cS.tag(e.activation(out=wdt_r[:, c % 2, :], in_=wdf[:, c % 2, :],
                                func=AFT.Copy, scale=wsc[:, 2:3], bias=C_RND))
            ck("s_rd_%d" % c)

    def pwd_vec(e, ck, w):
        for c in range(32):
            w(e, "s_rd_%d" % c, cS)
            cV.tag(e.tensor_scalar(out=wdt_r[:, c % 2, :], in0=wdt_r[:, c % 2, :],
                                   scalar1=C_RND, scalar2=1.0,
                                   op0=Alu.subtract, op1=Alu.min))
            ck("v_wd1_%d" % c)
            e.drain()
            cV.tag(e.tensor_scalar(out=wdq2[:, c, :], in0=wdt_r[:, c % 2, :],
                                   scalar1=-1.0, scalar2=None, op0=Alu.max))
            ck("v_wd2_%d" % c)

    run_phase("pwd", [
        (blk.sync, pwd_pre), (blk.sync, pwd_sync), (blk.scalar, pwd_scalar),
        (blk.vector, pwd_vec),
    ])

    def pd_pre(e, ck, w):
        ck("d0")

    def pd_sync(e, ck, w):
        for tt in range(NT):
            if tt >= 2:
                w(e, "v_fin_%d" % (tt - 2), cV)
            dSp[tt % 2].tag(e.dma_start(
                out=hrb[:, tt % 2, :],
                in_=h_scr[tt, :, :, :].rearrange("p a n -> p (a n)")), 16)
            dSp[tt % 2].tag(e.dma_start(out=atl[:, tt % 2, :],
                                        in_=attn_scr[tt, :, :]), 16)
            dSp[tt % 2].tag(e.dma_start(out=xres[:, tt % 2, :],
                                        in_=x_own[128 * tt:128 * tt + 128, :]), 16)

    def pd_vec(e, ck, w):
        base = CK.get(("pd", "d0"))
        for tt in range(NT + 1):
            if tt < NT:
                if base is not None and not isinstance(e, _StubEng):
                    c = dSp[tt % 2]
                    e.wait_ge(c.h, base[c.name] + 48 * (tt // 2 + 1))
                if tt >= 1:
                    w(e, "s_rh_%d" % (tt - 1), cS)
                n0 = cV.tag(e.reduce_max(am[:, :], hmax[:, 8 * tt:8 * tt + 8],
                                         axis=AX.X))
                e.wait_ge(cV.h, n0)
                n1 = cV.tag(e.tensor_scalar(out=t1[:, :], in0=am[:, :],
                                            scalar1=1e-5, scalar2=None,
                                            op0=Alu.max))
                e.wait_ge(cV.h, n1)
                n2 = cV.tag(e.reciprocal(out=t3[:, :], in_=t1[:, :]))
                e.wait_ge(cV.h, n2)
                cV.tag(e.tensor_scalar(out=sh_t[:, tt % 2:tt % 2 + 1],
                                       in0=t3[:, :], scalar1=127.0,
                                       scalar2=None, op0=Alu.mult))
                ck("v_sh_%d" % tt)
                cV.tag(e.tensor_scalar(out=dsc_t[:, tt % 2:tt % 2 + 1],
                                       in0=t1[:, :], scalar1=wsc[:, 5:6],
                                       scalar2=1.0 / 127.0,
                                       op0=Alu.mult, op1=Alu.mult))
                ck("v_dsc_%d" % tt)
                w(e, "s_rh_%d" % tt, cS)
                if tt >= 2:
                    w(e, "t_tr_%d_%d" % (tt - 2, 7), cT)
                cV.tag(e.tensor_scalar(out=xh_tm2[:, tt % 2, :], in0=tq_sb[:, :],
                                       scalar1=C_RND, scalar2=None,
                                       op0=Alu.subtract))
                ck("v_xh_%d" % tt)
                for rg in range(8):
                    w(e, "t_tr_%d_%d" % (tt, rg), cT)
                    bnk = pb[4 + rg % 4]
                    cV.tag(e.tensor_copy(
                        xh_fm[:, 4 * rg:4 * rg + 4, :].rearrange(
                            "p a n -> p (a n)"),
                        bnk[:, :]))
                    ck("v_ev_%d_%d" % (tt, rg))
            if tt >= 1:
                pt_ = tt - 1
                pp = pt_ % 2
                w(e, "t_dn_%d" % pt_, cT)
                if pt_ >= 2:
                    w(e, "g_y_%d" % (pt_ - 2), dGp[pt_ % 2])
                bd0, bd1 = pb[pt_ % 2], pb[2 + pt_ % 2]
                e.scalar_tensor_tensor(out=obuf[:, pp, 0:512], in0=bd0[:, :],
                                       scalar=dsc_t[:, pp:pp + 1],
                                       in1=atl[:, pp, 0:512],
                                       op0=Alu.mult, op1=Alu.add)
                nst = cV.tag(e.scalar_tensor_tensor(
                    out=obuf[:, pp, 512:1024], in0=bd1[:, :],
                    scalar=dsc_t[:, pp:pp + 1],
                    in1=atl[:, pp, 512:1024],
                    op0=Alu.mult, op1=Alu.add))
                e.wait_ge(cV.h, nst)
                e.tensor_tensor(out=obuf[:, pp, 0:512], in0=obuf[:, pp, 0:512],
                                in1=xres[:, pp, 0:512], op=Alu.add)
                cV.tag(e.tensor_tensor(out=obuf[:, pp, 512:1024],
                                       in0=obuf[:, pp, 512:1024],
                                       in1=xres[:, pp, 512:1024], op=Alu.add))
                ck("v_fin_%d" % pt_)

    def pd_scalar(e, ck, w):
        for tt in range(NT):
            w(e, "v_sh_%d" % tt, cV)
            if tt >= 1:
                w(e, "v_xh_%d" % (tt - 1), cV)
            cS.tag(e.activation(out=tq_sb[:, :], in_=hrb[:, tt % 2, :],
                                func=AFT.Copy, scale=sh_t[:, tt % 2:tt % 2 + 1],
                                bias=C_RND))
            ck("s_rh_%d" % tt)

    def pd_tensor(e, ck, w):
        for tt in range(NT):
            w(e, "v_xh_%d" % tt, cV)
            for rg in range(8):
                if rg >= 4:
                    w(e, "v_ev_%d_%d" % (tt, rg - 4), cV)
                elif tt >= 1:
                    w(e, "v_ev_%d_%d" % (tt - 1, rg + 4), cV)
                bnk = pb[4 + rg % 4]
                for u in range(4):
                    kc = 4 * rg + u
                    inst = e.matmul(bnk[:, 128 * u:128 * u + 128],
                                    xh_tm2[:, tt % 2, 128 * kc:128 * kc + 128],
                                    identb[:, :], start=True, stop=True)
                    if u == 3:
                        cT.tag(inst)
                ck("t_tr_%d_%d" % (tt, rg))
            w(e, "v_ev_%d_%d" % (tt, 7), cV)
            if tt >= 2:
                w(e, "v_fin_%d" % (tt - 2), cV)
            bd0, bd1 = pb[tt % 2], pb[2 + tt % 2]
            for kk in range(32):
                e.matmul(bd0[:, :], xh_fm[:, kk, :], wdq2[:, kk, 0:512],
                         start=(kk == 0), stop=(kk == 31))
                inst = e.matmul(bd1[:, :], xh_fm[:, kk, :], wdq2[:, kk, 512:1024],
                                start=(kk == 0), stop=(kk == 31))
                if kk == 31:
                    cT.tag(inst)
            ck("t_dn_%d" % tt)

    def pd_gp(e, ck, w):
        for tt in range(NT):
            w(e, "v_fin_%d" % tt, cV)
            dGp[tt % 2].tag(e.dma_start(out=y[128 * tt:128 * tt + 128, :],
                                        in_=obuf[:, tt % 2, :]), 16)
            ck("g_y_%d" % tt)
        e.wait_ge(dG.h, dG.n)
        e.wait_ge(dGb.h, dGb.n)

    run_phase("pd", [
        (blk.sync, pd_pre), (blk.sync, pd_sync), (blk.vector, pd_vec),
        (blk.scalar, pd_scalar), (blk.tensor, pd_tensor), (blk.gpsimd, pd_gp),
    ])

    es.close()
    return nc


def _masks_for_core(h):
    r = np.arange(128)
    c = np.arange(256)
    band = (c[None, :] >= r[:, None]) & (c[None, :] <= r[:, None] + 127)
    gen = np.where(band, 0.0, NEG).astype(np.float32)
    if h == 0:
        first = np.where(band & (c[None, :] >= 128), 0.0, NEG).astype(np.float32)
        lb = np.ones((128, 1), np.float32)
        lb[0, 0] = 0.0
        latb = np.ones((128, 1), np.float32)
        latb[0:64, 0] = 0.0
    else:
        first = gen.copy()
        lb = np.ones((128, 1), np.float32)
        latb = np.ones((128, 1), np.float32)
    lm = np.stack([np.concatenate([first, first], axis=1),
                   np.concatenate([gen, gen], axis=1)], axis=0)

    r0 = 2048 * h
    own_c0 = r0 // 64
    oth_c0 = (2048 - r0) // 64
    slot_chunk = np.concatenate([own_c0 + np.arange(32), oth_c0 + np.arange(32)])
    latm = np.empty((16, 128, 64), np.float32)
    for i in range(16):
        qchunk = (r0 + 128 * i + np.arange(128)) // 64
        allowed = slot_chunk[None, :] < qchunk[:, None]
        latm[i] = np.where(allowed, 0.0, NEG).astype(np.float32)
    return lm, latm, lb, latb


def _build_inputs(inputs):
    f = np.float32
    x = np.asarray(inputs["L0_attn_x"], f)
    qkv_w = np.ascontiguousarray(np.asarray(inputs["L0_attn_qkv_w"], f))
    qkv_b = np.asarray(inputs["L0_attn_qkv_b"], f)
    rq_w = np.ascontiguousarray(np.asarray(inputs["L0_attn_rq_w"], f))
    rq_b = np.asarray(inputs["L0_attn_rq_b"], f)
    rkv_w = np.ascontiguousarray(np.asarray(inputs["L0_attn_rkv_w"], f))
    rkv_b = np.asarray(inputs["L0_attn_rkv_b"], f)
    rout_w = np.ascontiguousarray(np.asarray(inputs["L0_attn_rout_w"], f))
    rout_b = np.asarray(inputs["L0_attn_rout_b"], f)
    out_w = np.ascontiguousarray(np.asarray(inputs["L0_attn_out_w"], f))
    out_b = np.asarray(inputs["L0_attn_out_b"], f)
    gl_w = np.asarray(inputs["L0_attn_gl_w"], f)
    gr_w = np.asarray(inputs["L0_attn_gr_w"], f)
    w_gate = np.asarray(inputs["L0_ffn_w_gate"], f)
    w_up = np.asarray(inputs["L0_ffn_w_up"], f)
    w_down = np.asarray(inputs["L0_ffn_w_down"], f)

    wgT = np.ascontiguousarray(w_gate.T)
    wuT = np.ascontiguousarray(w_up.T)
    wdT = np.ascontiguousarray(w_down.T)

    qkb_fm = np.ascontiguousarray(qkv_b[:2048].reshape(16, 128).T)
    vb_bc = np.ascontiguousarray(np.broadcast_to(qkv_b[2048:], (128, 1024)))
    rqb_fm = np.ascontiguousarray(rq_b.reshape(2, 128).T)
    rkvb_bc = np.ascontiguousarray(np.broadcast_to(rkv_b, (128, 512)))
    routb_fm = np.ascontiguousarray(rout_b.reshape(8, 128).T)
    outb_bc = np.ascontiguousarray(np.broadcast_to(out_b, (128, 1024)))

    glgr = np.zeros((128, 4), f)
    glgr[0:64, 0] = gl_w[:, 0]
    glgr[64:128, 1] = gl_w[:, 0]
    glgr[0:64, 2] = gr_w[:, 0]
    glgr[64:128, 3] = gr_w[:, 0]
    sel2 = np.zeros((128, 128), f)
    sel2[0, 0:64] = 1.0
    sel2[1, 64:128] = 1.0

    identb = np.eye(128, dtype=f).astype(ml_dtypes.bfloat16)
    poolz = np.zeros((128, 196), f)
    poolz[0:64, 64] = 1.0
    poolz[64:128, 65] = 1.0
    poolz = poolz.astype(ml_dtypes.bfloat16)

    shared = dict(qkv_w=qkv_w, rq_w=rq_w, rkv_w=rkv_w, rout_w=rout_w,
                  out_w=out_w, wgT=wgT, wuT=wuT, wdT=wdT,
                  qkb_fm=qkb_fm, vb_bc=vb_bc, rqb_fm=rqb_fm, rkvb_bc=rkvb_bc,
                  routb_fm=routb_fm, outb_bc=outb_bc,
                  glgr_in=glgr, sel2_in=sel2, identb_in=identb,
                  poolz_in=poolz)

    in_maps = []
    for core in range(8):
        b, h = core // 2, core % 2
        r0 = 2048 * h
        lm, latm, lb, latb = _masks_for_core(h)
        m = dict(shared)
        m["x_own"] = np.ascontiguousarray(x[b, r0:r0 + 2048])
        if h == 0:
            m["x_halo"] = np.zeros((128, 1024), f)
        else:
            m["x_halo"] = np.ascontiguousarray(x[b, r0 - 128:r0])
        m["x_other"] = np.ascontiguousarray(x[b, 2048 - r0:4096 - r0])
        m["lmask"] = lm
        m["latmask"] = latm
        m["lbin"] = lb
        m["latbin"] = latb
        in_maps.append(m)
    return in_maps


_nc_cache = [None]


def kernel(**inputs):
    in_maps = _build_inputs(inputs)
    if _nc_cache[0] is None:
        _nc_cache[0] = build_nc()
    nc = _nc_cache[0]
    res = run_bass_kernel_spmd(nc, in_maps, core_ids=list(range(8)))
    out = np.empty((B, S, D), np.float32)
    for core in range(8):
        b, h = core // 2, core % 2
        out[b, 2048 * h:2048 * h + 2048] = res.results[core]["y"]
    return out
